# version 04
# baseline (speedup 1.0000x reference)
import numpy as np
import ml_dtypes
from contextlib import ExitStack

import concourse.bass as bass
import concourse.mybir as mybir
from concourse.bass_utils import run_bass_kernel_spmd

F32 = mybir.dt.float32
BF16 = mybir.dt.bfloat16
AF = mybir.ActivationFunctionType
ALU = mybir.AluOpType
AX = mybir.AxisListType

B, T, D = 2, 8192, 1024
NCORES = 8
TC = 2048
HALO = 3
TCH = TC + HALO

BF = ml_dtypes.bfloat16


class Prog:
    def __init__(self, name="p"):
        self.nc = bass.Bass("TRN2", target_bir_lowering=False)
        self.es = ExitStack()
        self.sems = {}
        self.cnt = {}
        self.names = set()

    def _uniq(self, name):
        if name in self.names:
            k = 2
            while f"{name}_{k}" in self.names:
                k += 1
            name = f"{name}_{k}"
        self.names.add(name)
        return name

    def sem(self, tag):
        if tag not in self.sems:
            self.sems[tag] = self.es.enter_context(self.nc.semaphore(tag))
            self.cnt[tag] = 0
        return self.sems[tag]

    def inc(self, instr, tag, n=1):
        instr.then_inc(self.sem(tag), n)
        self.cnt[tag] += n
        return self.cnt[tag]

    def wait(self, eng, tag):
        if tag in self.sems and self.cnt[tag] > 0:
            eng.wait_ge(self.sems[tag], self.cnt[tag])

    def sb(self, name, shape, dt=F32, es=None):
        tgt = es or getattr(self, "default_es", None) or self.es
        return tgt.enter_context(self.nc.sbuf_tensor(self._uniq(name), shape, dt))

    def ps(self, name, shape, dt=F32, es=None):
        tgt = es or getattr(self, "default_es", None) or self.es
        return tgt.enter_context(self.nc.psum_tensor(self._uniq(name), shape, dt))

    def din(self, name, shape, dt=F32):
        self.names.add(name)
        return self.nc.dram_tensor(name, shape, dt, kind="ExternalInput")

    def dout(self, name, shape, dt=F32):
        self.names.add(name)
        return self.nc.dram_tensor(name, shape, dt, kind="ExternalOutput")



def emit_wT(p, nc, w_h, wT, ident, pst, es, ne=8, nd=8, tag="wt"):
    with ExitStack() as es2:
        wstg = p.sb(tag + "_stg", [128, nd * 128], F32, es=es2)
        wbf = p.sb(tag + "_bf", [128, nd * 128], BF16, es=es2)
        for ec in range(ne):
            with nc.Block() as blk:
                @blk.sync
                def _(sync, ec=ec):
                    p.inc(sync.dma_start(wstg[:, :], w_h[ec * 128:(ec + 1) * 128, :]), "ld", 16)

                @blk.vector
                def _(v):
                    p.wait(v, "ld")
                    v.tensor_copy(wbf[:], wstg[:])
            base = 8 * ec
            with nc.Block() as blk:
                @blk.tensor
                def _(t, ec=ec, base=base):
                    for n in range(nd):
                        if n >= 2:
                            t.wait_ge(p.sem(tag + "_ev"), base + n - 1)
                        t.transpose(pst[n % 2][:], wbf[:, n * 128:(n + 1) * 128],
                                    ident[:]).then_inc(p.sem(tag + "_t"), 1)
                        p.cnt[tag + "_t"] = p.cnt.get(tag + "_t", 0)

                @blk.vector
                def _(v, ec=ec, base=base):
                    for n in range(nd):
                        v.wait_ge(p.sem(tag + "_t"), base + n + 1)
                        v.tensor_copy(wT[:, n * (ne * 128) + ec * 128: n * (ne * 128) + ec * 128 + 128],
                                      pst[n % 2][:]).then_inc(p.sem(tag + "_ev"), 1)
            p.cnt[tag + "_t"] = p.cnt.get(tag + "_t", 0) + 8
            p.cnt[tag + "_ev"] = p.cnt.get(tag + "_ev", 0) + 8



def build_l0kv():
    p = Prog("l0kv")
    nc = p.nc
    x_h = p.din("x", [TC, D])
    wk_h = p.din("wk", [D, D])
    wv_h = p.din("wv", [D, D])
    id_h = p.din("identbf", [128, 128], BF16)
    kv_h = p.dout("kvpart", [D, D])

    with p.es:
        ident = p.sb("ident", [128, 128], BF16)
        scr = {"xt": p.sb("xt", [128, 1024]), "sq": p.sb("sq", [128, 1024]),
               "cols": p.sb("cols", [128, 8]), "rot": p.sb("rot", [128, 2048], BF16),
               "eps": p.sb("eps", [128, 1])}
        pst = [p.ps("pst%d" % z, [128, 128], BF16) for z in range(2)]
        psp = [p.ps("psp%d" % z, [128, 512]) for z in range(4)]
        xnT = p.sb("xnT", [128, 8 * 2048], BF16)
        ktok = p.sb("ktok", [128, 16 * 1024], BF16)
        vtok = p.sb("vtok", [128, 16 * 1024], BF16)

        with nc.Block() as blk:
            @blk.gpsimd
            def _(g):
                p.inc(g.dma_start(ident[:], id_h[:, :]), "gd", 16)
                g.memset(scr["eps"][:], 1e-5)
                g.wait_ge(p.sem("gd"), 16)
        with nc.Block() as blk:
            emit_ln_xnT_fixed(p, blk, x_h, [128] * 16, xnT, 2048, ident, pst, scr, tag="lnt")

        for wname, w_h in (("k", wk_h), ("v", wv_h)):
            wT = p.sb("w%sT" % wname, [128, 8 * 1024], BF16)
            emit_wT(p, nc, w_h, wT, ident, pst, p.es, tag="wt" + wname)
            tok = {"k": ktok, "v": vtok}[wname]
            with nc.Block() as blk:
                @blk.tensor
                def _(t, wT=wT, wname=wname):
                    n = 0
                    for j in range(16):
                        for half in range(2):
                            if n >= 2:
                                t.wait_ge(p.sem("pv" + wname), n - 1)
                            for dc in range(8):
                                ins = t.matmul(psp[n % 2][:],
                                               xnT[:, dc * 2048 + j * 128: dc * 2048 + j * 128 + 128],
                                               wT[:, dc * 1024 + half * 512: dc * 1024 + half * 512 + 512],
                                               start=(dc == 0), stop=(dc == 7))
                            p.inc(ins, "pe" + wname, 1)
                            n += 1

                @blk.vector
                def _(v, tok=tok, wname=wname):
                    n = 0
                    for j in range(16):
                        for half in range(2):
                            v.wait_ge(p.sem("pe" + wname), n + 1)
                            v.tensor_copy(tok[:, j * 1024 + half * 512: j * 1024 + half * 512 + 512],
                                          psp[n % 2][:]).then_inc(p.sem("pv" + wname), 1)
                            n += 1

        kvout = p.sb("kvout", [128, 8 * 1024], F32)
        with nc.Block() as blk:
            @blk.tensor
            def _(t):
                n = 0
                for kfc in range(8):
                    for half in range(2):
                        if n >= 4:
                            t.wait_ge(p.sem("kv_ev"), n - 3)
                        for j in range(16):
                            ins = t.matmul(psp[n % 4][:],
                                           ktok[:, j * 1024 + kfc * 128: j * 1024 + kfc * 128 + 128],
                                           vtok[:, j * 1024 + half * 512: j * 1024 + half * 512 + 512],
                                           start=(j == 0), stop=(j == 15))
                        p.inc(ins, "kv_pe", 1)
                        n += 1

            @blk.vector
            def _(v):
                n = 0
                for kfc in range(8):
                    for half in range(2):
                        v.wait_ge(p.sem("kv_pe"), n + 1)
                        v.tensor_copy(kvout[:, kfc * 1024 + half * 512: kfc * 1024 + half * 512 + 512],
                                      psp[n % 4][:]).then_inc(p.sem("kv_ev"), 1)
                        n += 1

            @blk.sync
            def _(sync):
                sync.wait_ge(p.sem("kv_ev"), 16)
                p.inc(sync.dma_start(kv_h[:, :].rearrange("(c p) n -> p c n", p=128),
                                     kvout[:, :].rearrange("p (c n) -> p c n", c=8)), "st", 16)
                sync.wait_ge(p.sem("st"), p.cnt["st"])
    return nc


def emit_ln_xnT_fixed(p, blk, x_h, rows, xnT, dst_cols, ident, pst, scr, tag,
                      xn_tok_slab=None, xn_cols=1024, xn_rot=0, xn_guard=None):
    nt = len(rows)
    xt, sq, cols_, rot = scr["xt"], scr["sq"], scr["cols"], scr["rot"]
    ld0 = p.cnt.get("ld", 0)

    @blk.sync
    def _(sync):
        r0 = 0
        for i in range(nt):
            r = rows[i]
            if i > 0:
                sync.wait_ge(p.sem(tag + "_ln"), i)
            p.inc(sync.dma_start(xt[0:r, :], x_h[r0:r0 + r, :]), "ld", 16)
            r0 += r

    @blk.scalar
    def _(s):
        for i in range(nt):
            r = rows[i]
            s.wait_ge(p.sem("ld"), ld0 + 16 * (i + 1))
            if i > 0:
                s.wait_ge(p.sem(tag + "_var"), i)
            s.activation(sq[0:r, :], xt[0:r, :], AF.Copy,
                         accum_out=cols_[0:r, 0:1]).then_inc(p.sem(tag + "_ac"), 1)
            s.activation(sq[0:r, :], xt[0:r, :], AF.Square,
                         accum_out=cols_[0:r, 1:2]).then_inc(p.sem(tag + "_ac"), 1)
            s.wait_ge(p.sem(tag + "_var"), i + 1)
            s.activation(cols_[0:r, 5:6], cols_[0:r, 4:5], AF.Sqrt,
                         bias=scr["eps"][0:r, 0:1]).then_inc(p.sem(tag + "_sq"), 1)

    @blk.vector
    def _(v):
        n = 0
        for i in range(nt):
            r = rows[i]
            v.wait_ge(p.sem(tag + "_ac"), 2 * (i + 1))
            v.tensor_scalar(cols_[0:r, 2:3], cols_[0:r, 0:1], 1.0 / 1024.0,
                            None, ALU.mult).then_inc(p.sem(tag + "_c"), 1)
            v.wait_ge(p.sem(tag + "_c"), 3 * i + 1)
            v.tensor_scalar(cols_[0:r, 3:4], cols_[0:r, 2:3], cols_[0:r, 2:3],
                            None, ALU.mult).then_inc(p.sem(tag + "_c"), 1)
            v.wait_ge(p.sem(tag + "_c"), 3 * i + 2)
            v.scalar_tensor_tensor(cols_[0:r, 4:5], cols_[0:r, 1:2], 1.0 / 1024.0,
                                   cols_[0:r, 3:4], ALU.mult, ALU.subtract).then_inc(
                p.sem(tag + "_var"), 1)
            v.wait_ge(p.sem(tag + "_sq"), i + 1)
            v.reciprocal(cols_[0:r, 6:7], cols_[0:r, 5:6]).then_inc(p.sem(tag + "_c"), 1)
            v.wait_ge(p.sem(tag + "_c"), 3 * i + 3)
            if i >= 2:
                v.wait_ge(p.sem(tag + "_t"), 8 * (i - 1))
            v.tensor_scalar(rot[0:r, (i % 2) * 1024:(i % 2) * 1024 + 1024], xt[0:r, :],
                            cols_[0:r, 2:3], cols_[0:r, 6:7], ALU.subtract, ALU.mult)
            if xn_tok_slab is not None:
                if xn_rot:
                    if i >= xn_rot and xn_guard is not None:
                        v.wait_ge(p.sem(xn_guard[0]), xn_guard[1] + 16 * (i - xn_rot + 1))
                    off = (i % xn_rot) * 1024
                else:
                    off = i * xn_cols
                ins = v.tensor_scalar(xn_tok_slab[0:r, off:off + 1024],
                                      xt[0:r, :], cols_[0:r, 2:3], cols_[0:r, 6:7],
                                      ALU.subtract, ALU.mult)
            else:
                ins = v.tensor_scalar(sq[0:r, 0:1], xt[0:r, 0:1],
                                      cols_[0:r, 2:3], cols_[0:r, 6:7], ALU.subtract, ALU.mult)
            p.inc(ins, tag + "_ln", 1)
            for c in range(8):
                v.wait_ge(p.sem(tag + "_t"), n + 1)
                v.tensor_copy(xnT[:, c * dst_cols + i * 128: c * dst_cols + i * 128 + r],
                              pst[n % 2][:, 0:r]).then_inc(p.sem(tag + "_ev"), 1)
                n += 1

    @blk.tensor
    def _(t):
        n = 0
        for i in range(nt):
            r = rows[i]
            t.wait_ge(p.sem(tag + "_ln"), i + 1)
            for c in range(8):
                if n >= 2:
                    t.wait_ge(p.sem(tag + "_ev"), n - 1)
                t.transpose(pst[n % 2][:, 0:r],
                            rot[0:r, (i % 2) * 1024 + c * 128:(i % 2) * 1024 + c * 128 + 128],
                            ident[0:r, 0:r]).then_inc(p.sem(tag + "_t"), 1)
                n += 1




GRP5 = [(0, 512), (512, 512), (1024, 512), (1536, 512), (2048, 3)]
ROWS17 = [128] * 16 + [3]


def emit_l0main(p, nc, h):
    x_h, wq_h, wg_h = h["x_h"], h["wq_h"], h["wg_h"]
    wl1_h, wl2_h, id_h, x1_h = h["wl1_h"], h["wl2_h"], h["id_h"], h["x1_h"]

    with ExitStack() as esPh:
        p.default_es = esPh
        ident = p.sb("ident", [128, 128], BF16, es=esPh)
        scr = {"xt": p.sb("xt", [128, 1024], es=esPh), "sq": p.sb("sq", [128, 1024], es=esPh),
               "cols": p.sb("cols", [128, 8], es=esPh), "rot": p.sb("rot", [128, 2048], BF16, es=esPh),
               "eps": p.sb("eps", [128, 1], es=esPh)}
        pst = [p.ps("pst%d" % z, [128, 128], BF16, es=esPh) for z in range(2)]
        psp = [p.ps("psp%d" % z, [128, 512], es=esPh) for z in range(4)]
        psy = [p.ps("psy%d" % z, [128, 512], es=esPh) for z in range(2)]
        xnT = p.sb("xnT", [128, 8 * TCH], BF16, es=esPh)
        y = p.sb("y", [128, 17 * 1024], BF16, es=esPh)
        gates = p.sb("gates", [128, 17 * 8], es=esPh)
        wgbf = p.sb("wgbf", [128, 64], BF16, es=esPh)

        with nc.Block() as blk:
            @blk.gpsimd
            def _(g):
                p.inc(g.dma_start(ident[:], id_h[:, :]), "gd", 16)
                g.memset(scr["eps"][:], 1e-5)
                g.wait_ge(p.sem("gd"), p.cnt["gd"])

        with nc.Block() as blk:
            emit_ln_xnT_fixed(p, blk, x_h, ROWS17, xnT, TCH, ident, pst, scr, tag="lnt0")

        if "kv_ar" in h:
            wk_h, wv_h, kv_in, kv_out = h["kv_ar"]
            with ExitStack() as esKV:
                ktok = p.sb("ktok", [128, 16 * 1024], BF16, es=esKV)
                vtok = p.sb("vtok", [128, 16 * 1024], BF16, es=esKV)
                for wname, w_h, tok in (("k", wk_h, ktok), ("v", wv_h, vtok)):
                    with ExitStack() as esW:
                        wT = p.sb("w%sT" % wname, [128, 8 * 1024], BF16, es=esW)
                        emit_wT(p, nc, w_h, wT, ident, pst, p.es, tag="wt" + wname)
                        with nc.Block() as blk:
                            @blk.tensor
                            def _(t, wT=wT, wname=wname):
                                n = 0
                                for j in range(16):
                                    for half in range(2):
                                        if n >= 2:
                                            t.wait_ge(p.sem("pv" + wname), n - 1)
                                        for dc in range(8):
                                            ins = t.matmul(
                                                psp[n % 2][:],
                                                xnT[:, dc * TCH + 3 + j * 128: dc * TCH + 3 + j * 128 + 128],
                                                wT[:, dc * 1024 + half * 512: dc * 1024 + half * 512 + 512],
                                                start=(dc == 0), stop=(dc == 7))
                                        p.inc(ins, "pe" + wname, 1)
                                        n += 1

                            @blk.vector
                            def _(v, tok=tok, wname=wname):
                                n = 0
                                for j in range(16):
                                    for half in range(2):
                                        v.wait_ge(p.sem("pe" + wname), n + 1)
                                        v.tensor_copy(
                                            tok[:, j * 1024 + half * 512: j * 1024 + half * 512 + 512],
                                            psp[n % 2][:]).then_inc(p.sem("pv" + wname), 1)
                                        n += 1
                with ExitStack() as esKO:
                    kvout = p.sb("kvout", [128, 8 * 1024], F32, es=esKO)
                    with nc.Block() as blk:
                        @blk.tensor
                        def _(t):
                            n = 0
                            for kfc in range(8):
                                for half in range(2):
                                    if n >= 4:
                                        t.wait_ge(p.sem("kv_ev"), n - 3)
                                    for j in range(16):
                                        ins = t.matmul(
                                            psp[n % 4][:],
                                            ktok[:, j * 1024 + kfc * 128: j * 1024 + kfc * 128 + 128],
                                            vtok[:, j * 1024 + half * 512: j * 1024 + half * 512 + 512],
                                            start=(j == 0), stop=(j == 15))
                                    p.inc(ins, "kv_pe", 1)
                                    n += 1

                        @blk.vector
                        def _(v):
                            n = 0
                            for kfc in range(8):
                                for half in range(2):
                                    v.wait_ge(p.sem("kv_pe"), n + 1)
                                    v.tensor_copy(
                                        kvout[:, kfc * 1024 + half * 512: kfc * 1024 + half * 512 + 512],
                                        psp[n % 4][:]).then_inc(p.sem("kv_ev"), 1)
                                    n += 1

                        @blk.sync
                        def _(sync):
                            sync.wait_ge(p.sem("kv_ev"), 16)
                            p.inc(sync.dma_start(kv_in[:, :].rearrange("(c p) n -> p c n", p=128),
                                                 kvout[:, :].rearrange("p (c n) -> p c n", c=8)), "st", 16)
                            sync.wait_ge(p.sem("st"), p.cnt["st"])
                    with nc.Block() as blk:
                        @blk.gpsimd
                        def _(g):
                            p.inc(g.collective_compute(
                                "AllReduce", ALU.add,
                                replica_groups=[[0, 1, 2, 3], [4, 5, 6, 7]],
                                ins=[kv_in.ap().opt()], outs=[kv_out.ap().opt()]), "cc", 1)
                            g.wait_ge(p.sem("cc"), p.cnt["cc"])

        with ExitStack() as esA:
            kvsum = p.sb("kvsum", [128, 8 * 1024], F32, es=esA)
            kvbf = p.sb("kvbf", [128, 8 * 1024], BF16, es=esA)
            qT = p.sb("qT", [128, 8 * TCH], BF16, es=esA)
            wqT = p.sb("wqT", [128, 8 * 1024], BF16, es=esA)
            wgstg = p.sb("wgstg", [128, 64], F32, es=esA)

            with nc.Block() as blk:
                @blk.gpsimd
                def _(g):
                    if "kvp_h" in h:
                        kvp_h = h["kvp_h"]
                        for part in range(4):
                            if part > 0:
                                g.wait_ge(p.sem("gd"), p.cnt["gd"])
                            ins = g.dma_start(
                                kvsum[:, :].rearrange("p (c n) -> p c n", c=8),
                                kvp_h[part, :, :].rearrange("(c p) n -> p c n", p=128),
                                **({} if part == 0 else {"accum_op": ALU.add}))
                            p.inc(ins, "gd", 16)
                    else:
                        kv_out = h["kv_ar"][3]
                        p.inc(g.dma_start(
                            kvsum[:, :].rearrange("p (c n) -> p c n", c=8),
                            kv_out[:, :].rearrange("(c p) n -> p c n", p=128)), "gd", 16)
                    g.wait_ge(p.sem("gd"), p.cnt["gd"])

                @blk.sync
                def _(sync):
                    p.inc(sync.dma_start(wgstg[:, :].rearrange("p (c n) -> p c n", c=8),
                                         wg_h[:, :].rearrange("(c p) n -> p c n", p=128)),
                          "ld", 16)

                @blk.vector
                def _(v):
                    p.wait(v, "ld")
                    v.tensor_copy(wgbf[:], wgstg[:])
            with nc.Block() as blk:
                @blk.vector
                def _(v):
                    v.tensor_copy(kvbf[:], kvsum[:])

            emit_wT(p, nc, wq_h, wqT, ident, pst, p.es, tag="wtq")

            with nc.Block() as blk:
                @blk.tensor
                def _(t):
                    n = 0
                    for ec in range(8):
                        for (goff, gw) in GRP5:
                            if n >= 2:
                                t.wait_ge(p.sem("q_ev"), n - 1)
                            for dc in range(8):
                                ins = t.matmul(psp[n % 2][:, 0:gw],
                                               wqT[:, dc * 1024 + ec * 128: dc * 1024 + ec * 128 + 128],
                                               xnT[:, dc * TCH + goff: dc * TCH + goff + gw],
                                               start=(dc == 0), stop=(dc == 7))
                            p.inc(ins, "q_pe", 1)
                            n += 1

                @blk.vector
                def _(v):
                    n = 0
                    for ec in range(8):
                        for (goff, gw) in GRP5:
                            v.wait_ge(p.sem("q_pe"), n + 1)
                            v.tensor_scalar(qT[:, ec * TCH + goff: ec * TCH + goff + gw],
                                            psp[n % 2][:, 0:gw], 1.0 / 32.0, None,
                                            ALU.mult).then_inc(p.sem("q_ev"), 1)
                            n += 1

            with nc.Block() as blk:
                @blk.tensor
                def _(t):
                    n = 0
                    for i in range(17):
                        r = ROWS17[i]
                        for half in range(2):
                            if n >= 2:
                                t.wait_ge(p.sem("at_ev"), n - 1)
                            for kfc in range(8):
                                ins = t.matmul(psp[n % 2][0:r, :],
                                               qT[:, kfc * TCH + i * 128: kfc * TCH + i * 128 + r],
                                               kvbf[:, kfc * 1024 + half * 512: kfc * 1024 + half * 512 + 512],
                                               start=(kfc == 0), stop=(kfc == 7))
                            p.inc(ins, "at_pe", 1)
                            n += 1

                @blk.vector
                def _(v):
                    n = 0
                    for i in range(17):
                        r = ROWS17[i]
                        for half in range(2):
                            v.wait_ge(p.sem("at_pe"), n + 1)
                            v.tensor_copy(y[0:r, i * 1024 + half * 512: i * 1024 + half * 512 + 512],
                                          psp[n % 2][0:r, :]).then_inc(p.sem("at_ev"), 1)
                            n += 1

            with nc.Block() as blk:
                @blk.tensor
                def _(t):
                    for i in range(17):
                        r = ROWS17[i]
                        if i > 0:
                            t.wait_ge(p.sem("g_ev"), i)
                        for dc in range(8):
                            ins = t.matmul(psy[0][0:r, 0:8],
                                           xnT[:, dc * TCH + i * 128: dc * TCH + i * 128 + r],
                                           wgbf[:, dc * 8: dc * 8 + 8],
                                           start=(dc == 0), stop=(dc == 7))
                        p.inc(ins, "g_pe", 1)

                @blk.vector
                def _(v):
                    for i in range(17):
                        r = ROWS17[i]
                        v.wait_ge(p.sem("g_pe"), i + 1)
                        v.reduce_max(scr["cols"][0:r, 7:8], psy[0][0:r, 0:7],
                                     AX.X).then_inc(p.sem("g_c"), 1)
                        v.wait_ge(p.sem("g_c"), 2 * i + 1)
                        v.tensor_scalar(gates[0:r, i * 8: i * 8 + 8], psy[0][0:r, 0:8],
                                        scr["cols"][0:r, 7:8], None,
                                        ALU.is_ge).then_inc(p.sem("g_c"), 1)
                        v.wait_ge(p.sem("g_c"), 2 * i + 2)
                        v.memset(gates[0:r, i * 8 + 7: i * 8 + 8], 1.0).then_inc(p.sem("g_ev"), 1)

        hslab = p.sb("hslab", [128, 16 * TCH], BF16, es=esPh)
        w2e = p.sb("w2e", [128, 16 * 1024], BF16, es=esPh)
        w2stg = [p.sb("w2stg%d" % z, [128, 1024], F32, es=esPh) for z in range(2)]
        w1stg = [p.sb("w1s%d" % z, [128, 1024], F32, es=esPh) for z in range(2)]
        w1a = [p.sb("w1a%d" % z, [128, 1024], BF16, es=esPh) for z in range(2)]
        w1b = [p.sb("w1b%d" % z, [128, 1024], BF16, es=esPh) for z in range(2)]
        asc = p.sb("asc", [128, 512], BF16, es=esPh)
        ssc = p.sb("ssc", [128, 512], BF16, es=esPh)
        for e in range(8):
            with nc.Block() as blk:
                gd0 = p.cnt.get("gd", 0)
                w2c0 = p.cnt.get("w2c", 0)

                @blk.gpsimd
                def _(g, e=e, gd0=gd0, w2c0=w2c0):
                    for j in range(16):
                        if j >= 2:
                            g.wait_ge(p.sem("w2c"), w2c0 + j - 1)
                        p.inc(g.dma_start(w2stg[j % 2][:, :],
                                          wl2_h[e, j * 128:(j + 1) * 128, :]), "gd", 16)

                @blk.vector
                def _(v, e=e, gd0=gd0, w2c0=w2c0):
                    for j in range(16):
                        v.wait_ge(p.sem("gd"), gd0 + 16 * (j + 1))
                        p.inc(v.tensor_copy(w2e[:, j * 1024:(j + 1) * 1024],
                                            w2stg[j % 2][:]), "w2c", 1)

            with nc.Block() as blk:
                ld0 = p.cnt.get("ld", 0)
                w1c0 = p.cnt.get("w1c", 0)
                up0 = p.cnt.get("up_g", 0)
                ha0 = p.cnt.get("h_a", 0)
                hs0 = p.cnt.get("h_s", 0)
                hh0 = p.cnt.get("h_h", 0)

                @blk.sync
                def _(sync, e=e, ld0=ld0, w1c0=w1c0):
                    for j in range(16):
                        if j >= 2:
                            sync.wait_ge(p.sem("w1c"), w1c0 + 2 * (j - 1))
                        p.inc(sync.dma_start(
                            w1stg[j % 2][:, :].rearrange("p (c n) -> p c n", c=8),
                            wl1_h[e, :, j * 128: j * 128 + 128].rearrange("(c p) n -> p c n", p=128)),
                            "ld", 16)
                        sync.wait_ge(p.sem("w1c"), w1c0 + 2 * j + 1)
                        p.inc(sync.dma_start(
                            w1stg[j % 2][:, :].rearrange("p (c n) -> p c n", c=8),
                            wl1_h[e, :, 2048 + j * 128: 2048 + j * 128 + 128].rearrange("(c p) n -> p c n", p=128)),
                            "ld", 16)

                @blk.vector
                def _(v, e=e, ld0=ld0, up0=up0, hs0=hs0, ha0=ha0, hh0=hh0):
                    for j in range(16):
                        v.wait_ge(p.sem("ld"), ld0 + 32 * j + 16)
                        if j >= 2:
                            v.wait_ge(p.sem("up_g"), up0 + 10 * (j - 1))
                        p.inc(v.tensor_copy(w1a[j % 2][:], w1stg[j % 2][:]), "w1c", 1)
                        v.wait_ge(p.sem("ld"), ld0 + 32 * (j + 1))
                        p.inc(v.tensor_copy(w1b[j % 2][:], w1stg[j % 2][:]), "w1c", 1)
                        for g5 in range(5):
                            idx = j * 5 + g5
                            gw = GRP5[g5][1]
                            v.wait_ge(p.sem("up_g"), up0 + 2 * idx + 1)
                            p.inc(v.tensor_copy(asc[:, 0:gw], psp[idx % 2][:, 0:gw]), "h_a", 1)
                            v.wait_ge(p.sem("h_s"), hs0 + idx + 1)
                            p.inc(v.tensor_tensor(
                                hslab[:, j * TCH + GRP5[g5][0]: j * TCH + GRP5[g5][0] + gw],
                                asc[:, 0:gw], ssc[:, 0:gw], ALU.mult), "h_h", 1)

                @blk.scalar
                def _(s, e=e, up0=up0, hh0=hh0):
                    for j in range(16):
                        for g5 in range(5):
                            idx = j * 5 + g5
                            gw = GRP5[g5][1]
                            s.wait_ge(p.sem("up_g"), up0 + 2 * idx + 2)
                            if idx >= 1:
                                s.wait_ge(p.sem("h_h"), hh0 + idx)
                            p.inc(s.activation(ssc[:, 0:gw], psp[2 + idx % 2][:, 0:gw],
                                               AF.Silu), "h_s", 1)

                @blk.tensor
                def _(t, e=e, w1c0=w1c0, ha0=ha0, hs0=hs0, up0=up0):
                    for j in range(16):
                        t.wait_ge(p.sem("w1c"), w1c0 + 2 * (j + 1))
                        for g5 in range(5):
                            idx = j * 5 + g5
                            goff, gw = GRP5[g5]
                            if idx >= 2:
                                t.wait_ge(p.sem("h_a"), ha0 + idx - 1)
                                t.wait_ge(p.sem("h_s"), hs0 + idx - 1)
                            for dc in range(8):
                                ins = t.matmul(psp[idx % 2][:, 0:gw],
                                               w1a[j % 2][:, dc * 128:(dc + 1) * 128],
                                               xnT[:, dc * TCH + goff: dc * TCH + goff + gw],
                                               start=(dc == 0), stop=(dc == 7))
                            p.inc(ins, "up_g", 1)
                            for dc in range(8):
                                ins = t.matmul(psp[2 + idx % 2][:, 0:gw],
                                               w1b[j % 2][:, dc * 128:(dc + 1) * 128],
                                               xnT[:, dc * TCH + goff: dc * TCH + goff + gw],
                                               start=(dc == 0), stop=(dc == 7))
                            p.inc(ins, "up_g", 1)

            with nc.Block() as blk:
                dn0 = p.cnt.get("dn_pe", 0)
                ye0 = p.cnt.get("y_ev", 0)

                @blk.tensor
                def _(t, e=e, dn0=dn0, ye0=ye0):
                    for i in range(17):
                        r = ROWS17[i]
                        if i >= 1:
                            t.wait_ge(p.sem("y_ev"), ye0 + 2 * i)
                        for j in range(16):
                            i0 = t.matmul(psy[0][0:r, :],
                                          hslab[:, j * TCH + i * 128: j * TCH + i * 128 + r],
                                          w2e[:, j * 1024: j * 1024 + 512],
                                          start=(j == 0), stop=(j == 15))
                            i1 = t.matmul(psy[1][0:r, :],
                                          hslab[:, j * TCH + i * 128: j * TCH + i * 128 + r],
                                          w2e[:, j * 1024 + 512: j * 1024 + 1024],
                                          start=(j == 0), stop=(j == 15))
                        p.inc(i0, "dn_pe", 1)
                        p.inc(i1, "dn_pe", 1)

                @blk.vector
                def _(v, e=e, dn0=dn0):
                    for i in range(17):
                        r = ROWS17[i]
                        for half in range(2):
                            v.wait_ge(p.sem("dn_pe"), dn0 + 2 * i + half + 1)
                            p.inc(v.tensor_scalar(asc[0:r, :], psy[half][0:r, :],
                                                  gates[0:r, i * 8 + e: i * 8 + e + 1], None,
                                                  ALU.mult), "y_c", 1)
                            v.wait_ge(p.sem("y_c"), p.cnt["y_c"])
                            p.inc(v.tensor_tensor(
                                y[0:r, i * 1024 + half * 512: i * 1024 + half * 512 + 512],
                                y[0:r, i * 1024 + half * 512: i * 1024 + half * 512 + 512],
                                asc[0:r, :], ALU.add), "y_ev", 1)

        with nc.Block() as blk:
            st0 = p.cnt.get("st", 0)
            ld0 = p.cnt.get("ld", 0)

            @blk.sync
            def _(sync, st0=st0, ld0=ld0):
                r0 = 0
                for i in range(17):
                    r = ROWS17[i]
                    if i >= 1:
                        sync.wait_ge(p.sem("st"), st0 + 16 * i)
                    p.inc(sync.dma_start(scr["xt"][0:r, :], x_h[r0:r0 + r, :]), "ld", 16)
                    sync.wait_ge(p.sem("fin"), i + 1)
                    p.inc(sync.dma_start(x1_h[r0:r0 + r, :], scr["xt"][0:r, :]), "st", 16)
                    r0 += r
                sync.wait_ge(p.sem("st"), p.cnt["st"])

            @blk.vector
            def _(v, ld0=ld0):
                for i in range(17):
                    r = ROWS17[i]
                    v.wait_ge(p.sem("ld"), ld0 + 16 * (i + 1))
                    p.inc(v.tensor_copy(scr["sq"][0:r, :],
                                        y[0:r, i * 1024: i * 1024 + 1024]), "fin_c", 1)
                    v.wait_ge(p.sem("fin_c"), p.cnt["fin_c"])
                    p.inc(v.tensor_tensor(scr["xt"][0:r, :], scr["xt"][0:r, :], scr["sq"][0:r, :],
                                          ALU.add), "fin", 1)
        p.default_es = None

def build_l0main():
    p = Prog("l0main")
    nc = p.nc
    h = {"x_h": p.din("x", [TCH, D]), "kvp_h": p.din("kvp", [4, D, D]),
         "wq_h": p.din("wq", [D, D]), "wg_h": p.din("wg", [D, 8]),
         "wl1_h": p.din("wl1", [8, D, 4096]), "wl2_h": p.din("wl2", [8, 2048, D]),
         "id_h": p.din("identbf", [128, 128], BF16), "x1_h": p.dout("x1", [TCH, D])}
    with p.es:
        emit_l0main(p, nc, h)
    return nc



class Chain:

    def __init__(self, p, tag="ck"):
        self.p = p
        self.tag = tag
        self.steps = []

    def add(self, eng, fn, inc=1):
        self.steps.append((eng, fn, inc))

    def dma(self, fn):
        self.add("sync", fn, 16)

    def emit(self, nc):
        p, tag = self.p, self.tag
        cum = []
        c = p.cnt.get(tag, 0)
        for (_, _, inc) in self.steps:
            cum.append(c)
            c += inc
        p.cnt[tag] = c
        sem = p.sem(tag)
        with nc.Block() as blk:
            for engname in ("tensor", "vector", "scalar", "sync", "gpsimd"):
                mysteps = [(i, f, inc) for i, (e, f, inc) in enumerate(self.steps) if e == engname]
                if not mysteps:
                    continue

                def body(eng, mysteps=mysteps):
                    for (i, f, inc) in mysteps:
                        if cum[i] > 0:
                            eng.wait_ge(sem, cum[i])
                        ins = f(eng)
                        ins.then_inc(sem, inc)
                getattr(blk, engname)(body)
            def fin(eng):
                eng.wait_ge(sem, c)
            blk.gpsimd(fin)


def emit_chunk_pass(p, nc, t, nchunks=16, nheads=16):
    ident, identf = t["ident"], t["identf"]
    maskD, maskG, onesr = t["maskD"], t["maskG"], t["onesr"]
    mdg = t["mdg"]
    psp, psx, psm, pst = t["psp"], t["psx"], t["psm"], t["pst"]
    q1c, k1c, v1c = t["q1c"], t["k1c"], t["v1c"]
    lnb, lnrq, lnrk, cpad = t["lnb"], t["lnrq"], t["lnrk"], t["cpad"]
    Sf, Sbf, Ncum, NcumT = t["Sf"], t["Sbf"], t["Ncum"], t["NcumT"]
    ot_h, z_h, pct_h, dscr_h = t["ot_h"], t["z_h"], t["pct_h"], t["dscr_h"]

    flat = p.sb("flat", [1, 4 * 2048])
    crel = p.sb("crel", [16, 128]); cprev = p.sb("cprev", [16, 128])
    biasQ1 = p.sb("biasQ1", [16, 128])
    rowD = p.sb("rowD", [16, 128]); rowG = p.sb("rowG", [16, 128])
    expQ3 = p.sb("expQ3", [16, 128]); expQ4r = p.sb("expQ4r", [16, 128])
    negQ2r = p.sb("negQ2r", [16, 128])
    bT = p.sb("bT", [128, 16]); negQ2T = p.sb("negQ2T", [128, 16]); q4T = p.sb("q4T", [128, 16])
    scrD = p.sb("scrD", [128, 128]); scrE = p.sb("scrE", [128, 128])
    Dbuf = p.sb("Dbuf", [128, 128]); Gbuf = p.sb("Gbuf", [128, 128], BF16)
    Ddg = p.sb("Ddg", [128, 128]); Doff = p.sb("Doff", [128, 128])
    pows = [p.sb("pow%d" % z, [128, 128]) for z in range(5)]
    ptmp = p.sb("ptmp", [128, 128])
    Xf = p.sb("Xf", [128, 128]); Yf = p.sb("Yf", [128, 128]); Xacc = p.sb("Xacc", [128, 128])
    Xbuf = p.sb("Xbuf", [128, 128], BF16)
    ktok = p.sb("ktok", [128, 64], BF16); vtok = p.sb("vtok", [128, 64], BF16)
    negk = p.sb("negk", [128, 128], BF16)
    ktb = p.sb("ktb", [128, 64], BF16)
    vsc = p.sb("vsc", [128, 64], BF16)
    qgbuf = p.sb("qgbuf", [128, 128], BF16); qbb2 = p.sb("qbb2", [128, 128], BF16)
    zbuf = p.sb("zbuf", [128, 128], BF16); zout = p.sb("zout", [64, 128], BF16)
    obuf = p.sb("obuf", [64, 128])
    pctbuf = p.sb("pctbuf", [64, 64], BF16)
    gcol = p.sb("gcol", [64, 1])

    ch = Chain(p)

    for ci in range(nchunks):
        c0 = ci * 128
        ch.add("vector", lambda v, c0=c0: v.tensor_scalar(
            crel[:], cpad[0:16, 1 + c0:1 + c0 + 128], cpad[0:16, c0:c0 + 1], None, ALU.subtract))
        ch.add("vector", lambda v, c0=c0: v.tensor_scalar(
            cprev[:], cpad[0:16, c0:c0 + 128], cpad[0:16, c0:c0 + 1], None, ALU.subtract))
        ch.add("vector", lambda v, c0=c0: v.tensor_tensor(
            biasQ1[:], lnb[0:16, c0:c0 + 128], crel[:], ALU.subtract))
        ch.add("vector", lambda v, c0=c0: v.tensor_tensor(
            biasQ1[:], biasQ1[:], lnrk[0:16, c0:c0 + 128], ALU.add))
        ch.add("vector", lambda v, c0=c0: v.tensor_tensor(
            rowD[:], cprev[:], lnrk[0:16, c0:c0 + 128], ALU.add))
        ch.add("vector", lambda v, c0=c0: v.tensor_tensor(
            rowG[:], crel[:], lnrq[0:16, c0:c0 + 128], ALU.add))
        ch.add("scalar", lambda s: s.activation(
            expQ4r[:], biasQ1[:], AF.Exp, bias=crel[0:16, 127:128]))
        ch.add("scalar", lambda s: s.activation(negQ2r[:], rowD[:], AF.Exp))
        ch.add("vector", lambda v: v.tensor_scalar(negQ2r[:], negQ2r[:], -1.0, None, ALU.mult))
        ch.add("scalar", lambda s: s.activation(expQ3[:], rowG[:], AF.Exp))
        for qi, buf in ((0, rowD), (1, rowG), (2, expQ3), (3, crel)):
            ch.dma(lambda sy, qi=qi, buf=buf: sy.dma_start(dscr_h[qi, :, :], buf[:]))
        for qi in range(4):
            ch.dma(lambda sy, qi=qi: sy.dma_start(
                flat[0:1, qi * 2048:(qi + 1) * 2048],
                dscr_h[qi:qi + 1, :, :].rearrange("a p n -> a (p n)")))
        for (srcb, dstb) in ((biasQ1, bT), (negQ2r, negQ2T), (expQ4r, q4T)):
            ch.add("tensor", lambda tt, srcb=srcb: tt.transpose(
                psm[:, 0:16], srcb[:], identf[0:16, 0:16]))
            ch.add("vector", lambda v, dstb=dstb: v.tensor_copy(dstb[:], psm[:, 0:16]))

        for h in range(nheads):
            hc, ho = h // 2, (h % 2) * 64
            kS = lambda sl=k1c, hc=hc, ho=ho, ci=ci: sl[ho:ho + 64, hc * 2048 + ci * 128: hc * 2048 + ci * 128 + 128]
            qS = lambda hc=hc, ho=ho, ci=ci: q1c[ho:ho + 64, hc * 2048 + ci * 128: hc * 2048 + ci * 128 + 128]
            vS = lambda hc=hc, ho=ho, ci=ci: v1c[ho:ho + 64, hc * 2048 + ci * 128: hc * 2048 + ci * 128 + 128]

            ch.add("tensor", lambda tt, kS=kS, ho=ho: tt.transpose(
                pst[0][:, 0:64], kS(), ident[ho:ho + 64, ho:ho + 64]))
            ch.add("vector", lambda v: v.tensor_copy(ktok[:], pst[0][:, 0:64]))
            ch.add("tensor", lambda tt, vS=vS, ho=ho: tt.transpose(
                pst[1][:, 0:64], vS(), ident[ho:ho + 64, ho:ho + 64]))
            ch.add("vector", lambda v: v.tensor_copy(vtok[:], pst[1][:, 0:64]))
            ch.add("vector", lambda v, kS=kS, ho=ho: v.tensor_scalar(
                negk[ho:ho + 64, :], kS(), -1.0, None, ALU.mult))
            ch.add("tensor", lambda tt, kS=kS, ho=ho: tt.matmul(
                psp[0][:, 0:128], negk[ho:ho + 64, :], kS(), start=True, stop=True, skip_group_check=True))
            ch.add("tensor", lambda tt, kS=kS, qS=qS: tt.matmul(
                psp[1][:, 0:128], kS(), qS(), start=True, stop=True, skip_group_check=True))
            ch.add("tensor", lambda tt, h=h: tt.matmul(
                psp[2][:, 0:128], onesr[0:1, :], flat[0:1, h * 128: h * 128 + 128],
                start=True, stop=True, skip_group_check=True))
            ch.add("vector", lambda v: v.tensor_tensor(scrD[:], psp[2][:, 0:128], maskD[:], ALU.add))
            ch.add("scalar", lambda s, h=h: s.activation(scrE[:], scrD[:], AF.Exp,
                                                         bias=bT[:, h:h + 1]))
            ch.add("vector", lambda v: v.tensor_tensor(Dbuf[:], scrE[:], psp[0][:, 0:128], ALU.mult))
            ch.add("tensor", lambda tt, h=h: tt.matmul(
                psp[2][:, 0:128], onesr[0:1, :], flat[0:1, 2048 + h * 128: 2048 + h * 128 + 128],
                start=True, stop=True, skip_group_check=True))
            ch.add("vector", lambda v: v.tensor_tensor(scrD[:], psp[2][:, 0:128], maskG[:], ALU.add))
            ch.add("scalar", lambda s, h=h: s.activation(scrE[:], scrD[:], AF.Exp,
                                                         bias=bT[:, h:h + 1]))
            ch.add("vector", lambda v: v.tensor_tensor(Gbuf[:], scrE[:], psp[1][:, 0:128], ALU.mult))
            ch.add("tensor", lambda tt, kS=kS, h=h, hc=hc, ho=ho: tt.matmul(
                psp[3][:, 0:64], kS(), Sbf[ho:ho + 64, hc * 64: hc * 64 + 64],
                start=True, stop=True, skip_group_check=True))
            ch.add("vector", lambda v, h=h: v.tensor_scalar(
                vsc[:], psp[3][:, 0:64], negQ2T[:, h:h + 1], None, ALU.mult))
            ch.add("vector", lambda v: v.tensor_tensor(Xf[:, 0:64], vsc[:], vtok[:], ALU.add))
            ch.add("vector", lambda v, h=h: v.tensor_scalar(
                Xf[:, 64:128], ktok[:], negQ2T[:, h:h + 1], None, ALU.mult))
            ch.add("vector", lambda v: v.tensor_tensor(Ddg[:], Dbuf[:], mdg[:], ALU.mult))
            ch.add("vector", lambda v: v.tensor_tensor(Doff[:], Dbuf[:], Ddg[:], ALU.subtract))
            ch.add("vector", lambda v: v.tensor_copy(pows[0][:], Ddg[:]))
            for kq in range(1, 5):
                ch.add("tensor", lambda tt, kq=kq: tt.transpose(psm[:] if False else psx[:],
                                                                pows[kq - 1][:], identf[:]))
                ch.add("vector", lambda v: v.tensor_copy(ptmp[:], psx[:]))
                ch.add("tensor", lambda tt, kq=kq: tt.matmul(
                    psm[:], ptmp[:], pows[kq - 1][:], start=True, stop=True, skip_group_check=True))
                ch.add("vector", lambda v, kq=kq: v.tensor_copy(pows[kq][:], psm[:]))

            def td_apply(ch):
                for kq in range(5):
                    ch.add("tensor", lambda tt, kq=kq: tt.matmul(
                        psm[:], pows[kq][:], Yf[:], start=True, stop=True, skip_group_check=True))
                    ch.add("vector", lambda v: v.tensor_tensor(Yf[:], Yf[:], psm[:], ALU.add))
            ch.add("vector", lambda v: v.tensor_copy(Yf[:], Xf[:]))
            td_apply(ch)
            ch.add("vector", lambda v: v.tensor_copy(Xacc[:], Yf[:]))
            for mh in range(3):
                ch.add("tensor", lambda tt: tt.matmul(psm[:], Doff[:], Yf[:],
                                                      start=True, stop=True, skip_group_check=True))
                ch.add("vector", lambda v: v.tensor_copy(Yf[:], psm[:]))
                td_apply(ch)
                ch.add("vector", lambda v: v.tensor_tensor(Xacc[:], Xacc[:], Yf[:], ALU.add))
            ch.add("vector", lambda v: v.tensor_copy(Xbuf[:], Xacc[:]))
            ch.add("tensor", lambda tt: tt.matmul(psp[2][0:64, 0:128], Xbuf[:, 64:128], Gbuf[:],
                                                  start=True, stop=True, skip_group_check=True))
            ch.add("tensor", lambda tt, h=h: tt.matmul(
                psp[3][0:64, 0:128], onesr[0:1, 0:64], flat[0:1, 2 * 2048 + h * 128: 2 * 2048 + h * 128 + 128],
                start=True, stop=True, skip_group_check=True))
            ch.add("vector", lambda v, ho=ho: v.tensor_copy(qbb2[ho:ho + 64, :], psp[3][0:64, 0:128]))
            ch.add("vector", lambda v, ho=ho, qS=qS: v.tensor_tensor(
                qgbuf[ho:ho + 64, :], qbb2[ho:ho + 64, :], qS(), ALU.mult))
            ch.add("tensor", lambda tt: tt.matmul(psp[0][0:64, 0:128], Xbuf[:, 0:64], Gbuf[:],
                                                  start=True, stop=False, skip_group_check=True))
            ch.add("tensor", lambda tt, h=h, hc=hc, ho=ho: tt.matmul(
                psp[0][0:64, 0:128], Sbf[ho:ho + 64, hc * 64: hc * 64 + 64], qgbuf[ho:ho + 64, :],
                start=False, stop=True, skip_group_check=True))
            ch.add("vector", lambda v: v.tensor_copy(obuf[:], psp[0][0:64, 0:128]))
            ch.dma(lambda sy, ci=ci, h=h: sy.dma_start(
                ot_h[h * 64:(h + 1) * 64, ci * 128:(ci + 1) * 128], obuf[:]))
            ch.add("vector", lambda v, ho=ho: v.tensor_copy(zbuf[ho:ho + 64, :], psp[2][0:64, 0:128]))
            ch.add("vector", lambda v, ho=ho: v.tensor_tensor(
                zout[:], zbuf[ho:ho + 64, :], qgbuf[ho:ho + 64, :], ALU.add))
            ch.dma(lambda sy, ci=ci, h=h: sy.dma_start(z_h[ci, h, :, :], zout[:]))
            ch.add("tensor", lambda tt, h=h: tt.matmul(
                psm[0:64, 0:1], onesr[0:1, 0:64], flat[0:1, 3 * 2048 + h * 128 + 127: 3 * 2048 + h * 128 + 128],
                start=True, stop=True, skip_group_check=True))
            ch.add("scalar", lambda s: s.activation(gcol[:], psm[0:64, 0:1], AF.Exp))
            ch.add("vector", lambda v, h=h: v.tensor_scalar(
                ktb[:], ktok[:], q4T[:, h:h + 1], None, ALU.mult))
            ch.add("tensor", lambda tt: tt.matmul(psp[1][0:64, 0:64], ktb[:], Xbuf[:, 0:64],
                                                  start=True, stop=True, skip_group_check=True))
            ch.add("vector", lambda v, h=h: v.scalar_tensor_tensor(
                Sf[0:64, h * 64: h * 64 + 64], Sf[0:64, h * 64: h * 64 + 64], gcol[0:64, 0:1],
                psp[1][0:64, 0:64], ALU.mult, ALU.add))
            ch.add("vector", lambda v, h=h, hc=hc, ho=ho: v.tensor_copy(
                Sbf[ho:ho + 64, hc * 64: hc * 64 + 64], Sf[0:64, h * 64: h * 64 + 64]))
            ch.add("tensor", lambda tt: tt.matmul(psp[2][0:64, 0:64], Xbuf[:, 64:128], ktb[:],
                                                  start=True, stop=True, skip_group_check=True))
            ch.add("vector", lambda v: v.scalar_tensor_tensor(
                pctbuf[:], identf[0:64, 0:64], gcol[0:64, 0:1], psp[2][0:64, 0:64],
                ALU.mult, ALU.add))
            ch.dma(lambda sy, ci=ci, h=h: sy.dma_start(pct_h[ci, h, :, :], pctbuf[:]))
            ch.add("tensor", lambda tt, h=h: tt.matmul(
                psp[3][0:64, 0:64], pctbuf[:], Ncum[0:64, h * 64: h * 64 + 64],
                start=True, stop=True, skip_group_check=True))
            ch.add("tensor", lambda tt, h=h: tt.matmul(
                psm[0:64, 64:128], Ncum[0:64, h * 64: h * 64 + 64], pctbuf[:],
                start=True, stop=True, skip_group_check=True))
            ch.add("vector", lambda v, h=h: v.tensor_copy(
                Ncum[0:64, h * 64: h * 64 + 64], psp[3][0:64, 0:64]))
            ch.add("vector", lambda v, h=h: v.tensor_copy(
                NcumT[0:64, h * 64: h * 64 + 64], psm[0:64, 64:128]))

    ch.emit(nc)



GRP4 = [(0, 512), (512, 512), (1024, 512), (1536, 512)]


def emit_l1loc(p, nc, h, nchunks=16, nheads=16):
    x_h = h["x1_h"]
    wq_h, wk_h, wv_h = h["wq1_h"], h["wk1_h"], h["wv1_h"]
    qcw_h, qcb_h = h["qcw_h"], h["qcb_h"]
    kcw_h, kcb_h = h["kcw_h"], h["kcb_h"]
    vcw_h, vcb_h = h["vcw_h"], h["vcb_h"]
    wa_h, ba_h = h["wa_h"], h["ba_h"]
    wb_h, bb_h = h["wb_h"], h["bb_h"]
    id_h, idf_h = h["id_h"], h["idf_h"]
    mD_h, mG_h, mdg_h = h["mD_h"], h["mG_h"], h["mdg_h"]
    e8_h = h["e8_h"]
    ot_h, z_h, pct_h = h["ot_h"], h["z_h"], h["pct_h"]
    dscr_h = h["dscr_h"]
    stex = h["stex"]
    with ExitStack() as esPh:
        p.default_es = esPh

        ident = p.sb("sident", [128, 128], BF16)
        identf = p.sb("sidentf", [128, 128], F32)
        maskD = p.sb("smaskD", [128, 128], F32)
        maskG = p.sb("smaskG", [128, 128], F32)
        mdg = p.sb("smdg", [128, 128], F32)
        e8 = p.sb("se8", [128, 8 * 16], BF16)
        onesr = p.sb("onesr", [1, 128], F32)
        epst = p.sb("eps", [128, 1])
        colst = p.sb("cols", [128, 8])
        lnb = p.sb("lnb", [16, 2048])
        lnrq = p.sb("lnrq", [16, 2048])
        lnrk = p.sb("lnrk", [16, 2048])
        cpad = p.sb("cpad", [16, 2049])
        zc16 = p.sb("zc16", [16, 1])
        q1c = p.sb("q1c", [128, 8 * 2048], BF16)
        k1c = p.sb("k1c", [128, 8 * 2048], BF16)
        v1c = p.sb("v1c", [128, 8 * 2048], BF16)
        Sf = p.sb("Sf", [64, 16 * 64])
        Sbf = p.sb("Sbf", [128, 8 * 64], BF16)
        Ncum = p.sb("Ncum", [64, 16 * 64], BF16)
        NcumT = p.sb("NcumT", [64, 16 * 64], BF16)
        psp = [p.ps("psp%d" % z, [128, 512]) for z in range(4)]
        psx = p.ps("psx", [128, 128])
        psm = p.ps("psm", [128, 128])
        pst = [p.ps("pst%d" % z, [128, 128], BF16) for z in range(2)]

        with nc.Block() as blk:
            @blk.gpsimd
            def _(g):
                for (dst, srch) in ((ident, id_h), (identf, idf_h), (maskD, mD_h), (maskG, mG_h), (mdg, mdg_h)):
                    p.inc(g.dma_start(dst[:], srch[:, :]), "gd", 16)
                p.inc(g.dma_start(e8[:, :].rearrange("p (e h) -> p e h", e=8),
                                  e8_h[:, :, :].rearrange("e p h -> p e h")), "gd", 16)
                g.memset(onesr[:], 1.0)
                g.memset(zc16[:], 0.0)
                g.memset(cpad[:, 0:1], 0.0)
                g.memset(epst[:], 1e-5)
                g.memset(Sf[:], 0.0)
                g.memset(Sbf[:], 0.0)
                g.wait_ge(p.sem("gd"), p.cnt["gd"])
            @blk.vector
            def _(v):
                pass

        with nc.Block() as blk:
            @blk.vector
            def _(v):
                for h in range(16):
                    v.tensor_copy(Ncum[0:64, h * 64:(h + 1) * 64], ident[0:64, 0:64])
                    v.tensor_copy(NcumT[0:64, h * 64:(h + 1) * 64], ident[0:64, 0:64])

        esP = ExitStack()
        xnT = p.sb("xnT", [128, 8 * TCH], BF16, es=esP)
        with ExitStack() as esLN:
            scr = {"xt": p.sb("xt", [128, 1024], es=esLN), "sq": p.sb("sq", [128, 1024], es=esLN),
                   "cols": colst, "rot": p.sb("rot", [128, 2048], BF16, es=esLN),
                   "eps": epst}
            with nc.Block() as blk:
                emit_ln_xnT_fixed(p, blk, x_h, ROWS17, xnT, TCH, ident, pst, scr, tag="lnt")

        with ExitStack() as esA:
            wab = p.sb("wab", [128, 8 * 16 * 2], BF16, es=esA)
            wstg = p.sb("wabstg", [128, 8 * 16], F32, es=esA)
            bac = p.sb("bac", [16, 1], F32, es=esA)
            bbc = p.sb("bbc", [16, 1], F32, es=esA)
            asig = p.sb("asig", [16, 2048], F32, es=esA)
            with nc.Block() as blk:
                @blk.sync
                def _(sync):
                    p.inc(sync.dma_start(wstg[:, :].rearrange("p (c n) -> p c n", c=8),
                                         wa_h[:, :].rearrange("(c p) n -> p c n", p=128)), "ld", 16)
                    sync.wait_ge(p.sem("ld"), p.cnt["ld"])
                    p.inc(sync.dma_start(bac[:], ba_h[:, :]), "ld", 16)
                    p.inc(sync.dma_start(bbc[:], bb_h[:, :]), "ld", 16)
                @blk.vector
                def _(v):
                    v.wait_ge(p.sem("ld"), p.cnt["ld"] - 32)
                    v.tensor_copy(wab[:, 0:128], wstg[:])
            with nc.Block() as blk:
                @blk.sync
                def _(sync):
                    p.inc(sync.dma_start(wstg[:, :].rearrange("p (c n) -> p c n", c=8),
                                         wb_h[:, :].rearrange("(c p) n -> p c n", p=128)), "ld", 16)
                @blk.vector
                def _(v):
                    p.wait(v, "ld")
                    v.tensor_copy(wab[:, 128:256], wstg[:])
            for which in range(2):
                with nc.Block() as blk:
                    pe0 = p.cnt.get("ab_pe", 0)
                    sg0 = p.cnt.get("ab_sg", 0)

                    @blk.tensor
                    def _(t, which=which, pe0=pe0, sg0=sg0):
                        for g4 in range(4):
                            goff, gw = GRP4[g4]
                            if g4 >= 2:
                                t.wait_ge(p.sem("ab_sg"), sg0 + g4 - 1)
                            for dc in range(8):
                                ins = t.matmul(psp[g4 % 2][0:16, :],
                                               wab[:, which * 128 + dc * 16: which * 128 + dc * 16 + 16],
                                               xnT[:, dc * TCH + 3 + goff: dc * TCH + 3 + goff + gw],
                                               start=(dc == 0), stop=(dc == 7))
                            p.inc(ins, "ab_pe", 1)

                    @blk.scalar
                    def _(s, which=which, pe0=pe0):
                        for g4 in range(4):
                            goff, gw = GRP4[g4]
                            s.wait_ge(p.sem("ab_pe"), pe0 + g4 + 1)
                            p.inc(s.activation((asig if which == 0 else lnb)[0:16, goff:goff + gw],
                                               psp[g4 % 2][0:16, :], AF.Sigmoid,
                                               bias=(bac if which == 0 else bbc)[0:16, 0:1]),
                                  "ab_sg", 1)
            with nc.Block() as blk:
                @blk.vector
                def _(v):
                    p.inc(v.tensor_scalar(asig[:], asig[:], 0.1, None, ALU.max), "c_c", 1)
                @blk.scalar
                def _(s):
                    s.wait_ge(p.sem("c_c"), p.cnt["c_c"])
                    p.inc(s.activation(cpad[0:16, 1:2049], asig[:], AF.Ln), "c_c", 1)
                    p.inc(s.activation(lnb[:], lnb[:], AF.Ln), "c_c", 1)
            with nc.Block() as blk:
                @blk.vector
                def _(v):
                    p.inc(v.tensor_tensor_scan(cpad[0:16, 1:2049],
                                               bass.AP(zc16, 0, [[1, 16], [0, 2048]]),
                                               cpad[0:16, 1:2049], 0.0, ALU.add, ALU.add), "c_c", 1)

        with ExitStack() as esB:
            wbf = p.sb("w1bf", [128, 8 * 1024], BF16, es=esB)
            wstg2 = p.sb("wstg2", [128, 1024], F32, es=esB)
            raw = [p.sb("raw%d" % z, [128, TCH], BF16, es=esB) for z in range(2)]
            cwc = p.sb("cwc", [128, 8 * 4], F32, es=esB)
            cbc = p.sb("cbc", [128, 8], F32, es=esB)
            acc = [p.sb("acc%d" % z, [128, 512], F32, es=esB) for z in range(2)]
            sqrot = raw[0]

            for pi, (w_h, cw_h, cb_h, outslab) in enumerate(
                    ((wq_h, qcw_h, qcb_h, q1c), (wk_h, kcw_h, kcb_h, k1c), (wv_h, vcw_h, vcb_h, v1c))):
                for dc in range(8):
                    with nc.Block() as blk:
                        @blk.sync
                        def _(sync, dc=dc, w_h=w_h):
                            p.inc(sync.dma_start(wstg2[:, :], w_h[dc * 128:(dc + 1) * 128, :]), "ld", 16)
                        @blk.vector
                        def _(v, dc=dc):
                            p.wait(v, "ld")
                            v.tensor_copy(wbf[:, dc * 1024:(dc + 1) * 1024], wstg2[:])
                with nc.Block() as blk:
                    @blk.sync
                    def _(sync, cw_h=cw_h, cb_h=cb_h):
                        p.inc(sync.dma_start(cwc[:, :].rearrange("p (c n) -> p c n", c=8),
                                             cw_h[:, :].rearrange("(c p) n -> p c n", p=128)), "ld", 16)
                        p.inc(sync.dma_start(cbc[:, :], cb_h[:, :]), "ld", 16)
                with nc.Block() as blk:
                    pe0 = p.cnt.get("pj_pe", 0)
                    ev0 = p.cnt.get("pj_ev", 0)
                    cv0 = p.cnt.get("cv_c", 0)
                    cvs0 = p.cnt.get("cv_s", 0)

                    @blk.tensor
                    def _(t, pe0=pe0, ev0=ev0, cvs0=cvs0):
                        n = 0
                        for ec in range(8):
                            if ec >= 2:
                                t.wait_ge(p.sem("cv_s"), cvs0 + 4 * (ec - 1))
                            for (goff, gw) in GRP5:
                                if n >= 2:
                                    t.wait_ge(p.sem("pj_ev"), ev0 + n - 1)
                                for dc in range(8):
                                    ins = t.matmul(psp[n % 2][:, 0:gw],
                                                   wbf[:, dc * 1024 + ec * 128: dc * 1024 + ec * 128 + 128],
                                                   xnT[:, dc * TCH + goff: dc * TCH + goff + gw],
                                                   start=(dc == 0), stop=(dc == 7))
                                p.inc(ins, "pj_pe", 1)
                                n += 1

                    @blk.vector
                    def _(v, pe0=pe0, cvs0=cvs0):
                        n = 0
                        for ec in range(8):
                            for gi, (goff, gw) in enumerate(GRP5):
                                v.wait_ge(p.sem("pj_pe"), pe0 + n + 1)
                                p.inc(v.tensor_copy(raw[ec % 2][:, goff: goff + gw],
                                                    psp[n % 2][:, 0:gw]), "pj_ev", 1)
                                n += 1
                            for g4 in range(4):
                                goff, gw = GRP4[g4]
                                m = ec * 4 + g4
                                if m >= 2:
                                    v.wait_ge(p.sem("cv_s"), cvs0 + m - 1)
                                v.tensor_scalar(acc[m % 2][:, :], raw[ec % 2][:, goff: goff + 512],
                                                cwc[:, ec * 4: ec * 4 + 1], None, ALU.mult)
                                for j in range(1, 4):
                                    ins = v.scalar_tensor_tensor(acc[m % 2][:, :],
                                                                 raw[ec % 2][:, goff + j: goff + j + 512],
                                                                 cwc[:, ec * 4 + j: ec * 4 + j + 1],
                                                                 acc[m % 2][:, :], ALU.mult, ALU.add)
                                p.inc(ins, "cv_c", 1)

                    @blk.scalar
                    def _(s, cv0=cv0, outslab=outslab):
                        for ec in range(8):
                            for g4 in range(4):
                                goff, gw = GRP4[g4]
                                m = ec * 4 + g4
                                s.wait_ge(p.sem("cv_c"), cv0 + m + 1)
                                p.inc(s.activation(outslab[:, ec * 2048 + goff: ec * 2048 + goff + 512],
                                                   acc[m % 2][:, :], AF.Silu,
                                                   bias=cbc[:, ec: ec + 1]), "cv_s", 1)
            for pi, (slab, lnout) in enumerate(((q1c, lnrq), (k1c, lnrk))):
                with nc.Block() as blk:
                    sq0 = p.cnt.get("rn_sq", 0)
                    pe0 = p.cnt.get("rn_pe", 0)
                    @blk.scalar
                    def _(s, slab=slab, sq0=sq0, pe0=pe0):
                        for dc in range(8):
                            if dc >= 1:
                                s.wait_ge(p.sem("rn_pe"), pe0 + 4 * dc)
                            p.inc(s.activation(sqrot[:, 0:2048], slab[:, dc * 2048:(dc + 1) * 2048],
                                               AF.Square), "rn_sq", 1)
                    @blk.tensor
                    def _(t, sq0=sq0):
                        for dc in range(8):
                            t.wait_ge(p.sem("rn_sq"), sq0 + dc + 1)
                            for g4 in range(4):
                                goff, gw = GRP4[g4]
                                ins = t.matmul(psp[g4][0:16, :], e8[:, dc * 16:(dc + 1) * 16],
                                               sqrot[:, goff:goff + 512],
                                               start=(dc == 0), stop=(dc == 7), skip_group_check=True)
                                p.inc(ins, "rn_pe", 1)
                with nc.Block() as blk:
                    @blk.scalar
                    def _(s, lnout=lnout):
                        for g4 in range(4):
                            goff, gw = GRP4[g4]
                            p.inc(s.activation(lnout[0:16, goff:goff + 512], psp[g4][0:16, :],
                                               AF.Sqrt), "rn_c", 1)
                with nc.Block() as blk:
                    @blk.vector
                    def _(v, lnout=lnout):
                        p.inc(v.tensor_scalar(lnout[:], lnout[:], 1e-6, None, ALU.max), "rn_c", 1)
                        v.wait_ge(p.sem("rn_c"), p.cnt["rn_c"])
                        p.inc(v.reciprocal(lnout[:], lnout[:]), "rn_c", 1)
                with nc.Block() as blk:
                    @blk.scalar
                    def _(s, lnout=lnout):
                        p.inc(s.activation(lnout[:], lnout[:], AF.Ln), "rn_c", 1)

        esP.close()
        emit_chunk_pass(p, nc, dict(
            q1c=q1c, k1c=k1c, v1c=v1c, lnb=lnb, lnrq=lnrq, lnrk=lnrk,
            cpad=cpad, Sf=Sf, Sbf=Sbf, Ncum=Ncum, NcumT=NcumT,
            ident=ident, identf=identf, maskD=maskD, maskG=maskG, mdg=mdg, onesr=onesr,
            psp=psp, psx=psx, psm=psm, pst=pst,
            ot_h=ot_h, z_h=z_h, pct_h=pct_h, dscr_h=dscr_h),
            nchunks=nchunks, nheads=nheads)

        if stex[0] == "out":
            sloc_h, pcore_h = stex[1], stex[2]
            with nc.Block() as blk:
                @blk.sync
                def _(sync):
                    p.inc(sync.dma_start(sloc_h[:, :], Sf[:, :]), "st", 16)
                    p.inc(sync.dma_start(pcore_h[:, :], NcumT[:, :]), "st", 16)
                    sync.wait_ge(p.sem("st"), p.cnt["st"])
            p.default_es = None
        else:
            ag_in, ag_out = stex[1], stex[2]
            sfb = p.sb("sfbf", [64, 16 * 64], BF16, es=esPh)
            with nc.Block() as blk:
                @blk.vector
                def _(v):
                    p.inc(v.tensor_copy(sfb[:, :], Sf[:, :]), "agc", 1)
                @blk.sync
                def _(sync):
                    sync.wait_ge(p.sem("agc"), p.cnt["agc"])
                    p.inc(sync.dma_start(ag_in[0:64, :], NcumT[:, :]), "st", 16)
                    p.inc(sync.dma_start(ag_in[64:128, :], sfb[:, :]), "st", 16)
                    sync.wait_ge(p.sem("st"), p.cnt["st"])
            with nc.Block() as blk:
                @blk.gpsimd
                def _(g):
                    p.inc(g.collective_compute(
                        "AllGather", ALU.bypass,
                        replica_groups=[[0, 1, 2, 3, 4, 5, 6, 7]],
                        ins=[ag_in.ap().opt()], outs=[ag_out.ap().opt()]), "cc", 1)
                    g.wait_ge(p.sem("cc"), p.cnt["cc"])
        p.default_es = None




def build_l1loc(nchunks=16, nheads=16):
    p = Prog("l1loc")
    nc = p.nc
    h = {"x1_h": p.din("x1", [TCH, D]),
         "wq1_h": p.din("wq1", [D, D]), "wk1_h": p.din("wk1", [D, D]), "wv1_h": p.din("wv1", [D, D]),
         "qcw_h": p.din("qcw", [D, 4]), "qcb_h": p.din("qcb", [128, 8]),
         "kcw_h": p.din("kcw", [D, 4]), "kcb_h": p.din("kcb", [128, 8]),
         "vcw_h": p.din("vcw", [D, 4]), "vcb_h": p.din("vcb", [128, 8]),
         "wa_h": p.din("wa", [D, 16]), "ba_h": p.din("ba", [16, 1]),
         "wb_h": p.din("wb", [D, 16]), "bb_h": p.din("bb", [16, 1]),
         "id_h": p.din("identbf", [128, 128], BF16), "idf_h": p.din("identf32", [128, 128], F32),
         "mD_h": p.din("maskD", [128, 128], F32), "mG_h": p.din("maskG", [128, 128], F32),
         "mdg_h": p.din("mdg", [128, 128], F32), "e8_h": p.din("e8", [8, 128, 16], BF16),
         "ot_h": p.dout("OT", [D, TC]), "z_h": p.dout("Z", [16, 16, 64, 128], BF16),
         "pct_h": p.dout("PcT", [16, 16, 64, 64], BF16),
         "dscr_h": nc.dram_tensor("dscr", [4, 16, 128], F32),
         "stex": ("out", None, None)}
    h["stex"] = ("out", p.dout("Sloc", [64, 16 * 64]), p.dout("PcoreT", [64, 16 * 64], BF16))
    with p.es:
        emit_l1loc(p, nc, h, nchunks, nheads)
    return nc



def emit_l1fix(p, nc, h):
    x_h, ot_h, z_h, pct_h = h["x1_h"], h["ot_h"], h["z_h"], h["pct_h"]
    rmsw_h = h["rmsw_h"]
    wgd_h, wgu_h, wo_h = h["wgd_h"], h["wgu_h"], h["wo_h"]
    wg8_h, wu8_h, wd8_h = h["wg8_h"], h["wu8_h"], h["wd8_h"]
    id_h, idf_h = h["id_h"], h["idf_h"]
    e8_h, e2t_h, e816_h = h["e8_h"], h["e2t_h"], h["e816_h"]
    x2_h = h["x2_h"]
    xn1_d, wqg_d, wqu_d, wqd_d, rr_d = h["xn1_d"], h["wqg_d"], h["wqu_d"], h["wqd_d"], h["rr_d"]
    pred = h["pred"]
    with ExitStack() as esFix:
        p.default_es = esFix

        ident = p.sb("sident", [128, 128], BF16)
        identf = p.sb("sidentf", [128, 128], F32)
        e8 = p.sb("se8", [128, 8 * 16], BF16)
        e2t = p.sb("se2t", [2, 128], F32)
        e816 = p.sb("se816", [8, 128], F32)
        rmsw = p.sb("srmsw", [128, 8], F32)
        epst = p.sb("eps", [128, 1])
        colst = p.sb("cols", [128, 8])
        y1 = p.sb("y1", [128, 16 * 1024], BF16)
        psp = [p.ps("psp%d" % z, [128, 512]) for z in range(4)]
        psx = p.ps("psx", [128, 128])
        psm = p.ps("psm", [128, 128])
        pst = [p.ps("pst%d" % z, [128, 128], BF16) for z in range(2)]
        zb = p.sb("zbl", [64, 128], BF16)
        pcb = p.sb("pcbl", [64, 64], BF16)
        rr2 = p.sb("rr2", [2, 128], F32)

        with nc.Block() as blk:
            @blk.gpsimd
            def _(g):
                for (dst, srch) in ((ident, id_h), (identf, idf_h), (e2t, e2t_h),
                                    (e816, e816_h), (rmsw, rmsw_h)):
                    p.inc(g.dma_start(dst[:], srch[:, :]), "gd", 16)
                p.inc(g.dma_start(e8[:, :].rearrange("p (e h) -> p e h", e=8),
                                  e8_h[:, :, :].rearrange("e p h -> p e h")), "gd", 16)
                g.memset(epst[:], 1e-5)
                g.wait_ge(p.sem("gd"), p.cnt["gd"])

        esP = ExitStack()
        xnT = p.sb("xnT", [128, 8 * TCH], BF16, es=esP)
        oslab = p.sb("oslab", [128, 8 * 2048], BF16, es=esP)
        grw = p.sb("grw", [128, 8 * 2048], BF16, es=esP)
        DS = p.sb("DSl", [64, 16 * 128], BF16, es=esP)
        Sin = p.sb("Sinl", [64, 16 * 64], BF16, es=esP)
        npred = 3 if pred[0] == "input" else 6
        predp = p.sb("predp", [64, npred * 1024], BF16, es=esP)
        preds = p.sb("preds", [64, npred * 1024], BF16 if pred[0] == "ag" else F32, es=esP)
        hmask = p.sb("hmask", [64, 8], es=esP)
        himask = p.sb("himask", [64, 6 * 64], BF16, es=esP)
        with nc.Block() as blk:
            @blk.gpsimd
            def _(g):
                if pred[0] == "input":
                    predp_h, preds_h = pred[1], pred[2]
                    p.inc(g.dma_start(predp[:, 0:3 * 1024].rearrange("p (l n) -> p l n", l=3),
                                      predp_h[:, :, :].rearrange("l p n -> p l n")), "gd", 16)
                    p.inc(g.dma_start(preds[:, 0:3 * 1024].rearrange("p (l n) -> p l n", l=3),
                                      preds_h[:, :, :].rearrange("l p n -> p l n")), "gd", 16)
                else:
                    ag_out, hmask_h, himask_h = pred[1], pred[2], pred[3]
                    for l, s in enumerate((0, 1, 2, 4, 5, 6)):
                        p.inc(g.dma_start(predp[:, l * 1024:(l + 1) * 1024],
                                          ag_out[s * 128: s * 128 + 64, :]), "gd", 16)
                        p.inc(g.dma_start(preds[:, l * 1024:(l + 1) * 1024],
                                          ag_out[s * 128 + 64: s * 128 + 128, :]), "gd", 16)
                    p.inc(g.dma_start(hmask[:, 0:6], hmask_h[:, :]), "gd", 16)
                    p.inc(g.dma_start(himask[:, :], himask_h[:, :]), "gd", 16)
                g.memset(DS[:], 0.0)
                g.memset(Sin[:], 0.0)
                g.wait_ge(p.sem("gd"), p.cnt["gd"])
        with ExitStack() as esLN:
            scr = {"xt": p.sb("xt", [128, 1024], es=esLN), "sq": p.sb("sq", [128, 1024], es=esLN),
                   "cols": colst, "rot": p.sb("rot", [128, 2048], BF16, es=esLN),
                   "eps": epst}
            xnf = p.sb("xnf", [128, 2 * 1024], F32, es=esLN)
            gd0 = p.cnt.get("gd", 0)
            with nc.Block() as blk:
                emit_ln_xnT_fixed(p, blk, x_h, ROWS17, xnT, TCH, ident, pst, scr, tag="lnt2",
                                  xn_tok_slab=xnf, xn_rot=2, xn_guard=("gd", gd0))

                @blk.gpsimd
                def _(g, gd0=gd0):
                    r0 = 0
                    for i in range(17):
                        r = ROWS17[i]
                        g.wait_ge(p.sem("lnt2_ln"), i + 1)
                        p.inc(g.dma_start(xn1_d[r0:r0 + r, :], xnf[0:r, (i % 2) * 1024:(i % 2) * 1024 + 1024]), "gd", 16)
                        r0 += r

        with ExitStack() as esO:
            ostg = p.sb("ostg", [128, 2048], F32, es=esO)
            for hc in range(8):
                with nc.Block() as blk:
                    @blk.sync
                    def _(sync, hc=hc):
                        p.inc(sync.dma_start(ostg[:, :], ot_h[hc * 128:(hc + 1) * 128, :]), "ld", 16)
                    @blk.vector
                    def _(v, hc=hc):
                        p.wait(v, "ld")
                        v.tensor_copy(oslab[:, hc * 2048:(hc + 1) * 2048], ostg[:])

        ch = Chain(p, tag="fx")
        pb1 = p.sb("pb1", [64, 64], BF16, es=esP)
        sb1 = p.sb("sb1", [64, 64], es=esP)
        for hh in range(16):
            for l in range(npred):
                if pred[0] == "ag":
                    ch.add("vector", lambda v, hh=hh, l=l: v.tensor_scalar(
                        pb1[:, :], predp[0:64, l * 1024 + hh * 64: l * 1024 + hh * 64 + 64],
                        hmask[0:64, l:l + 1], None, ALU.mult))
                    ch.add("vector", lambda v, l=l: v.tensor_tensor(
                        pb1[:, :], pb1[:, :], himask[0:64, l * 64:(l + 1) * 64], ALU.add))
                    ch.add("vector", lambda v, hh=hh, l=l: v.tensor_scalar(
                        sb1[:, :], preds[0:64, l * 1024 + hh * 64: l * 1024 + hh * 64 + 64],
                        hmask[0:64, l:l + 1], None, ALU.mult))
                    ch.add("tensor", lambda tt, hh=hh: tt.matmul(
                        psx[0:64, 0:64], pb1[:, :], Sin[0:64, hh * 64: hh * 64 + 64],
                        start=True, stop=True, skip_group_check=True))
                    ch.add("vector", lambda v, hh=hh: v.tensor_tensor(
                        Sin[0:64, hh * 64: hh * 64 + 64], sb1[:, :], psx[0:64, 0:64], ALU.add))
                else:
                    ch.add("tensor", lambda tt, hh=hh, l=l: tt.matmul(
                        psx[0:64, 0:64], predp[0:64, l * 1024 + hh * 64: l * 1024 + hh * 64 + 64],
                        Sin[0:64, hh * 64: hh * 64 + 64], start=True, stop=True, skip_group_check=True))
                    ch.add("vector", lambda v, hh=hh, l=l: v.tensor_tensor(
                        Sin[0:64, hh * 64: hh * 64 + 64],
                        preds[0:64, l * 1024 + hh * 64: l * 1024 + hh * 64 + 64],
                        psx[0:64, 0:64], ALU.add))
            ho = (hh % 2) * 64
            ch.add("vector", lambda v, hh=hh, ho=ho: v.tensor_copy(
                DS[0:64, hh * 128 + ho: hh * 128 + ho + 64], Sin[0:64, hh * 64: hh * 64 + 64]))
        for ci in range(16):
            for h in range(16):
                hc, ho = h // 2, (h % 2) * 64
                ch.dma(lambda sy, ci=ci, h=h: sy.dma_start(zb[:, :], z_h[ci, h, :, :]))
                ch.dma(lambda sy, ci=ci, h=h: sy.dma_start(pcb[:, :], pct_h[ci, h, :, :]))
                ch.add("tensor", lambda tt, h=h: tt.matmul(
                    psm[:], DS[0:64, h * 128: h * 128 + 128], zb[:, :],
                    start=True, stop=True, skip_group_check=True))
                ch.add("vector", lambda v, ci=ci, hc=hc: v.tensor_tensor(
                    oslab[:, hc * 2048 + ci * 128: hc * 2048 + ci * 128 + 128],
                    oslab[:, hc * 2048 + ci * 128: hc * 2048 + ci * 128 + 128],
                    psm[:], ALU.add))
                ch.add("tensor", lambda tt, h=h: tt.matmul(
                    psx[0:64, 0:128], pcb[:, :], DS[0:64, h * 128: h * 128 + 128],
                    start=True, stop=True, skip_group_check=True))
                ch.add("vector", lambda v, h=h: v.tensor_copy(
                    DS[0:64, h * 128: h * 128 + 128], psx[0:64, 0:128]))
        ch.emit(nc)

        with ExitStack() as esR:
            sqs = p.sb("sqs", [128, 2048], BF16, es=esR)
            rrow = p.sb("rrow", [16, 2048], F32, es=esR)
            with nc.Block() as blk:
                sq0 = p.cnt.get("rm_sq", 0)
                pe0 = p.cnt.get("rm_pe", 0)
                @blk.scalar
                def _(s, sq0=sq0, pe0=pe0):
                    for hc in range(8):
                        if hc >= 1:
                            s.wait_ge(p.sem("rm_pe"), pe0 + 4 * hc)
                        p.inc(s.activation(sqs[:, :], oslab[:, hc * 2048:(hc + 1) * 2048],
                                           AF.Square), "rm_sq", 1)
                @blk.tensor
                def _(t, sq0=sq0):
                    for hc in range(8):
                        t.wait_ge(p.sem("rm_sq"), sq0 + hc + 1)
                        for g4 in range(4):
                            goff = g4 * 512
                            ins = t.matmul(psp[g4][0:16, :], e8[:, hc * 16:(hc + 1) * 16],
                                           sqs[:, goff:goff + 512],
                                           start=(hc == 0), stop=(hc == 7), skip_group_check=True)
                            p.inc(ins, "rm_pe", 1)
            with nc.Block() as blk:
                @blk.vector
                def _(v):
                    for g4 in range(4):
                        goff = g4 * 512
                        p.inc(v.tensor_scalar(rrow[0:16, goff:goff + 512], psp[g4][0:16, :],
                                              1.0 / 64.0, 1e-6, ALU.mult, ALU.add), "rm_c", 1)
            with nc.Block() as blk:
                @blk.scalar
                def _(s):
                    p.inc(s.activation(rrow[:], rrow[:], AF.Sqrt), "rm_c", 1)
            with nc.Block() as blk:
                @blk.vector
                def _(v):
                    p.inc(v.reciprocal(rrow[:], rrow[:]), "rm_c", 1)
            with nc.Block() as blk:
                @blk.sync
                def _(sync):
                    p.inc(sync.dma_start(rr_d[:, :], rrow[:]), "st", 16)
                    sync.wait_ge(p.sem("st"), p.cnt["st"])

        with ExitStack() as esG:
            uT = p.sb("uT", [128, 4 * 2048], BF16, es=esG)
            wstg = p.sb("wstg", [128, 1024], F32, es=esG)
            wbf = p.sb("wbf", [128, 8 * 1024], BF16, es=esG)
            for dc in range(8):
                with nc.Block() as blk:
                    @blk.sync
                    def _(sync, dc=dc):
                        p.inc(sync.dma_start(wstg[:, 0:512], wgd_h[dc * 128:(dc + 1) * 128, :]), "ld", 16)
                    @blk.vector
                    def _(v, dc=dc):
                        p.wait(v, "ld")
                        v.tensor_copy(wbf[:, dc * 512: dc * 512 + 512], wstg[:, 0:512])
            with nc.Block() as blk:
                pe0 = p.cnt.get("ug_pe", 0)
                @blk.tensor
                def _(t, pe0=pe0):
                    n = 0
                    for uc in range(4):
                        for g4 in range(4):
                            goff = g4 * 512
                            if n >= 2:
                                t.wait_ge(p.sem("ug_ev"), p.cnt.get("ug_ev", 0) - 16 + n - 1) if False else \
                                    t.wait_ge(p.sem("ug_ev"), n - 1)
                            for dc in range(8):
                                ins = t.matmul(psp[n % 2][:, 0:512],
                                               wbf[:, dc * 512 + uc * 128: dc * 512 + uc * 128 + 128],
                                               xnT[:, dc * TCH + 3 + goff: dc * TCH + 3 + goff + 512],
                                               start=(dc == 0), stop=(dc == 7))
                            p.inc(ins, "ug_pe", 1)
                            n += 1
                @blk.scalar
                def _(s, pe0=pe0):
                    n = 0
                    for uc in range(4):
                        for g4 in range(4):
                            goff = g4 * 512
                            s.wait_ge(p.sem("ug_pe"), pe0 + n + 1)
                            p.inc(s.activation(uT[:, uc * 2048 + goff: uc * 2048 + goff + 512],
                                               psp[n % 2][:, 0:512], AF.Silu), "ug_ev", 1)
                            n += 1
            for uc2 in range(4):
                with nc.Block() as blk:
                    @blk.sync
                    def _(sync, uc2=uc2):
                        p.inc(sync.dma_start(wstg[:, :], wgu_h[uc2 * 128:(uc2 + 1) * 128, :]), "ld", 16)
                    @blk.vector
                    def _(v, uc2=uc2):
                        p.wait(v, "ld")
                        v.tensor_copy(wbf[:, uc2 * 1024:(uc2 + 1) * 1024], wstg[:])
            with nc.Block() as blk:
                pe0 = p.cnt.get("gg_pe", 0)
                sg0 = p.cnt.get("gg_sg", 0)
                @blk.tensor
                def _(t, pe0=pe0, sg0=sg0):
                    n = 0
                    for gc in range(8):
                        for g4 in range(4):
                            goff = g4 * 512
                            if n >= 2:
                                t.wait_ge(p.sem("gg_ev"), n - 1)
                            for uc in range(4):
                                ins = t.matmul(psp[n % 2][:, 0:512],
                                               wbf[:, uc * 1024 + gc * 128: uc * 1024 + gc * 128 + 128],
                                               uT[:, uc * 2048 + goff: uc * 2048 + goff + 512],
                                               start=(uc == 0), stop=(uc == 3))
                            p.inc(ins, "gg_pe", 1)
                            n += 1
                @blk.scalar
                def _(s, pe0=pe0):
                    n = 0
                    for gc in range(8):
                        for g4 in range(4):
                            goff = g4 * 512
                            s.wait_ge(p.sem("gg_pe"), pe0 + n + 1)
                            p.inc(s.activation(grw[:, gc * 2048 + goff: gc * 2048 + goff + 512],
                                               psp[n % 2][:, 0:512], AF.Sigmoid), "gg_sg", 1)
                            n += 1
                @blk.vector
                def _(v, sg0=sg0):
                    n = 0
                    for gc in range(8):
                        for g4 in range(4):
                            goff = g4 * 512
                            v.wait_ge(p.sem("gg_sg"), sg0 + n + 1)
                            p.inc(v.tensor_scalar(grw[:, gc * 2048 + goff: gc * 2048 + goff + 512],
                                                  grw[:, gc * 2048 + goff: gc * 2048 + goff + 512],
                                                  rmsw[:, gc: gc + 1], None, ALU.mult), "gg_ev", 1)
                            n += 1

        ch2 = Chain(p, tag="ra")
        for ci in range(16):
            for hc in range(8):
                ch2.dma(lambda sy, ci=ci, hc=hc: sy.dma_start(
                    rr2[:, :], rr_d[2 * hc: 2 * hc + 2, ci * 128:(ci + 1) * 128]))
                ch2.add("tensor", lambda tt: tt.matmul(
                    psm[:], e2t[:, :], rr2[:, :], start=True, stop=True, skip_group_check=True))
                ch2.add("vector", lambda v, ci=ci, hc=hc: v.tensor_tensor(
                    oslab[:, hc * 2048 + ci * 128: hc * 2048 + ci * 128 + 128],
                    oslab[:, hc * 2048 + ci * 128: hc * 2048 + ci * 128 + 128],
                    psm[:], ALU.mult))
                ch2.add("vector", lambda v, ci=ci, hc=hc: v.tensor_tensor(
                    oslab[:, hc * 2048 + ci * 128: hc * 2048 + ci * 128 + 128],
                    oslab[:, hc * 2048 + ci * 128: hc * 2048 + ci * 128 + 128],
                    grw[:, hc * 2048 + ci * 128: hc * 2048 + ci * 128 + 128], ALU.mult))
        ch2.emit(nc)

        with ExitStack() as esW:
            wstg = p.sb("wstg2", [128, 1024], F32, es=esW)
            wbf = p.sb("wobf", [128, 8 * 1024], BF16, es=esW)
            for hc in range(8):
                with nc.Block() as blk:
                    @blk.sync
                    def _(sync, hc=hc):
                        p.inc(sync.dma_start(wstg[:, :], wo_h[hc * 128:(hc + 1) * 128, :]), "ld", 16)
                    @blk.vector
                    def _(v, hc=hc):
                        p.wait(v, "ld")
                        v.tensor_copy(wbf[:, hc * 1024:(hc + 1) * 1024], wstg[:])
            with nc.Block() as blk:
                pe0 = p.cnt.get("wo_pe", 0)
                @blk.tensor
                def _(t, pe0=pe0):
                    n = 0
                    for i in range(16):
                        for half in range(2):
                            if n >= 2:
                                t.wait_ge(p.sem("wo_ev"), n - 1)
                            for hc in range(8):
                                ins = t.matmul(psp[n % 2][:, 0:512],
                                               oslab[:, hc * 2048 + i * 128: hc * 2048 + i * 128 + 128],
                                               wbf[:, hc * 1024 + half * 512: hc * 1024 + half * 512 + 512],
                                               start=(hc == 0), stop=(hc == 7))
                            p.inc(ins, "wo_pe", 1)
                            n += 1
                @blk.vector
                def _(v, pe0=pe0):
                    n = 0
                    for i in range(16):
                        for half in range(2):
                            v.wait_ge(p.sem("wo_pe"), pe0 + n + 1)
                            p.inc(v.tensor_copy(y1[:, i * 1024 + half * 512: i * 1024 + half * 512 + 512],
                                                psp[n % 2][:, 0:512]), "wo_ev", 1)
                            n += 1
        esP.close()

        ffn_bufs = emit_ffn(p, nc, dict(ident=ident, identf=identf, e816=e816,
                                        psp=psp, psx=psx, psm=psm, pst=pst,
                                        y1=y1, xn1_d=xn1_d, wg8_h=wg8_h, wu8_h=wu8_h, wd8_h=wd8_h,
                                        wqg_d=wqg_d, wqu_d=wqu_d, wqd_d=wqd_d, epst=epst))

        with nc.Block() as blk:
            xo = ffn_bufs["xstg"]
            yo = ffn_bufs["wtile"][:, 0:1024] if False else ffn_bufs["wtile"]
            st0 = p.cnt.get("st", 0)
            ld0 = p.cnt.get("ld", 0)

            @blk.sync
            def _(sync, st0=st0, ld0=ld0):
                for i in range(16):
                    if i >= 1:
                        sync.wait_ge(p.sem("st"), st0 + 16 * i)
                    p.inc(sync.dma_start(xo[:, 0:1024], x_h[HALO + i * 128: HALO + i * 128 + 128, :]), "ld", 16)
                    sync.wait_ge(p.sem("x2f"), i + 1)
                    p.inc(sync.dma_start(x2_h[i * 128:(i + 1) * 128, :], xo[:, 0:1024]), "st", 16)
                sync.wait_ge(p.sem("st"), p.cnt["st"])

            @blk.vector
            def _(v, ld0=ld0):
                for i in range(16):
                    v.wait_ge(p.sem("ld"), ld0 + 16 * (i + 1))
                    p.inc(v.tensor_copy(yo[:, 0:1024], y1[:, i * 1024:(i + 1) * 1024]), "x2c", 1)
                    v.wait_ge(p.sem("x2c"), p.cnt["x2c"])
                    p.inc(v.tensor_tensor(xo[:, 0:1024], xo[:, 0:1024], yo[:, 0:1024], ALU.add), "x2f", 1)
    return nc




def build_l1fix():
    p = Prog("l1fix")
    nc = p.nc
    h = {"x1_h": p.din("x1", [TCH, D]), "ot_h": p.din("OT", [D, TC]),
         "z_h": p.din("Z", [16, 16, 64, 128], BF16), "pct_h": p.din("PcT", [16, 16, 64, 64], BF16),
         "pred": ("input", p.din("predP", [3, 64, 16 * 64], BF16), p.din("predS", [3, 64, 16 * 64])),
         "rmsw_h": p.din("rmsw", [128, 8]),
         "wgd_h": p.din("wgd", [D, 512]), "wgu_h": p.din("wgu", [512, D]), "wo_h": p.din("wo", [D, D]),
         "wg8_h": p.din("wgate", [2816, D]), "wu8_h": p.din("wup", [2816, D]),
         "wd8_h": p.din("wdown", [D, 2816]),
         "id_h": p.din("identbf", [128, 128], BF16), "idf_h": p.din("identf32", [128, 128], F32),
         "e8_h": p.din("e8", [8, 128, 16], BF16), "e2t_h": p.din("e2t", [2, 128], F32),
         "e816_h": p.din("e816", [8, 128], F32),
         "x2_h": p.dout("x2", [TC, D]),
         "xn1_d": p.nc.dram_tensor("xn1d", [TCH, D], F32),
         "wqg_d": p.nc.dram_tensor("wqgd", [D, 2816], BF16),
         "wqu_d": p.nc.dram_tensor("wqud", [D, 2816], BF16),
         "wqd_d": p.nc.dram_tensor("wqdd", [2816, D], BF16),
         "rr_d": p.nc.dram_tensor("rrd", [16, 2048], F32)}
    with p.es:
        emit_l1fix(p, nc, h)
    return nc


def emit_quant_fm(p, ch, xin, W, qs, dst_slab, dst_off, identf, psp, e816):
    nb = W // 16
    nch = W // 128
    bm, inv = qs["bm"], qs["inv"]
    sT = qs["sT"]
    xT = qs["xT"]
    ex = qs["ex"]
    ax, qb, qt = qs["ax"], qs["qb"], qs["qt"]
    ch.add("vector", lambda v: v.tensor_reduce(
        bm[:, 0:nb], xin.rearrange("p (b k) -> p b k", k=16), AX.X, ALU.max,
        apply_absolute_value=True))
    ch.add("vector", lambda v: v.tensor_scalar(bm[:, 0:nb], bm[:, 0:nb], 1.0 / 6.0, 1e-12,
                                               ALU.mult, ALU.max))
    ch.add("vector", lambda v: v.reciprocal(inv[:, 0:nb], bm[:, 0:nb]))
    ch.add("vector", lambda v: v.tensor_scalar(bm[:, 0:nb], bm[:, 0:nb], 0.5, None, ALU.mult))
    for c in range(nch):
        ch.add("tensor", lambda tt, c=c: tt.transpose(
            psp[2][:, 0:128], xin[:, c * 128:(c + 1) * 128], identf[:]))
        ch.add("vector", lambda v, c=c: v.tensor_copy(xT[:, c * 128:(c + 1) * 128],
                                                      psp[2][:, 0:128]))
    for c in range(nch):
        ch.add("tensor", lambda tt, c=c: tt.transpose(psp[3][0:8, 0:128], inv[:, c * 8:(c + 1) * 8],
                                                      identf[:]))
        ch.add("vector", lambda v, c=c: v.tensor_copy(sT[0:8, c * 128:(c + 1) * 128],
                                                      psp[3][0:8, 0:128]))
    for c in range(nch):
        ch.add("tensor", lambda tt, c=c: tt.matmul(
            psp[0][:, (c % 4) * 128:(c % 4) * 128 + 128], e816[0:8, :],
            sT[0:8, c * 128:(c + 1) * 128], start=True, stop=True, skip_group_check=True))
        ch.add("vector", lambda v, c=c: v.tensor_copy(
            ex[:, c * 128:(c + 1) * 128], psp[0][:, (c % 4) * 128:(c % 4) * 128 + 128]))
    ch.add("scalar", lambda s: s.activation(ax[:, 0:W], xT[:, 0:W], AF.Abs))
    ch.add("vector", lambda v: v.tensor_tensor(ax[:, 0:W], ax[:, 0:W], ex[:, 0:W], ALU.mult))
    ch.add("vector", lambda v: v.tensor_scalar(qb[:, 0:W], ax[:, 0:W], 5.0, None, ALU.is_gt))
    for m in (3.5, 2.5):
        ch.add("vector", lambda v, m=m: v.scalar_tensor_tensor(
            qb[:, 0:W], ax[:, 0:W], m, qb[:, 0:W], ALU.is_gt, ALU.add))
    ch.add("vector", lambda v: v.tensor_scalar(qb[:, 0:W], qb[:, 0:W], 2.0, None, ALU.mult))
    for m in (1.75, 1.25, 0.75, 0.25):
        ch.add("vector", lambda v, m=m: v.scalar_tensor_tensor(
            qb[:, 0:W], ax[:, 0:W], m, qb[:, 0:W], ALU.is_gt, ALU.add))
    for c in range(nch):
        ch.add("tensor", lambda tt, c=c: tt.transpose(psp[3][0:8, 0:128], bm[:, c * 8:(c + 1) * 8],
                                                      identf[:]))
        ch.add("vector", lambda v, c=c: v.tensor_copy(sT[0:8, c * 128:(c + 1) * 128],
                                                      psp[3][0:8, 0:128]))
    for c in range(nch):
        ch.add("tensor", lambda tt, c=c: tt.matmul(
            psp[0][:, (c % 4) * 128:(c % 4) * 128 + 128], e816[0:8, :],
            sT[0:8, c * 128:(c + 1) * 128], start=True, stop=True, skip_group_check=True))
        ch.add("vector", lambda v, c=c: v.tensor_copy(
            ex[:, c * 128:(c + 1) * 128], psp[0][:, (c % 4) * 128:(c % 4) * 128 + 128]))
    ch.add("scalar", lambda s: s.activation(qt[:, 0:W], xT[:, 0:W], AF.Sign))
    ch.add("vector", lambda v: v.tensor_tensor(qt[:, 0:W], qt[:, 0:W], ex[:, 0:W], ALU.mult))
    ch.add("vector", lambda v, dst_off=dst_off, W=W: v.tensor_tensor(
        dst_slab[:, dst_off:dst_off + W], qb[:, 0:W], qt[:, 0:W], ALU.mult))


def emit_ffn(p, nc, t):
    ident, psp, psx, psm, pst = t["ident"], t["psp"], t["psx"], t["psm"], t["pst"]
    identf = t["identf"]
    e816 = t["e816"]
    y1, xn1_d = t["y1"], t["xn1_d"]
    wg8_h, wu8_h, wd8_h = t["wg8_h"], t["wu8_h"], t["wd8_h"]
    wqg_d, wqu_d, wqd_d = t["wqg_d"], t["wqu_d"], t["wqd_d"]

    qs = {"bm": p.sb("q_bm", [128, 192]), "inv": p.sb("q_inv", [128, 192]),
          "sT": p.sb("q_sT", [8, 2816]),
          "xT": p.sb("q_xT", [128, 2816]), "ex": p.sb("q_ex", [128, 2816]),
          "ax": p.sb("q_ax", [128, 2816]),
          "qb": p.sb("q_qb", [128, 2816]), "qt": p.sb("q_qt", [128, 2816])}
    wqt = p.sb("f_wqt", [128, 2816], BF16)
    xqt = p.sb("f_xqt", [128, 16 * 1024], BF16)

    ch = Chain(p, tag="fw")
    for (w_h, wq_d) in ((wg8_h, wqg_d), (wu8_h, wqu_d)):
        for nt in range(22):
            ch.dma(lambda sy, w_h=w_h, nt=nt: sy.dma_start(
                qs["ax"][:, 0:1024], w_h[nt * 128:(nt + 1) * 128, :]))
            emit_quant_fm(p, ch, qs["ax"][:, 0:1024], 1024, qs, wqt, 0, identf, psp, e816)
            ch.dma(lambda sy, wq_d=wq_d, nt=nt: sy.dma_start(
                wq_d[:, nt * 128:(nt + 1) * 128].rearrange("(c p) n -> p c n", p=128),
                wqt[:, 0:1024].rearrange("p (c n) -> p c n", c=8)))
    for dt_ in range(8):
        ch.dma(lambda sy, dt_=dt_: sy.dma_start(
            qs["ax"][:, 0:2816], wd8_h[dt_ * 128:(dt_ + 1) * 128, :]))
        emit_quant_fm(p, ch, qs["ax"][:, 0:2816], 2816, qs, wqt, 0, identf, psp, e816)
        ch.dma(lambda sy, dt_=dt_: sy.dma_start(
            wqd_d[:, dt_ * 128:(dt_ + 1) * 128].rearrange("(c p) n -> p c n", p=128),
            wqt[:, 0:2816].rearrange("p (c n) -> p c n", c=22)))
    for i in range(16):
        ch.dma(lambda sy, i=i: sy.dma_start(
            qs["ax"][:, 0:1024], xn1_d[HALO + i * 128: HALO + i * 128 + 128, :]))
        emit_quant_fm(p, ch, qs["ax"][:, 0:1024], 1024, qs, xqt, i * 1024, identf, psp, e816)
    ch.emit(nc)

    hgrp = p.sb("f_h", [128, 4 * 2816], BF16)
    hqT = p.sb("f_hqT", [128, 4 * 2816], BF16)
    wrhs = [p.sb("f_wrhs%d" % z, [128, 512], BF16) for z in range(2)]
    gtile = p.sb("f_gt", [128, 512], BF16)
    for grp in range(4):
        ch = Chain(p, tag="ff%d" % grp)
        for which, wq_d in ((0, wqg_d), (1, wqu_d)):
            for ti in range(4):
                i = grp * 4 + ti
                for ng in range(6):
                    noff = ng * 512
                    nw = 512 if ng < 5 else 256
                    for kc in range(8):
                        ch.dma(lambda sy, wq_d=wq_d, kc=kc, noff=noff, nw=nw: sy.dma_start(
                            wrhs[kc % 2][:, 0:nw], wq_d[kc * 128:(kc + 1) * 128, noff:noff + nw]))
                        ch.add("tensor", lambda tt, kc=kc, i=i, noff=noff, nw=nw: tt.matmul(
                            psp[0][:, 0:nw], xqt[:, i * 1024 + kc * 128: i * 1024 + kc * 128 + 128],
                            wrhs[kc % 2][:, 0:nw],
                            start=(kc == 0), stop=(kc == 7), skip_group_check=True))
                    if which == 0:
                        ch.add("scalar", lambda s, ti=ti, noff=noff, nw=nw: s.activation(
                            hgrp[:, ti * 2816 + noff: ti * 2816 + noff + nw],
                            psp[0][:, 0:nw], AF.Silu))
                    else:
                        ch.add("vector", lambda v, nw=nw: v.tensor_copy(
                            gtile[:, 0:nw], psp[0][:, 0:nw]))
                        ch.add("vector", lambda v, ti=ti, noff=noff, nw=nw: v.tensor_tensor(
                            hgrp[:, ti * 2816 + noff: ti * 2816 + noff + nw],
                            hgrp[:, ti * 2816 + noff: ti * 2816 + noff + nw],
                            gtile[:, 0:nw], ALU.mult))
        for ti in range(4):
            ch.add("vector", lambda v, ti=ti: v.tensor_copy(
                qs["ax"][:, 0:2816], hgrp[:, ti * 2816:(ti + 1) * 2816]))
            emit_quant_fm(p, ch, qs["ax"][:, 0:2816], 2816, qs, hqT, ti * 2816, identf, psp, e816)
        for ti in range(4):
            i = grp * 4 + ti
            for half in range(2):
                for c in range(22):
                    ch.dma(lambda sy, c=c, half=half: sy.dma_start(
                        wrhs[c % 2][:, 0:512], wqd_d[c * 128:(c + 1) * 128, half * 512: half * 512 + 512]))
                    ch.add("tensor", lambda tt, c=c, ti=ti, half=half: tt.matmul(
                        psp[1][:, 0:512], hqT[:, ti * 2816 + c * 128: ti * 2816 + c * 128 + 128],
                        wrhs[c % 2][:, 0:512],
                        start=(c == 0), stop=(c == 21), skip_group_check=True))
                ch.add("vector", lambda v: v.tensor_copy(gtile[:, 0:512], psp[1][:, 0:512]))
                ch.add("vector", lambda v, i=i, half=half: v.tensor_tensor(
                    y1[:, i * 1024 + half * 512: i * 1024 + half * 512 + 512],
                    y1[:, i * 1024 + half * 512: i * 1024 + half * 512 + 512],
                    gtile[:, 0:512], ALU.add))
        ch.emit(nc)
    return {"xstg": qs["qb"], "wtile": qs["qt"]}



def build_fused():
    p = Prog("fused")
    nc = p.nc
    x_h = p.din("x", [TCH, D])
    id_h = p.din("identbf", [128, 128], BF16)
    idf_h = p.din("identf32", [128, 128], F32)
    kv_in = nc.dram_tensor("kv_in", [D, D], F32)
    kv_out = nc.dram_tensor("kv_out", [D, D], F32)
    x1_d = nc.dram_tensor("x1d", [TCH, D], F32)
    ot_d = nc.dram_tensor("otd", [D, TC], F32)
    z_d = nc.dram_tensor("zd", [16, 16, 64, 128], BF16)
    pct_d = nc.dram_tensor("pctd", [16, 16, 64, 64], BF16)
    ag_in = nc.dram_tensor("ag_in", [128, 1024], BF16)
    from concourse.replica_groups import maybe_share_collective_output_space
    sp_ag = maybe_share_collective_output_space("AllGather", [[0, 1, 2, 3, 4, 5, 6, 7]])
    ag_out = nc.dram_tensor("ag_out", [8 * 128, 1024], BF16, addr_space=sp_ag)

    h0 = {"x_h": x_h, "wq_h": p.din("wq", [D, D]), "wg_h": p.din("wg", [D, 8]),
          "wl1_h": p.din("wl1", [8, D, 4096]), "wl2_h": p.din("wl2", [8, 2048, D]),
          "id_h": id_h, "x1_h": x1_d,
          "kv_ar": (p.din("wk", [D, D]), p.din("wv", [D, D]), kv_in, kv_out)}
    h1 = {"x1_h": x1_d,
          "wq1_h": p.din("wq1", [D, D]), "wk1_h": p.din("wk1", [D, D]), "wv1_h": p.din("wv1", [D, D]),
          "qcw_h": p.din("qcw", [D, 4]), "qcb_h": p.din("qcb", [128, 8]),
          "kcw_h": p.din("kcw", [D, 4]), "kcb_h": p.din("kcb", [128, 8]),
          "vcw_h": p.din("vcw", [D, 4]), "vcb_h": p.din("vcb", [128, 8]),
          "wa_h": p.din("wa", [D, 16]), "ba_h": p.din("ba", [16, 1]),
          "wb_h": p.din("wb", [D, 16]), "bb_h": p.din("bb", [16, 1]),
          "id_h": id_h, "idf_h": idf_h,
          "mD_h": p.din("maskD", [128, 128], F32), "mG_h": p.din("maskG", [128, 128], F32),
          "mdg_h": p.din("mdg", [128, 128], F32), "e8_h": p.din("e8", [8, 128, 16], BF16),
          "ot_h": ot_d, "z_h": z_d, "pct_h": pct_d,
          "dscr_h": nc.dram_tensor("dscr", [4, 16, 128], F32),
          "stex": ("ag", ag_in, ag_out)}
    h2 = {"x1_h": x1_d, "ot_h": ot_d, "z_h": z_d, "pct_h": pct_d,
          "pred": ("ag", ag_out, p.din("hmask", [64, 6], F32), p.din("himask", [64, 6 * 64], BF16)),
          "rmsw_h": p.din("rmsw", [128, 8]),
          "wgd_h": p.din("wgd", [D, 512]), "wgu_h": p.din("wgu", [512, D]),
          "wo_h": p.din("wo", [D, D]),
          "wg8_h": p.din("wgate", [2816, D]), "wu8_h": p.din("wup", [2816, D]),
          "wd8_h": p.din("wdown", [D, 2816]),
          "id_h": id_h, "idf_h": idf_h,
          "e8_h": h1["e8_h"], "e2t_h": p.din("e2t", [2, 128], F32),
          "e816_h": p.din("e816", [8, 128], F32),
          "x2_h": p.dout("x2", [TC, D]),
          "xn1_d": nc.dram_tensor("xn1d", [TCH, D], F32),
          "wqg_d": nc.dram_tensor("wqgd", [D, 2816], BF16),
          "wqu_d": nc.dram_tensor("wqud", [D, 2816], BF16),
          "wqd_d": nc.dram_tensor("wqdd", [2816, D], BF16),
          "rr_d": nc.dram_tensor("rrd", [16, 2048], F32)}
    with p.es:
        emit_l0main(p, nc, h0)
        emit_l1loc(p, nc, h1)
        emit_l1fix(p, nc, h2)
    return nc


def kernel_fused(inp):
    f32 = np.float32
    xf = np.ascontiguousarray(inp["L0_attn_x"].reshape(B * T, D).astype(f32))
    cores = list(range(NCORES))
    xsh = shard_x(xf)
    nc = build_fused()
    rmsw = np.ascontiguousarray(np.asarray(inp["L1_attn_rms_w"], f32).reshape(1024).reshape(8, 128).T)
    maps = []
    for c in cores:
        b0 = (c // 4) * 4
        preds = list(range(b0, c))
        slots = [0, 1, 2, 4, 5, 6]
        hm = np.zeros((64, 6), f32)
        him = np.zeros((64, 6 * 64), f32)
        for l, s in enumerate(slots):
            if s in preds:
                hm[:, l] = 1.0
            else:
                him[:, l * 64:(l + 1) * 64] = np.eye(64, dtype=f32)
        maps.append({
            "x": xsh[c],
            "wq": inp["L0_attn_Wq"], "wk": inp["L0_attn_Wk"], "wv": inp["L0_attn_Wv"],
            "wg": inp["L0_ffn_Wg_DN"], "wl1": inp["L0_ffn_Wl1_ND2H"], "wl2": inp["L0_ffn_Wl2_NHD"],
            "wq1": inp["L1_attn_Wq"], "wk1": inp["L1_attn_Wk"], "wv1": inp["L1_attn_Wv"],
            "qcw": inp["L1_attn_qcw"], "qcb": np.ascontiguousarray(np.asarray(inp["L1_attn_qcb"], f32).reshape(8, 128).T),
            "kcw": inp["L1_attn_kcw"], "kcb": np.ascontiguousarray(np.asarray(inp["L1_attn_kcb"], f32).reshape(8, 128).T),
            "vcw": inp["L1_attn_vcw"], "vcb": np.ascontiguousarray(np.asarray(inp["L1_attn_vcb"], f32).reshape(8, 128).T),
            "wa": inp["L1_attn_Wa"], "ba": np.asarray(inp["L1_attn_ba"], f32).reshape(16, 1),
            "wb": inp["L1_attn_Wb"], "bb": np.asarray(inp["L1_attn_bb"], f32).reshape(16, 1),
            "rmsw": rmsw, "wgd": inp["L1_attn_Wgd"], "wgu": inp["L1_attn_Wgu"],
            "wo": inp["L1_attn_Wo"],
            "wgate": inp["L1_ffn_w_gate"], "wup": inp["L1_ffn_w_up"], "wdown": inp["L1_ffn_w_down"],
            "identbf": IDENT_BF, "identf32": IDENT_F32, "maskD": MASKD, "maskG": MASKG,
            "mdg": MDG, "e8": E8C, "e2t": E2T, "e816": E816,
            "hmask": hm, "himask": him.astype(BF),
        })
    r = run_bass_kernel_spmd(nc, maps, core_ids=cores)
    x2 = np.concatenate([np.asarray(r.results[c]["x2"]) for c in cores], 0)
    return np.ascontiguousarray(x2.reshape(B, T, D))



IDENT_BF = np.eye(128, dtype=np.float32).astype(BF)
IDENT_F32 = np.eye(128, dtype=np.float32)
MASKD = np.where(np.arange(128)[:, None] < np.arange(128)[None, :], 0.0, -1e30).astype(np.float32)
MASKG = np.where(np.arange(128)[:, None] <= np.arange(128)[None, :], 0.0, -1e30).astype(np.float32)
MDG = (np.arange(128)[:, None] // 32 == np.arange(128)[None, :] // 32).astype(np.float32)
E8C = np.zeros((8, 128, 16), np.float32)
for _dc in range(8):
    for _pp in range(128):
        E8C[_dc, _pp, 2 * _dc + _pp // 64] = 1.0
E8C = E8C.astype(BF)
E2T = np.zeros((2, 128), np.float32)
for _pp in range(128):
    E2T[_pp // 64, _pp] = 1.0
E816 = np.zeros((8, 128), np.float32)
for _kk in range(128):
    E816[_kk // 16, _kk] = 1.0
EYEH = np.tile(np.eye(64, dtype=np.float32), (1, 16)).astype(BF)


def shard_x(xf):
    out = []
    for c in range(NCORES):
        own = xf[c * TC:(c + 1) * TC]
        if c % 4 == 0:
            h = np.zeros((HALO, D), np.float32)
        else:
            h = xf[c * TC - HALO: c * TC]
        out.append(np.ascontiguousarray(np.concatenate([h, own], 0)))
    return out


def kernel_multi(**inp):
    f32 = np.float32
    xf = np.ascontiguousarray(inp["L0_attn_x"].reshape(B * T, D).astype(f32))
    cores = list(range(NCORES))

    nc1 = build_l0kv()
    maps1 = [{"x": np.ascontiguousarray(xf[c * TC:(c + 1) * TC]),
              "wk": inp["L0_attn_Wk"], "wv": inp["L0_attn_Wv"], "identbf": IDENT_BF}
             for c in cores]
    r1 = run_bass_kernel_spmd(nc1, maps1, core_ids=cores)
    kvp = [np.asarray(r1.results[c]["kvpart"]) for c in cores]

    nc2 = build_l0main()
    xsh = shard_x(xf)
    maps2 = [{"x": xsh[c], "kvp": np.ascontiguousarray(np.stack(kvp[(c // 4) * 4:(c // 4) * 4 + 4])),
              "wq": inp["L0_attn_Wq"], "wg": inp["L0_ffn_Wg_DN"],
              "wl1": inp["L0_ffn_Wl1_ND2H"], "wl2": inp["L0_ffn_Wl2_NHD"],
              "identbf": IDENT_BF} for c in cores]
    r2 = run_bass_kernel_spmd(nc2, maps2, core_ids=cores)
    x1sh = [np.array(r2.results[c]["x1"]) for c in cores]
    for c in cores:
        if c % 4 != 0:
            x1sh[c][0:HALO] = x1sh[c - 1][HALO + TC - HALO: HALO + TC]

    nc3 = build_l1loc()
    maps3 = [{
        "x1": x1sh[c], "wq1": inp["L1_attn_Wq"], "wk1": inp["L1_attn_Wk"], "wv1": inp["L1_attn_Wv"],
        "qcw": inp["L1_attn_qcw"], "qcb": np.ascontiguousarray(np.asarray(inp["L1_attn_qcb"], f32).reshape(8, 128).T),
        "kcw": inp["L1_attn_kcw"], "kcb": np.ascontiguousarray(np.asarray(inp["L1_attn_kcb"], f32).reshape(8, 128).T),
        "vcw": inp["L1_attn_vcw"], "vcb": np.ascontiguousarray(np.asarray(inp["L1_attn_vcb"], f32).reshape(8, 128).T),
        "wa": inp["L1_attn_Wa"], "ba": np.asarray(inp["L1_attn_ba"], f32).reshape(16, 1),
        "wb": inp["L1_attn_Wb"], "bb": np.asarray(inp["L1_attn_bb"], f32).reshape(16, 1),
        "identbf": IDENT_BF, "identf32": IDENT_F32, "maskD": MASKD, "maskG": MASKG,
        "mdg": MDG, "e8": E8C,
    } for c in cores]
    r3 = run_bass_kernel_spmd(nc3, maps3, core_ids=cores)

    nc4 = build_l1fix()
    rmsw = np.ascontiguousarray(np.asarray(inp["L1_attn_rms_w"], f32).reshape(1024).reshape(8, 128).T)
    maps4 = []
    for c in cores:
        b0 = (c // 4) * 4
        preds = list(range(b0, c))
        predP = [EYEH] * (3 - len(preds)) + [np.asarray(r3.results[j]["PcoreT"]) for j in preds]
        predS = [np.zeros((64, 1024), f32)] * (3 - len(preds)) + \
                [np.asarray(r3.results[j]["Sloc"]) for j in preds]
        maps4.append({
            "x1": x1sh[c], "OT": np.asarray(r3.results[c]["OT"]),
            "Z": np.asarray(r3.results[c]["Z"]), "PcT": np.asarray(r3.results[c]["PcT"]),
            "predP": np.ascontiguousarray(np.stack(predP)), "predS": np.ascontiguousarray(np.stack(predS)),
            "rmsw": rmsw, "wgd": inp["L1_attn_Wgd"], "wgu": inp["L1_attn_Wgu"],
            "wo": inp["L1_attn_Wo"],
            "wgate": inp["L1_ffn_w_gate"], "wup": inp["L1_ffn_w_up"], "wdown": inp["L1_ffn_w_down"],
            "identbf": IDENT_BF, "identf32": IDENT_F32, "e8": E8C, "e2t": E2T, "e816": E816,
        })
    r4 = run_bass_kernel_spmd(nc4, maps4, core_ids=cores)
    x2 = np.concatenate([np.asarray(r4.results[c]["x2"]) for c in cores], 0)
    return np.ascontiguousarray(x2.reshape(B, T, D))


def kernel(**inp):
    import os
    if os.environ.get("MK_MULTI"):
        return kernel_multi(**inp)
    return kernel_fused(inp)


# version 05
# speedup vs baseline: 1.0077x; 1.0077x over previous
import numpy as np
import ml_dtypes
from contextlib import ExitStack

import concourse.bass as bass
import concourse.mybir as mybir
from concourse.bass_utils import run_bass_kernel_spmd

F32 = mybir.dt.float32
BF16 = mybir.dt.bfloat16
AF = mybir.ActivationFunctionType
ALU = mybir.AluOpType
AX = mybir.AxisListType

B, T, D = 2, 8192, 1024
NCORES = 8
TC = 2048
HALO = 3
TCH = TC + HALO

BF = ml_dtypes.bfloat16


class Prog:
    def __init__(self, name="p"):
        self.nc = bass.Bass("TRN2", target_bir_lowering=False)
        self.es = ExitStack()
        self.sems = {}
        self.cnt = {}
        self.names = set()

    def _uniq(self, name):
        if name in self.names:
            k = 2
            while f"{name}_{k}" in self.names:
                k += 1
            name = f"{name}_{k}"
        self.names.add(name)
        return name

    def sem(self, tag):
        if tag not in self.sems:
            self.sems[tag] = self.es.enter_context(self.nc.semaphore(tag))
            self.cnt[tag] = 0
        return self.sems[tag]

    def inc(self, instr, tag, n=1):
        instr.then_inc(self.sem(tag), n)
        self.cnt[tag] += n
        return self.cnt[tag]

    def wait(self, eng, tag):
        if tag in self.sems and self.cnt[tag] > 0:
            eng.wait_ge(self.sems[tag], self.cnt[tag])

    def sb(self, name, shape, dt=F32, es=None):
        tgt = es or getattr(self, "default_es", None) or self.es
        return tgt.enter_context(self.nc.sbuf_tensor(self._uniq(name), shape, dt))

    def ps(self, name, shape, dt=F32, es=None):
        tgt = es or getattr(self, "default_es", None) or self.es
        return tgt.enter_context(self.nc.psum_tensor(self._uniq(name), shape, dt))

    def din(self, name, shape, dt=F32):
        self.names.add(name)
        return self.nc.dram_tensor(name, shape, dt, kind="ExternalInput")

    def dout(self, name, shape, dt=F32):
        self.names.add(name)
        return self.nc.dram_tensor(name, shape, dt, kind="ExternalOutput")



def emit_wT(p, nc, w_h, wT, ident, pst, es, ne=8, nd=8, tag="wt"):
    with ExitStack() as es2:
        wstg = p.sb(tag + "_stg", [128, nd * 128], F32, es=es2)
        wbf = p.sb(tag + "_bf", [128, nd * 128], BF16, es=es2)
        for ec in range(ne):
            with nc.Block() as blk:
                @blk.sync
                def _(sync, ec=ec):
                    p.inc(sync.dma_start(wstg[:, :], w_h[ec * 128:(ec + 1) * 128, :]), "ld", 16)

                @blk.vector
                def _(v):
                    p.wait(v, "ld")
                    v.tensor_copy(wbf[:], wstg[:])
            base = 8 * ec
            with nc.Block() as blk:
                @blk.tensor
                def _(t, ec=ec, base=base):
                    for n in range(nd):
                        if n >= 2:
                            t.wait_ge(p.sem(tag + "_ev"), base + n - 1)
                        t.transpose(pst[n % 2][:], wbf[:, n * 128:(n + 1) * 128],
                                    ident[:]).then_inc(p.sem(tag + "_t"), 1)
                        p.cnt[tag + "_t"] = p.cnt.get(tag + "_t", 0)

                @blk.vector
                def _(v, ec=ec, base=base):
                    for n in range(nd):
                        v.wait_ge(p.sem(tag + "_t"), base + n + 1)
                        v.tensor_copy(wT[:, n * (ne * 128) + ec * 128: n * (ne * 128) + ec * 128 + 128],
                                      pst[n % 2][:]).then_inc(p.sem(tag + "_ev"), 1)
            p.cnt[tag + "_t"] = p.cnt.get(tag + "_t", 0) + 8
            p.cnt[tag + "_ev"] = p.cnt.get(tag + "_ev", 0) + 8



def build_l0kv():
    p = Prog("l0kv")
    nc = p.nc
    x_h = p.din("x", [TC, D])
    wk_h = p.din("wk", [D, D])
    wv_h = p.din("wv", [D, D])
    id_h = p.din("identbf", [128, 128], BF16)
    kv_h = p.dout("kvpart", [D, D])

    with p.es:
        ident = p.sb("ident", [128, 128], BF16)
        scr = {"xt": p.sb("xt", [128, 1024]), "sq": p.sb("sq", [128, 1024]),
               "cols": p.sb("cols", [128, 8]), "rot": p.sb("rot", [128, 2048], BF16),
               "eps": p.sb("eps", [128, 1])}
        pst = [p.ps("pst%d" % z, [128, 128], BF16) for z in range(2)]
        psp = [p.ps("psp%d" % z, [128, 512]) for z in range(4)]
        xnT = p.sb("xnT", [128, 8 * 2048], BF16)
        ktok = p.sb("ktok", [128, 16 * 1024], BF16)
        vtok = p.sb("vtok", [128, 16 * 1024], BF16)

        with nc.Block() as blk:
            @blk.gpsimd
            def _(g):
                p.inc(g.dma_start(ident[:], id_h[:, :]), "gd", 16)
                g.memset(scr["eps"][:], 1e-5)
                g.wait_ge(p.sem("gd"), 16)
        with nc.Block() as blk:
            emit_ln_xnT_fixed(p, blk, x_h, [128] * 16, xnT, 2048, ident, pst, scr, tag="lnt")

        for wname, w_h in (("k", wk_h), ("v", wv_h)):
            wT = p.sb("w%sT" % wname, [128, 8 * 1024], BF16)
            emit_wT(p, nc, w_h, wT, ident, pst, p.es, tag="wt" + wname)
            tok = {"k": ktok, "v": vtok}[wname]
            with nc.Block() as blk:
                @blk.tensor
                def _(t, wT=wT, wname=wname):
                    n = 0
                    for j in range(16):
                        for half in range(2):
                            if n >= 2:
                                t.wait_ge(p.sem("pv" + wname), n - 1)
                            for dc in range(8):
                                ins = t.matmul(psp[n % 2][:],
                                               xnT[:, dc * 2048 + j * 128: dc * 2048 + j * 128 + 128],
                                               wT[:, dc * 1024 + half * 512: dc * 1024 + half * 512 + 512],
                                               start=(dc == 0), stop=(dc == 7))
                            p.inc(ins, "pe" + wname, 1)
                            n += 1

                @blk.vector
                def _(v, tok=tok, wname=wname):
                    n = 0
                    for j in range(16):
                        for half in range(2):
                            v.wait_ge(p.sem("pe" + wname), n + 1)
                            v.tensor_copy(tok[:, j * 1024 + half * 512: j * 1024 + half * 512 + 512],
                                          psp[n % 2][:]).then_inc(p.sem("pv" + wname), 1)
                            n += 1

        kvout = p.sb("kvout", [128, 8 * 1024], F32)
        with nc.Block() as blk:
            @blk.tensor
            def _(t):
                n = 0
                for kfc in range(8):
                    for half in range(2):
                        if n >= 4:
                            t.wait_ge(p.sem("kv_ev"), n - 3)
                        for j in range(16):
                            ins = t.matmul(psp[n % 4][:],
                                           ktok[:, j * 1024 + kfc * 128: j * 1024 + kfc * 128 + 128],
                                           vtok[:, j * 1024 + half * 512: j * 1024 + half * 512 + 512],
                                           start=(j == 0), stop=(j == 15))
                        p.inc(ins, "kv_pe", 1)
                        n += 1

            @blk.vector
            def _(v):
                n = 0
                for kfc in range(8):
                    for half in range(2):
                        v.wait_ge(p.sem("kv_pe"), n + 1)
                        v.tensor_copy(kvout[:, kfc * 1024 + half * 512: kfc * 1024 + half * 512 + 512],
                                      psp[n % 4][:]).then_inc(p.sem("kv_ev"), 1)
                        n += 1

            @blk.sync
            def _(sync):
                sync.wait_ge(p.sem("kv_ev"), 16)
                p.inc(sync.dma_start(kv_h[:, :].rearrange("(c p) n -> p c n", p=128),
                                     kvout[:, :].rearrange("p (c n) -> p c n", c=8)), "st", 16)
                sync.wait_ge(p.sem("st"), p.cnt["st"])
    return nc


def emit_ln_xnT_fixed(p, blk, x_h, rows, xnT, dst_cols, ident, pst, scr, tag,
                      xn_tok_slab=None, xn_cols=1024, xn_rot=0, xn_guard=None):
    nt = len(rows)
    xt, sq, cols_, rot = scr["xt"], scr["sq"], scr["cols"], scr["rot"]
    ld0 = p.cnt.get("ld", 0)

    @blk.sync
    def _(sync):
        r0 = 0
        for i in range(nt):
            r = rows[i]
            if i > 0:
                sync.wait_ge(p.sem(tag + "_ln"), i)
            p.inc(sync.dma_start(xt[0:r, :], x_h[r0:r0 + r, :]), "ld", 16)
            r0 += r

    @blk.scalar
    def _(s):
        for i in range(nt):
            r = rows[i]
            s.wait_ge(p.sem("ld"), ld0 + 16 * (i + 1))
            if i > 0:
                s.wait_ge(p.sem(tag + "_var"), i)
            s.activation(sq[0:r, :], xt[0:r, :], AF.Copy,
                         accum_out=cols_[0:r, 0:1]).then_inc(p.sem(tag + "_ac"), 1)
            s.activation(sq[0:r, :], xt[0:r, :], AF.Square,
                         accum_out=cols_[0:r, 1:2]).then_inc(p.sem(tag + "_ac"), 1)
            s.wait_ge(p.sem(tag + "_var"), i + 1)
            s.activation(cols_[0:r, 5:6], cols_[0:r, 4:5], AF.Sqrt,
                         bias=scr["eps"][0:r, 0:1]).then_inc(p.sem(tag + "_sq"), 1)

    @blk.vector
    def _(v):
        n = 0
        for i in range(nt):
            r = rows[i]
            v.wait_ge(p.sem(tag + "_ac"), 2 * (i + 1))
            v.tensor_scalar(cols_[0:r, 2:3], cols_[0:r, 0:1], 1.0 / 1024.0,
                            None, ALU.mult).then_inc(p.sem(tag + "_c"), 1)
            v.wait_ge(p.sem(tag + "_c"), 3 * i + 1)
            v.tensor_scalar(cols_[0:r, 3:4], cols_[0:r, 2:3], cols_[0:r, 2:3],
                            None, ALU.mult).then_inc(p.sem(tag + "_c"), 1)
            v.wait_ge(p.sem(tag + "_c"), 3 * i + 2)
            v.scalar_tensor_tensor(cols_[0:r, 4:5], cols_[0:r, 1:2], 1.0 / 1024.0,
                                   cols_[0:r, 3:4], ALU.mult, ALU.subtract).then_inc(
                p.sem(tag + "_var"), 1)
            v.wait_ge(p.sem(tag + "_sq"), i + 1)
            v.reciprocal(cols_[0:r, 6:7], cols_[0:r, 5:6]).then_inc(p.sem(tag + "_c"), 1)
            v.wait_ge(p.sem(tag + "_c"), 3 * i + 3)
            if i >= 2:
                v.wait_ge(p.sem(tag + "_t"), 8 * (i - 1))
            v.tensor_scalar(rot[0:r, (i % 2) * 1024:(i % 2) * 1024 + 1024], xt[0:r, :],
                            cols_[0:r, 2:3], cols_[0:r, 6:7], ALU.subtract, ALU.mult)
            if xn_tok_slab is not None:
                if xn_rot:
                    if i >= xn_rot and xn_guard is not None:
                        v.wait_ge(p.sem(xn_guard[0]), xn_guard[1] + 16 * (i - xn_rot + 1))
                    off = (i % xn_rot) * 1024
                else:
                    off = i * xn_cols
                ins = v.tensor_scalar(xn_tok_slab[0:r, off:off + 1024],
                                      xt[0:r, :], cols_[0:r, 2:3], cols_[0:r, 6:7],
                                      ALU.subtract, ALU.mult)
            else:
                ins = v.tensor_scalar(sq[0:r, 0:1], xt[0:r, 0:1],
                                      cols_[0:r, 2:3], cols_[0:r, 6:7], ALU.subtract, ALU.mult)
            p.inc(ins, tag + "_ln", 1)
            for c in range(8):
                v.wait_ge(p.sem(tag + "_t"), n + 1)
                v.tensor_copy(xnT[:, c * dst_cols + i * 128: c * dst_cols + i * 128 + r],
                              pst[n % 2][:, 0:r]).then_inc(p.sem(tag + "_ev"), 1)
                n += 1

    @blk.tensor
    def _(t):
        n = 0
        for i in range(nt):
            r = rows[i]
            t.wait_ge(p.sem(tag + "_ln"), i + 1)
            for c in range(8):
                if n >= 2:
                    t.wait_ge(p.sem(tag + "_ev"), n - 1)
                t.transpose(pst[n % 2][:, 0:r],
                            rot[0:r, (i % 2) * 1024 + c * 128:(i % 2) * 1024 + c * 128 + 128],
                            ident[0:r, 0:r]).then_inc(p.sem(tag + "_t"), 1)
                n += 1




GRP5 = [(0, 512), (512, 512), (1024, 512), (1536, 512), (2048, 3)]
ROWS17 = [128] * 16 + [3]


def emit_l0main(p, nc, h):
    x_h, wq_h, wg_h = h["x_h"], h["wq_h"], h["wg_h"]
    wl1_h, wl2_h, id_h, x1_h = h["wl1_h"], h["wl2_h"], h["id_h"], h["x1_h"]

    with ExitStack() as esPh:
        p.default_es = esPh
        ident = p.sb("ident", [128, 128], BF16, es=esPh)
        scr = {"xt": p.sb("xt", [128, 1024], es=esPh), "sq": p.sb("sq", [128, 1024], es=esPh),
               "cols": p.sb("cols", [128, 8], es=esPh), "rot": p.sb("rot", [128, 2048], BF16, es=esPh),
               "eps": p.sb("eps", [128, 1], es=esPh)}
        pst = [p.ps("pst%d" % z, [128, 128], BF16, es=esPh) for z in range(2)]
        psp = [p.ps("psp%d" % z, [128, 512], es=esPh) for z in range(4)]
        psy = [p.ps("psy%d" % z, [128, 512], es=esPh) for z in range(2)]
        xnT = p.sb("xnT", [128, 8 * TCH], BF16, es=esPh)
        y = p.sb("y", [128, 17 * 1024], BF16, es=esPh)
        gates = p.sb("gates", [128, 17 * 8], es=esPh)
        wgbf = p.sb("wgbf", [128, 64], BF16, es=esPh)

        with nc.Block() as blk:
            @blk.gpsimd
            def _(g):
                p.inc(g.dma_start(ident[:], id_h[:, :]), "gd", 16)
                g.memset(scr["eps"][:], 1e-5)
                g.wait_ge(p.sem("gd"), p.cnt["gd"])

        with nc.Block() as blk:
            emit_ln_xnT_fixed(p, blk, x_h, ROWS17, xnT, TCH, ident, pst, scr, tag="lnt0")

        if "kv_ar" in h:
            wk_h, wv_h, kv_in, kv_out = h["kv_ar"]
            with ExitStack() as esKV:
                ktok = p.sb("ktok", [128, 16 * 1024], BF16, es=esKV)
                vtok = p.sb("vtok", [128, 16 * 1024], BF16, es=esKV)
                for wname, w_h, tok in (("k", wk_h, ktok), ("v", wv_h, vtok)):
                    with ExitStack() as esW:
                        wT = p.sb("w%sT" % wname, [128, 8 * 1024], BF16, es=esW)
                        emit_wT(p, nc, w_h, wT, ident, pst, p.es, tag="wt" + wname)
                        with nc.Block() as blk:
                            @blk.tensor
                            def _(t, wT=wT, wname=wname):
                                n = 0
                                for j in range(16):
                                    for half in range(2):
                                        if n >= 2:
                                            t.wait_ge(p.sem("pv" + wname), n - 1)
                                        for dc in range(8):
                                            ins = t.matmul(
                                                psp[n % 2][:],
                                                xnT[:, dc * TCH + 3 + j * 128: dc * TCH + 3 + j * 128 + 128],
                                                wT[:, dc * 1024 + half * 512: dc * 1024 + half * 512 + 512],
                                                start=(dc == 0), stop=(dc == 7))
                                        p.inc(ins, "pe" + wname, 1)
                                        n += 1

                            @blk.vector
                            def _(v, tok=tok, wname=wname):
                                n = 0
                                for j in range(16):
                                    for half in range(2):
                                        v.wait_ge(p.sem("pe" + wname), n + 1)
                                        v.tensor_copy(
                                            tok[:, j * 1024 + half * 512: j * 1024 + half * 512 + 512],
                                            psp[n % 2][:]).then_inc(p.sem("pv" + wname), 1)
                                        n += 1
                with ExitStack() as esKO:
                    kvout = p.sb("kvout", [128, 8 * 1024], F32, es=esKO)
                    with nc.Block() as blk:
                        @blk.tensor
                        def _(t):
                            n = 0
                            for kfc in range(8):
                                for half in range(2):
                                    if n >= 4:
                                        t.wait_ge(p.sem("kv_ev"), n - 3)
                                    for j in range(16):
                                        ins = t.matmul(
                                            psp[n % 4][:],
                                            ktok[:, j * 1024 + kfc * 128: j * 1024 + kfc * 128 + 128],
                                            vtok[:, j * 1024 + half * 512: j * 1024 + half * 512 + 512],
                                            start=(j == 0), stop=(j == 15))
                                    p.inc(ins, "kv_pe", 1)
                                    n += 1

                        @blk.vector
                        def _(v):
                            n = 0
                            for kfc in range(8):
                                for half in range(2):
                                    v.wait_ge(p.sem("kv_pe"), n + 1)
                                    v.tensor_copy(
                                        kvout[:, kfc * 1024 + half * 512: kfc * 1024 + half * 512 + 512],
                                        psp[n % 4][:]).then_inc(p.sem("kv_ev"), 1)
                                    n += 1

                        @blk.sync
                        def _(sync):
                            sync.wait_ge(p.sem("kv_ev"), 16)
                            p.inc(sync.dma_start(kv_in[:, :].rearrange("(c p) n -> p c n", p=128),
                                                 kvout[:, :].rearrange("p (c n) -> p c n", c=8)), "st", 16)
                            sync.wait_ge(p.sem("st"), p.cnt["st"])
                    with nc.Block() as blk:
                        @blk.gpsimd
                        def _(g):
                            p.inc(g.collective_compute(
                                "AllReduce", ALU.add,
                                replica_groups=[[0, 1, 2, 3], [4, 5, 6, 7]],
                                ins=[kv_in.ap().opt()], outs=[kv_out.ap().opt()]), "cc", 1)
                            g.wait_ge(p.sem("cc"), p.cnt["cc"])

        with ExitStack() as esA:
            kvsum = p.sb("kvsum", [128, 8 * 1024], F32, es=esA)
            kvbf = p.sb("kvbf", [128, 8 * 1024], BF16, es=esA)
            qT = p.sb("qT", [128, 8 * TCH], BF16, es=esA)
            wqT = p.sb("wqT", [128, 8 * 1024], BF16, es=esA)
            wgstg = p.sb("wgstg", [128, 64], F32, es=esA)

            with nc.Block() as blk:
                @blk.gpsimd
                def _(g):
                    if "kvp_h" in h:
                        kvp_h = h["kvp_h"]
                        for part in range(4):
                            if part > 0:
                                g.wait_ge(p.sem("gd"), p.cnt["gd"])
                            ins = g.dma_start(
                                kvsum[:, :].rearrange("p (c n) -> p c n", c=8),
                                kvp_h[part, :, :].rearrange("(c p) n -> p c n", p=128),
                                **({} if part == 0 else {"accum_op": ALU.add}))
                            p.inc(ins, "gd", 16)
                    else:
                        kv_out = h["kv_ar"][3]
                        p.inc(g.dma_start(
                            kvsum[:, :].rearrange("p (c n) -> p c n", c=8),
                            kv_out[:, :].rearrange("(c p) n -> p c n", p=128)), "gd", 16)
                    g.wait_ge(p.sem("gd"), p.cnt["gd"])

                @blk.sync
                def _(sync):
                    p.inc(sync.dma_start(wgstg[:, :].rearrange("p (c n) -> p c n", c=8),
                                         wg_h[:, :].rearrange("(c p) n -> p c n", p=128)),
                          "ld", 16)

                @blk.vector
                def _(v):
                    p.wait(v, "ld")
                    v.tensor_copy(wgbf[:], wgstg[:])
            with nc.Block() as blk:
                @blk.vector
                def _(v):
                    v.tensor_copy(kvbf[:], kvsum[:])

            emit_wT(p, nc, wq_h, wqT, ident, pst, p.es, tag="wtq")

            with nc.Block() as blk:
                @blk.tensor
                def _(t):
                    n = 0
                    for ec in range(8):
                        for (goff, gw) in GRP5:
                            if n >= 2:
                                t.wait_ge(p.sem("q_ev"), n - 1)
                            for dc in range(8):
                                ins = t.matmul(psp[n % 2][:, 0:gw],
                                               wqT[:, dc * 1024 + ec * 128: dc * 1024 + ec * 128 + 128],
                                               xnT[:, dc * TCH + goff: dc * TCH + goff + gw],
                                               start=(dc == 0), stop=(dc == 7))
                            p.inc(ins, "q_pe", 1)
                            n += 1

                @blk.vector
                def _(v):
                    n = 0
                    for ec in range(8):
                        for (goff, gw) in GRP5:
                            v.wait_ge(p.sem("q_pe"), n + 1)
                            v.tensor_scalar(qT[:, ec * TCH + goff: ec * TCH + goff + gw],
                                            psp[n % 2][:, 0:gw], 1.0 / 32.0, None,
                                            ALU.mult).then_inc(p.sem("q_ev"), 1)
                            n += 1

            with nc.Block() as blk:
                @blk.tensor
                def _(t):
                    n = 0
                    for i in range(17):
                        r = ROWS17[i]
                        for half in range(2):
                            if n >= 2:
                                t.wait_ge(p.sem("at_ev"), n - 1)
                            for kfc in range(8):
                                ins = t.matmul(psp[n % 2][0:r, :],
                                               qT[:, kfc * TCH + i * 128: kfc * TCH + i * 128 + r],
                                               kvbf[:, kfc * 1024 + half * 512: kfc * 1024 + half * 512 + 512],
                                               start=(kfc == 0), stop=(kfc == 7))
                            p.inc(ins, "at_pe", 1)
                            n += 1

                @blk.vector
                def _(v):
                    n = 0
                    for i in range(17):
                        r = ROWS17[i]
                        for half in range(2):
                            v.wait_ge(p.sem("at_pe"), n + 1)
                            v.tensor_copy(y[0:r, i * 1024 + half * 512: i * 1024 + half * 512 + 512],
                                          psp[n % 2][0:r, :]).then_inc(p.sem("at_ev"), 1)
                            n += 1

            with nc.Block() as blk:
                @blk.tensor
                def _(t):
                    for i in range(17):
                        r = ROWS17[i]
                        if i > 0:
                            t.wait_ge(p.sem("g_ev"), i)
                        for dc in range(8):
                            ins = t.matmul(psy[0][0:r, 0:8],
                                           xnT[:, dc * TCH + i * 128: dc * TCH + i * 128 + r],
                                           wgbf[:, dc * 8: dc * 8 + 8],
                                           start=(dc == 0), stop=(dc == 7))
                        p.inc(ins, "g_pe", 1)

                @blk.vector
                def _(v):
                    for i in range(17):
                        r = ROWS17[i]
                        v.wait_ge(p.sem("g_pe"), i + 1)
                        v.reduce_max(scr["cols"][0:r, 7:8], psy[0][0:r, 0:7],
                                     AX.X).then_inc(p.sem("g_c"), 1)
                        v.wait_ge(p.sem("g_c"), 2 * i + 1)
                        v.tensor_scalar(gates[0:r, i * 8: i * 8 + 8], psy[0][0:r, 0:8],
                                        scr["cols"][0:r, 7:8], None,
                                        ALU.is_ge).then_inc(p.sem("g_c"), 1)
                        v.wait_ge(p.sem("g_c"), 2 * i + 2)
                        v.memset(gates[0:r, i * 8 + 7: i * 8 + 8], 1.0).then_inc(p.sem("g_ev"), 1)

        hslab = p.sb("hslab", [128, 16 * TCH], BF16, es=esPh)
        w2e = p.sb("w2e", [128, 16 * 1024], BF16, es=esPh)
        w2stg = [p.sb("w2stg%d" % z, [128, 1024], F32, es=esPh) for z in range(2)]
        w1stg = [p.sb("w1s%d" % z, [128, 1024], F32, es=esPh) for z in range(2)]
        w1a = [p.sb("w1a%d" % z, [128, 1024], BF16, es=esPh) for z in range(2)]
        w1b = [p.sb("w1b%d" % z, [128, 1024], BF16, es=esPh) for z in range(2)]
        asc = p.sb("asc", [128, 512], BF16, es=esPh)
        ssc = p.sb("ssc", [128, 512], BF16, es=esPh)
        for e in range(8):
            with nc.Block() as blk:
                gd0 = p.cnt.get("gd", 0)
                w2c0 = p.cnt.get("w2c", 0)

                @blk.gpsimd
                def _(g, e=e, gd0=gd0, w2c0=w2c0):
                    for j in range(16):
                        if j >= 2:
                            g.wait_ge(p.sem("w2c"), w2c0 + j - 1)
                        p.inc(g.dma_start(w2stg[j % 2][:, :],
                                          wl2_h[e, j * 128:(j + 1) * 128, :]), "gd", 16)

                @blk.vector
                def _(v, e=e, gd0=gd0, w2c0=w2c0):
                    for j in range(16):
                        v.wait_ge(p.sem("gd"), gd0 + 16 * (j + 1))
                        p.inc(v.tensor_copy(w2e[:, j * 1024:(j + 1) * 1024],
                                            w2stg[j % 2][:]), "w2c", 1)

            with nc.Block() as blk:
                ld0 = p.cnt.get("ld", 0)
                w1c0 = p.cnt.get("w1c", 0)
                up0 = p.cnt.get("up_g", 0)
                ha0 = p.cnt.get("h_a", 0)
                hs0 = p.cnt.get("h_s", 0)
                hh0 = p.cnt.get("h_h", 0)

                @blk.sync
                def _(sync, e=e, ld0=ld0, w1c0=w1c0):
                    for j in range(16):
                        if j >= 2:
                            sync.wait_ge(p.sem("w1c"), w1c0 + 2 * (j - 1))
                        p.inc(sync.dma_start(
                            w1stg[j % 2][:, :].rearrange("p (c n) -> p c n", c=8),
                            wl1_h[e, :, j * 128: j * 128 + 128].rearrange("(c p) n -> p c n", p=128)),
                            "ld", 16)
                        sync.wait_ge(p.sem("w1c"), w1c0 + 2 * j + 1)
                        p.inc(sync.dma_start(
                            w1stg[j % 2][:, :].rearrange("p (c n) -> p c n", c=8),
                            wl1_h[e, :, 2048 + j * 128: 2048 + j * 128 + 128].rearrange("(c p) n -> p c n", p=128)),
                            "ld", 16)

                @blk.vector
                def _(v, e=e, ld0=ld0, up0=up0, hs0=hs0, ha0=ha0, hh0=hh0):
                    for j in range(16):
                        v.wait_ge(p.sem("ld"), ld0 + 32 * j + 16)
                        if j >= 2:
                            v.wait_ge(p.sem("up_g"), up0 + 10 * (j - 1))
                        p.inc(v.tensor_copy(w1a[j % 2][:], w1stg[j % 2][:]), "w1c", 1)
                        v.wait_ge(p.sem("ld"), ld0 + 32 * (j + 1))
                        p.inc(v.tensor_copy(w1b[j % 2][:], w1stg[j % 2][:]), "w1c", 1)
                        for g5 in range(5):
                            idx = j * 5 + g5
                            gw = GRP5[g5][1]
                            v.wait_ge(p.sem("up_g"), up0 + 2 * idx + 1)
                            p.inc(v.tensor_copy(asc[:, 0:gw], psp[idx % 2][:, 0:gw]), "h_a", 1)
                            v.wait_ge(p.sem("h_s"), hs0 + idx + 1)
                            p.inc(v.tensor_tensor(
                                hslab[:, j * TCH + GRP5[g5][0]: j * TCH + GRP5[g5][0] + gw],
                                asc[:, 0:gw], ssc[:, 0:gw], ALU.mult), "h_h", 1)

                @blk.scalar
                def _(s, e=e, up0=up0, hh0=hh0):
                    for j in range(16):
                        for g5 in range(5):
                            idx = j * 5 + g5
                            gw = GRP5[g5][1]
                            s.wait_ge(p.sem("up_g"), up0 + 2 * idx + 2)
                            if idx >= 1:
                                s.wait_ge(p.sem("h_h"), hh0 + idx)
                            p.inc(s.activation(ssc[:, 0:gw], psp[2 + idx % 2][:, 0:gw],
                                               AF.Silu), "h_s", 1)

                @blk.tensor
                def _(t, e=e, w1c0=w1c0, ha0=ha0, hs0=hs0, up0=up0):
                    for j in range(16):
                        t.wait_ge(p.sem("w1c"), w1c0 + 2 * (j + 1))
                        for g5 in range(5):
                            idx = j * 5 + g5
                            goff, gw = GRP5[g5]
                            if idx >= 2:
                                t.wait_ge(p.sem("h_a"), ha0 + idx - 1)
                                t.wait_ge(p.sem("h_s"), hs0 + idx - 1)
                            for dc in range(8):
                                ins = t.matmul(psp[idx % 2][:, 0:gw],
                                               w1a[j % 2][:, dc * 128:(dc + 1) * 128],
                                               xnT[:, dc * TCH + goff: dc * TCH + goff + gw],
                                               start=(dc == 0), stop=(dc == 7))
                            p.inc(ins, "up_g", 1)
                            for dc in range(8):
                                ins = t.matmul(psp[2 + idx % 2][:, 0:gw],
                                               w1b[j % 2][:, dc * 128:(dc + 1) * 128],
                                               xnT[:, dc * TCH + goff: dc * TCH + goff + gw],
                                               start=(dc == 0), stop=(dc == 7))
                            p.inc(ins, "up_g", 1)

            with nc.Block() as blk:
                dn0 = p.cnt.get("dn_pe", 0)
                ye0 = p.cnt.get("y_ev", 0)

                @blk.tensor
                def _(t, e=e, dn0=dn0, ye0=ye0):
                    for i in range(17):
                        r = ROWS17[i]
                        if i >= 1:
                            t.wait_ge(p.sem("y_ev"), ye0 + 2 * i)
                        for j in range(16):
                            i0 = t.matmul(psy[0][0:r, :],
                                          hslab[:, j * TCH + i * 128: j * TCH + i * 128 + r],
                                          w2e[:, j * 1024: j * 1024 + 512],
                                          start=(j == 0), stop=(j == 15))
                            i1 = t.matmul(psy[1][0:r, :],
                                          hslab[:, j * TCH + i * 128: j * TCH + i * 128 + r],
                                          w2e[:, j * 1024 + 512: j * 1024 + 1024],
                                          start=(j == 0), stop=(j == 15))
                        p.inc(i0, "dn_pe", 1)
                        p.inc(i1, "dn_pe", 1)

                @blk.vector
                def _(v, e=e, dn0=dn0):
                    for i in range(17):
                        r = ROWS17[i]
                        for half in range(2):
                            v.wait_ge(p.sem("dn_pe"), dn0 + 2 * i + half + 1)
                            p.inc(v.tensor_scalar(asc[0:r, :], psy[half][0:r, :],
                                                  gates[0:r, i * 8 + e: i * 8 + e + 1], None,
                                                  ALU.mult), "y_c", 1)
                            v.wait_ge(p.sem("y_c"), p.cnt["y_c"])
                            p.inc(v.tensor_tensor(
                                y[0:r, i * 1024 + half * 512: i * 1024 + half * 512 + 512],
                                y[0:r, i * 1024 + half * 512: i * 1024 + half * 512 + 512],
                                asc[0:r, :], ALU.add), "y_ev", 1)

        with nc.Block() as blk:
            st0 = p.cnt.get("st", 0)
            ld0 = p.cnt.get("ld", 0)

            @blk.sync
            def _(sync, st0=st0, ld0=ld0):
                r0 = 0
                for i in range(17):
                    r = ROWS17[i]
                    if i >= 1:
                        sync.wait_ge(p.sem("st"), st0 + 16 * i)
                    p.inc(sync.dma_start(scr["xt"][0:r, :], x_h[r0:r0 + r, :]), "ld", 16)
                    sync.wait_ge(p.sem("fin"), i + 1)
                    p.inc(sync.dma_start(x1_h[r0:r0 + r, :], scr["xt"][0:r, :]), "st", 16)
                    r0 += r
                sync.wait_ge(p.sem("st"), p.cnt["st"])

            @blk.vector
            def _(v, ld0=ld0):
                for i in range(17):
                    r = ROWS17[i]
                    v.wait_ge(p.sem("ld"), ld0 + 16 * (i + 1))
                    p.inc(v.tensor_copy(scr["sq"][0:r, :],
                                        y[0:r, i * 1024: i * 1024 + 1024]), "fin_c", 1)
                    v.wait_ge(p.sem("fin_c"), p.cnt["fin_c"])
                    p.inc(v.tensor_tensor(scr["xt"][0:r, :], scr["xt"][0:r, :], scr["sq"][0:r, :],
                                          ALU.add), "fin", 1)
        p.default_es = None

def build_l0main():
    p = Prog("l0main")
    nc = p.nc
    h = {"x_h": p.din("x", [TCH, D]), "kvp_h": p.din("kvp", [4, D, D]),
         "wq_h": p.din("wq", [D, D]), "wg_h": p.din("wg", [D, 8]),
         "wl1_h": p.din("wl1", [8, D, 4096]), "wl2_h": p.din("wl2", [8, 2048, D]),
         "id_h": p.din("identbf", [128, 128], BF16), "x1_h": p.dout("x1", [TCH, D])}
    with p.es:
        emit_l0main(p, nc, h)
    return nc



class Chain2:
    ENGSEM = {"tensor": "pe", "vector": "dv", "scalar": "ac", "sync": "sy", "gpsimd": "gp"}
    ENGINC = {"tensor": 1, "vector": 1, "scalar": 1, "sync": 16, "gpsimd": 16}

    def __init__(self, p, tag="c2"):
        self.p = p
        self.tag = tag
        self.steps = []

    def add(self, eng, fn, reads=(), writes=()):
        self.steps.append((eng, fn, tuple(reads), tuple(writes)))

    def dma(self, fn, reads=(), writes=()):
        self.add("sync", fn, reads, writes)

    def gdma(self, fn, reads=(), writes=()):
        self.add("gpsimd", fn, reads, writes)

    def emit(self, nc):
        p, tag = self.p, self.tag
        n = len(self.steps)
        eng_order = []
        eng_count = {}
        for (e, _, _, _) in self.steps:
            eng_order.append(eng_count.get(e, 0))
            eng_count[e] = eng_count.get(e, 0) + 1
        last_writer = {}
        readers = {}
        deps = [dict() for _ in range(n)]
        for i, (e, fn, rd, wr) in enumerate(self.steps):
            dd = {}
            def need(j):
                ej = self.steps[j][0]
                if ej == e and eng_order[j] < eng_order[i]:
                    pass
                dd[ej] = max(dd.get(ej, -1), eng_order[j])
            for k in rd:
                if k in last_writer:
                    need(last_writer[k])
            for k in wr:
                if k in last_writer:
                    need(last_writer[k])
                for j in readers.get(k, ()):
                    need(j)
            for k in wr:
                last_writer[k] = i
                readers[k] = []
            for k in rd:
                readers.setdefault(k, []).append(i)
            deps[i] = dd
        base = {e: p.cnt.get(tag + s, 0) for e, s in self.ENGSEM.items()}
        sems = {e: p.sem(tag + s) for e, s in self.ENGSEM.items()}
        with nc.Block() as blk:
            for engname in ("tensor", "vector", "scalar", "sync", "gpsimd"):
                mysteps = [(i, f) for i, (e, f, _, _) in enumerate(self.steps) if e == engname]
                if not mysteps:
                    continue

                def body(eng, mysteps=mysteps, engname=engname):
                    for (i, f) in mysteps:
                        for (de, dj) in sorted(deps[i].items()):
                            if de == engname and dj == eng_order[i] - 1:
                                pass
                            eng.wait_ge(sems[de], base[de] + self.ENGINC[de] * (dj + 1))
                        ins = f(eng)
                        ins.then_inc(sems[engname], self.ENGINC[engname])
                getattr(blk, engname)(body)
            for e in eng_count:
                p.cnt[tag + self.ENGSEM[e]] = base[e] + self.ENGINC[e] * eng_count[e]

            def fin(eng):
                for e, cnt in eng_count.items():
                    if e != "gpsimd":
                        eng.wait_ge(sems[e], base[e] + self.ENGINC[e] * cnt)
            blk.gpsimd(fin)


class Chain:

    def __init__(self, p, tag="ck"):
        self.p = p
        self.tag = tag
        self.steps = []

    def add(self, eng, fn, inc=1):
        self.steps.append((eng, fn, inc))

    def dma(self, fn):
        self.add("sync", fn, 16)

    def emit(self, nc):
        p, tag = self.p, self.tag
        cum = []
        c = p.cnt.get(tag, 0)
        for (_, _, inc) in self.steps:
            cum.append(c)
            c += inc
        p.cnt[tag] = c
        sem = p.sem(tag)
        with nc.Block() as blk:
            for engname in ("tensor", "vector", "scalar", "sync", "gpsimd"):
                mysteps = [(i, f, inc) for i, (e, f, inc) in enumerate(self.steps) if e == engname]
                if not mysteps:
                    continue

                def body(eng, mysteps=mysteps):
                    for (i, f, inc) in mysteps:
                        if cum[i] > 0:
                            eng.wait_ge(sem, cum[i])
                        ins = f(eng)
                        ins.then_inc(sem, inc)
                getattr(blk, engname)(body)
            def fin(eng):
                eng.wait_ge(sem, c)
            blk.gpsimd(fin)


def emit_chunk_pass(p, nc, t, nchunks=16, nheads=16):
    ident, identf = t["ident"], t["identf"]
    maskD, maskG, onesr = t["maskD"], t["maskG"], t["onesr"]
    mdg = t["mdg"]
    psp, psx, psm, pst = t["psp"], t["psx"], t["psm"], t["pst"]
    q1c, k1c, v1c = t["q1c"], t["k1c"], t["v1c"]
    lnb, lnrq, lnrk, cpad = t["lnb"], t["lnrq"], t["lnrk"], t["cpad"]
    Sf, Sbf, Ncum, NcumT = t["Sf"], t["Sbf"], t["Ncum"], t["NcumT"]
    ot_h, z_h, pct_h, dscr_h = t["ot_h"], t["z_h"], t["pct_h"], t["dscr_h"]

    def rot2(name, shape, dt=F32):
        return [p.sb(name + str(z), shape, dt) for z in range(2)]
    flat = rot2("flat", [1, 4 * 2048])
    crel = rot2("crel", [16, 128]); cprev = rot2("cprev", [16, 128])
    biasQ1 = rot2("biasQ1", [16, 128])
    rowD = rot2("rowD", [16, 128]); rowG = rot2("rowG", [16, 128])
    expQ3 = rot2("expQ3", [16, 128]); expQ4r = rot2("expQ4r", [16, 128])
    negQ2r = rot2("negQ2r", [16, 128])
    bT = rot2("bT", [128, 16]); negQ2T = rot2("negQ2T", [128, 16]); q4T = rot2("q4T", [128, 16])
    scrD = rot2("scrD", [128, 128]); scrE = rot2("scrE", [128, 128])
    Dbuf = rot2("Dbuf", [128, 128]); Gbuf = rot2("Gbuf", [128, 128], BF16)
    Ddg = rot2("Ddg", [128, 128]); Doff = rot2("Doff", [128, 128])
    pows = [rot2("pow%d" % z, [128, 128]) for z in range(5)]
    ptmp = rot2("ptmp", [128, 128])
    Xf = rot2("Xf", [128, 128]); Yf = rot2("Yf", [128, 128]); Xacc = rot2("Xacc", [128, 128])
    Xbuf = rot2("Xbuf", [128, 128], BF16)
    ktok = rot2("ktok", [128, 64], BF16); vtok = rot2("vtok", [128, 64], BF16)
    ktb = rot2("ktb", [128, 64], BF16)
    negk = rot2("negk", [128, 128], BF16)
    vsc = rot2("vsc", [128, 64], BF16)
    qgbuf = rot2("qgbuf", [128, 128], BF16); qbb2 = rot2("qbb2", [128, 128], BF16)
    zbuf = rot2("zbuf", [128, 128], BF16); zout = rot2("zout", [64, 128], BF16)
    obuf = rot2("obuf", [64, 128])
    pctbuf = rot2("pctbuf", [64, 64], BF16)
    gcol = rot2("gcol", [64, 1])

    def reg(k, r):
        return lambda rows=128: psp[k][0:rows, r * 128:(r + 1) * 128]
    PS = [
        {"SK": reg(0, 0), "SQ": reg(0, 1), "B": reg(0, 2), "KS": reg(0, 3),
         "QB": reg(1, 0), "O": reg(1, 1), "ST": reg(1, 2), "TT": psx, "MM": reg(1, 3)},
        {"SK": reg(2, 0), "SQ": reg(2, 1), "B": reg(2, 2), "KS": reg(2, 3),
         "QB": reg(3, 0), "O": reg(3, 1), "ST": reg(3, 2), "TT": psm, "MM": reg(3, 3)},
    ]

    ch = Chain2(p, tag="ck")
    gidx = 0
    for ci in range(nchunks):
        cp = ci % 2
        PRE = "prep%d" % cp
        c0 = ci * 128
        ch.add("vector", lambda v, c0=c0, cp=cp: v.tensor_scalar(
            crel[cp][:], cpad[0:16, 1 + c0:1 + c0 + 128], cpad[0:16, c0:c0 + 1], None, ALU.subtract),
            reads=("cpad",), writes=(PRE + "crel",))
        ch.add("vector", lambda v, c0=c0, cp=cp: v.tensor_scalar(
            cprev[cp][:], cpad[0:16, c0:c0 + 128], cpad[0:16, c0:c0 + 1], None, ALU.subtract),
            reads=("cpad",), writes=(PRE + "cprev",))
        ch.add("vector", lambda v, c0=c0, cp=cp: v.tensor_tensor(
            biasQ1[cp][:], lnb[0:16, c0:c0 + 128], crel[cp][:], ALU.subtract),
            reads=(PRE + "crel",), writes=(PRE + "b1",))
        ch.add("vector", lambda v, c0=c0, cp=cp: v.tensor_tensor(
            biasQ1[cp][:], biasQ1[cp][:], lnrk[0:16, c0:c0 + 128], ALU.add),
            reads=(PRE + "b1",), writes=(PRE + "b1",))
        ch.add("vector", lambda v, c0=c0, cp=cp: v.tensor_tensor(
            rowD[cp][:], cprev[cp][:], lnrk[0:16, c0:c0 + 128], ALU.add),
            reads=(PRE + "cprev",), writes=(PRE + "rowD",))
        ch.add("vector", lambda v, c0=c0, cp=cp: v.tensor_tensor(
            rowG[cp][:], crel[cp][:], lnrq[0:16, c0:c0 + 128], ALU.add),
            reads=(PRE + "crel",), writes=(PRE + "rowG",))
        ch.add("scalar", lambda s, cp=cp: s.activation(
            expQ4r[cp][:], biasQ1[cp][:], AF.Exp, bias=crel[cp][0:16, 127:128]),
            reads=(PRE + "b1", PRE + "crel"), writes=(PRE + "q4",))
        ch.add("scalar", lambda s, cp=cp: s.activation(negQ2r[cp][:], rowD[cp][:], AF.Exp),
               reads=(PRE + "rowD",), writes=(PRE + "q2",))
        ch.add("vector", lambda v, cp=cp: v.tensor_scalar(
            negQ2r[cp][:], negQ2r[cp][:], -1.0, None, ALU.mult),
            reads=(PRE + "q2",), writes=(PRE + "q2",))
        ch.add("scalar", lambda s, cp=cp: s.activation(expQ3[cp][:], rowG[cp][:], AF.Exp),
               reads=(PRE + "rowG",), writes=(PRE + "q3",))
        for qi, bufk, key in ((0, "rowD", PRE + "rowD"), (1, "rowG", PRE + "rowG"),
                              (2, "expQ3", PRE + "q3"), (3, "crel", PRE + "crel")):
            buf = {"rowD": rowD, "rowG": rowG, "expQ3": expQ3, "crel": crel}[bufk]
            ch.dma(lambda sy, qi=qi, buf=buf, cp=cp: sy.dma_start(dscr_h[cp, qi, :, :], buf[cp][:]),
                   reads=(key,), writes=(PRE + "dscr%d" % qi,))
        for qi in range(4):
            ch.dma(lambda sy, qi=qi, cp=cp: sy.dma_start(
                flat[cp][0:1, qi * 2048:(qi + 1) * 2048],
                dscr_h[cp, qi:qi + 1, :, :].rearrange("a p n -> a (p n)")),
                reads=(PRE + "dscr%d" % qi,), writes=(PRE + "flat",))
        for (srcb, dstb, sk, dk) in ((biasQ1, bT, "b1", "bT"), (negQ2r, negQ2T, "q2", "q2T"),
                                     (expQ4r, q4T, "q4", "q4T")):
            ch.add("tensor", lambda tt, srcb=srcb, cp=cp: tt.transpose(
                PS[cp]["TT"][:, 0:16], srcb[cp][:], identf[0:16, 0:16]),
                reads=(PRE + sk,), writes=("hp%d" % cp + "psTT2",))
            ch.add("vector", lambda v, dstb=dstb, cp=cp: v.tensor_copy(dstb[cp][:], PS[cp]["TT"][:, 0:16]),
                   reads=("hp%d" % cp + "psTT2",), writes=(PRE + dk,))

        for h in range(nheads):
            hp = gidx % 2
            gidx += 1
            hc, ho = h // 2, (h % 2) * 64
            P = PS[hp]
            HP = "hp%d" % hp
            kS = lambda hc=hc, ho=ho, ci=ci: k1c[ho:ho + 64, hc * 2048 + ci * 128: hc * 2048 + ci * 128 + 128]
            qS = lambda hc=hc, ho=ho, ci=ci: q1c[ho:ho + 64, hc * 2048 + ci * 128: hc * 2048 + ci * 128 + 128]
            vS = lambda hc=hc, ho=ho, ci=ci: v1c[ho:ho + 64, hc * 2048 + ci * 128: hc * 2048 + ci * 128 + 128]
            SH = "S%d" % h
            NH = "N%d" % h

            ch.add("tensor", lambda tt, kS=kS, ho=ho, hp=hp: tt.transpose(
                pst[hp][:, 0:64], kS(), ident[ho:ho + 64, ho:ho + 64]),
                reads=(), writes=(HP + "pstk",))
            ch.add("vector", lambda v, hp=hp: v.tensor_copy(ktok[hp][:], pst[hp][:, 0:64]),
                   reads=(HP + "pstk",), writes=(HP + "ktok",))
            ch.add("tensor", lambda tt, vS=vS, ho=ho, hp=hp: tt.transpose(
                pst[hp][:, 64:128], vS(), ident[ho:ho + 64, ho:ho + 64]),
                reads=(), writes=(HP + "pstv",))
            ch.add("vector", lambda v, hp=hp: v.tensor_copy(vtok[hp][:], pst[hp][:, 64:128]),
                   reads=(HP + "pstv",), writes=(HP + "vtok",))
            ch.add("vector", lambda v, kS=kS, ho=ho, hp=hp: v.tensor_scalar(
                negk[hp][ho:ho + 64, :], kS(), -1.0, None, ALU.mult),
                reads=(), writes=(HP + "negk",))
            ch.add("tensor", lambda tt, kS=kS, ho=ho, hp=hp: tt.matmul(
                P["SK"](), negk[hp][ho:ho + 64, :], kS(), start=True, stop=True, skip_group_check=True),
                reads=(HP + "negk",), writes=(HP + "psSK",))
            ch.add("tensor", lambda tt, kS=kS, qS=qS, hp=hp: tt.matmul(
                P["SQ"](), kS(), qS(), start=True, stop=True, skip_group_check=True),
                reads=(), writes=(HP + "psSQ",))
            ch.add("tensor", lambda tt, h=h, hp=hp, cp=cp: tt.matmul(
                P["B"](), onesr[0:1, :], flat[cp][0:1, h * 128: h * 128 + 128],
                start=True, stop=True, skip_group_check=True),
                reads=(PRE + "flat",), writes=(HP + "psB",))
            ch.add("vector", lambda v, hp=hp: v.tensor_tensor(scrD[hp][:], P["B"](), maskD[:], ALU.add),
                   reads=(HP + "psB",), writes=(HP + "scrD",))
            ch.add("scalar", lambda s, h=h, hp=hp, cp=cp: s.activation(
                scrE[hp][:], scrD[hp][:], AF.Exp, bias=bT[cp][:, h:h + 1]),
                reads=(HP + "scrD", PRE + "bT"), writes=(HP + "scrE",))
            ch.add("vector", lambda v, hp=hp: v.tensor_tensor(Dbuf[hp][:], scrE[hp][:], P["SK"](), ALU.mult),
                   reads=(HP + "scrE", HP + "psSK"), writes=(HP + "D",))
            ch.add("tensor", lambda tt, h=h, hp=hp, cp=cp: tt.matmul(
                P["B"](), onesr[0:1, :], flat[cp][0:1, 2048 + h * 128: 2048 + h * 128 + 128],
                start=True, stop=True, skip_group_check=True),
                reads=(PRE + "flat",), writes=(HP + "psB",))
            ch.add("vector", lambda v, hp=hp: v.tensor_tensor(scrD[hp][:], P["B"](), maskG[:], ALU.add),
                   reads=(HP + "psB",), writes=(HP + "scrD",))
            ch.add("scalar", lambda s, h=h, hp=hp, cp=cp: s.activation(
                scrE[hp][:], scrD[hp][:], AF.Exp, bias=bT[cp][:, h:h + 1]),
                reads=(HP + "scrD", PRE + "bT"), writes=(HP + "scrE",))
            ch.add("vector", lambda v, hp=hp: v.tensor_tensor(Gbuf[hp][:], scrE[hp][:], P["SQ"](), ALU.mult),
                   reads=(HP + "scrE", HP + "psSQ"), writes=(HP + "G",))
            ch.add("tensor", lambda tt, kS=kS, h=h, hc=hc, ho=ho, hp=hp: tt.matmul(
                P["KS"](), kS(), Sbf[ho:ho + 64, hc * 64: hc * 64 + 64],
                start=True, stop=True, skip_group_check=True),
                reads=(SH,), writes=(HP + "psKS",))
            ch.add("vector", lambda v, h=h, hp=hp, cp=cp: v.tensor_scalar(
                vsc[hp][:], P["KS"](), negQ2T[cp][:, h:h + 1], None, ALU.mult),
                reads=(HP + "psKS", PRE + "q2T"), writes=(HP + "vsc",))
            ch.add("vector", lambda v, hp=hp: v.tensor_tensor(Xf[hp][:, 0:64], vsc[hp][:], vtok[hp][:], ALU.add),
                   reads=(HP + "vsc", HP + "vtok"), writes=(HP + "Xf",))
            ch.add("vector", lambda v, h=h, hp=hp, cp=cp: v.tensor_scalar(
                Xf[hp][:, 64:128], ktok[hp][:], negQ2T[cp][:, h:h + 1], None, ALU.mult),
                reads=(HP + "ktok", PRE + "q2T"), writes=(HP + "Xf",))
            ch.add("vector", lambda v, hp=hp: v.tensor_tensor(Ddg[hp][:], Dbuf[hp][:], mdg[:], ALU.mult),
                   reads=(HP + "D",), writes=(HP + "Ddg",))
            ch.add("vector", lambda v, hp=hp: v.tensor_tensor(Doff[hp][:], Dbuf[hp][:], Ddg[hp][:], ALU.subtract),
                   reads=(HP + "D", HP + "Ddg"), writes=(HP + "Doff",))
            ch.add("vector", lambda v, hp=hp: v.tensor_copy(pows[0][hp][:], Ddg[hp][:]),
                   reads=(HP + "Ddg",), writes=(HP + "pow0",))
            for kq in range(1, 5):
                ch.add("tensor", lambda tt, kq=kq, hp=hp: tt.transpose(
                    PS[hp]["TT"][:], pows[kq - 1][hp][:], identf[:]),
                    reads=(HP + "pow%d" % (kq - 1),), writes=(HP + "psTT2",))
                ch.add("vector", lambda v, hp=hp: v.tensor_copy(ptmp[hp][:], PS[hp]["TT"][:]),
                       reads=(HP + "psTT2",), writes=(HP + "ptmp",))
                ch.add("tensor", lambda tt, kq=kq, hp=hp: tt.matmul(
                    P["MM"](), ptmp[hp][:], pows[kq - 1][hp][:], start=True, stop=True,
                    skip_group_check=True),
                    reads=(HP + "ptmp", HP + "pow%d" % (kq - 1)), writes=(HP + "psMM",))
                ch.add("vector", lambda v, kq=kq, hp=hp: v.tensor_copy(pows[kq][hp][:], P["MM"]()),
                       reads=(HP + "psMM",), writes=(HP + "pow%d" % kq,))

            def td_apply(hp, P, HP):
                for kq in range(5):
                    ch.add("tensor", lambda tt, kq=kq, hp=hp: tt.matmul(
                        P["MM"](), pows[kq][hp][:], Yf[hp][:], start=True, stop=True,
                        skip_group_check=True),
                        reads=(HP + "pow%d" % kq, HP + "Yf"), writes=(HP + "psMM",))
                    ch.add("vector", lambda v, hp=hp: v.tensor_tensor(Yf[hp][:], Yf[hp][:], P["MM"](), ALU.add),
                           reads=(HP + "psMM", HP + "Yf"), writes=(HP + "Yf",))
            ch.add("vector", lambda v, hp=hp: v.tensor_copy(Yf[hp][:], Xf[hp][:]),
                   reads=(HP + "Xf",), writes=(HP + "Yf",))
            td_apply(hp, P, HP)
            ch.add("vector", lambda v, hp=hp: v.tensor_copy(Xacc[hp][:], Yf[hp][:]),
                   reads=(HP + "Yf",), writes=(HP + "Xacc",))
            for mh in range(3):
                ch.add("tensor", lambda tt, hp=hp: tt.matmul(
                    P["MM"](), Doff[hp][:], Yf[hp][:], start=True, stop=True, skip_group_check=True),
                    reads=(HP + "Doff", HP + "Yf"), writes=(HP + "psMM",))
                ch.add("vector", lambda v, hp=hp: v.tensor_copy(Yf[hp][:], P["MM"]()),
                       reads=(HP + "psMM",), writes=(HP + "Yf",))
                td_apply(hp, P, HP)
                ch.add("vector", lambda v, hp=hp: v.tensor_tensor(Xacc[hp][:], Xacc[hp][:], Yf[hp][:], ALU.add),
                       reads=(HP + "Yf", HP + "Xacc"), writes=(HP + "Xacc",))
            ch.add("vector", lambda v, hp=hp: v.tensor_copy(Xbuf[hp][:], Xacc[hp][:]),
                   reads=(HP + "Xacc",), writes=(HP + "X",))
            ch.add("tensor", lambda tt, hp=hp: tt.matmul(
                P["B"](64), Xbuf[hp][:, 64:128], Gbuf[hp][:], start=True, stop=True,
                skip_group_check=True),
                reads=(HP + "X", HP + "G"), writes=(HP + "psB",))
            ch.add("tensor", lambda tt, h=h, hp=hp, cp=cp: tt.matmul(
                P["QB"](64), onesr[0:1, 0:64], flat[cp][0:1, 2 * 2048 + h * 128: 2 * 2048 + h * 128 + 128],
                start=True, stop=True, skip_group_check=True),
                reads=(PRE + "flat",), writes=(HP + "psQB",))
            ch.add("vector", lambda v, ho=ho, hp=hp: v.tensor_copy(qbb2[hp][ho:ho + 64, :], P["QB"](64)),
                   reads=(HP + "psQB",), writes=(HP + "qbb2",))
            ch.add("vector", lambda v, ho=ho, hp=hp, qS=qS: v.tensor_tensor(
                qgbuf[hp][ho:ho + 64, :], qbb2[hp][ho:ho + 64, :], qS(), ALU.mult),
                reads=(HP + "qbb2",), writes=(HP + "qg",))
            ch.add("tensor", lambda tt, hp=hp: tt.matmul(
                P["O"](64), Xbuf[hp][:, 0:64], Gbuf[hp][:], start=True, stop=False,
                skip_group_check=True),
                reads=(HP + "X", HP + "G"), writes=(HP + "psO",))
            ch.add("tensor", lambda tt, h=h, hc=hc, ho=ho, hp=hp: tt.matmul(
                P["O"](64), Sbf[ho:ho + 64, hc * 64: hc * 64 + 64], qgbuf[hp][ho:ho + 64, :],
                start=False, stop=True, skip_group_check=True),
                reads=(SH, HP + "qg"), writes=(HP + "psO",))
            ch.add("vector", lambda v, hp=hp: v.tensor_copy(obuf[hp][:], P["O"](64)),
                   reads=(HP + "psO",), writes=(HP + "obuf",))
            ch.dma(lambda sy, ci=ci, h=h, hp=hp: sy.dma_start(
                ot_h[h * 64:(h + 1) * 64, ci * 128:(ci + 1) * 128], obuf[hp][:]),
                reads=(HP + "obuf",), writes=())
            ch.add("vector", lambda v, ho=ho, hp=hp: v.tensor_copy(zbuf[hp][ho:ho + 64, :], P["B"](64)),
                   reads=(HP + "psB",), writes=(HP + "zbuf",))
            ch.add("vector", lambda v, ho=ho, hp=hp: v.tensor_tensor(
                zout[hp][:], zbuf[hp][ho:ho + 64, :], qgbuf[hp][ho:ho + 64, :], ALU.add),
                reads=(HP + "zbuf", HP + "qg"), writes=(HP + "zout",))
            ch.dma(lambda sy, ci=ci, h=h, hp=hp: sy.dma_start(z_h[ci, h, :, :], zout[hp][:]),
                   reads=(HP + "zout",), writes=())
            ch.add("tensor", lambda tt, h=h, hp=hp, cp=cp: tt.matmul(
                P["ST"](64), onesr[0:1, 0:64],
                flat[cp][0:1, 3 * 2048 + h * 128 + 127: 3 * 2048 + h * 128 + 128],
                start=True, stop=True, skip_group_check=True),
                reads=(PRE + "flat",), writes=(HP + "psST",))
            ch.add("scalar", lambda s, hp=hp: s.activation(gcol[hp][:], P["ST"](64)[0:64, 0:1], AF.Exp),
                   reads=(HP + "psST",), writes=(HP + "gcol",))
            ch.add("vector", lambda v, h=h, hp=hp, cp=cp: v.tensor_scalar(
                ktb[hp][:], ktok[hp][:], q4T[cp][:, h:h + 1], None, ALU.mult),
                reads=(HP + "ktok", PRE + "q4T"), writes=(HP + "ktb",))
            ch.add("tensor", lambda tt, hp=hp: tt.matmul(
                P["ST"](64), ktb[hp][:], Xbuf[hp][:, 0:64], start=True, stop=True,
                skip_group_check=True),
                reads=(HP + "ktb", HP + "X"), writes=(HP + "psST",))
            ch.add("vector", lambda v, h=h, hp=hp: v.scalar_tensor_tensor(
                Sf[0:64, h * 64: h * 64 + 64], Sf[0:64, h * 64: h * 64 + 64], gcol[hp][0:64, 0:1],
                P["ST"](64)[0:64, 0:64], ALU.mult, ALU.add),
                reads=(SH, HP + "gcol", HP + "psST"), writes=(SH,))
            ch.add("vector", lambda v, h=h, hc=hc, ho=ho, hp=hp: v.tensor_copy(
                Sbf[ho:ho + 64, hc * 64: hc * 64 + 64], Sf[0:64, h * 64: h * 64 + 64]),
                reads=(SH,), writes=(SH,))
            ch.add("tensor", lambda tt, hp=hp: tt.matmul(
                P["KS"](64), Xbuf[hp][:, 64:128], ktb[hp][:], start=True, stop=True,
                skip_group_check=True),
                reads=(HP + "X", HP + "ktb"), writes=(HP + "psKS",))
            ch.add("vector", lambda v, hp=hp: v.scalar_tensor_tensor(
                pctbuf[hp][:], identf[0:64, 0:64], gcol[hp][0:64, 0:1],
                P["KS"](64)[0:64, 0:64], ALU.mult, ALU.add),
                reads=(HP + "gcol", HP + "psKS"), writes=(HP + "pct",))
            ch.dma(lambda sy, ci=ci, h=h, hp=hp: sy.dma_start(pct_h[ci, h, :, :], pctbuf[hp][:]),
                   reads=(HP + "pct",), writes=())
            ch.add("tensor", lambda tt, h=h, hp=hp: tt.matmul(
                P["QB"](64), pctbuf[hp][:], Ncum[0:64, h * 64: h * 64 + 64],
                start=True, stop=True, skip_group_check=True),
                reads=(HP + "pct", NH), writes=(HP + "psQB",))
            ch.add("tensor", lambda tt, h=h, hp=hp: tt.matmul(
                P["SQ"](64), Ncum[0:64, h * 64: h * 64 + 64], pctbuf[hp][:],
                start=True, stop=True, skip_group_check=True),
                reads=(HP + "pct", NH), writes=(HP + "psSQ",))
            ch.add("vector", lambda v, h=h, hp=hp: v.tensor_copy(
                Ncum[0:64, h * 64: h * 64 + 64], P["QB"](64)),
                reads=(HP + "psQB",), writes=(NH,))
            ch.add("vector", lambda v, h=h, hp=hp: v.tensor_copy(
                NcumT[0:64, h * 64: h * 64 + 64], P["SQ"](64)),
                reads=(HP + "psSQ",), writes=(NH,))

    ch.emit(nc)


def emit_l0main(p, nc, h):
    x_h, wq_h, wg_h = h["x_h"], h["wq_h"], h["wg_h"]
    wl1_h, wl2_h, id_h, x1_h = h["wl1_h"], h["wl2_h"], h["id_h"], h["x1_h"]

    with ExitStack() as esPh:
        p.default_es = esPh
        ident = p.sb("ident", [128, 128], BF16, es=esPh)
        scr = {"xt": p.sb("xt", [128, 1024], es=esPh), "sq": p.sb("sq", [128, 1024], es=esPh),
               "cols": p.sb("cols", [128, 8], es=esPh), "rot": p.sb("rot", [128, 2048], BF16, es=esPh),
               "eps": p.sb("eps", [128, 1], es=esPh)}
        pst = [p.ps("pst%d" % z, [128, 128], BF16, es=esPh) for z in range(2)]
        psp = [p.ps("psp%d" % z, [128, 512], es=esPh) for z in range(4)]
        psy = [p.ps("psy%d" % z, [128, 512], es=esPh) for z in range(2)]
        xnT = p.sb("xnT", [128, 8 * TCH], BF16, es=esPh)
        y = p.sb("y", [128, 17 * 1024], BF16, es=esPh)
        gates = p.sb("gates", [128, 17 * 8], es=esPh)
        wgbf = p.sb("wgbf", [128, 64], BF16, es=esPh)

        with nc.Block() as blk:
            @blk.gpsimd
            def _(g):
                p.inc(g.dma_start(ident[:], id_h[:, :]), "gd", 16)
                g.memset(scr["eps"][:], 1e-5)
                g.wait_ge(p.sem("gd"), p.cnt["gd"])

        with nc.Block() as blk:
            emit_ln_xnT_fixed(p, blk, x_h, ROWS17, xnT, TCH, ident, pst, scr, tag="lnt0")

        if "kv_ar" in h:
            wk_h, wv_h, kv_in, kv_out = h["kv_ar"]
            with ExitStack() as esKV:
                ktok = p.sb("ktok", [128, 16 * 1024], BF16, es=esKV)
                vtok = p.sb("vtok", [128, 16 * 1024], BF16, es=esKV)
                for wname, w_h, tok in (("k", wk_h, ktok), ("v", wv_h, vtok)):
                    with ExitStack() as esW:
                        wT = p.sb("w%sT" % wname, [128, 8 * 1024], BF16, es=esW)
                        emit_wT(p, nc, w_h, wT, ident, pst, p.es, tag="wt" + wname)
                        with nc.Block() as blk:
                            @blk.tensor
                            def _(t, wT=wT, wname=wname):
                                n = 0
                                for j in range(16):
                                    for half in range(2):
                                        if n >= 2:
                                            t.wait_ge(p.sem("pv" + wname), n - 1)
                                        for dc in range(8):
                                            ins = t.matmul(
                                                psp[n % 2][:],
                                                xnT[:, dc * TCH + 3 + j * 128: dc * TCH + 3 + j * 128 + 128],
                                                wT[:, dc * 1024 + half * 512: dc * 1024 + half * 512 + 512],
                                                start=(dc == 0), stop=(dc == 7))
                                        p.inc(ins, "pe" + wname, 1)
                                        n += 1

                            @blk.vector
                            def _(v, tok=tok, wname=wname):
                                n = 0
                                for j in range(16):
                                    for half in range(2):
                                        v.wait_ge(p.sem("pe" + wname), n + 1)
                                        v.tensor_copy(
                                            tok[:, j * 1024 + half * 512: j * 1024 + half * 512 + 512],
                                            psp[n % 2][:]).then_inc(p.sem("pv" + wname), 1)
                                        n += 1
                with ExitStack() as esKO:
                    kvout = p.sb("kvout", [128, 8 * 1024], F32, es=esKO)
                    with nc.Block() as blk:
                        @blk.tensor
                        def _(t):
                            n = 0
                            for kfc in range(8):
                                for half in range(2):
                                    if n >= 4:
                                        t.wait_ge(p.sem("kv_ev"), n - 3)
                                    for j in range(16):
                                        ins = t.matmul(
                                            psp[n % 4][:],
                                            ktok[:, j * 1024 + kfc * 128: j * 1024 + kfc * 128 + 128],
                                            vtok[:, j * 1024 + half * 512: j * 1024 + half * 512 + 512],
                                            start=(j == 0), stop=(j == 15))
                                    p.inc(ins, "kv_pe", 1)
                                    n += 1

                        @blk.vector
                        def _(v):
                            n = 0
                            for kfc in range(8):
                                for half in range(2):
                                    v.wait_ge(p.sem("kv_pe"), n + 1)
                                    v.tensor_copy(
                                        kvout[:, kfc * 1024 + half * 512: kfc * 1024 + half * 512 + 512],
                                        psp[n % 4][:]).then_inc(p.sem("kv_ev"), 1)
                                    n += 1

                        @blk.sync
                        def _(sync):
                            sync.wait_ge(p.sem("kv_ev"), 16)
                            p.inc(sync.dma_start(kv_in[:, :].rearrange("(c p) n -> p c n", p=128),
                                                 kvout[:, :].rearrange("p (c n) -> p c n", c=8)), "st", 16)
                            sync.wait_ge(p.sem("st"), p.cnt["st"])
                    with nc.Block() as blk:
                        @blk.gpsimd
                        def _(g):
                            p.inc(g.collective_compute(
                                "AllReduce", ALU.add,
                                replica_groups=[[0, 1, 2, 3], [4, 5, 6, 7]],
                                ins=[kv_in.ap().opt()], outs=[kv_out.ap().opt()]), "cc", 1)
                            g.wait_ge(p.sem("cc"), p.cnt["cc"])

        with ExitStack() as esA:
            kvsum = p.sb("kvsum", [128, 8 * 1024], F32, es=esA)
            kvbf = p.sb("kvbf", [128, 8 * 1024], BF16, es=esA)
            qT = p.sb("qT", [128, 8 * TCH], BF16, es=esA)
            wqT = p.sb("wqT", [128, 8 * 1024], BF16, es=esA)
            wgstg = p.sb("wgstg", [128, 64], F32, es=esA)

            with nc.Block() as blk:
                @blk.gpsimd
                def _(g):
                    if "kvp_h" in h:
                        kvp_h = h["kvp_h"]
                        for part in range(4):
                            if part > 0:
                                g.wait_ge(p.sem("gd"), p.cnt["gd"])
                            ins = g.dma_start(
                                kvsum[:, :].rearrange("p (c n) -> p c n", c=8),
                                kvp_h[part, :, :].rearrange("(c p) n -> p c n", p=128),
                                **({} if part == 0 else {"accum_op": ALU.add}))
                            p.inc(ins, "gd", 16)
                    else:
                        kv_out = h["kv_ar"][3]
                        p.inc(g.dma_start(
                            kvsum[:, :].rearrange("p (c n) -> p c n", c=8),
                            kv_out[:, :].rearrange("(c p) n -> p c n", p=128)), "gd", 16)
                    g.wait_ge(p.sem("gd"), p.cnt["gd"])

                @blk.sync
                def _(sync):
                    p.inc(sync.dma_start(wgstg[:, :].rearrange("p (c n) -> p c n", c=8),
                                         wg_h[:, :].rearrange("(c p) n -> p c n", p=128)),
                          "ld", 16)

                @blk.vector
                def _(v):
                    p.wait(v, "ld")
                    v.tensor_copy(wgbf[:], wgstg[:])
            with nc.Block() as blk:
                @blk.vector
                def _(v):
                    v.tensor_copy(kvbf[:], kvsum[:])

            emit_wT(p, nc, wq_h, wqT, ident, pst, p.es, tag="wtq")

            with nc.Block() as blk:
                @blk.tensor
                def _(t):
                    n = 0
                    for ec in range(8):
                        for (goff, gw) in GRP5:
                            if n >= 2:
                                t.wait_ge(p.sem("q_ev"), n - 1)
                            for dc in range(8):
                                ins = t.matmul(psp[n % 2][:, 0:gw],
                                               wqT[:, dc * 1024 + ec * 128: dc * 1024 + ec * 128 + 128],
                                               xnT[:, dc * TCH + goff: dc * TCH + goff + gw],
                                               start=(dc == 0), stop=(dc == 7))
                            p.inc(ins, "q_pe", 1)
                            n += 1

                @blk.vector
                def _(v):
                    n = 0
                    for ec in range(8):
                        for (goff, gw) in GRP5:
                            v.wait_ge(p.sem("q_pe"), n + 1)
                            v.tensor_scalar(qT[:, ec * TCH + goff: ec * TCH + goff + gw],
                                            psp[n % 2][:, 0:gw], 1.0 / 32.0, None,
                                            ALU.mult).then_inc(p.sem("q_ev"), 1)
                            n += 1

            with nc.Block() as blk:
                @blk.tensor
                def _(t):
                    n = 0
                    for i in range(17):
                        r = ROWS17[i]
                        for half in range(2):
                            if n >= 2:
                                t.wait_ge(p.sem("at_ev"), n - 1)
                            for kfc in range(8):
                                ins = t.matmul(psp[n % 2][0:r, :],
                                               qT[:, kfc * TCH + i * 128: kfc * TCH + i * 128 + r],
                                               kvbf[:, kfc * 1024 + half * 512: kfc * 1024 + half * 512 + 512],
                                               start=(kfc == 0), stop=(kfc == 7))
                            p.inc(ins, "at_pe", 1)
                            n += 1

                @blk.vector
                def _(v):
                    n = 0
                    for i in range(17):
                        r = ROWS17[i]
                        for half in range(2):
                            v.wait_ge(p.sem("at_pe"), n + 1)
                            v.tensor_copy(y[0:r, i * 1024 + half * 512: i * 1024 + half * 512 + 512],
                                          psp[n % 2][0:r, :]).then_inc(p.sem("at_ev"), 1)
                            n += 1

            with nc.Block() as blk:
                @blk.tensor
                def _(t):
                    for i in range(17):
                        r = ROWS17[i]
                        if i > 0:
                            t.wait_ge(p.sem("g_ev"), i)
                        for dc in range(8):
                            ins = t.matmul(psy[0][0:r, 0:8],
                                           xnT[:, dc * TCH + i * 128: dc * TCH + i * 128 + r],
                                           wgbf[:, dc * 8: dc * 8 + 8],
                                           start=(dc == 0), stop=(dc == 7))
                        p.inc(ins, "g_pe", 1)

                @blk.vector
                def _(v):
                    for i in range(17):
                        r = ROWS17[i]
                        v.wait_ge(p.sem("g_pe"), i + 1)
                        v.reduce_max(scr["cols"][0:r, 7:8], psy[0][0:r, 0:7],
                                     AX.X).then_inc(p.sem("g_c"), 1)
                        v.wait_ge(p.sem("g_c"), 2 * i + 1)
                        v.tensor_scalar(gates[0:r, i * 8: i * 8 + 8], psy[0][0:r, 0:8],
                                        scr["cols"][0:r, 7:8], None,
                                        ALU.is_ge).then_inc(p.sem("g_c"), 1)
                        v.wait_ge(p.sem("g_c"), 2 * i + 2)
                        v.memset(gates[0:r, i * 8 + 7: i * 8 + 8], 1.0).then_inc(p.sem("g_ev"), 1)

        hslab = p.sb("hslab", [128, 16 * TCH], BF16, es=esPh)
        w2e = p.sb("w2e", [128, 16 * 1024], BF16, es=esPh)
        w2stg = [p.sb("w2stg%d" % z, [128, 1024], F32, es=esPh) for z in range(2)]
        w1stg = [p.sb("w1s%d" % z, [128, 1024], F32, es=esPh) for z in range(2)]
        w1a = [p.sb("w1a%d" % z, [128, 1024], BF16, es=esPh) for z in range(2)]
        w1b = [p.sb("w1b%d" % z, [128, 1024], BF16, es=esPh) for z in range(2)]
        asc = p.sb("asc", [128, 512], BF16, es=esPh)
        ssc = p.sb("ssc", [128, 512], BF16, es=esPh)
        for e in range(8):
            with nc.Block() as blk:
                gd0 = p.cnt.get("gd", 0)
                w2c0 = p.cnt.get("w2c", 0)

                @blk.gpsimd
                def _(g, e=e, gd0=gd0, w2c0=w2c0):
                    for j in range(16):
                        if j >= 2:
                            g.wait_ge(p.sem("w2c"), w2c0 + j - 1)
                        p.inc(g.dma_start(w2stg[j % 2][:, :],
                                          wl2_h[e, j * 128:(j + 1) * 128, :]), "gd", 16)

                @blk.vector
                def _(v, e=e, gd0=gd0, w2c0=w2c0):
                    for j in range(16):
                        v.wait_ge(p.sem("gd"), gd0 + 16 * (j + 1))
                        p.inc(v.tensor_copy(w2e[:, j * 1024:(j + 1) * 1024],
                                            w2stg[j % 2][:]), "w2c", 1)

            with nc.Block() as blk:
                ld0 = p.cnt.get("ld", 0)
                w1c0 = p.cnt.get("w1c", 0)
                up0 = p.cnt.get("up_g", 0)
                ha0 = p.cnt.get("h_a", 0)
                hs0 = p.cnt.get("h_s", 0)
                hh0 = p.cnt.get("h_h", 0)

                @blk.sync
                def _(sync, e=e, ld0=ld0, w1c0=w1c0):
                    for j in range(16):
                        if j >= 2:
                            sync.wait_ge(p.sem("w1c"), w1c0 + 2 * (j - 1))
                        p.inc(sync.dma_start(
                            w1stg[j % 2][:, :].rearrange("p (c n) -> p c n", c=8),
                            wl1_h[e, :, j * 128: j * 128 + 128].rearrange("(c p) n -> p c n", p=128)),
                            "ld", 16)
                        sync.wait_ge(p.sem("w1c"), w1c0 + 2 * j + 1)
                        p.inc(sync.dma_start(
                            w1stg[j % 2][:, :].rearrange("p (c n) -> p c n", c=8),
                            wl1_h[e, :, 2048 + j * 128: 2048 + j * 128 + 128].rearrange("(c p) n -> p c n", p=128)),
                            "ld", 16)

                @blk.vector
                def _(v, e=e, ld0=ld0, up0=up0, hs0=hs0, ha0=ha0, hh0=hh0):
                    for j in range(16):
                        v.wait_ge(p.sem("ld"), ld0 + 32 * j + 16)
                        if j >= 2:
                            v.wait_ge(p.sem("up_g"), up0 + 10 * (j - 1))
                        p.inc(v.tensor_copy(w1a[j % 2][:], w1stg[j % 2][:]), "w1c", 1)
                        v.wait_ge(p.sem("ld"), ld0 + 32 * (j + 1))
                        p.inc(v.tensor_copy(w1b[j % 2][:], w1stg[j % 2][:]), "w1c", 1)
                        for g5 in range(5):
                            idx = j * 5 + g5
                            gw = GRP5[g5][1]
                            v.wait_ge(p.sem("up_g"), up0 + 2 * idx + 1)
                            p.inc(v.tensor_copy(asc[:, 0:gw], psp[idx % 2][:, 0:gw]), "h_a", 1)
                            v.wait_ge(p.sem("h_s"), hs0 + idx + 1)
                            p.inc(v.tensor_tensor(
                                hslab[:, j * TCH + GRP5[g5][0]: j * TCH + GRP5[g5][0] + gw],
                                asc[:, 0:gw], ssc[:, 0:gw], ALU.mult), "h_h", 1)

                @blk.scalar
                def _(s, e=e, up0=up0, hh0=hh0):
                    for j in range(16):
                        for g5 in range(5):
                            idx = j * 5 + g5
                            gw = GRP5[g5][1]
                            s.wait_ge(p.sem("up_g"), up0 + 2 * idx + 2)
                            if idx >= 1:
                                s.wait_ge(p.sem("h_h"), hh0 + idx)
                            p.inc(s.activation(ssc[:, 0:gw], psp[2 + idx % 2][:, 0:gw],
                                               AF.Silu), "h_s", 1)

                @blk.tensor
                def _(t, e=e, w1c0=w1c0, ha0=ha0, hs0=hs0, up0=up0):
                    for j in range(16):
                        t.wait_ge(p.sem("w1c"), w1c0 + 2 * (j + 1))
                        for g5 in range(5):
                            idx = j * 5 + g5
                            goff, gw = GRP5[g5]
                            if idx >= 2:
                                t.wait_ge(p.sem("h_a"), ha0 + idx - 1)
                                t.wait_ge(p.sem("h_s"), hs0 + idx - 1)
                            for dc in range(8):
                                ins = t.matmul(psp[idx % 2][:, 0:gw],
                                               w1a[j % 2][:, dc * 128:(dc + 1) * 128],
                                               xnT[:, dc * TCH + goff: dc * TCH + goff + gw],
                                               start=(dc == 0), stop=(dc == 7))
                            p.inc(ins, "up_g", 1)
                            for dc in range(8):
                                ins = t.matmul(psp[2 + idx % 2][:, 0:gw],
                                               w1b[j % 2][:, dc * 128:(dc + 1) * 128],
                                               xnT[:, dc * TCH + goff: dc * TCH + goff + gw],
                                               start=(dc == 0), stop=(dc == 7))
                            p.inc(ins, "up_g", 1)

            with nc.Block() as blk:
                dn0 = p.cnt.get("dn_pe", 0)
                ye0 = p.cnt.get("y_ev", 0)

                @blk.tensor
                def _(t, e=e, dn0=dn0, ye0=ye0):
                    for i in range(17):
                        r = ROWS17[i]
                        if i >= 1:
                            t.wait_ge(p.sem("y_ev"), ye0 + 2 * i)
                        for j in range(16):
                            i0 = t.matmul(psy[0][0:r, :],
                                          hslab[:, j * TCH + i * 128: j * TCH + i * 128 + r],
                                          w2e[:, j * 1024: j * 1024 + 512],
                                          start=(j == 0), stop=(j == 15))
                            i1 = t.matmul(psy[1][0:r, :],
                                          hslab[:, j * TCH + i * 128: j * TCH + i * 128 + r],
                                          w2e[:, j * 1024 + 512: j * 1024 + 1024],
                                          start=(j == 0), stop=(j == 15))
                        p.inc(i0, "dn_pe", 1)
                        p.inc(i1, "dn_pe", 1)

                @blk.vector
                def _(v, e=e, dn0=dn0):
                    for i in range(17):
                        r = ROWS17[i]
                        for half in range(2):
                            v.wait_ge(p.sem("dn_pe"), dn0 + 2 * i + half + 1)
                            p.inc(v.tensor_scalar(asc[0:r, :], psy[half][0:r, :],
                                                  gates[0:r, i * 8 + e: i * 8 + e + 1], None,
                                                  ALU.mult), "y_c", 1)
                            v.wait_ge(p.sem("y_c"), p.cnt["y_c"])
                            p.inc(v.tensor_tensor(
                                y[0:r, i * 1024 + half * 512: i * 1024 + half * 512 + 512],
                                y[0:r, i * 1024 + half * 512: i * 1024 + half * 512 + 512],
                                asc[0:r, :], ALU.add), "y_ev", 1)

        with nc.Block() as blk:
            st0 = p.cnt.get("st", 0)
            ld0 = p.cnt.get("ld", 0)

            @blk.sync
            def _(sync, st0=st0, ld0=ld0):
                r0 = 0
                for i in range(17):
                    r = ROWS17[i]
                    if i >= 1:
                        sync.wait_ge(p.sem("st"), st0 + 16 * i)
                    p.inc(sync.dma_start(scr["xt"][0:r, :], x_h[r0:r0 + r, :]), "ld", 16)
                    sync.wait_ge(p.sem("fin"), i + 1)
                    p.inc(sync.dma_start(x1_h[r0:r0 + r, :], scr["xt"][0:r, :]), "st", 16)
                    r0 += r
                sync.wait_ge(p.sem("st"), p.cnt["st"])

            @blk.vector
            def _(v, ld0=ld0):
                for i in range(17):
                    r = ROWS17[i]
                    v.wait_ge(p.sem("ld"), ld0 + 16 * (i + 1))
                    p.inc(v.tensor_copy(scr["sq"][0:r, :],
                                        y[0:r, i * 1024: i * 1024 + 1024]), "fin_c", 1)
                    v.wait_ge(p.sem("fin_c"), p.cnt["fin_c"])
                    p.inc(v.tensor_tensor(scr["xt"][0:r, :], scr["xt"][0:r, :], scr["sq"][0:r, :],
                                          ALU.add), "fin", 1)
        p.default_es = None

def build_l0main():
    p = Prog("l0main")
    nc = p.nc
    h = {"x_h": p.din("x", [TCH, D]), "kvp_h": p.din("kvp", [4, D, D]),
         "wq_h": p.din("wq", [D, D]), "wg_h": p.din("wg", [D, 8]),
         "wl1_h": p.din("wl1", [8, D, 4096]), "wl2_h": p.din("wl2", [8, 2048, D]),
         "id_h": p.din("identbf", [128, 128], BF16), "x1_h": p.dout("x1", [TCH, D])}
    with p.es:
        emit_l0main(p, nc, h)
    return nc



class Chain2:
    ENGSEM = {"tensor": "pe", "vector": "dv", "scalar": "ac", "sync": "sy", "gpsimd": "gp"}
    ENGINC = {"tensor": 1, "vector": 1, "scalar": 1, "sync": 16, "gpsimd": 16}

    def __init__(self, p, tag="c2"):
        self.p = p
        self.tag = tag
        self.steps = []

    def add(self, eng, fn, reads=(), writes=()):
        self.steps.append((eng, fn, tuple(reads), tuple(writes)))

    def dma(self, fn, reads=(), writes=()):
        self.add("sync", fn, reads, writes)

    def gdma(self, fn, reads=(), writes=()):
        self.add("gpsimd", fn, reads, writes)

    def emit(self, nc):
        p, tag = self.p, self.tag
        n = len(self.steps)
        eng_order = []
        eng_count = {}
        for (e, _, _, _) in self.steps:
            eng_order.append(eng_count.get(e, 0))
            eng_count[e] = eng_count.get(e, 0) + 1
        last_writer = {}
        readers = {}
        deps = [dict() for _ in range(n)]
        for i, (e, fn, rd, wr) in enumerate(self.steps):
            dd = {}
            def need(j):
                ej = self.steps[j][0]
                if ej == e and eng_order[j] < eng_order[i]:
                    pass
                dd[ej] = max(dd.get(ej, -1), eng_order[j])
            for k in rd:
                if k in last_writer:
                    need(last_writer[k])
            for k in wr:
                if k in last_writer:
                    need(last_writer[k])
                for j in readers.get(k, ()):
                    need(j)
            for k in wr:
                last_writer[k] = i
                readers[k] = []
            for k in rd:
                readers.setdefault(k, []).append(i)
            deps[i] = dd
        base = {e: p.cnt.get(tag + s, 0) for e, s in self.ENGSEM.items()}
        sems = {e: p.sem(tag + s) for e, s in self.ENGSEM.items()}
        with nc.Block() as blk:
            for engname in ("tensor", "vector", "scalar", "sync", "gpsimd"):
                mysteps = [(i, f) for i, (e, f, _, _) in enumerate(self.steps) if e == engname]
                if not mysteps:
                    continue

                def body(eng, mysteps=mysteps, engname=engname):
                    for (i, f) in mysteps:
                        for (de, dj) in sorted(deps[i].items()):
                            if de == engname and dj == eng_order[i] - 1:
                                pass
                            eng.wait_ge(sems[de], base[de] + self.ENGINC[de] * (dj + 1))
                        ins = f(eng)
                        ins.then_inc(sems[engname], self.ENGINC[engname])
                getattr(blk, engname)(body)
            for e in eng_count:
                p.cnt[tag + self.ENGSEM[e]] = base[e] + self.ENGINC[e] * eng_count[e]

            def fin(eng):
                for e, cnt in eng_count.items():
                    if e != "gpsimd":
                        eng.wait_ge(sems[e], base[e] + self.ENGINC[e] * cnt)
            blk.gpsimd(fin)


class Chain:

    def __init__(self, p, tag="ck"):
        self.p = p
        self.tag = tag
        self.steps = []

    def add(self, eng, fn, inc=1):
        self.steps.append((eng, fn, inc))

    def dma(self, fn):
        self.add("sync", fn, 16)

    def emit(self, nc):
        p, tag = self.p, self.tag
        cum = []
        c = p.cnt.get(tag, 0)
        for (_, _, inc) in self.steps:
            cum.append(c)
            c += inc
        p.cnt[tag] = c
        sem = p.sem(tag)
        with nc.Block() as blk:
            for engname in ("tensor", "vector", "scalar", "sync", "gpsimd"):
                mysteps = [(i, f, inc) for i, (e, f, inc) in enumerate(self.steps) if e == engname]
                if not mysteps:
                    continue

                def body(eng, mysteps=mysteps):
                    for (i, f, inc) in mysteps:
                        if cum[i] > 0:
                            eng.wait_ge(sem, cum[i])
                        ins = f(eng)
                        ins.then_inc(sem, inc)
                getattr(blk, engname)(body)
            def fin(eng):
                eng.wait_ge(sem, c)
            blk.gpsimd(fin)


def emit_chunk_pass(p, nc, t, nchunks=16, nheads=16):
    ident, identf = t["ident"], t["identf"]
    maskD, maskG, onesr = t["maskD"], t["maskG"], t["onesr"]
    mdg = t["mdg"]
    psp, psx, psm, pst = t["psp"], t["psx"], t["psm"], t["pst"]
    q1c, k1c, v1c = t["q1c"], t["k1c"], t["v1c"]
    lnb, lnrq, lnrk, cpad = t["lnb"], t["lnrq"], t["lnrk"], t["cpad"]
    Sf, Sbf, Ncum, NcumT = t["Sf"], t["Sbf"], t["Ncum"], t["NcumT"]
    ot_h, z_h, pct_h, dscr_h = t["ot_h"], t["z_h"], t["pct_h"], t["dscr_h"]

    flat = p.sb("flat", [1, 4 * 2048])
    crel = p.sb("crel", [16, 128]); cprev = p.sb("cprev", [16, 128])
    biasQ1 = p.sb("biasQ1", [16, 128])
    rowD = p.sb("rowD", [16, 128]); rowG = p.sb("rowG", [16, 128])
    expQ3 = p.sb("expQ3", [16, 128]); expQ4r = p.sb("expQ4r", [16, 128])
    negQ2r = p.sb("negQ2r", [16, 128])
    bT = p.sb("bT", [128, 16]); negQ2T = p.sb("negQ2T", [128, 16]); q4T = p.sb("q4T", [128, 16])
    scrD = p.sb("scrD", [128, 128]); scrE = p.sb("scrE", [128, 128])
    Dbuf = p.sb("Dbuf", [128, 128]); Gbuf = p.sb("Gbuf", [128, 128], BF16)
    Ddg = p.sb("Ddg", [128, 128]); Doff = p.sb("Doff", [128, 128])
    pows = [p.sb("pow%d" % z, [128, 128]) for z in range(5)]
    ptmp = p.sb("ptmp", [128, 128])
    Xf = p.sb("Xf", [128, 128]); Yf = p.sb("Yf", [128, 128]); Xacc = p.sb("Xacc", [128, 128])
    Xbuf = p.sb("Xbuf", [128, 128], BF16)
    ktok = p.sb("ktok", [128, 64], BF16); vtok = p.sb("vtok", [128, 64], BF16)
    negk = p.sb("negk", [128, 128], BF16)
    ktb = p.sb("ktb", [128, 64], BF16)
    vsc = p.sb("vsc", [128, 64], BF16)
    qgbuf = p.sb("qgbuf", [128, 128], BF16); qbb2 = p.sb("qbb2", [128, 128], BF16)
    zbuf = p.sb("zbuf", [128, 128], BF16); zout = p.sb("zout", [64, 128], BF16)
    obuf = p.sb("obuf", [64, 128])
    pctbuf = p.sb("pctbuf", [64, 64], BF16)
    gcol = p.sb("gcol", [64, 1])

    ch = Chain(p)

    for ci in range(nchunks):
        c0 = ci * 128
        ch.add("vector", lambda v, c0=c0: v.tensor_scalar(
            crel[:], cpad[0:16, 1 + c0:1 + c0 + 128], cpad[0:16, c0:c0 + 1], None, ALU.subtract))
        ch.add("vector", lambda v, c0=c0: v.tensor_scalar(
            cprev[:], cpad[0:16, c0:c0 + 128], cpad[0:16, c0:c0 + 1], None, ALU.subtract))
        ch.add("vector", lambda v, c0=c0: v.tensor_tensor(
            biasQ1[:], lnb[0:16, c0:c0 + 128], crel[:], ALU.subtract))
        ch.add("vector", lambda v, c0=c0: v.tensor_tensor(
            biasQ1[:], biasQ1[:], lnrk[0:16, c0:c0 + 128], ALU.add))
        ch.add("vector", lambda v, c0=c0: v.tensor_tensor(
            rowD[:], cprev[:], lnrk[0:16, c0:c0 + 128], ALU.add))
        ch.add("vector", lambda v, c0=c0: v.tensor_tensor(
            rowG[:], crel[:], lnrq[0:16, c0:c0 + 128], ALU.add))
        ch.add("scalar", lambda s: s.activation(
            expQ4r[:], biasQ1[:], AF.Exp, bias=crel[0:16, 127:128]))
        ch.add("scalar", lambda s: s.activation(negQ2r[:], rowD[:], AF.Exp))
        ch.add("vector", lambda v: v.tensor_scalar(negQ2r[:], negQ2r[:], -1.0, None, ALU.mult))
        ch.add("scalar", lambda s: s.activation(expQ3[:], rowG[:], AF.Exp))
        for qi, buf in ((0, rowD), (1, rowG), (2, expQ3), (3, crel)):
            ch.dma(lambda sy, qi=qi, buf=buf: sy.dma_start(dscr_h[qi, :, :], buf[:]))
        for qi in range(4):
            ch.dma(lambda sy, qi=qi: sy.dma_start(
                flat[0:1, qi * 2048:(qi + 1) * 2048],
                dscr_h[qi:qi + 1, :, :].rearrange("a p n -> a (p n)")))
        for (srcb, dstb) in ((biasQ1, bT), (negQ2r, negQ2T), (expQ4r, q4T)):
            ch.add("tensor", lambda tt, srcb=srcb: tt.transpose(
                psm[:, 0:16], srcb[:], identf[0:16, 0:16]))
            ch.add("vector", lambda v, dstb=dstb: v.tensor_copy(dstb[:], psm[:, 0:16]))

        for h in range(nheads):
            hc, ho = h // 2, (h % 2) * 64
            kS = lambda sl=k1c, hc=hc, ho=ho, ci=ci: sl[ho:ho + 64, hc * 2048 + ci * 128: hc * 2048 + ci * 128 + 128]
            qS = lambda hc=hc, ho=ho, ci=ci: q1c[ho:ho + 64, hc * 2048 + ci * 128: hc * 2048 + ci * 128 + 128]
            vS = lambda hc=hc, ho=ho, ci=ci: v1c[ho:ho + 64, hc * 2048 + ci * 128: hc * 2048 + ci * 128 + 128]

            ch.add("tensor", lambda tt, kS=kS, ho=ho: tt.transpose(
                pst[0][:, 0:64], kS(), ident[ho:ho + 64, ho:ho + 64]))
            ch.add("vector", lambda v: v.tensor_copy(ktok[:], pst[0][:, 0:64]))
            ch.add("tensor", lambda tt, vS=vS, ho=ho: tt.transpose(
                pst[1][:, 0:64], vS(), ident[ho:ho + 64, ho:ho + 64]))
            ch.add("vector", lambda v: v.tensor_copy(vtok[:], pst[1][:, 0:64]))
            ch.add("vector", lambda v, kS=kS, ho=ho: v.tensor_scalar(
                negk[ho:ho + 64, :], kS(), -1.0, None, ALU.mult))
            ch.add("tensor", lambda tt, kS=kS, ho=ho: tt.matmul(
                psp[0][:, 0:128], negk[ho:ho + 64, :], kS(), start=True, stop=True, skip_group_check=True))
            ch.add("tensor", lambda tt, kS=kS, qS=qS: tt.matmul(
                psp[1][:, 0:128], kS(), qS(), start=True, stop=True, skip_group_check=True))
            ch.add("tensor", lambda tt, h=h: tt.matmul(
                psp[2][:, 0:128], onesr[0:1, :], flat[0:1, h * 128: h * 128 + 128],
                start=True, stop=True, skip_group_check=True))
            ch.add("vector", lambda v: v.tensor_tensor(scrD[:], psp[2][:, 0:128], maskD[:], ALU.add))
            ch.add("scalar", lambda s, h=h: s.activation(scrE[:], scrD[:], AF.Exp,
                                                         bias=bT[:, h:h + 1]))
            ch.add("vector", lambda v: v.tensor_tensor(Dbuf[:], scrE[:], psp[0][:, 0:128], ALU.mult))
            ch.add("tensor", lambda tt, h=h: tt.matmul(
                psp[2][:, 0:128], onesr[0:1, :], flat[0:1, 2048 + h * 128: 2048 + h * 128 + 128],
                start=True, stop=True, skip_group_check=True))
            ch.add("vector", lambda v: v.tensor_tensor(scrD[:], psp[2][:, 0:128], maskG[:], ALU.add))
            ch.add("scalar", lambda s, h=h: s.activation(scrE[:], scrD[:], AF.Exp,
                                                         bias=bT[:, h:h + 1]))
            ch.add("vector", lambda v: v.tensor_tensor(Gbuf[:], scrE[:], psp[1][:, 0:128], ALU.mult))
            ch.add("tensor", lambda tt, kS=kS, h=h, hc=hc, ho=ho: tt.matmul(
                psp[3][:, 0:64], kS(), Sbf[ho:ho + 64, hc * 64: hc * 64 + 64],
                start=True, stop=True, skip_group_check=True))
            ch.add("vector", lambda v, h=h: v.tensor_scalar(
                vsc[:], psp[3][:, 0:64], negQ2T[:, h:h + 1], None, ALU.mult))
            ch.add("vector", lambda v: v.tensor_tensor(Xf[:, 0:64], vsc[:], vtok[:], ALU.add))
            ch.add("vector", lambda v, h=h: v.tensor_scalar(
                Xf[:, 64:128], ktok[:], negQ2T[:, h:h + 1], None, ALU.mult))
            ch.add("vector", lambda v: v.tensor_tensor(Ddg[:], Dbuf[:], mdg[:], ALU.mult))
            ch.add("vector", lambda v: v.tensor_tensor(Doff[:], Dbuf[:], Ddg[:], ALU.subtract))
            ch.add("vector", lambda v: v.tensor_copy(pows[0][:], Ddg[:]))
            for kq in range(1, 5):
                ch.add("tensor", lambda tt, kq=kq: tt.transpose(psm[:] if False else psx[:],
                                                                pows[kq - 1][:], identf[:]))
                ch.add("vector", lambda v: v.tensor_copy(ptmp[:], psx[:]))
                ch.add("tensor", lambda tt, kq=kq: tt.matmul(
                    psm[:], ptmp[:], pows[kq - 1][:], start=True, stop=True, skip_group_check=True))
                ch.add("vector", lambda v, kq=kq: v.tensor_copy(pows[kq][:], psm[:]))

            def td_apply(ch):
                for kq in range(5):
                    ch.add("tensor", lambda tt, kq=kq: tt.matmul(
                        psm[:], pows[kq][:], Yf[:], start=True, stop=True, skip_group_check=True))
                    ch.add("vector", lambda v: v.tensor_tensor(Yf[:], Yf[:], psm[:], ALU.add))
            ch.add("vector", lambda v: v.tensor_copy(Yf[:], Xf[:]))
            td_apply(ch)
            ch.add("vector", lambda v: v.tensor_copy(Xacc[:], Yf[:]))
            for mh in range(3):
                ch.add("tensor", lambda tt: tt.matmul(psm[:], Doff[:], Yf[:],
                                                      start=True, stop=True, skip_group_check=True))
                ch.add("vector", lambda v: v.tensor_copy(Yf[:], psm[:]))
                td_apply(ch)
                ch.add("vector", lambda v: v.tensor_tensor(Xacc[:], Xacc[:], Yf[:], ALU.add))
            ch.add("vector", lambda v: v.tensor_copy(Xbuf[:], Xacc[:]))
            ch.add("tensor", lambda tt: tt.matmul(psp[2][0:64, 0:128], Xbuf[:, 64:128], Gbuf[:],
                                                  start=True, stop=True, skip_group_check=True))
            ch.add("tensor", lambda tt, h=h: tt.matmul(
                psp[3][0:64, 0:128], onesr[0:1, 0:64], flat[0:1, 2 * 2048 + h * 128: 2 * 2048 + h * 128 + 128],
                start=True, stop=True, skip_group_check=True))
            ch.add("vector", lambda v, ho=ho: v.tensor_copy(qbb2[ho:ho + 64, :], psp[3][0:64, 0:128]))
            ch.add("vector", lambda v, ho=ho, qS=qS: v.tensor_tensor(
                qgbuf[ho:ho + 64, :], qbb2[ho:ho + 64, :], qS(), ALU.mult))
            ch.add("tensor", lambda tt: tt.matmul(psp[0][0:64, 0:128], Xbuf[:, 0:64], Gbuf[:],
                                                  start=True, stop=False, skip_group_check=True))
            ch.add("tensor", lambda tt, h=h, hc=hc, ho=ho: tt.matmul(
                psp[0][0:64, 0:128], Sbf[ho:ho + 64, hc * 64: hc * 64 + 64], qgbuf[ho:ho + 64, :],
                start=False, stop=True, skip_group_check=True))
            ch.add("vector", lambda v: v.tensor_copy(obuf[:], psp[0][0:64, 0:128]))
            ch.dma(lambda sy, ci=ci, h=h: sy.dma_start(
                ot_h[h * 64:(h + 1) * 64, ci * 128:(ci + 1) * 128], obuf[:]))
            ch.add("vector", lambda v, ho=ho: v.tensor_copy(zbuf[ho:ho + 64, :], psp[2][0:64, 0:128]))
            ch.add("vector", lambda v, ho=ho: v.tensor_tensor(
                zout[:], zbuf[ho:ho + 64, :], qgbuf[ho:ho + 64, :], ALU.add))
            ch.dma(lambda sy, ci=ci, h=h: sy.dma_start(z_h[ci, h, :, :], zout[:]))
            ch.add("tensor", lambda tt, h=h: tt.matmul(
                psm[0:64, 0:1], onesr[0:1, 0:64], flat[0:1, 3 * 2048 + h * 128 + 127: 3 * 2048 + h * 128 + 128],
                start=True, stop=True, skip_group_check=True))
            ch.add("scalar", lambda s: s.activation(gcol[:], psm[0:64, 0:1], AF.Exp))
            ch.add("vector", lambda v, h=h: v.tensor_scalar(
                ktb[:], ktok[:], q4T[:, h:h + 1], None, ALU.mult))
            ch.add("tensor", lambda tt: tt.matmul(psp[1][0:64, 0:64], ktb[:], Xbuf[:, 0:64],
                                                  start=True, stop=True, skip_group_check=True))
            ch.add("vector", lambda v, h=h: v.scalar_tensor_tensor(
                Sf[0:64, h * 64: h * 64 + 64], Sf[0:64, h * 64: h * 64 + 64], gcol[0:64, 0:1],
                psp[1][0:64, 0:64], ALU.mult, ALU.add))
            ch.add("vector", lambda v, h=h, hc=hc, ho=ho: v.tensor_copy(
                Sbf[ho:ho + 64, hc * 64: hc * 64 + 64], Sf[0:64, h * 64: h * 64 + 64]))
            ch.add("tensor", lambda tt: tt.matmul(psp[2][0:64, 0:64], Xbuf[:, 64:128], ktb[:],
                                                  start=True, stop=True, skip_group_check=True))
            ch.add("vector", lambda v: v.scalar_tensor_tensor(
                pctbuf[:], identf[0:64, 0:64], gcol[0:64, 0:1], psp[2][0:64, 0:64],
                ALU.mult, ALU.add))
            ch.dma(lambda sy, ci=ci, h=h: sy.dma_start(pct_h[ci, h, :, :], pctbuf[:]))
            ch.add("tensor", lambda tt, h=h: tt.matmul(
                psp[3][0:64, 0:64], pctbuf[:], Ncum[0:64, h * 64: h * 64 + 64],
                start=True, stop=True, skip_group_check=True))
            ch.add("tensor", lambda tt, h=h: tt.matmul(
                psm[0:64, 64:128], Ncum[0:64, h * 64: h * 64 + 64], pctbuf[:],
                start=True, stop=True, skip_group_check=True))
            ch.add("vector", lambda v, h=h: v.tensor_copy(
                Ncum[0:64, h * 64: h * 64 + 64], psp[3][0:64, 0:64]))
            ch.add("vector", lambda v, h=h: v.tensor_copy(
                NcumT[0:64, h * 64: h * 64 + 64], psm[0:64, 64:128]))

    ch.emit(nc)



GRP4 = [(0, 512), (512, 512), (1024, 512), (1536, 512)]


def emit_l1loc(p, nc, h, nchunks=16, nheads=16):
    x_h = h["x1_h"]
    wq_h, wk_h, wv_h = h["wq1_h"], h["wk1_h"], h["wv1_h"]
    qcw_h, qcb_h = h["qcw_h"], h["qcb_h"]
    kcw_h, kcb_h = h["kcw_h"], h["kcb_h"]
    vcw_h, vcb_h = h["vcw_h"], h["vcb_h"]
    wa_h, ba_h = h["wa_h"], h["ba_h"]
    wb_h, bb_h = h["wb_h"], h["bb_h"]
    id_h, idf_h = h["id_h"], h["idf_h"]
    mD_h, mG_h, mdg_h = h["mD_h"], h["mG_h"], h["mdg_h"]
    e8_h = h["e8_h"]
    ot_h, z_h, pct_h = h["ot_h"], h["z_h"], h["pct_h"]
    dscr_h = h["dscr_h"]
    stex = h["stex"]
    with ExitStack() as esPh:
        p.default_es = esPh

        ident = p.sb("sident", [128, 128], BF16)
        identf = p.sb("sidentf", [128, 128], F32)
        maskD = p.sb("smaskD", [128, 128], F32)
        maskG = p.sb("smaskG", [128, 128], F32)
        mdg = p.sb("smdg", [128, 128], F32)
        e8 = p.sb("se8", [128, 8 * 16], BF16)
        onesr = p.sb("onesr", [1, 128], F32)
        epst = p.sb("eps", [128, 1])
        colst = p.sb("cols", [128, 8])
        lnb = p.sb("lnb", [16, 2048])
        lnrq = p.sb("lnrq", [16, 2048])
        lnrk = p.sb("lnrk", [16, 2048])
        cpad = p.sb("cpad", [16, 2049])
        zc16 = p.sb("zc16", [16, 1])
        q1c = p.sb("q1c", [128, 8 * 2048], BF16)
        k1c = p.sb("k1c", [128, 8 * 2048], BF16)
        v1c = p.sb("v1c", [128, 8 * 2048], BF16)
        Sf = p.sb("Sf", [64, 16 * 64])
        Sbf = p.sb("Sbf", [128, 8 * 64], BF16)
        Ncum = p.sb("Ncum", [64, 16 * 64], BF16)
        NcumT = p.sb("NcumT", [64, 16 * 64], BF16)
        psp = [p.ps("psp%d" % z, [128, 512]) for z in range(4)]
        psx = p.ps("psx", [128, 128])
        psm = p.ps("psm", [128, 128])
        pst = [p.ps("pst%d" % z, [128, 128], BF16) for z in range(2)]

        with nc.Block() as blk:
            @blk.gpsimd
            def _(g):
                for (dst, srch) in ((ident, id_h), (identf, idf_h), (maskD, mD_h), (maskG, mG_h), (mdg, mdg_h)):
                    p.inc(g.dma_start(dst[:], srch[:, :]), "gd", 16)
                p.inc(g.dma_start(e8[:, :].rearrange("p (e h) -> p e h", e=8),
                                  e8_h[:, :, :].rearrange("e p h -> p e h")), "gd", 16)
                g.memset(onesr[:], 1.0)
                g.memset(zc16[:], 0.0)
                g.memset(cpad[:, 0:1], 0.0)
                g.memset(epst[:], 1e-5)
                g.memset(Sf[:], 0.0)
                g.memset(Sbf[:], 0.0)
                g.wait_ge(p.sem("gd"), p.cnt["gd"])
            @blk.vector
            def _(v):
                pass

        with nc.Block() as blk:
            @blk.vector
            def _(v):
                for h in range(16):
                    v.tensor_copy(Ncum[0:64, h * 64:(h + 1) * 64], ident[0:64, 0:64])
                    v.tensor_copy(NcumT[0:64, h * 64:(h + 1) * 64], ident[0:64, 0:64])

        esP = ExitStack()
        xnT = p.sb("xnT", [128, 8 * TCH], BF16, es=esP)
        with ExitStack() as esLN:
            scr = {"xt": p.sb("xt", [128, 1024], es=esLN), "sq": p.sb("sq", [128, 1024], es=esLN),
                   "cols": colst, "rot": p.sb("rot", [128, 2048], BF16, es=esLN),
                   "eps": epst}
            with nc.Block() as blk:
                emit_ln_xnT_fixed(p, blk, x_h, ROWS17, xnT, TCH, ident, pst, scr, tag="lnt")

        with ExitStack() as esA:
            wab = p.sb("wab", [128, 8 * 16 * 2], BF16, es=esA)
            wstg = p.sb("wabstg", [128, 8 * 16], F32, es=esA)
            bac = p.sb("bac", [16, 1], F32, es=esA)
            bbc = p.sb("bbc", [16, 1], F32, es=esA)
            asig = p.sb("asig", [16, 2048], F32, es=esA)
            with nc.Block() as blk:
                @blk.sync
                def _(sync):
                    p.inc(sync.dma_start(wstg[:, :].rearrange("p (c n) -> p c n", c=8),
                                         wa_h[:, :].rearrange("(c p) n -> p c n", p=128)), "ld", 16)
                    sync.wait_ge(p.sem("ld"), p.cnt["ld"])
                    p.inc(sync.dma_start(bac[:], ba_h[:, :]), "ld", 16)
                    p.inc(sync.dma_start(bbc[:], bb_h[:, :]), "ld", 16)
                @blk.vector
                def _(v):
                    v.wait_ge(p.sem("ld"), p.cnt["ld"] - 32)
                    v.tensor_copy(wab[:, 0:128], wstg[:])
            with nc.Block() as blk:
                @blk.sync
                def _(sync):
                    p.inc(sync.dma_start(wstg[:, :].rearrange("p (c n) -> p c n", c=8),
                                         wb_h[:, :].rearrange("(c p) n -> p c n", p=128)), "ld", 16)
                @blk.vector
                def _(v):
                    p.wait(v, "ld")
                    v.tensor_copy(wab[:, 128:256], wstg[:])
            for which in range(2):
                with nc.Block() as blk:
                    pe0 = p.cnt.get("ab_pe", 0)
                    sg0 = p.cnt.get("ab_sg", 0)

                    @blk.tensor
                    def _(t, which=which, pe0=pe0, sg0=sg0):
                        for g4 in range(4):
                            goff, gw = GRP4[g4]
                            if g4 >= 2:
                                t.wait_ge(p.sem("ab_sg"), sg0 + g4 - 1)
                            for dc in range(8):
                                ins = t.matmul(psp[g4 % 2][0:16, :],
                                               wab[:, which * 128 + dc * 16: which * 128 + dc * 16 + 16],
                                               xnT[:, dc * TCH + 3 + goff: dc * TCH + 3 + goff + gw],
                                               start=(dc == 0), stop=(dc == 7))
                            p.inc(ins, "ab_pe", 1)

                    @blk.scalar
                    def _(s, which=which, pe0=pe0):
                        for g4 in range(4):
                            goff, gw = GRP4[g4]
                            s.wait_ge(p.sem("ab_pe"), pe0 + g4 + 1)
                            p.inc(s.activation((asig if which == 0 else lnb)[0:16, goff:goff + gw],
                                               psp[g4 % 2][0:16, :], AF.Sigmoid,
                                               bias=(bac if which == 0 else bbc)[0:16, 0:1]),
                                  "ab_sg", 1)
            with nc.Block() as blk:
                @blk.vector
                def _(v):
                    p.inc(v.tensor_scalar(asig[:], asig[:], 0.1, None, ALU.max), "c_c", 1)
                @blk.scalar
                def _(s):
                    s.wait_ge(p.sem("c_c"), p.cnt["c_c"])
                    p.inc(s.activation(cpad[0:16, 1:2049], asig[:], AF.Ln), "c_c", 1)
                    p.inc(s.activation(lnb[:], lnb[:], AF.Ln), "c_c", 1)
            with nc.Block() as blk:
                @blk.vector
                def _(v):
                    p.inc(v.tensor_tensor_scan(cpad[0:16, 1:2049],
                                               bass.AP(zc16, 0, [[1, 16], [0, 2048]]),
                                               cpad[0:16, 1:2049], 0.0, ALU.add, ALU.add), "c_c", 1)

        with ExitStack() as esB:
            wbf = p.sb("w1bf", [128, 8 * 1024], BF16, es=esB)
            wstg2 = p.sb("wstg2", [128, 1024], F32, es=esB)
            raw = [p.sb("raw%d" % z, [128, TCH], BF16, es=esB) for z in range(2)]
            cwc = p.sb("cwc", [128, 8 * 4], F32, es=esB)
            cbc = p.sb("cbc", [128, 8], F32, es=esB)
            acc = [p.sb("acc%d" % z, [128, 512], F32, es=esB) for z in range(2)]
            sqrot = raw[0]

            for pi, (w_h, cw_h, cb_h, outslab) in enumerate(
                    ((wq_h, qcw_h, qcb_h, q1c), (wk_h, kcw_h, kcb_h, k1c), (wv_h, vcw_h, vcb_h, v1c))):
                for dc in range(8):
                    with nc.Block() as blk:
                        @blk.sync
                        def _(sync, dc=dc, w_h=w_h):
                            p.inc(sync.dma_start(wstg2[:, :], w_h[dc * 128:(dc + 1) * 128, :]), "ld", 16)
                        @blk.vector
                        def _(v, dc=dc):
                            p.wait(v, "ld")
                            v.tensor_copy(wbf[:, dc * 1024:(dc + 1) * 1024], wstg2[:])
                with nc.Block() as blk:
                    @blk.sync
                    def _(sync, cw_h=cw_h, cb_h=cb_h):
                        p.inc(sync.dma_start(cwc[:, :].rearrange("p (c n) -> p c n", c=8),
                                             cw_h[:, :].rearrange("(c p) n -> p c n", p=128)), "ld", 16)
                        p.inc(sync.dma_start(cbc[:, :], cb_h[:, :]), "ld", 16)
                with nc.Block() as blk:
                    pe0 = p.cnt.get("pj_pe", 0)
                    ev0 = p.cnt.get("pj_ev", 0)
                    cv0 = p.cnt.get("cv_c", 0)
                    cvs0 = p.cnt.get("cv_s", 0)

                    @blk.tensor
                    def _(t, pe0=pe0, ev0=ev0, cvs0=cvs0):
                        n = 0
                        for ec in range(8):
                            if ec >= 2:
                                t.wait_ge(p.sem("cv_s"), cvs0 + 4 * (ec - 1))
                            for (goff, gw) in GRP5:
                                if n >= 2:
                                    t.wait_ge(p.sem("pj_ev"), ev0 + n - 1)
                                for dc in range(8):
                                    ins = t.matmul(psp[n % 2][:, 0:gw],
                                                   wbf[:, dc * 1024 + ec * 128: dc * 1024 + ec * 128 + 128],
                                                   xnT[:, dc * TCH + goff: dc * TCH + goff + gw],
                                                   start=(dc == 0), stop=(dc == 7))
                                p.inc(ins, "pj_pe", 1)
                                n += 1

                    @blk.vector
                    def _(v, pe0=pe0, cvs0=cvs0):
                        n = 0
                        for ec in range(8):
                            for gi, (goff, gw) in enumerate(GRP5):
                                v.wait_ge(p.sem("pj_pe"), pe0 + n + 1)
                                p.inc(v.tensor_copy(raw[ec % 2][:, goff: goff + gw],
                                                    psp[n % 2][:, 0:gw]), "pj_ev", 1)
                                n += 1
                            for g4 in range(4):
                                goff, gw = GRP4[g4]
                                m = ec * 4 + g4
                                if m >= 2:
                                    v.wait_ge(p.sem("cv_s"), cvs0 + m - 1)
                                v.tensor_scalar(acc[m % 2][:, :], raw[ec % 2][:, goff: goff + 512],
                                                cwc[:, ec * 4: ec * 4 + 1], None, ALU.mult)
                                for j in range(1, 4):
                                    ins = v.scalar_tensor_tensor(acc[m % 2][:, :],
                                                                 raw[ec % 2][:, goff + j: goff + j + 512],
                                                                 cwc[:, ec * 4 + j: ec * 4 + j + 1],
                                                                 acc[m % 2][:, :], ALU.mult, ALU.add)
                                p.inc(ins, "cv_c", 1)

                    @blk.scalar
                    def _(s, cv0=cv0, outslab=outslab):
                        for ec in range(8):
                            for g4 in range(4):
                                goff, gw = GRP4[g4]
                                m = ec * 4 + g4
                                s.wait_ge(p.sem("cv_c"), cv0 + m + 1)
                                p.inc(s.activation(outslab[:, ec * 2048 + goff: ec * 2048 + goff + 512],
                                                   acc[m % 2][:, :], AF.Silu,
                                                   bias=cbc[:, ec: ec + 1]), "cv_s", 1)
            for pi, (slab, lnout) in enumerate(((q1c, lnrq), (k1c, lnrk))):
                with nc.Block() as blk:
                    sq0 = p.cnt.get("rn_sq", 0)
                    pe0 = p.cnt.get("rn_pe", 0)
                    @blk.scalar
                    def _(s, slab=slab, sq0=sq0, pe0=pe0):
                        for dc in range(8):
                            if dc >= 1:
                                s.wait_ge(p.sem("rn_pe"), pe0 + 4 * dc)
                            p.inc(s.activation(sqrot[:, 0:2048], slab[:, dc * 2048:(dc + 1) * 2048],
                                               AF.Square), "rn_sq", 1)
                    @blk.tensor
                    def _(t, sq0=sq0):
                        for dc in range(8):
                            t.wait_ge(p.sem("rn_sq"), sq0 + dc + 1)
                            for g4 in range(4):
                                goff, gw = GRP4[g4]
                                ins = t.matmul(psp[g4][0:16, :], e8[:, dc * 16:(dc + 1) * 16],
                                               sqrot[:, goff:goff + 512],
                                               start=(dc == 0), stop=(dc == 7), skip_group_check=True)
                                p.inc(ins, "rn_pe", 1)
                with nc.Block() as blk:
                    @blk.scalar
                    def _(s, lnout=lnout):
                        for g4 in range(4):
                            goff, gw = GRP4[g4]
                            p.inc(s.activation(lnout[0:16, goff:goff + 512], psp[g4][0:16, :],
                                               AF.Sqrt), "rn_c", 1)
                with nc.Block() as blk:
                    @blk.vector
                    def _(v, lnout=lnout):
                        p.inc(v.tensor_scalar(lnout[:], lnout[:], 1e-6, None, ALU.max), "rn_c", 1)
                        v.wait_ge(p.sem("rn_c"), p.cnt["rn_c"])
                        p.inc(v.reciprocal(lnout[:], lnout[:]), "rn_c", 1)
                with nc.Block() as blk:
                    @blk.scalar
                    def _(s, lnout=lnout):
                        p.inc(s.activation(lnout[:], lnout[:], AF.Ln), "rn_c", 1)

        esP.close()
        emit_chunk_pass(p, nc, dict(
            q1c=q1c, k1c=k1c, v1c=v1c, lnb=lnb, lnrq=lnrq, lnrk=lnrk,
            cpad=cpad, Sf=Sf, Sbf=Sbf, Ncum=Ncum, NcumT=NcumT,
            ident=ident, identf=identf, maskD=maskD, maskG=maskG, mdg=mdg, onesr=onesr,
            psp=psp, psx=psx, psm=psm, pst=pst,
            ot_h=ot_h, z_h=z_h, pct_h=pct_h, dscr_h=dscr_h),
            nchunks=nchunks, nheads=nheads)

        if stex[0] == "out":
            sloc_h, pcore_h = stex[1], stex[2]
            with nc.Block() as blk:
                @blk.sync
                def _(sync):
                    p.inc(sync.dma_start(sloc_h[:, :], Sf[:, :]), "st", 16)
                    p.inc(sync.dma_start(pcore_h[:, :], NcumT[:, :]), "st", 16)
                    sync.wait_ge(p.sem("st"), p.cnt["st"])
            p.default_es = None
        else:
            ag_in, ag_out = stex[1], stex[2]
            sfb = p.sb("sfbf", [64, 16 * 64], BF16, es=esPh)
            with nc.Block() as blk:
                @blk.vector
                def _(v):
                    p.inc(v.tensor_copy(sfb[:, :], Sf[:, :]), "agc", 1)
                @blk.sync
                def _(sync):
                    sync.wait_ge(p.sem("agc"), p.cnt["agc"])
                    p.inc(sync.dma_start(ag_in[0:64, :], NcumT[:, :]), "st", 16)
                    p.inc(sync.dma_start(ag_in[64:128, :], sfb[:, :]), "st", 16)
                    sync.wait_ge(p.sem("st"), p.cnt["st"])
            with nc.Block() as blk:
                @blk.gpsimd
                def _(g):
                    p.inc(g.collective_compute(
                        "AllGather", ALU.bypass,
                        replica_groups=[[0, 1, 2, 3, 4, 5, 6, 7]],
                        ins=[ag_in.ap().opt()], outs=[ag_out.ap().opt()]), "cc", 1)
                    g.wait_ge(p.sem("cc"), p.cnt["cc"])
        p.default_es = None




def build_l1loc(nchunks=16, nheads=16):
    p = Prog("l1loc")
    nc = p.nc
    h = {"x1_h": p.din("x1", [TCH, D]),
         "wq1_h": p.din("wq1", [D, D]), "wk1_h": p.din("wk1", [D, D]), "wv1_h": p.din("wv1", [D, D]),
         "qcw_h": p.din("qcw", [D, 4]), "qcb_h": p.din("qcb", [128, 8]),
         "kcw_h": p.din("kcw", [D, 4]), "kcb_h": p.din("kcb", [128, 8]),
         "vcw_h": p.din("vcw", [D, 4]), "vcb_h": p.din("vcb", [128, 8]),
         "wa_h": p.din("wa", [D, 16]), "ba_h": p.din("ba", [16, 1]),
         "wb_h": p.din("wb", [D, 16]), "bb_h": p.din("bb", [16, 1]),
         "id_h": p.din("identbf", [128, 128], BF16), "idf_h": p.din("identf32", [128, 128], F32),
         "mD_h": p.din("maskD", [128, 128], F32), "mG_h": p.din("maskG", [128, 128], F32),
         "mdg_h": p.din("mdg", [128, 128], F32), "e8_h": p.din("e8", [8, 128, 16], BF16),
         "ot_h": p.dout("OT", [D, TC]), "z_h": p.dout("Z", [16, 16, 64, 128], BF16),
         "pct_h": p.dout("PcT", [16, 16, 64, 64], BF16),
         "dscr_h": nc.dram_tensor("dscr", [4, 16, 128], F32),
         "stex": ("out", None, None)}
    h["stex"] = ("out", p.dout("Sloc", [64, 16 * 64]), p.dout("PcoreT", [64, 16 * 64], BF16))
    with p.es:
        emit_l1loc(p, nc, h, nchunks, nheads)
    return nc



def emit_l1fix(p, nc, h):
    x_h, ot_h, z_h, pct_h = h["x1_h"], h["ot_h"], h["z_h"], h["pct_h"]
    rmsw_h = h["rmsw_h"]
    wgd_h, wgu_h, wo_h = h["wgd_h"], h["wgu_h"], h["wo_h"]
    wg8_h, wu8_h, wd8_h = h["wg8_h"], h["wu8_h"], h["wd8_h"]
    id_h, idf_h = h["id_h"], h["idf_h"]
    e8_h, e2t_h, e816_h = h["e8_h"], h["e2t_h"], h["e816_h"]
    x2_h = h["x2_h"]
    xn1_d, wqg_d, wqu_d, wqd_d, rr_d = h["xn1_d"], h["wqg_d"], h["wqu_d"], h["wqd_d"], h["rr_d"]
    pred = h["pred"]
    with ExitStack() as esFix:
        p.default_es = esFix

        ident = p.sb("sident", [128, 128], BF16)
        identf = p.sb("sidentf", [128, 128], F32)
        e8 = p.sb("se8", [128, 8 * 16], BF16)
        e2t = p.sb("se2t", [2, 128], F32)
        e816 = p.sb("se816", [8, 128], F32)
        rmsw = p.sb("srmsw", [128, 8], F32)
        epst = p.sb("eps", [128, 1])
        colst = p.sb("cols", [128, 8])
        y1 = p.sb("y1", [128, 16 * 1024], BF16)
        psp = [p.ps("psp%d" % z, [128, 512]) for z in range(4)]
        psx = p.ps("psx", [128, 128])
        psm = p.ps("psm", [128, 128])
        pst = [p.ps("pst%d" % z, [128, 128], BF16) for z in range(2)]
        zb = p.sb("zbl", [64, 128], BF16)
        pcb = p.sb("pcbl", [64, 64], BF16)
        rr2 = p.sb("rr2", [2, 128], F32)

        with nc.Block() as blk:
            @blk.gpsimd
            def _(g):
                for (dst, srch) in ((ident, id_h), (identf, idf_h), (e2t, e2t_h),
                                    (e816, e816_h), (rmsw, rmsw_h)):
                    p.inc(g.dma_start(dst[:], srch[:, :]), "gd", 16)
                p.inc(g.dma_start(e8[:, :].rearrange("p (e h) -> p e h", e=8),
                                  e8_h[:, :, :].rearrange("e p h -> p e h")), "gd", 16)
                g.memset(epst[:], 1e-5)
                g.wait_ge(p.sem("gd"), p.cnt["gd"])

        esP = ExitStack()
        xnT = p.sb("xnT", [128, 8 * TCH], BF16, es=esP)
        oslab = p.sb("oslab", [128, 8 * 2048], BF16, es=esP)
        grw = p.sb("grw", [128, 8 * 2048], BF16, es=esP)
        DS = p.sb("DSl", [64, 16 * 128], BF16, es=esP)
        Sin = p.sb("Sinl", [64, 16 * 64], BF16, es=esP)
        npred = 3 if pred[0] == "input" else 6
        predp = p.sb("predp", [64, npred * 1024], BF16, es=esP)
        preds = p.sb("preds", [64, npred * 1024], BF16 if pred[0] == "ag" else F32, es=esP)
        hmask = p.sb("hmask", [64, 8], es=esP)
        himask = p.sb("himask", [64, 6 * 64], BF16, es=esP)
        with nc.Block() as blk:
            @blk.gpsimd
            def _(g):
                if pred[0] == "input":
                    predp_h, preds_h = pred[1], pred[2]
                    p.inc(g.dma_start(predp[:, 0:3 * 1024].rearrange("p (l n) -> p l n", l=3),
                                      predp_h[:, :, :].rearrange("l p n -> p l n")), "gd", 16)
                    p.inc(g.dma_start(preds[:, 0:3 * 1024].rearrange("p (l n) -> p l n", l=3),
                                      preds_h[:, :, :].rearrange("l p n -> p l n")), "gd", 16)
                else:
                    ag_out, hmask_h, himask_h = pred[1], pred[2], pred[3]
                    for l, s in enumerate((0, 1, 2, 4, 5, 6)):
                        p.inc(g.dma_start(predp[:, l * 1024:(l + 1) * 1024],
                                          ag_out[s * 128: s * 128 + 64, :]), "gd", 16)
                        p.inc(g.dma_start(preds[:, l * 1024:(l + 1) * 1024],
                                          ag_out[s * 128 + 64: s * 128 + 128, :]), "gd", 16)
                    p.inc(g.dma_start(hmask[:, 0:6], hmask_h[:, :]), "gd", 16)
                    p.inc(g.dma_start(himask[:, :], himask_h[:, :]), "gd", 16)
                g.memset(DS[:], 0.0)
                g.memset(Sin[:], 0.0)
                g.wait_ge(p.sem("gd"), p.cnt["gd"])
        with ExitStack() as esLN:
            scr = {"xt": p.sb("xt", [128, 1024], es=esLN), "sq": p.sb("sq", [128, 1024], es=esLN),
                   "cols": colst, "rot": p.sb("rot", [128, 2048], BF16, es=esLN),
                   "eps": epst}
            xnf = p.sb("xnf", [128, 2 * 1024], F32, es=esLN)
            gd0 = p.cnt.get("gd", 0)
            with nc.Block() as blk:
                emit_ln_xnT_fixed(p, blk, x_h, ROWS17, xnT, TCH, ident, pst, scr, tag="lnt2",
                                  xn_tok_slab=xnf, xn_rot=2, xn_guard=("gd", gd0))

                @blk.gpsimd
                def _(g, gd0=gd0):
                    r0 = 0
                    for i in range(17):
                        r = ROWS17[i]
                        g.wait_ge(p.sem("lnt2_ln"), i + 1)
                        p.inc(g.dma_start(xn1_d[r0:r0 + r, :], xnf[0:r, (i % 2) * 1024:(i % 2) * 1024 + 1024]), "gd", 16)
                        r0 += r

        with ExitStack() as esO:
            ostg = p.sb("ostg", [128, 2048], F32, es=esO)
            for hc in range(8):
                with nc.Block() as blk:
                    @blk.sync
                    def _(sync, hc=hc):
                        p.inc(sync.dma_start(ostg[:, :], ot_h[hc * 128:(hc + 1) * 128, :]), "ld", 16)
                    @blk.vector
                    def _(v, hc=hc):
                        p.wait(v, "ld")
                        v.tensor_copy(oslab[:, hc * 2048:(hc + 1) * 2048], ostg[:])

        ch = Chain(p, tag="fx")
        pb1 = p.sb("pb1", [64, 64], BF16, es=esP)
        sb1 = p.sb("sb1", [64, 64], es=esP)
        for hh in range(16):
            for l in range(npred):
                if pred[0] == "ag":
                    ch.add("vector", lambda v, hh=hh, l=l: v.tensor_scalar(
                        pb1[:, :], predp[0:64, l * 1024 + hh * 64: l * 1024 + hh * 64 + 64],
                        hmask[0:64, l:l + 1], None, ALU.mult))
                    ch.add("vector", lambda v, l=l: v.tensor_tensor(
                        pb1[:, :], pb1[:, :], himask[0:64, l * 64:(l + 1) * 64], ALU.add))
                    ch.add("vector", lambda v, hh=hh, l=l: v.tensor_scalar(
                        sb1[:, :], preds[0:64, l * 1024 + hh * 64: l * 1024 + hh * 64 + 64],
                        hmask[0:64, l:l + 1], None, ALU.mult))
                    ch.add("tensor", lambda tt, hh=hh: tt.matmul(
                        psx[0:64, 0:64], pb1[:, :], Sin[0:64, hh * 64: hh * 64 + 64],
                        start=True, stop=True, skip_group_check=True))
                    ch.add("vector", lambda v, hh=hh: v.tensor_tensor(
                        Sin[0:64, hh * 64: hh * 64 + 64], sb1[:, :], psx[0:64, 0:64], ALU.add))
                else:
                    ch.add("tensor", lambda tt, hh=hh, l=l: tt.matmul(
                        psx[0:64, 0:64], predp[0:64, l * 1024 + hh * 64: l * 1024 + hh * 64 + 64],
                        Sin[0:64, hh * 64: hh * 64 + 64], start=True, stop=True, skip_group_check=True))
                    ch.add("vector", lambda v, hh=hh, l=l: v.tensor_tensor(
                        Sin[0:64, hh * 64: hh * 64 + 64],
                        preds[0:64, l * 1024 + hh * 64: l * 1024 + hh * 64 + 64],
                        psx[0:64, 0:64], ALU.add))
            ho = (hh % 2) * 64
            ch.add("vector", lambda v, hh=hh, ho=ho: v.tensor_copy(
                DS[0:64, hh * 128 + ho: hh * 128 + ho + 64], Sin[0:64, hh * 64: hh * 64 + 64]))
        for ci in range(16):
            for h in range(16):
                hc, ho = h // 2, (h % 2) * 64
                ch.dma(lambda sy, ci=ci, h=h: sy.dma_start(zb[:, :], z_h[ci, h, :, :]))
                ch.dma(lambda sy, ci=ci, h=h: sy.dma_start(pcb[:, :], pct_h[ci, h, :, :]))
                ch.add("tensor", lambda tt, h=h: tt.matmul(
                    psm[:], DS[0:64, h * 128: h * 128 + 128], zb[:, :],
                    start=True, stop=True, skip_group_check=True))
                ch.add("vector", lambda v, ci=ci, hc=hc: v.tensor_tensor(
                    oslab[:, hc * 2048 + ci * 128: hc * 2048 + ci * 128 + 128],
                    oslab[:, hc * 2048 + ci * 128: hc * 2048 + ci * 128 + 128],
                    psm[:], ALU.add))
                ch.add("tensor", lambda tt, h=h: tt.matmul(
                    psx[0:64, 0:128], pcb[:, :], DS[0:64, h * 128: h * 128 + 128],
                    start=True, stop=True, skip_group_check=True))
                ch.add("vector", lambda v, h=h: v.tensor_copy(
                    DS[0:64, h * 128: h * 128 + 128], psx[0:64, 0:128]))
        ch.emit(nc)

        with ExitStack() as esR:
            sqs = p.sb("sqs", [128, 2048], BF16, es=esR)
            rrow = p.sb("rrow", [16, 2048], F32, es=esR)
            with nc.Block() as blk:
                sq0 = p.cnt.get("rm_sq", 0)
                pe0 = p.cnt.get("rm_pe", 0)
                @blk.scalar
                def _(s, sq0=sq0, pe0=pe0):
                    for hc in range(8):
                        if hc >= 1:
                            s.wait_ge(p.sem("rm_pe"), pe0 + 4 * hc)
                        p.inc(s.activation(sqs[:, :], oslab[:, hc * 2048:(hc + 1) * 2048],
                                           AF.Square), "rm_sq", 1)
                @blk.tensor
                def _(t, sq0=sq0):
                    for hc in range(8):
                        t.wait_ge(p.sem("rm_sq"), sq0 + hc + 1)
                        for g4 in range(4):
                            goff = g4 * 512
                            ins = t.matmul(psp[g4][0:16, :], e8[:, hc * 16:(hc + 1) * 16],
                                           sqs[:, goff:goff + 512],
                                           start=(hc == 0), stop=(hc == 7), skip_group_check=True)
                            p.inc(ins, "rm_pe", 1)
            with nc.Block() as blk:
                @blk.vector
                def _(v):
                    for g4 in range(4):
                        goff = g4 * 512
                        p.inc(v.tensor_scalar(rrow[0:16, goff:goff + 512], psp[g4][0:16, :],
                                              1.0 / 64.0, 1e-6, ALU.mult, ALU.add), "rm_c", 1)
            with nc.Block() as blk:
                @blk.scalar
                def _(s):
                    p.inc(s.activation(rrow[:], rrow[:], AF.Sqrt), "rm_c", 1)
            with nc.Block() as blk:
                @blk.vector
                def _(v):
                    p.inc(v.reciprocal(rrow[:], rrow[:]), "rm_c", 1)
            with nc.Block() as blk:
                @blk.sync
                def _(sync):
                    p.inc(sync.dma_start(rr_d[:, :], rrow[:]), "st", 16)
                    sync.wait_ge(p.sem("st"), p.cnt["st"])

        with ExitStack() as esG:
            uT = p.sb("uT", [128, 4 * 2048], BF16, es=esG)
            wstg = p.sb("wstg", [128, 1024], F32, es=esG)
            wbf = p.sb("wbf", [128, 8 * 1024], BF16, es=esG)
            for dc in range(8):
                with nc.Block() as blk:
                    @blk.sync
                    def _(sync, dc=dc):
                        p.inc(sync.dma_start(wstg[:, 0:512], wgd_h[dc * 128:(dc + 1) * 128, :]), "ld", 16)
                    @blk.vector
                    def _(v, dc=dc):
                        p.wait(v, "ld")
                        v.tensor_copy(wbf[:, dc * 512: dc * 512 + 512], wstg[:, 0:512])
            with nc.Block() as blk:
                pe0 = p.cnt.get("ug_pe", 0)
                @blk.tensor
                def _(t, pe0=pe0):
                    n = 0
                    for uc in range(4):
                        for g4 in range(4):
                            goff = g4 * 512
                            if n >= 2:
                                t.wait_ge(p.sem("ug_ev"), p.cnt.get("ug_ev", 0) - 16 + n - 1) if False else \
                                    t.wait_ge(p.sem("ug_ev"), n - 1)
                            for dc in range(8):
                                ins = t.matmul(psp[n % 2][:, 0:512],
                                               wbf[:, dc * 512 + uc * 128: dc * 512 + uc * 128 + 128],
                                               xnT[:, dc * TCH + 3 + goff: dc * TCH + 3 + goff + 512],
                                               start=(dc == 0), stop=(dc == 7))
                            p.inc(ins, "ug_pe", 1)
                            n += 1
                @blk.scalar
                def _(s, pe0=pe0):
                    n = 0
                    for uc in range(4):
                        for g4 in range(4):
                            goff = g4 * 512
                            s.wait_ge(p.sem("ug_pe"), pe0 + n + 1)
                            p.inc(s.activation(uT[:, uc * 2048 + goff: uc * 2048 + goff + 512],
                                               psp[n % 2][:, 0:512], AF.Silu), "ug_ev", 1)
                            n += 1
            for uc2 in range(4):
                with nc.Block() as blk:
                    @blk.sync
                    def _(sync, uc2=uc2):
                        p.inc(sync.dma_start(wstg[:, :], wgu_h[uc2 * 128:(uc2 + 1) * 128, :]), "ld", 16)
                    @blk.vector
                    def _(v, uc2=uc2):
                        p.wait(v, "ld")
                        v.tensor_copy(wbf[:, uc2 * 1024:(uc2 + 1) * 1024], wstg[:])
            with nc.Block() as blk:
                pe0 = p.cnt.get("gg_pe", 0)
                sg0 = p.cnt.get("gg_sg", 0)
                @blk.tensor
                def _(t, pe0=pe0, sg0=sg0):
                    n = 0
                    for gc in range(8):
                        for g4 in range(4):
                            goff = g4 * 512
                            if n >= 2:
                                t.wait_ge(p.sem("gg_ev"), n - 1)
                            for uc in range(4):
                                ins = t.matmul(psp[n % 2][:, 0:512],
                                               wbf[:, uc * 1024 + gc * 128: uc * 1024 + gc * 128 + 128],
                                               uT[:, uc * 2048 + goff: uc * 2048 + goff + 512],
                                               start=(uc == 0), stop=(uc == 3))
                            p.inc(ins, "gg_pe", 1)
                            n += 1
                @blk.scalar
                def _(s, pe0=pe0):
                    n = 0
                    for gc in range(8):
                        for g4 in range(4):
                            goff = g4 * 512
                            s.wait_ge(p.sem("gg_pe"), pe0 + n + 1)
                            p.inc(s.activation(grw[:, gc * 2048 + goff: gc * 2048 + goff + 512],
                                               psp[n % 2][:, 0:512], AF.Sigmoid), "gg_sg", 1)
                            n += 1
                @blk.vector
                def _(v, sg0=sg0):
                    n = 0
                    for gc in range(8):
                        for g4 in range(4):
                            goff = g4 * 512
                            v.wait_ge(p.sem("gg_sg"), sg0 + n + 1)
                            p.inc(v.tensor_scalar(grw[:, gc * 2048 + goff: gc * 2048 + goff + 512],
                                                  grw[:, gc * 2048 + goff: gc * 2048 + goff + 512],
                                                  rmsw[:, gc: gc + 1], None, ALU.mult), "gg_ev", 1)
                            n += 1

        ch2 = Chain(p, tag="ra")
        for ci in range(16):
            for hc in range(8):
                ch2.dma(lambda sy, ci=ci, hc=hc: sy.dma_start(
                    rr2[:, :], rr_d[2 * hc: 2 * hc + 2, ci * 128:(ci + 1) * 128]))
                ch2.add("tensor", lambda tt: tt.matmul(
                    psm[:], e2t[:, :], rr2[:, :], start=True, stop=True, skip_group_check=True))
                ch2.add("vector", lambda v, ci=ci, hc=hc: v.tensor_tensor(
                    oslab[:, hc * 2048 + ci * 128: hc * 2048 + ci * 128 + 128],
                    oslab[:, hc * 2048 + ci * 128: hc * 2048 + ci * 128 + 128],
                    psm[:], ALU.mult))
                ch2.add("vector", lambda v, ci=ci, hc=hc: v.tensor_tensor(
                    oslab[:, hc * 2048 + ci * 128: hc * 2048 + ci * 128 + 128],
                    oslab[:, hc * 2048 + ci * 128: hc * 2048 + ci * 128 + 128],
                    grw[:, hc * 2048 + ci * 128: hc * 2048 + ci * 128 + 128], ALU.mult))
        ch2.emit(nc)

        with ExitStack() as esW:
            wstg = p.sb("wstg2", [128, 1024], F32, es=esW)
            wbf = p.sb("wobf", [128, 8 * 1024], BF16, es=esW)
            for hc in range(8):
                with nc.Block() as blk:
                    @blk.sync
                    def _(sync, hc=hc):
                        p.inc(sync.dma_start(wstg[:, :], wo_h[hc * 128:(hc + 1) * 128, :]), "ld", 16)
                    @blk.vector
                    def _(v, hc=hc):
                        p.wait(v, "ld")
                        v.tensor_copy(wbf[:, hc * 1024:(hc + 1) * 1024], wstg[:])
            with nc.Block() as blk:
                pe0 = p.cnt.get("wo_pe", 0)
                @blk.tensor
                def _(t, pe0=pe0):
                    n = 0
                    for i in range(16):
                        for half in range(2):
                            if n >= 2:
                                t.wait_ge(p.sem("wo_ev"), n - 1)
                            for hc in range(8):
                                ins = t.matmul(psp[n % 2][:, 0:512],
                                               oslab[:, hc * 2048 + i * 128: hc * 2048 + i * 128 + 128],
                                               wbf[:, hc * 1024 + half * 512: hc * 1024 + half * 512 + 512],
                                               start=(hc == 0), stop=(hc == 7))
                            p.inc(ins, "wo_pe", 1)
                            n += 1
                @blk.vector
                def _(v, pe0=pe0):
                    n = 0
                    for i in range(16):
                        for half in range(2):
                            v.wait_ge(p.sem("wo_pe"), pe0 + n + 1)
                            p.inc(v.tensor_copy(y1[:, i * 1024 + half * 512: i * 1024 + half * 512 + 512],
                                                psp[n % 2][:, 0:512]), "wo_ev", 1)
                            n += 1
        esP.close()

        ffn_bufs = emit_ffn(p, nc, dict(ident=ident, identf=identf, e816=e816,
                                        psp=psp, psx=psx, psm=psm, pst=pst,
                                        y1=y1, xn1_d=xn1_d, wg8_h=wg8_h, wu8_h=wu8_h, wd8_h=wd8_h,
                                        wqg_d=wqg_d, wqu_d=wqu_d, wqd_d=wqd_d, epst=epst))

        with nc.Block() as blk:
            xo = ffn_bufs["xstg"]
            yo = ffn_bufs["wtile"][:, 0:1024] if False else ffn_bufs["wtile"]
            st0 = p.cnt.get("st", 0)
            ld0 = p.cnt.get("ld", 0)

            @blk.sync
            def _(sync, st0=st0, ld0=ld0):
                for i in range(16):
                    if i >= 1:
                        sync.wait_ge(p.sem("st"), st0 + 16 * i)
                    p.inc(sync.dma_start(xo[:, 0:1024], x_h[HALO + i * 128: HALO + i * 128 + 128, :]), "ld", 16)
                    sync.wait_ge(p.sem("x2f"), i + 1)
                    p.inc(sync.dma_start(x2_h[i * 128:(i + 1) * 128, :], xo[:, 0:1024]), "st", 16)
                sync.wait_ge(p.sem("st"), p.cnt["st"])

            @blk.vector
            def _(v, ld0=ld0):
                for i in range(16):
                    v.wait_ge(p.sem("ld"), ld0 + 16 * (i + 1))
                    p.inc(v.tensor_copy(yo[:, 0:1024], y1[:, i * 1024:(i + 1) * 1024]), "x2c", 1)
                    v.wait_ge(p.sem("x2c"), p.cnt["x2c"])
                    p.inc(v.tensor_tensor(xo[:, 0:1024], xo[:, 0:1024], yo[:, 0:1024], ALU.add), "x2f", 1)
    return nc




def build_l1fix():
    p = Prog("l1fix")
    nc = p.nc
    h = {"x1_h": p.din("x1", [TCH, D]), "ot_h": p.din("OT", [D, TC]),
         "z_h": p.din("Z", [16, 16, 64, 128], BF16), "pct_h": p.din("PcT", [16, 16, 64, 64], BF16),
         "pred": ("input", p.din("predP", [3, 64, 16 * 64], BF16), p.din("predS", [3, 64, 16 * 64])),
         "rmsw_h": p.din("rmsw", [128, 8]),
         "wgd_h": p.din("wgd", [D, 512]), "wgu_h": p.din("wgu", [512, D]), "wo_h": p.din("wo", [D, D]),
         "wg8_h": p.din("wgate", [2816, D]), "wu8_h": p.din("wup", [2816, D]),
         "wd8_h": p.din("wdown", [D, 2816]),
         "id_h": p.din("identbf", [128, 128], BF16), "idf_h": p.din("identf32", [128, 128], F32),
         "e8_h": p.din("e8", [8, 128, 16], BF16), "e2t_h": p.din("e2t", [2, 128], F32),
         "e816_h": p.din("e816", [8, 128], F32),
         "x2_h": p.dout("x2", [TC, D]),
         "xn1_d": p.nc.dram_tensor("xn1d", [TCH, D], F32),
         "wqg_d": p.nc.dram_tensor("wqgd", [D, 2816], BF16),
         "wqu_d": p.nc.dram_tensor("wqud", [D, 2816], BF16),
         "wqd_d": p.nc.dram_tensor("wqdd", [2816, D], BF16),
         "rr_d": p.nc.dram_tensor("rrd", [16, 2048], F32)}
    with p.es:
        emit_l1fix(p, nc, h)
    return nc


def emit_quant_fm(p, ch, xin, W, qs, dst_slab, dst_off, identf, psp, e816):
    nb = W // 16
    nch = W // 128
    bm, inv = qs["bm"], qs["inv"]
    sT = qs["sT"]
    xT = qs["xT"]
    ex = qs["ex"]
    ax, qb, qt = qs["ax"], qs["qb"], qs["qt"]
    ch.add("vector", lambda v: v.tensor_reduce(
        bm[:, 0:nb], xin.rearrange("p (b k) -> p b k", k=16), AX.X, ALU.max,
        apply_absolute_value=True))
    ch.add("vector", lambda v: v.tensor_scalar(bm[:, 0:nb], bm[:, 0:nb], 1.0 / 6.0, 1e-12,
                                               ALU.mult, ALU.max))
    ch.add("vector", lambda v: v.reciprocal(inv[:, 0:nb], bm[:, 0:nb]))
    ch.add("vector", lambda v: v.tensor_scalar(bm[:, 0:nb], bm[:, 0:nb], 0.5, None, ALU.mult))
    for c in range(nch):
        ch.add("tensor", lambda tt, c=c: tt.transpose(
            psp[2][:, 0:128], xin[:, c * 128:(c + 1) * 128], identf[:]))
        ch.add("vector", lambda v, c=c: v.tensor_copy(xT[:, c * 128:(c + 1) * 128],
                                                      psp[2][:, 0:128]))
    for c in range(nch):
        ch.add("tensor", lambda tt, c=c: tt.transpose(psp[3][0:8, 0:128], inv[:, c * 8:(c + 1) * 8],
                                                      identf[:]))
        ch.add("vector", lambda v, c=c: v.tensor_copy(sT[0:8, c * 128:(c + 1) * 128],
                                                      psp[3][0:8, 0:128]))
    for c in range(nch):
        ch.add("tensor", lambda tt, c=c: tt.matmul(
            psp[0][:, (c % 4) * 128:(c % 4) * 128 + 128], e816[0:8, :],
            sT[0:8, c * 128:(c + 1) * 128], start=True, stop=True, skip_group_check=True))
        ch.add("vector", lambda v, c=c: v.tensor_copy(
            ex[:, c * 128:(c + 1) * 128], psp[0][:, (c % 4) * 128:(c % 4) * 128 + 128]))
    ch.add("scalar", lambda s: s.activation(ax[:, 0:W], xT[:, 0:W], AF.Abs))
    ch.add("vector", lambda v: v.tensor_tensor(ax[:, 0:W], ax[:, 0:W], ex[:, 0:W], ALU.mult))
    ch.add("vector", lambda v: v.tensor_scalar(qb[:, 0:W], ax[:, 0:W], 5.0, None, ALU.is_gt))
    for m in (3.5, 2.5):
        ch.add("vector", lambda v, m=m: v.scalar_tensor_tensor(
            qb[:, 0:W], ax[:, 0:W], m, qb[:, 0:W], ALU.is_gt, ALU.add))
    ch.add("vector", lambda v: v.tensor_scalar(qb[:, 0:W], qb[:, 0:W], 2.0, None, ALU.mult))
    for m in (1.75, 1.25, 0.75, 0.25):
        ch.add("vector", lambda v, m=m: v.scalar_tensor_tensor(
            qb[:, 0:W], ax[:, 0:W], m, qb[:, 0:W], ALU.is_gt, ALU.add))
    for c in range(nch):
        ch.add("tensor", lambda tt, c=c: tt.transpose(psp[3][0:8, 0:128], bm[:, c * 8:(c + 1) * 8],
                                                      identf[:]))
        ch.add("vector", lambda v, c=c: v.tensor_copy(sT[0:8, c * 128:(c + 1) * 128],
                                                      psp[3][0:8, 0:128]))
    for c in range(nch):
        ch.add("tensor", lambda tt, c=c: tt.matmul(
            psp[0][:, (c % 4) * 128:(c % 4) * 128 + 128], e816[0:8, :],
            sT[0:8, c * 128:(c + 1) * 128], start=True, stop=True, skip_group_check=True))
        ch.add("vector", lambda v, c=c: v.tensor_copy(
            ex[:, c * 128:(c + 1) * 128], psp[0][:, (c % 4) * 128:(c % 4) * 128 + 128]))
    ch.add("scalar", lambda s: s.activation(qt[:, 0:W], xT[:, 0:W], AF.Sign))
    ch.add("vector", lambda v: v.tensor_tensor(qt[:, 0:W], qt[:, 0:W], ex[:, 0:W], ALU.mult))
    ch.add("vector", lambda v, dst_off=dst_off, W=W: v.tensor_tensor(
        dst_slab[:, dst_off:dst_off + W], qb[:, 0:W], qt[:, 0:W], ALU.mult))


def emit_ffn(p, nc, t):
    ident, psp, psx, psm, pst = t["ident"], t["psp"], t["psx"], t["psm"], t["pst"]
    identf = t["identf"]
    e816 = t["e816"]
    y1, xn1_d = t["y1"], t["xn1_d"]
    wg8_h, wu8_h, wd8_h = t["wg8_h"], t["wu8_h"], t["wd8_h"]
    wqg_d, wqu_d, wqd_d = t["wqg_d"], t["wqu_d"], t["wqd_d"]

    qs = {"bm": p.sb("q_bm", [128, 192]), "inv": p.sb("q_inv", [128, 192]),
          "sT": p.sb("q_sT", [8, 2816]),
          "xT": p.sb("q_xT", [128, 2816]), "ex": p.sb("q_ex", [128, 2816]),
          "ax": p.sb("q_ax", [128, 2816]),
          "qb": p.sb("q_qb", [128, 2816]), "qt": p.sb("q_qt", [128, 2816])}
    wqt = p.sb("f_wqt", [128, 2816], BF16)
    xqt = p.sb("f_xqt", [128, 16 * 1024], BF16)

    ch = Chain(p, tag="fw")
    for (w_h, wq_d) in ((wg8_h, wqg_d), (wu8_h, wqu_d)):
        for nt in range(22):
            ch.dma(lambda sy, w_h=w_h, nt=nt: sy.dma_start(
                qs["ax"][:, 0:1024], w_h[nt * 128:(nt + 1) * 128, :]))
            emit_quant_fm(p, ch, qs["ax"][:, 0:1024], 1024, qs, wqt, 0, identf, psp, e816)
            ch.dma(lambda sy, wq_d=wq_d, nt=nt: sy.dma_start(
                wq_d[:, nt * 128:(nt + 1) * 128].rearrange("(c p) n -> p c n", p=128),
                wqt[:, 0:1024].rearrange("p (c n) -> p c n", c=8)))
    for dt_ in range(8):
        ch.dma(lambda sy, dt_=dt_: sy.dma_start(
            qs["ax"][:, 0:2816], wd8_h[dt_ * 128:(dt_ + 1) * 128, :]))
        emit_quant_fm(p, ch, qs["ax"][:, 0:2816], 2816, qs, wqt, 0, identf, psp, e816)
        ch.dma(lambda sy, dt_=dt_: sy.dma_start(
            wqd_d[:, dt_ * 128:(dt_ + 1) * 128].rearrange("(c p) n -> p c n", p=128),
            wqt[:, 0:2816].rearrange("p (c n) -> p c n", c=22)))
    for i in range(16):
        ch.dma(lambda sy, i=i: sy.dma_start(
            qs["ax"][:, 0:1024], xn1_d[HALO + i * 128: HALO + i * 128 + 128, :]))
        emit_quant_fm(p, ch, qs["ax"][:, 0:1024], 1024, qs, xqt, i * 1024, identf, psp, e816)
    ch.emit(nc)

    hgrp = p.sb("f_h", [128, 4 * 2816], BF16)
    hqT = p.sb("f_hqT", [128, 4 * 2816], BF16)
    wrhs = [p.sb("f_wrhs%d" % z, [128, 512], BF16) for z in range(2)]
    gtile = p.sb("f_gt", [128, 512], BF16)
    for grp in range(4):
        ch = Chain(p, tag="ff%d" % grp)
        for which, wq_d in ((0, wqg_d), (1, wqu_d)):
            for ti in range(4):
                i = grp * 4 + ti
                for ng in range(6):
                    noff = ng * 512
                    nw = 512 if ng < 5 else 256
                    for kc in range(8):
                        ch.dma(lambda sy, wq_d=wq_d, kc=kc, noff=noff, nw=nw: sy.dma_start(
                            wrhs[kc % 2][:, 0:nw], wq_d[kc * 128:(kc + 1) * 128, noff:noff + nw]))
                        ch.add("tensor", lambda tt, kc=kc, i=i, noff=noff, nw=nw: tt.matmul(
                            psp[0][:, 0:nw], xqt[:, i * 1024 + kc * 128: i * 1024 + kc * 128 + 128],
                            wrhs[kc % 2][:, 0:nw],
                            start=(kc == 0), stop=(kc == 7), skip_group_check=True))
                    if which == 0:
                        ch.add("scalar", lambda s, ti=ti, noff=noff, nw=nw: s.activation(
                            hgrp[:, ti * 2816 + noff: ti * 2816 + noff + nw],
                            psp[0][:, 0:nw], AF.Silu))
                    else:
                        ch.add("vector", lambda v, nw=nw: v.tensor_copy(
                            gtile[:, 0:nw], psp[0][:, 0:nw]))
                        ch.add("vector", lambda v, ti=ti, noff=noff, nw=nw: v.tensor_tensor(
                            hgrp[:, ti * 2816 + noff: ti * 2816 + noff + nw],
                            hgrp[:, ti * 2816 + noff: ti * 2816 + noff + nw],
                            gtile[:, 0:nw], ALU.mult))
        for ti in range(4):
            ch.add("vector", lambda v, ti=ti: v.tensor_copy(
                qs["ax"][:, 0:2816], hgrp[:, ti * 2816:(ti + 1) * 2816]))
            emit_quant_fm(p, ch, qs["ax"][:, 0:2816], 2816, qs, hqT, ti * 2816, identf, psp, e816)
        for ti in range(4):
            i = grp * 4 + ti
            for half in range(2):
                for c in range(22):
                    ch.dma(lambda sy, c=c, half=half: sy.dma_start(
                        wrhs[c % 2][:, 0:512], wqd_d[c * 128:(c + 1) * 128, half * 512: half * 512 + 512]))
                    ch.add("tensor", lambda tt, c=c, ti=ti, half=half: tt.matmul(
                        psp[1][:, 0:512], hqT[:, ti * 2816 + c * 128: ti * 2816 + c * 128 + 128],
                        wrhs[c % 2][:, 0:512],
                        start=(c == 0), stop=(c == 21), skip_group_check=True))
                ch.add("vector", lambda v: v.tensor_copy(gtile[:, 0:512], psp[1][:, 0:512]))
                ch.add("vector", lambda v, i=i, half=half: v.tensor_tensor(
                    y1[:, i * 1024 + half * 512: i * 1024 + half * 512 + 512],
                    y1[:, i * 1024 + half * 512: i * 1024 + half * 512 + 512],
                    gtile[:, 0:512], ALU.add))
        ch.emit(nc)
    return {"xstg": qs["qb"], "wtile": qs["qt"]}



def build_fused():
    p = Prog("fused")
    nc = p.nc
    x_h = p.din("x", [TCH, D])
    id_h = p.din("identbf", [128, 128], BF16)
    idf_h = p.din("identf32", [128, 128], F32)
    kv_in = nc.dram_tensor("kv_in", [D, D], F32)
    kv_out = nc.dram_tensor("kv_out", [D, D], F32)
    x1_d = nc.dram_tensor("x1d", [TCH, D], F32)
    ot_d = nc.dram_tensor("otd", [D, TC], F32)
    z_d = nc.dram_tensor("zd", [16, 16, 64, 128], BF16)
    pct_d = nc.dram_tensor("pctd", [16, 16, 64, 64], BF16)
    ag_in = nc.dram_tensor("ag_in", [128, 1024], BF16)
    from concourse.replica_groups import maybe_share_collective_output_space
    sp_ag = maybe_share_collective_output_space("AllGather", [[0, 1, 2, 3, 4, 5, 6, 7]])
    ag_out = nc.dram_tensor("ag_out", [8 * 128, 1024], BF16, addr_space=sp_ag)

    h0 = {"x_h": x_h, "wq_h": p.din("wq", [D, D]), "wg_h": p.din("wg", [D, 8]),
          "wl1_h": p.din("wl1", [8, D, 4096]), "wl2_h": p.din("wl2", [8, 2048, D]),
          "id_h": id_h, "x1_h": x1_d,
          "kv_ar": (p.din("wk", [D, D]), p.din("wv", [D, D]), kv_in, kv_out)}
    h1 = {"x1_h": x1_d,
          "wq1_h": p.din("wq1", [D, D]), "wk1_h": p.din("wk1", [D, D]), "wv1_h": p.din("wv1", [D, D]),
          "qcw_h": p.din("qcw", [D, 4]), "qcb_h": p.din("qcb", [128, 8]),
          "kcw_h": p.din("kcw", [D, 4]), "kcb_h": p.din("kcb", [128, 8]),
          "vcw_h": p.din("vcw", [D, 4]), "vcb_h": p.din("vcb", [128, 8]),
          "wa_h": p.din("wa", [D, 16]), "ba_h": p.din("ba", [16, 1]),
          "wb_h": p.din("wb", [D, 16]), "bb_h": p.din("bb", [16, 1]),
          "id_h": id_h, "idf_h": idf_h,
          "mD_h": p.din("maskD", [128, 128], F32), "mG_h": p.din("maskG", [128, 128], F32),
          "mdg_h": p.din("mdg", [128, 128], F32), "e8_h": p.din("e8", [8, 128, 16], BF16),
          "ot_h": ot_d, "z_h": z_d, "pct_h": pct_d,
          "dscr_h": nc.dram_tensor("dscr", [4, 16, 128], F32),
          "stex": ("ag", ag_in, ag_out)}
    h2 = {"x1_h": x1_d, "ot_h": ot_d, "z_h": z_d, "pct_h": pct_d,
          "pred": ("ag", ag_out, p.din("hmask", [64, 6], F32), p.din("himask", [64, 6 * 64], BF16)),
          "rmsw_h": p.din("rmsw", [128, 8]),
          "wgd_h": p.din("wgd", [D, 512]), "wgu_h": p.din("wgu", [512, D]),
          "wo_h": p.din("wo", [D, D]),
          "wg8_h": p.din("wgate", [2816, D]), "wu8_h": p.din("wup", [2816, D]),
          "wd8_h": p.din("wdown", [D, 2816]),
          "id_h": id_h, "idf_h": idf_h,
          "e8_h": h1["e8_h"], "e2t_h": p.din("e2t", [2, 128], F32),
          "e816_h": p.din("e816", [8, 128], F32),
          "x2_h": p.dout("x2", [TC, D]),
          "xn1_d": nc.dram_tensor("xn1d", [TCH, D], F32),
          "wqg_d": nc.dram_tensor("wqgd", [D, 2816], BF16),
          "wqu_d": nc.dram_tensor("wqud", [D, 2816], BF16),
          "wqd_d": nc.dram_tensor("wqdd", [2816, D], BF16),
          "rr_d": nc.dram_tensor("rrd", [16, 2048], F32)}
    with p.es:
        emit_l0main(p, nc, h0)
        emit_l1loc(p, nc, h1)
        emit_l1fix(p, nc, h2)
    return nc


def kernel_fused(inp):
    f32 = np.float32
    xf = np.ascontiguousarray(inp["L0_attn_x"].reshape(B * T, D).astype(f32))
    cores = list(range(NCORES))
    xsh = shard_x(xf)
    nc = build_fused()
    rmsw = np.ascontiguousarray(np.asarray(inp["L1_attn_rms_w"], f32).reshape(1024).reshape(8, 128).T)
    maps = []
    for c in cores:
        b0 = (c // 4) * 4
        preds = list(range(b0, c))
        slots = [0, 1, 2, 4, 5, 6]
        hm = np.zeros((64, 6), f32)
        him = np.zeros((64, 6 * 64), f32)
        for l, s in enumerate(slots):
            if s in preds:
                hm[:, l] = 1.0
            else:
                him[:, l * 64:(l + 1) * 64] = np.eye(64, dtype=f32)
        maps.append({
            "x": xsh[c],
            "wq": inp["L0_attn_Wq"], "wk": inp["L0_attn_Wk"], "wv": inp["L0_attn_Wv"],
            "wg": inp["L0_ffn_Wg_DN"], "wl1": inp["L0_ffn_Wl1_ND2H"], "wl2": inp["L0_ffn_Wl2_NHD"],
            "wq1": inp["L1_attn_Wq"], "wk1": inp["L1_attn_Wk"], "wv1": inp["L1_attn_Wv"],
            "qcw": inp["L1_attn_qcw"], "qcb": np.ascontiguousarray(np.asarray(inp["L1_attn_qcb"], f32).reshape(8, 128).T),
            "kcw": inp["L1_attn_kcw"], "kcb": np.ascontiguousarray(np.asarray(inp["L1_attn_kcb"], f32).reshape(8, 128).T),
            "vcw": inp["L1_attn_vcw"], "vcb": np.ascontiguousarray(np.asarray(inp["L1_attn_vcb"], f32).reshape(8, 128).T),
            "wa": inp["L1_attn_Wa"], "ba": np.asarray(inp["L1_attn_ba"], f32).reshape(16, 1),
            "wb": inp["L1_attn_Wb"], "bb": np.asarray(inp["L1_attn_bb"], f32).reshape(16, 1),
            "rmsw": rmsw, "wgd": inp["L1_attn_Wgd"], "wgu": inp["L1_attn_Wgu"],
            "wo": inp["L1_attn_Wo"],
            "wgate": inp["L1_ffn_w_gate"], "wup": inp["L1_ffn_w_up"], "wdown": inp["L1_ffn_w_down"],
            "identbf": IDENT_BF, "identf32": IDENT_F32, "maskD": MASKD, "maskG": MASKG,
            "mdg": MDG, "e8": E8C, "e2t": E2T, "e816": E816,
            "hmask": hm, "himask": him.astype(BF),
        })
    r = run_bass_kernel_spmd(nc, maps, core_ids=cores)
    x2 = np.concatenate([np.asarray(r.results[c]["x2"]) for c in cores], 0)
    return np.ascontiguousarray(x2.reshape(B, T, D))



IDENT_BF = np.eye(128, dtype=np.float32).astype(BF)
IDENT_F32 = np.eye(128, dtype=np.float32)
MASKD = np.where(np.arange(128)[:, None] < np.arange(128)[None, :], 0.0, -1e30).astype(np.float32)
MASKG = np.where(np.arange(128)[:, None] <= np.arange(128)[None, :], 0.0, -1e30).astype(np.float32)
MDG = (np.arange(128)[:, None] // 32 == np.arange(128)[None, :] // 32).astype(np.float32)
E8C = np.zeros((8, 128, 16), np.float32)
for _dc in range(8):
    for _pp in range(128):
        E8C[_dc, _pp, 2 * _dc + _pp // 64] = 1.0
E8C = E8C.astype(BF)
E2T = np.zeros((2, 128), np.float32)
for _pp in range(128):
    E2T[_pp // 64, _pp] = 1.0
E816 = np.zeros((8, 128), np.float32)
for _kk in range(128):
    E816[_kk // 16, _kk] = 1.0
EYEH = np.tile(np.eye(64, dtype=np.float32), (1, 16)).astype(BF)


def shard_x(xf):
    out = []
    for c in range(NCORES):
        own = xf[c * TC:(c + 1) * TC]
        if c % 4 == 0:
            h = np.zeros((HALO, D), np.float32)
        else:
            h = xf[c * TC - HALO: c * TC]
        out.append(np.ascontiguousarray(np.concatenate([h, own], 0)))
    return out


def kernel_multi(**inp):
    f32 = np.float32
    xf = np.ascontiguousarray(inp["L0_attn_x"].reshape(B * T, D).astype(f32))
    cores = list(range(NCORES))

    nc1 = build_l0kv()
    maps1 = [{"x": np.ascontiguousarray(xf[c * TC:(c + 1) * TC]),
              "wk": inp["L0_attn_Wk"], "wv": inp["L0_attn_Wv"], "identbf": IDENT_BF}
             for c in cores]
    r1 = run_bass_kernel_spmd(nc1, maps1, core_ids=cores)
    kvp = [np.asarray(r1.results[c]["kvpart"]) for c in cores]

    nc2 = build_l0main()
    xsh = shard_x(xf)
    maps2 = [{"x": xsh[c], "kvp": np.ascontiguousarray(np.stack(kvp[(c // 4) * 4:(c // 4) * 4 + 4])),
              "wq": inp["L0_attn_Wq"], "wg": inp["L0_ffn_Wg_DN"],
              "wl1": inp["L0_ffn_Wl1_ND2H"], "wl2": inp["L0_ffn_Wl2_NHD"],
              "identbf": IDENT_BF} for c in cores]
    r2 = run_bass_kernel_spmd(nc2, maps2, core_ids=cores)
    x1sh = [np.array(r2.results[c]["x1"]) for c in cores]
    for c in cores:
        if c % 4 != 0:
            x1sh[c][0:HALO] = x1sh[c - 1][HALO + TC - HALO: HALO + TC]

    nc3 = build_l1loc()
    maps3 = [{
        "x1": x1sh[c], "wq1": inp["L1_attn_Wq"], "wk1": inp["L1_attn_Wk"], "wv1": inp["L1_attn_Wv"],
        "qcw": inp["L1_attn_qcw"], "qcb": np.ascontiguousarray(np.asarray(inp["L1_attn_qcb"], f32).reshape(8, 128).T),
        "kcw": inp["L1_attn_kcw"], "kcb": np.ascontiguousarray(np.asarray(inp["L1_attn_kcb"], f32).reshape(8, 128).T),
        "vcw": inp["L1_attn_vcw"], "vcb": np.ascontiguousarray(np.asarray(inp["L1_attn_vcb"], f32).reshape(8, 128).T),
        "wa": inp["L1_attn_Wa"], "ba": np.asarray(inp["L1_attn_ba"], f32).reshape(16, 1),
        "wb": inp["L1_attn_Wb"], "bb": np.asarray(inp["L1_attn_bb"], f32).reshape(16, 1),
        "identbf": IDENT_BF, "identf32": IDENT_F32, "maskD": MASKD, "maskG": MASKG,
        "mdg": MDG, "e8": E8C,
    } for c in cores]
    r3 = run_bass_kernel_spmd(nc3, maps3, core_ids=cores)

    nc4 = build_l1fix()
    rmsw = np.ascontiguousarray(np.asarray(inp["L1_attn_rms_w"], f32).reshape(1024).reshape(8, 128).T)
    maps4 = []
    for c in cores:
        b0 = (c // 4) * 4
        preds = list(range(b0, c))
        predP = [EYEH] * (3 - len(preds)) + [np.asarray(r3.results[j]["PcoreT"]) for j in preds]
        predS = [np.zeros((64, 1024), f32)] * (3 - len(preds)) + \
                [np.asarray(r3.results[j]["Sloc"]) for j in preds]
        maps4.append({
            "x1": x1sh[c], "OT": np.asarray(r3.results[c]["OT"]),
            "Z": np.asarray(r3.results[c]["Z"]), "PcT": np.asarray(r3.results[c]["PcT"]),
            "predP": np.ascontiguousarray(np.stack(predP)), "predS": np.ascontiguousarray(np.stack(predS)),
            "rmsw": rmsw, "wgd": inp["L1_attn_Wgd"], "wgu": inp["L1_attn_Wgu"],
            "wo": inp["L1_attn_Wo"],
            "wgate": inp["L1_ffn_w_gate"], "wup": inp["L1_ffn_w_up"], "wdown": inp["L1_ffn_w_down"],
            "identbf": IDENT_BF, "identf32": IDENT_F32, "e8": E8C, "e2t": E2T, "e816": E816,
        })
    r4 = run_bass_kernel_spmd(nc4, maps4, core_ids=cores)
    x2 = np.concatenate([np.asarray(r4.results[c]["x2"]) for c in cores], 0)
    return np.ascontiguousarray(x2.reshape(B, T, D))


def kernel(**inp):
    import os
    if os.environ.get("MK_MULTI"):
        return kernel_multi(**inp)
    return kernel_fused(inp)


# version 07
# speedup vs baseline: 1.2512x; 1.2416x over previous
import os
import numpy as np
import ml_dtypes
from contextlib import ExitStack

import concourse.bass as bass
import concourse.mybir as mybir
from concourse.bass_utils import run_bass_kernel_spmd

F32 = mybir.dt.float32
BF16 = mybir.dt.bfloat16
AF = mybir.ActivationFunctionType
ALU = mybir.AluOpType
AX = mybir.AxisListType

B, T, D = 2, 8192, 1024
NCORES = 8
TC = 2048
HALO = 3
TCH = TC + HALO

BF = ml_dtypes.bfloat16


class Prog:
    def __init__(self, name="p"):
        self.nc = bass.Bass("TRN2", target_bir_lowering=False)
        self.es = ExitStack()
        self.sems = {}
        self.cnt = {}
        self.names = set()

    def _uniq(self, name):
        if name in self.names:
            k = 2
            while f"{name}_{k}" in self.names:
                k += 1
            name = f"{name}_{k}"
        self.names.add(name)
        return name

    def sem(self, tag):
        if tag not in self.sems:
            self.sems[tag] = self.es.enter_context(self.nc.semaphore(tag))
            self.cnt[tag] = 0
        return self.sems[tag]

    def inc(self, instr, tag, n=1):
        instr.then_inc(self.sem(tag), n)
        self.cnt[tag] += n
        return self.cnt[tag]

    def wait(self, eng, tag):
        if tag in self.sems and self.cnt[tag] > 0:
            eng.wait_ge(self.sems[tag], self.cnt[tag])

    def sb(self, name, shape, dt=F32, es=None):
        tgt = es or getattr(self, "default_es", None) or self.es
        return tgt.enter_context(self.nc.sbuf_tensor(self._uniq(name), shape, dt))

    def ps(self, name, shape, dt=F32, es=None):
        tgt = es or getattr(self, "default_es", None) or self.es
        return tgt.enter_context(self.nc.psum_tensor(self._uniq(name), shape, dt))

    def barrier(self, nc, engines=("tensor", "vector", "scalar", "sync", "gpsimd")):
        bar = self.sem("bar")
        n = len(engines)
        base = self.cnt.get("bar_n", 0)
        self.cnt["bar_n"] = base + n
        with nc.Block() as blk:
            for engname in engines:
                def body(eng):
                    eng.sem_inc(bar, 1)
                    eng.wait_ge(bar, base + n)
                getattr(blk, engname)(body)

    def din(self, name, shape, dt=F32):
        self.names.add(name)
        return self.nc.dram_tensor(name, shape, dt, kind="ExternalInput")

    def dout(self, name, shape, dt=F32):
        self.names.add(name)
        return self.nc.dram_tensor(name, shape, dt, kind="ExternalOutput")



def emit_wT(p, nc, w_h, wT, ident, pst, es, ne=8, nd=8, tag="wt"):
    with ExitStack() as es2:
        wstg = p.sb(tag + "_stg", [128, nd * 128], F32, es=es2)
        wbf = p.sb(tag + "_bf", [128, nd * 128], BF16, es=es2)
        for ec in range(ne):
            with nc.Block() as blk:
                @blk.sync
                def _(sync, ec=ec):
                    p.inc(sync.dma_start(wstg[:, :], w_h[ec * 128:(ec + 1) * 128, :]), "ld", 16)

                @blk.vector
                def _(v):
                    p.wait(v, "ld")
                    v.tensor_copy(wbf[:], wstg[:])
            base = 8 * ec
            with nc.Block() as blk:
                @blk.tensor
                def _(t, ec=ec, base=base):
                    for n in range(nd):
                        if n >= 2:
                            t.wait_ge(p.sem(tag + "_ev"), base + n - 1)
                        t.transpose(pst[n % 2][:], wbf[:, n * 128:(n + 1) * 128],
                                    ident[:]).then_inc(p.sem(tag + "_t"), 1)
                        p.cnt[tag + "_t"] = p.cnt.get(tag + "_t", 0)

                @blk.vector
                def _(v, ec=ec, base=base):
                    for n in range(nd):
                        v.wait_ge(p.sem(tag + "_t"), base + n + 1)
                        v.tensor_copy(wT[:, n * (ne * 128) + ec * 128: n * (ne * 128) + ec * 128 + 128],
                                      pst[n % 2][:]).then_inc(p.sem(tag + "_ev"), 1)
            p.cnt[tag + "_t"] = p.cnt.get(tag + "_t", 0) + 8
            p.cnt[tag + "_ev"] = p.cnt.get(tag + "_ev", 0) + 8



def build_l0kv():
    p = Prog("l0kv")
    nc = p.nc
    x_h = p.din("x", [TC, D])
    wk_h = p.din("wk", [D, D])
    wv_h = p.din("wv", [D, D])
    id_h = p.din("identbf", [128, 128], BF16)
    kv_h = p.dout("kvpart", [D, D])

    with p.es:
        ident = p.sb("ident", [128, 128], BF16)
        scr = {"xt": p.sb("xt", [128, 1024]), "sq": p.sb("sq", [128, 1024]),
               "cols": p.sb("cols", [128, 8]), "rot": p.sb("rot", [128, 2048], BF16),
               "eps": p.sb("eps", [128, 1])}
        pst = [p.ps("pst%d" % z, [128, 128], BF16) for z in range(2)]
        psp = [p.ps("psp%d" % z, [128, 512]) for z in range(4)]
        xnT = p.sb("xnT", [128, 8 * 2048], BF16)
        ktok = p.sb("ktok", [128, 16 * 1024], BF16)
        vtok = p.sb("vtok", [128, 16 * 1024], BF16)

        with nc.Block() as blk:
            @blk.gpsimd
            def _(g):
                p.inc(g.dma_start(ident[:], id_h[:, :]), "gd", 16)
                g.memset(scr["eps"][:], 1e-5)
                g.wait_ge(p.sem("gd"), 16)
        with nc.Block() as blk:
            emit_ln_xnT_fixed(p, blk, x_h, [128] * 16, xnT, 2048, ident, pst, scr, tag="lnt")

        for wname, w_h in (("k", wk_h), ("v", wv_h)):
            wT = p.sb("w%sT" % wname, [128, 8 * 1024], BF16)
            emit_wT(p, nc, w_h, wT, ident, pst, p.es, tag="wt" + wname)
            tok = {"k": ktok, "v": vtok}[wname]
            with nc.Block() as blk:
                @blk.tensor
                def _(t, wT=wT, wname=wname):
                    n = 0
                    for j in range(16):
                        for half in range(2):
                            if n >= 2:
                                t.wait_ge(p.sem("pv" + wname), n - 1)
                            for dc in range(8):
                                ins = t.matmul(psp[n % 2][:],
                                               xnT[:, dc * 2048 + j * 128: dc * 2048 + j * 128 + 128],
                                               wT[:, dc * 1024 + half * 512: dc * 1024 + half * 512 + 512],
                                               start=(dc == 0), stop=(dc == 7))
                            p.inc(ins, "pe" + wname, 1)
                            n += 1

                @blk.vector
                def _(v, tok=tok, wname=wname):
                    n = 0
                    for j in range(16):
                        for half in range(2):
                            v.wait_ge(p.sem("pe" + wname), n + 1)
                            v.tensor_copy(tok[:, j * 1024 + half * 512: j * 1024 + half * 512 + 512],
                                          psp[n % 2][:]).then_inc(p.sem("pv" + wname), 1)
                            n += 1

        kvout = p.sb("kvout", [128, 8 * 1024], F32)
        with nc.Block() as blk:
            @blk.tensor
            def _(t):
                n = 0
                for kfc in range(8):
                    for half in range(2):
                        if n >= 4:
                            t.wait_ge(p.sem("kv_ev"), n - 3)
                        for j in range(16):
                            ins = t.matmul(psp[n % 4][:],
                                           ktok[:, j * 1024 + kfc * 128: j * 1024 + kfc * 128 + 128],
                                           vtok[:, j * 1024 + half * 512: j * 1024 + half * 512 + 512],
                                           start=(j == 0), stop=(j == 15))
                        p.inc(ins, "kv_pe", 1)
                        n += 1

            @blk.vector
            def _(v):
                n = 0
                for kfc in range(8):
                    for half in range(2):
                        v.wait_ge(p.sem("kv_pe"), n + 1)
                        v.tensor_copy(kvout[:, kfc * 1024 + half * 512: kfc * 1024 + half * 512 + 512],
                                      psp[n % 4][:]).then_inc(p.sem("kv_ev"), 1)
                        n += 1

            @blk.sync
            def _(sync):
                sync.wait_ge(p.sem("kv_ev"), 16)
                p.inc(sync.dma_start(kv_h[:, :].rearrange("(c p) n -> p c n", p=128),
                                     kvout[:, :].rearrange("p (c n) -> p c n", c=8)), "st", 16)
                sync.wait_ge(p.sem("st"), p.cnt["st"])
    return nc


def emit_ln_xnT_fixed(p, blk, x_h, rows, xnT, dst_cols, ident, pst, scr, tag,
                      xn_tok_slab=None, xn_cols=1024, xn_rot=0, xn_guard=None):
    nt = len(rows)
    xt, sq, cols_, rot = scr["xt"], scr["sq"], scr["cols"], scr["rot"]
    ld0 = p.cnt.get("ld", 0)

    @blk.sync
    def _(sync):
        r0 = 0
        for i in range(nt):
            r = rows[i]
            if i > 0:
                sync.wait_ge(p.sem(tag + "_ln"), i)
            p.inc(sync.dma_start(xt[0:r, :], x_h[r0:r0 + r, :]), "ld", 16)
            r0 += r

    @blk.scalar
    def _(s):
        for i in range(nt):
            r = rows[i]
            s.wait_ge(p.sem("ld"), ld0 + 16 * (i + 1))
            if i > 0:
                s.wait_ge(p.sem(tag + "_var"), i)
            s.activation(sq[0:r, :], xt[0:r, :], AF.Copy,
                         accum_out=cols_[0:r, 0:1]).then_inc(p.sem(tag + "_ac"), 1)
            s.activation(sq[0:r, :], xt[0:r, :], AF.Square,
                         accum_out=cols_[0:r, 1:2]).then_inc(p.sem(tag + "_ac"), 1)
            s.wait_ge(p.sem(tag + "_var"), i + 1)
            s.activation(cols_[0:r, 5:6], cols_[0:r, 4:5], AF.Sqrt,
                         bias=scr["eps"][0:r, 0:1]).then_inc(p.sem(tag + "_sq"), 1)

    @blk.vector
    def _(v):
        n = 0
        for i in range(nt):
            r = rows[i]
            v.wait_ge(p.sem(tag + "_ac"), 2 * (i + 1))
            v.tensor_scalar(cols_[0:r, 2:3], cols_[0:r, 0:1], 1.0 / 1024.0,
                            None, ALU.mult).then_inc(p.sem(tag + "_c"), 1)
            v.wait_ge(p.sem(tag + "_c"), 3 * i + 1)
            v.tensor_scalar(cols_[0:r, 3:4], cols_[0:r, 2:3], cols_[0:r, 2:3],
                            None, ALU.mult).then_inc(p.sem(tag + "_c"), 1)
            v.wait_ge(p.sem(tag + "_c"), 3 * i + 2)
            v.scalar_tensor_tensor(cols_[0:r, 4:5], cols_[0:r, 1:2], 1.0 / 1024.0,
                                   cols_[0:r, 3:4], ALU.mult, ALU.subtract).then_inc(
                p.sem(tag + "_var"), 1)
            v.wait_ge(p.sem(tag + "_sq"), i + 1)
            v.reciprocal(cols_[0:r, 6:7], cols_[0:r, 5:6]).then_inc(p.sem(tag + "_c"), 1)
            v.wait_ge(p.sem(tag + "_c"), 3 * i + 3)
            if i >= 2:
                v.wait_ge(p.sem(tag + "_t"), 8 * (i - 1))
            v.tensor_scalar(rot[0:r, (i % 2) * 1024:(i % 2) * 1024 + 1024], xt[0:r, :],
                            cols_[0:r, 2:3], cols_[0:r, 6:7], ALU.subtract, ALU.mult)
            if xn_tok_slab is not None:
                if xn_rot:
                    if i >= xn_rot and xn_guard is not None:
                        v.wait_ge(p.sem(xn_guard[0]), xn_guard[1] + 16 * (i - xn_rot + 1))
                    off = (i % xn_rot) * 1024
                else:
                    off = i * xn_cols
                ins = v.tensor_scalar(xn_tok_slab[0:r, off:off + 1024],
                                      xt[0:r, :], cols_[0:r, 2:3], cols_[0:r, 6:7],
                                      ALU.subtract, ALU.mult)
            else:
                ins = v.tensor_scalar(sq[0:r, 0:1], xt[0:r, 0:1],
                                      cols_[0:r, 2:3], cols_[0:r, 6:7], ALU.subtract, ALU.mult)
            p.inc(ins, tag + "_ln", 1)
            for c in range(8):
                v.wait_ge(p.sem(tag + "_t"), n + 1)
                v.tensor_copy(xnT[:, c * dst_cols + i * 128: c * dst_cols + i * 128 + r],
                              pst[n % 2][:, 0:r]).then_inc(p.sem(tag + "_ev"), 1)
                n += 1

    @blk.tensor
    def _(t):
        n = 0
        for i in range(nt):
            r = rows[i]
            t.wait_ge(p.sem(tag + "_ln"), i + 1)
            for c in range(8):
                if n >= 2:
                    t.wait_ge(p.sem(tag + "_ev"), n - 1)
                t.transpose(pst[n % 2][:, 0:r],
                            rot[0:r, (i % 2) * 1024 + c * 128:(i % 2) * 1024 + c * 128 + 128],
                            ident[0:r, 0:r]).then_inc(p.sem(tag + "_t"), 1)
                n += 1




GRP5 = [(0, 512), (512, 512), (1024, 512), (1536, 512), (2048, 3)]
GRP4 = [(0, 512), (512, 512), (1024, 512), (1536, 512)]
ROWS17 = [128] * 16 + [3]


def emit_l0main(p, nc, h):
    x_h, wq_h, wg_h = h["x_h"], h["wq_h"], h["wg_h"]
    wl1_h, wl2_h, id_h, x1_h = h["wl1_h"], h["wl2_h"], h["id_h"], h["x1_h"]

    with ExitStack() as esPh:
        p.default_es = esPh
        ident = p.sb("ident", [128, 128], BF16, es=esPh)
        scr = {"xt": p.sb("xt", [128, 1024], es=esPh), "sq": p.sb("sq", [128, 1024], es=esPh),
               "cols": p.sb("cols", [128, 8], es=esPh), "rot": p.sb("rot", [128, 2048], BF16, es=esPh),
               "eps": p.sb("eps", [128, 1], es=esPh)}
        pst = [p.ps("pst%d" % z, [128, 128], BF16, es=esPh) for z in range(2)]
        psp = [p.ps("psp%d" % z, [128, 512], es=esPh) for z in range(4)]
        psy = [p.ps("psy%d" % z, [128, 512], es=esPh) for z in range(2)]
        xnT = p.sb("xnT", [128, 8 * TCH], BF16, es=esPh)
        y = p.sb("y", [128, 17 * 1024], BF16, es=esPh)
        gates = p.sb("gates", [128, 17 * 8], es=esPh)
        wgbf = p.sb("wgbf", [128, 64], BF16, es=esPh)

        with nc.Block() as blk:
            @blk.gpsimd
            def _(g):
                p.inc(g.dma_start(ident[:], id_h[:, :]), "gd", 16)
                g.memset(scr["eps"][:], 1e-5)
                g.wait_ge(p.sem("gd"), p.cnt["gd"])

        with nc.Block() as blk:
            emit_ln_xnT_fixed(p, blk, x_h, ROWS17, xnT, TCH, ident, pst, scr, tag="lnt0")

        if "kv_ar" in h:
            wk_h, wv_h, kv_in, kv_out = h["kv_ar"]
            with ExitStack() as esKV:
                ktok = p.sb("ktok", [128, 16 * 1024], BF16, es=esKV)
                vtok = p.sb("vtok", [128, 16 * 1024], BF16, es=esKV)
                for wname, w_h, tok in (("k", wk_h, ktok), ("v", wv_h, vtok)):
                    with ExitStack() as esW:
                        wT = p.sb("w%sT" % wname, [128, 8 * 1024], BF16, es=esW)
                        emit_wT(p, nc, w_h, wT, ident, pst, p.es, tag="wt" + wname)
                        with nc.Block() as blk:
                            @blk.tensor
                            def _(t, wT=wT, wname=wname):
                                n = 0
                                for j in range(16):
                                    for half in range(2):
                                        if n >= 2:
                                            t.wait_ge(p.sem("pv" + wname), n - 1)
                                        for dc in range(8):
                                            ins = t.matmul(
                                                psp[n % 2][:],
                                                xnT[:, dc * TCH + 3 + j * 128: dc * TCH + 3 + j * 128 + 128],
                                                wT[:, dc * 1024 + half * 512: dc * 1024 + half * 512 + 512],
                                                start=(dc == 0), stop=(dc == 7))
                                        p.inc(ins, "pe" + wname, 1)
                                        n += 1

                            @blk.vector
                            def _(v, tok=tok, wname=wname):
                                n = 0
                                for j in range(16):
                                    for half in range(2):
                                        v.wait_ge(p.sem("pe" + wname), n + 1)
                                        v.tensor_copy(
                                            tok[:, j * 1024 + half * 512: j * 1024 + half * 512 + 512],
                                            psp[n % 2][:]).then_inc(p.sem("pv" + wname), 1)
                                        n += 1
                with ExitStack() as esKO:
                    kvout = p.sb("kvout", [128, 8 * 1024], F32, es=esKO)
                    with nc.Block() as blk:
                        @blk.tensor
                        def _(t):
                            n = 0
                            for kfc in range(8):
                                for half in range(2):
                                    if n >= 4:
                                        t.wait_ge(p.sem("kv_ev"), n - 3)
                                    for j in range(16):
                                        ins = t.matmul(
                                            psp[n % 4][:],
                                            ktok[:, j * 1024 + kfc * 128: j * 1024 + kfc * 128 + 128],
                                            vtok[:, j * 1024 + half * 512: j * 1024 + half * 512 + 512],
                                            start=(j == 0), stop=(j == 15))
                                    p.inc(ins, "kv_pe", 1)
                                    n += 1

                        @blk.vector
                        def _(v):
                            n = 0
                            for kfc in range(8):
                                for half in range(2):
                                    v.wait_ge(p.sem("kv_pe"), n + 1)
                                    v.tensor_copy(
                                        kvout[:, kfc * 1024 + half * 512: kfc * 1024 + half * 512 + 512],
                                        psp[n % 4][:]).then_inc(p.sem("kv_ev"), 1)
                                    n += 1

                        @blk.sync
                        def _(sync):
                            sync.wait_ge(p.sem("kv_ev"), 16)
                            p.inc(sync.dma_start(kv_in[:, :].rearrange("(c p) n -> p c n", p=128),
                                                 kvout[:, :].rearrange("p (c n) -> p c n", c=8)), "st", 16)
                            sync.wait_ge(p.sem("st"), p.cnt["st"])
                    with nc.Block() as blk:
                        @blk.gpsimd
                        def _(g):
                            p.inc(g.collective_compute(
                                "AllReduce", ALU.add,
                                replica_groups=[[0, 1, 2, 3], [4, 5, 6, 7]],
                                ins=[kv_in.ap().opt()], outs=[kv_out.ap().opt()]), "cc", 1)
                            g.wait_ge(p.sem("cc"), p.cnt["cc"])

        with ExitStack() as esA:
            kvsum = p.sb("kvsum", [128, 8 * 1024], F32, es=esA)
            kvbf = p.sb("kvbf", [128, 8 * 1024], BF16, es=esA)
            qT = p.sb("qT", [128, 8 * TCH], BF16, es=esA)
            wqT = p.sb("wqT", [128, 8 * 1024], BF16, es=esA)
            wgstg = p.sb("wgstg", [128, 64], F32, es=esA)

            with nc.Block() as blk:
                @blk.gpsimd
                def _(g):
                    if "kvp_h" in h:
                        kvp_h = h["kvp_h"]
                        for part in range(4):
                            if part > 0:
                                g.wait_ge(p.sem("gd"), p.cnt["gd"])
                            ins = g.dma_start(
                                kvsum[:, :].rearrange("p (c n) -> p c n", c=8),
                                kvp_h[part, :, :].rearrange("(c p) n -> p c n", p=128),
                                **({} if part == 0 else {"accum_op": ALU.add}))
                            p.inc(ins, "gd", 16)
                    else:
                        kv_out = h["kv_ar"][3]
                        p.inc(g.dma_start(
                            kvsum[:, :].rearrange("p (c n) -> p c n", c=8),
                            kv_out[:, :].rearrange("(c p) n -> p c n", p=128)), "gd", 16)
                    g.wait_ge(p.sem("gd"), p.cnt["gd"])

                @blk.sync
                def _(sync):
                    p.inc(sync.dma_start(wgstg[:, :].rearrange("p (c n) -> p c n", c=8),
                                         wg_h[:, :].rearrange("(c p) n -> p c n", p=128)),
                          "ld", 16)

                @blk.vector
                def _(v):
                    p.wait(v, "ld")
                    v.tensor_copy(wgbf[:], wgstg[:])
            with nc.Block() as blk:
                @blk.vector
                def _(v):
                    v.tensor_copy(kvbf[:], kvsum[:])

            emit_wT(p, nc, wq_h, wqT, ident, pst, p.es, tag="wtq")

            with nc.Block() as blk:
                @blk.tensor
                def _(t):
                    n = 0
                    for ec in range(8):
                        for (goff, gw) in GRP5:
                            if n >= 2:
                                t.wait_ge(p.sem("q_ev"), n - 1)
                            for dc in range(8):
                                ins = t.matmul(psp[n % 2][:, 0:gw],
                                               wqT[:, dc * 1024 + ec * 128: dc * 1024 + ec * 128 + 128],
                                               xnT[:, dc * TCH + goff: dc * TCH + goff + gw],
                                               start=(dc == 0), stop=(dc == 7))
                            p.inc(ins, "q_pe", 1)
                            n += 1

                @blk.vector
                def _(v):
                    n = 0
                    for ec in range(8):
                        for (goff, gw) in GRP5:
                            v.wait_ge(p.sem("q_pe"), n + 1)
                            v.tensor_scalar(qT[:, ec * TCH + goff: ec * TCH + goff + gw],
                                            psp[n % 2][:, 0:gw], 1.0 / 32.0, None,
                                            ALU.mult).then_inc(p.sem("q_ev"), 1)
                            n += 1

            with nc.Block() as blk:
                @blk.tensor
                def _(t):
                    n = 0
                    for i in range(17):
                        r = ROWS17[i]
                        for half in range(2):
                            if n >= 2:
                                t.wait_ge(p.sem("at_ev"), n - 1)
                            for kfc in range(8):
                                ins = t.matmul(psp[n % 2][0:r, :],
                                               qT[:, kfc * TCH + i * 128: kfc * TCH + i * 128 + r],
                                               kvbf[:, kfc * 1024 + half * 512: kfc * 1024 + half * 512 + 512],
                                               start=(kfc == 0), stop=(kfc == 7))
                            p.inc(ins, "at_pe", 1)
                            n += 1

                @blk.vector
                def _(v):
                    n = 0
                    for i in range(17):
                        r = ROWS17[i]
                        for half in range(2):
                            v.wait_ge(p.sem("at_pe"), n + 1)
                            v.tensor_copy(y[0:r, i * 1024 + half * 512: i * 1024 + half * 512 + 512],
                                          psp[n % 2][0:r, :]).then_inc(p.sem("at_ev"), 1)
                            n += 1

            with nc.Block() as blk:
                @blk.tensor
                def _(t):
                    for i in range(17):
                        r = ROWS17[i]
                        if i > 0:
                            t.wait_ge(p.sem("g_ev"), i)
                        for dc in range(8):
                            ins = t.matmul(psy[0][0:r, 0:8],
                                           xnT[:, dc * TCH + i * 128: dc * TCH + i * 128 + r],
                                           wgbf[:, dc * 8: dc * 8 + 8],
                                           start=(dc == 0), stop=(dc == 7))
                        p.inc(ins, "g_pe", 1)

                @blk.vector
                def _(v):
                    for i in range(17):
                        r = ROWS17[i]
                        v.wait_ge(p.sem("g_pe"), i + 1)
                        v.reduce_max(scr["cols"][0:r, 7:8], psy[0][0:r, 0:7],
                                     AX.X).then_inc(p.sem("g_c"), 1)
                        v.wait_ge(p.sem("g_c"), 2 * i + 1)
                        v.tensor_scalar(gates[0:r, i * 8: i * 8 + 8], psy[0][0:r, 0:8],
                                        scr["cols"][0:r, 7:8], None,
                                        ALU.is_ge).then_inc(p.sem("g_c"), 1)
                        v.wait_ge(p.sem("g_c"), 2 * i + 2)
                        v.memset(gates[0:r, i * 8 + 7: i * 8 + 8], 1.0).then_inc(p.sem("g_ev"), 1)

        hslab = p.sb("hslab", [128, 16 * TCH], BF16, es=esPh)
        w2e = p.sb("w2e", [128, 16 * 1024], BF16, es=esPh)
        w2stg = [p.sb("w2stg%d" % z, [128, 1024], F32, es=esPh) for z in range(2)]
        w1stg = [p.sb("w1s%d" % z, [128, 1024], F32, es=esPh) for z in range(2)]
        w1a = [p.sb("w1a%d" % z, [128, 1024], BF16, es=esPh) for z in range(2)]
        w1b = [p.sb("w1b%d" % z, [128, 1024], BF16, es=esPh) for z in range(2)]
        asc = p.sb("asc", [128, 512], BF16, es=esPh)
        ssc = p.sb("ssc", [128, 512], BF16, es=esPh)
        for e in range(8):
            with nc.Block() as blk:
                gd0 = p.cnt.get("gd", 0)
                w2c0 = p.cnt.get("w2c", 0)

                @blk.gpsimd
                def _(g, e=e, gd0=gd0, w2c0=w2c0):
                    for j in range(16):
                        if j >= 2:
                            g.wait_ge(p.sem("w2c"), w2c0 + j - 1)
                        p.inc(g.dma_start(w2stg[j % 2][:, :],
                                          wl2_h[e, j * 128:(j + 1) * 128, :]), "gd", 16)

                @blk.vector
                def _(v, e=e, gd0=gd0, w2c0=w2c0):
                    for j in range(16):
                        v.wait_ge(p.sem("gd"), gd0 + 16 * (j + 1))
                        p.inc(v.tensor_copy(w2e[:, j * 1024:(j + 1) * 1024],
                                            w2stg[j % 2][:]), "w2c", 1)

            with nc.Block() as blk:
                ld0 = p.cnt.get("ld", 0)
                w1c0 = p.cnt.get("w1c", 0)
                up0 = p.cnt.get("up_g", 0)
                ha0 = p.cnt.get("h_a", 0)
                hs0 = p.cnt.get("h_s", 0)
                hh0 = p.cnt.get("h_h", 0)

                @blk.sync
                def _(sync, e=e, ld0=ld0, w1c0=w1c0):
                    for j in range(16):
                        if j >= 2:
                            sync.wait_ge(p.sem("w1c"), w1c0 + 2 * (j - 1))
                        p.inc(sync.dma_start(
                            w1stg[j % 2][:, :].rearrange("p (c n) -> p c n", c=8),
                            wl1_h[e, :, j * 128: j * 128 + 128].rearrange("(c p) n -> p c n", p=128)),
                            "ld", 16)
                        sync.wait_ge(p.sem("w1c"), w1c0 + 2 * j + 1)
                        p.inc(sync.dma_start(
                            w1stg[j % 2][:, :].rearrange("p (c n) -> p c n", c=8),
                            wl1_h[e, :, 2048 + j * 128: 2048 + j * 128 + 128].rearrange("(c p) n -> p c n", p=128)),
                            "ld", 16)

                @blk.vector
                def _(v, e=e, ld0=ld0, up0=up0, hs0=hs0, ha0=ha0, hh0=hh0):
                    for j in range(16):
                        v.wait_ge(p.sem("ld"), ld0 + 32 * j + 16)
                        if j >= 2:
                            v.wait_ge(p.sem("up_g"), up0 + 10 * (j - 1))
                        p.inc(v.tensor_copy(w1a[j % 2][:], w1stg[j % 2][:]), "w1c", 1)
                        v.wait_ge(p.sem("ld"), ld0 + 32 * (j + 1))
                        p.inc(v.tensor_copy(w1b[j % 2][:], w1stg[j % 2][:]), "w1c", 1)
                        for g5 in range(5):
                            idx = j * 5 + g5
                            gw = GRP5[g5][1]
                            v.wait_ge(p.sem("up_g"), up0 + 2 * idx + 1)
                            p.inc(v.tensor_copy(asc[:, 0:gw], psp[idx % 2][:, 0:gw]), "h_a", 1)
                            v.wait_ge(p.sem("h_s"), hs0 + idx + 1)
                            p.inc(v.tensor_tensor(
                                hslab[:, j * TCH + GRP5[g5][0]: j * TCH + GRP5[g5][0] + gw],
                                asc[:, 0:gw], ssc[:, 0:gw], ALU.mult), "h_h", 1)

                @blk.scalar
                def _(s, e=e, up0=up0, hh0=hh0):
                    for j in range(16):
                        for g5 in range(5):
                            idx = j * 5 + g5
                            gw = GRP5[g5][1]
                            s.wait_ge(p.sem("up_g"), up0 + 2 * idx + 2)
                            if idx >= 1:
                                s.wait_ge(p.sem("h_h"), hh0 + idx)
                            p.inc(s.activation(ssc[:, 0:gw], psp[2 + idx % 2][:, 0:gw],
                                               AF.Silu), "h_s", 1)

                @blk.tensor
                def _(t, e=e, w1c0=w1c0, ha0=ha0, hs0=hs0, up0=up0):
                    for j in range(16):
                        t.wait_ge(p.sem("w1c"), w1c0 + 2 * (j + 1))
                        for g5 in range(5):
                            idx = j * 5 + g5
                            goff, gw = GRP5[g5]
                            if idx >= 2:
                                t.wait_ge(p.sem("h_a"), ha0 + idx - 1)
                                t.wait_ge(p.sem("h_s"), hs0 + idx - 1)
                            for dc in range(8):
                                ins = t.matmul(psp[idx % 2][:, 0:gw],
                                               w1a[j % 2][:, dc * 128:(dc + 1) * 128],
                                               xnT[:, dc * TCH + goff: dc * TCH + goff + gw],
                                               start=(dc == 0), stop=(dc == 7))
                            p.inc(ins, "up_g", 1)
                            for dc in range(8):
                                ins = t.matmul(psp[2 + idx % 2][:, 0:gw],
                                               w1b[j % 2][:, dc * 128:(dc + 1) * 128],
                                               xnT[:, dc * TCH + goff: dc * TCH + goff + gw],
                                               start=(dc == 0), stop=(dc == 7))
                            p.inc(ins, "up_g", 1)

            with nc.Block() as blk:
                dn0 = p.cnt.get("dn_pe", 0)
                ye0 = p.cnt.get("y_ev", 0)

                @blk.tensor
                def _(t, e=e, dn0=dn0, ye0=ye0):
                    for i in range(17):
                        r = ROWS17[i]
                        if i >= 1:
                            t.wait_ge(p.sem("y_ev"), ye0 + 2 * i)
                        for j in range(16):
                            i0 = t.matmul(psy[0][0:r, :],
                                          hslab[:, j * TCH + i * 128: j * TCH + i * 128 + r],
                                          w2e[:, j * 1024: j * 1024 + 512],
                                          start=(j == 0), stop=(j == 15))
                            i1 = t.matmul(psy[1][0:r, :],
                                          hslab[:, j * TCH + i * 128: j * TCH + i * 128 + r],
                                          w2e[:, j * 1024 + 512: j * 1024 + 1024],
                                          start=(j == 0), stop=(j == 15))
                        p.inc(i0, "dn_pe", 1)
                        p.inc(i1, "dn_pe", 1)

                @blk.vector
                def _(v, e=e, dn0=dn0):
                    for i in range(17):
                        r = ROWS17[i]
                        for half in range(2):
                            v.wait_ge(p.sem("dn_pe"), dn0 + 2 * i + half + 1)
                            p.inc(v.tensor_scalar(asc[0:r, :], psy[half][0:r, :],
                                                  gates[0:r, i * 8 + e: i * 8 + e + 1], None,
                                                  ALU.mult), "y_c", 1)
                            v.wait_ge(p.sem("y_c"), p.cnt["y_c"])
                            p.inc(v.tensor_tensor(
                                y[0:r, i * 1024 + half * 512: i * 1024 + half * 512 + 512],
                                y[0:r, i * 1024 + half * 512: i * 1024 + half * 512 + 512],
                                asc[0:r, :], ALU.add), "y_ev", 1)

        with nc.Block() as blk:
            st0 = p.cnt.get("st", 0)
            ld0 = p.cnt.get("ld", 0)

            @blk.sync
            def _(sync, st0=st0, ld0=ld0):
                r0 = 0
                for i in range(17):
                    r = ROWS17[i]
                    if i >= 1:
                        sync.wait_ge(p.sem("st"), st0 + 16 * i)
                    p.inc(sync.dma_start(scr["xt"][0:r, :], x_h[r0:r0 + r, :]), "ld", 16)
                    sync.wait_ge(p.sem("fin"), i + 1)
                    p.inc(sync.dma_start(x1_h[r0:r0 + r, :], scr["xt"][0:r, :]), "st", 16)
                    r0 += r
                sync.wait_ge(p.sem("st"), p.cnt["st"])

            @blk.vector
            def _(v, ld0=ld0):
                for i in range(17):
                    r = ROWS17[i]
                    v.wait_ge(p.sem("ld"), ld0 + 16 * (i + 1))
                    p.inc(v.tensor_copy(scr["sq"][0:r, :],
                                        y[0:r, i * 1024: i * 1024 + 1024]), "fin_c", 1)
                    v.wait_ge(p.sem("fin_c"), p.cnt["fin_c"])
                    p.inc(v.tensor_tensor(scr["xt"][0:r, :], scr["xt"][0:r, :], scr["sq"][0:r, :],
                                          ALU.add), "fin", 1)
        p.default_es = None

def build_l0main():
    p = Prog("l0main")
    nc = p.nc
    h = {"x_h": p.din("x", [TCH, D]), "kvp_h": p.din("kvp", [4, D, D]),
         "wq_h": p.din("wq", [D, D]), "wg_h": p.din("wg", [D, 8]),
         "wl1_h": p.din("wl1", [8, D, 4096]), "wl2_h": p.din("wl2", [8, 2048, D]),
         "id_h": p.din("identbf", [128, 128], BF16), "x1_h": p.dout("x1", [TCH, D])}
    with p.es:
        emit_l0main(p, nc, h)
    return nc



class Chain2:
    ENGSEM = {"tensor": "pe", "vector": "dv", "scalar": "ac", "sync": "sy", "gpsimd": "gp"}
    ENGINC = {"tensor": 1, "vector": 1, "scalar": 1, "sync": 16, "gpsimd": 16}

    def __init__(self, p, tag="c2"):
        self.p = p
        self.tag = tag
        self.steps = []

    def add(self, eng, fn, reads=(), writes=()):
        self.steps.append((eng, fn, tuple(reads), tuple(writes)))

    def dma(self, fn, reads=(), writes=()):
        self.add("sync", fn, reads, writes)

    def gdma(self, fn, reads=(), writes=()):
        self.add("gpsimd", fn, reads, writes)

    def emit(self, nc):
        p = self.p
        alias = getattr(p, "c2_alias", {})
        mutex = getattr(p, "c2_mutex", set())
        if alias:
            nsteps = []
            for (e, f, rd, wr) in self.steps:
                rd2, wr2 = [], []
                for k in rd:
                    k2 = alias.get(k, k)
                    (wr2 if k2 in mutex else rd2).append(k2)
                for k in wr:
                    wr2.append(alias.get(k, k))
                nsteps.append((e, f, tuple(rd2), tuple(dict.fromkeys(wr2))))
            self.steps = nsteps
        n = len(self.steps)
        if not hasattr(p, "c2"):
            p.c2 = {"ord": {}, "lw": {}, "rd": {}}
        st = p.c2
        eng_order = []
        for (e, _, _, _) in self.steps:
            eng_order.append(st["ord"].get(e, 0))
            st["ord"][e] = st["ord"].get(e, 0) + 1
        last_writer, readers = st["lw"], st["rd"]
        deps = [dict() for _ in range(n)]
        for i, (e, fn, rd, wr) in enumerate(self.steps):
            dd = {}
            def need(j):
                ej, gj = j
                dd[ej] = max(dd.get(ej, -1), gj)
            for k in rd:
                if k in last_writer:
                    need(last_writer[k])
            for k in wr:
                if k in last_writer:
                    need(last_writer[k])
                for j in readers.get(k, ()):
                    need(j)
            for k in wr:
                last_writer[k] = (e, eng_order[i])
                readers[k] = []
            for k in rd:
                readers.setdefault(k, []).append((e, eng_order[i]))
            deps[i] = dd
        sems = {e: p.sem("C2" + s) for e, s in self.ENGSEM.items()}
        with nc.Block() as blk:
            for engname in ("tensor", "vector", "scalar", "sync", "gpsimd"):
                mysteps = [(i, f) for i, (e, f, _, _) in enumerate(self.steps) if e == engname]
                if not mysteps:
                    continue

                def body(eng, mysteps=mysteps, engname=engname):
                    for (i, f) in mysteps:
                        for (de, dj) in sorted(deps[i].items()):
                            eng.wait_ge(sems[de], self.ENGINC[de] * (dj + 1))
                        ins = f(eng)
                        ins.then_inc(sems[engname], self.ENGINC[engname])
                getattr(blk, engname)(body)
            for e, s in self.ENGSEM.items():
                p.cnt["C2" + s] = self.ENGINC[e] * st["ord"].get(e, 0)

            def fin(eng):
                for e, s in self.ENGSEM.items():
                    if e != "gpsimd" and st["ord"].get(e, 0):
                        eng.wait_ge(sems[e], self.ENGINC[e] * st["ord"][e])
            blk.gpsimd(fin)


class Chain:

    def __init__(self, p, tag="ck"):
        self.p = p
        self.tag = tag
        self.steps = []

    def add(self, eng, fn, inc=1):
        self.steps.append((eng, fn, inc))

    def dma(self, fn):
        self.add("sync", fn, 16)

    def emit(self, nc):
        p, tag = self.p, self.tag
        cum = []
        c = p.cnt.get(tag, 0)
        for (_, _, inc) in self.steps:
            cum.append(c)
            c += inc
        p.cnt[tag] = c
        sem = p.sem(tag)
        with nc.Block() as blk:
            for engname in ("tensor", "vector", "scalar", "sync", "gpsimd"):
                mysteps = [(i, f, inc) for i, (e, f, inc) in enumerate(self.steps) if e == engname]
                if not mysteps:
                    continue

                def body(eng, mysteps=mysteps):
                    for (i, f, inc) in mysteps:
                        if cum[i] > 0:
                            eng.wait_ge(sem, cum[i])
                        ins = f(eng)
                        ins.then_inc(sem, inc)
                getattr(blk, engname)(body)
            def fin(eng):
                eng.wait_ge(sem, c)
            blk.gpsimd(fin)


def emit_chunk_pass(p, nc, t, nchunks=16, nheads=16):
    ident, identf = t["ident"], t["identf"]
    maskD, maskG = t["maskD"], t["maskG"]
    mdg = t["mdg"]
    psp, psx, psm, pst = t["psp"], t["psx"], t["psm"], t["pst"]
    q1c, k1c, v1c = t["q1c"], t["k1c"], t["v1c"]
    lnb, lnrq, lnrk, cpad = t["lnb"], t["lnrq"], t["lnrk"], t["cpad"]
    Sf, Sbf, Ncum, NcumT = t["Sf"], t["Sbf"], t["Ncum"], t["NcumT"]
    ot_h, z_h, pct_h = t["ot_h"], t["z_h"], t["pct_h"]
    hsel = t["hsel"]

    def rot2(name, shape, dt=F32):
        return [p.sb(name + str(z), shape, dt) for z in range(2)]
    crel = rot2("crel", [16, 128]); cprev = rot2("cprev", [16, 128])
    biasQ1 = rot2("biasQ1", [16, 128])
    rowD = rot2("rowD", [16, 128]); rowG = rot2("rowG", [16, 128])
    expQ3 = rot2("expQ3", [16, 128]); expQ4r = rot2("expQ4r", [16, 128])
    negQ2r = rot2("negQ2r", [16, 128])
    bT = rot2("bT", [128, 16]); negQ2T = rot2("negQ2T", [128, 16]); q4T = rot2("q4T", [128, 16])
    scrD = rot2("scrD", [128, 128]); scrE = rot2("scrE", [128, 128])
    Dbuf = rot2("Dbuf", [128, 128]); Gbuf = rot2("Gbuf", [128, 128], BF16)
    Ddg = rot2("Ddg", [128, 128]); Doff = rot2("Doff", [128, 128])
    pows = [rot2("pow%d" % z, [128, 128]) for z in range(5)]
    ptmp = rot2("ptmp", [128, 128])
    Xf = rot2("Xf", [128, 128]); Yf = rot2("Yf", [128, 128]); Xacc = rot2("Xacc", [128, 128])
    Xbuf = rot2("Xbuf", [128, 128], BF16)
    ktok = rot2("ktok", [128, 64], BF16); vtok = rot2("vtok", [128, 64], BF16)
    ktb = rot2("ktb", [128, 64], BF16)
    negk = rot2("negk", [128, 128], BF16)
    vsc = rot2("vsc", [128, 64], BF16)
    qgbuf = rot2("qgbuf", [128, 128], BF16); qbb2 = rot2("qbb2", [128, 128], BF16)
    zbuf = rot2("zbuf", [128, 128], BF16); zout = rot2("zout", [64, 128], BF16)
    obuf = rot2("obuf", [64, 128])
    pctbuf = rot2("pctbuf", [64, 64], BF16)
    gcol = rot2("gcol", [64, 1])

    def reg(k, r):
        return lambda rows=128, cols=128: psp[k][0:rows, r * 128: r * 128 + cols]
    PS = [
        {"SK": reg(0, 0), "SQ": reg(0, 1), "B": reg(0, 2), "KS": reg(0, 3),
         "QB": reg(1, 0), "O": reg(1, 1), "ST": reg(1, 2), "TT": psx, "MM": reg(1, 3)},
        {"SK": reg(2, 0), "SQ": reg(2, 1), "B": reg(2, 2), "KS": reg(2, 3),
         "QB": reg(3, 0), "O": reg(3, 1), "ST": reg(3, 2), "TT": psm, "MM": reg(3, 3)},
    ]

    if not hasattr(p, "c2_alias"):
        p.c2_alias, p.c2_mutex = {}, set()
    for z in range(2):
        hpz = "hp%d" % z
        for role in ("psSK", "psSQ", "psB", "psKS"):
            p.c2_alias[hpz + role] = "mxP%d" % (2 * z)
        for role in ("psQB", "psO", "psST", "psMM"):
            p.c2_alias[hpz + role] = "mxP%d" % (2 * z + 1)
        p.c2_alias[hpz + "psTT2"] = "mxPX" if z == 0 else "mxPM"
        p.c2_alias[hpz + "pstk"] = "mxPT%d" % z
        p.c2_alias[hpz + "pstv"] = "mxPT%d" % z
    p.c2_mutex |= {"mxP0", "mxP1", "mxP2", "mxP3", "mxPX", "mxPM", "mxPT0", "mxPT1"}
    p.barrier(nc)

    ch = Chain2(p, tag="ck")
    gidx = 0
    for ci in range(nchunks):
        cp = ci % 2
        PRE = "prep%d" % cp
        c0 = ci * 128
        ch.add("vector", lambda v, c0=c0, cp=cp: v.tensor_scalar(
            crel[cp][:], cpad[0:16, 1 + c0:1 + c0 + 128], cpad[0:16, c0:c0 + 1], None, ALU.subtract),
            reads=("cpad",), writes=(PRE + "crel",))
        ch.add("vector", lambda v, c0=c0, cp=cp: v.tensor_scalar(
            cprev[cp][:], cpad[0:16, c0:c0 + 128], cpad[0:16, c0:c0 + 1], None, ALU.subtract),
            reads=("cpad",), writes=(PRE + "cprev",))
        ch.add("vector", lambda v, c0=c0, cp=cp: v.tensor_tensor(
            biasQ1[cp][:], lnb[0:16, c0:c0 + 128], crel[cp][:], ALU.subtract),
            reads=(PRE + "crel",), writes=(PRE + "b1",))
        ch.add("vector", lambda v, c0=c0, cp=cp: v.tensor_tensor(
            biasQ1[cp][:], biasQ1[cp][:], lnrk[0:16, c0:c0 + 128], ALU.add),
            reads=(PRE + "b1",), writes=(PRE + "b1",))
        ch.add("vector", lambda v, c0=c0, cp=cp: v.tensor_tensor(
            rowD[cp][:], cprev[cp][:], lnrk[0:16, c0:c0 + 128], ALU.add),
            reads=(PRE + "cprev",), writes=(PRE + "rowD",))
        ch.add("vector", lambda v, c0=c0, cp=cp: v.tensor_tensor(
            rowG[cp][:], crel[cp][:], lnrq[0:16, c0:c0 + 128], ALU.add),
            reads=(PRE + "crel",), writes=(PRE + "rowG",))
        ch.add("scalar", lambda s, cp=cp: s.activation(
            expQ4r[cp][:], biasQ1[cp][:], AF.Exp, bias=crel[cp][0:16, 127:128]),
            reads=(PRE + "b1", PRE + "crel"), writes=(PRE + "q4",))
        ch.add("scalar", lambda s, cp=cp: s.activation(negQ2r[cp][:], rowD[cp][:], AF.Exp),
               reads=(PRE + "rowD",), writes=(PRE + "q2",))
        ch.add("vector", lambda v, cp=cp: v.tensor_scalar(
            negQ2r[cp][:], negQ2r[cp][:], -1.0, None, ALU.mult),
            reads=(PRE + "q2",), writes=(PRE + "q2",))
        ch.add("scalar", lambda s, cp=cp: s.activation(expQ3[cp][:], rowG[cp][:], AF.Exp),
               reads=(PRE + "rowG",), writes=(PRE + "q3",))
        for (srcb, dstb, sk, dk) in ((biasQ1, bT, "b1", "bT"), (negQ2r, negQ2T, "q2", "q2T"),
                                     (expQ4r, q4T, "q4", "q4T")):
            ch.add("tensor", lambda tt, srcb=srcb, cp=cp: tt.transpose(
                PS[cp]["TT"][:, 0:16], srcb[cp][:], identf[0:16, 0:16]),
                reads=(PRE + sk,), writes=("hp%d" % cp + "psTT2",))
            ch.add("vector", lambda v, dstb=dstb, cp=cp: v.tensor_copy(dstb[cp][:], PS[cp]["TT"][:, 0:16]),
                   reads=("hp%d" % cp + "psTT2",), writes=(PRE + dk,))

        for h in range(nheads):
            hp = gidx % 2
            gidx += 1
            hc, ho = h // 2, (h % 2) * 64
            P = PS[hp]
            HP = "hp%d" % hp
            kS = lambda hc=hc, ho=ho, ci=ci: k1c[ho:ho + 64, hc * 2048 + ci * 128: hc * 2048 + ci * 128 + 128]
            qS = lambda hc=hc, ho=ho, ci=ci: q1c[ho:ho + 64, hc * 2048 + ci * 128: hc * 2048 + ci * 128 + 128]
            vS = lambda hc=hc, ho=ho, ci=ci: v1c[ho:ho + 64, hc * 2048 + ci * 128: hc * 2048 + ci * 128 + 128]
            SH = "S%d" % h
            NH = "N%d" % h

            ch.add("tensor", lambda tt, kS=kS, ho=ho, hp=hp: tt.transpose(
                pst[hp][:, 0:64], kS(), ident[ho:ho + 64, ho:ho + 64]),
                reads=(), writes=(HP + "pstk",))
            ch.add("vector", lambda v, hp=hp: v.tensor_copy(ktok[hp][:], pst[hp][:, 0:64]),
                   reads=(HP + "pstk",), writes=(HP + "ktok",))
            ch.add("tensor", lambda tt, vS=vS, ho=ho, hp=hp: tt.transpose(
                pst[hp][:, 64:128], vS(), ident[ho:ho + 64, ho:ho + 64]),
                reads=(), writes=(HP + "pstv",))
            ch.add("vector", lambda v, hp=hp: v.tensor_copy(vtok[hp][:], pst[hp][:, 64:128]),
                   reads=(HP + "pstv",), writes=(HP + "vtok",))
            ch.add("vector", lambda v, kS=kS, ho=ho, hp=hp: v.tensor_scalar(
                negk[hp][ho:ho + 64, :], kS(), -1.0, None, ALU.mult),
                reads=(), writes=(HP + "negk",))
            ch.add("tensor", lambda tt, kS=kS, ho=ho, hp=hp: tt.matmul(
                P["SK"](), negk[hp][ho:ho + 64, :], kS(), start=True, stop=True, skip_group_check=True),
                reads=(HP + "negk",), writes=(HP + "psSK",))
            ch.add("tensor", lambda tt, kS=kS, qS=qS, hp=hp: tt.matmul(
                P["SQ"](), kS(), qS(), start=True, stop=True, skip_group_check=True),
                reads=(), writes=(HP + "psSQ",))
            ch.add("tensor", lambda tt, h=h, hp=hp, cp=cp: tt.matmul(
                P["B"](), hsel[0:16, h * 128: h * 128 + 128], rowD[cp][:],
                start=True, stop=True, skip_group_check=True),
                reads=(PRE + "rowD",), writes=(HP + "psB",))
            ch.add("vector", lambda v, hp=hp: v.tensor_tensor(scrD[hp][:], P["B"](), maskD[:], ALU.add),
                   reads=(HP + "psB",), writes=(HP + "scrD",))
            ch.add("scalar", lambda s, h=h, hp=hp, cp=cp: s.activation(
                scrE[hp][:], scrD[hp][:], AF.Exp, bias=bT[cp][:, h:h + 1]),
                reads=(HP + "scrD", PRE + "bT"), writes=(HP + "scrE",))
            ch.add("vector", lambda v, hp=hp: v.tensor_tensor(Dbuf[hp][:], scrE[hp][:], P["SK"](), ALU.mult),
                   reads=(HP + "scrE", HP + "psSK"), writes=(HP + "D",))
            ch.add("tensor", lambda tt, h=h, hp=hp, cp=cp: tt.matmul(
                P["B"](), hsel[0:16, h * 128: h * 128 + 128], rowG[cp][:],
                start=True, stop=True, skip_group_check=True),
                reads=(PRE + "rowG",), writes=(HP + "psB",))
            ch.add("vector", lambda v, hp=hp: v.tensor_tensor(scrD[hp][:], P["B"](), maskG[:], ALU.add),
                   reads=(HP + "psB",), writes=(HP + "scrD",))
            ch.add("scalar", lambda s, h=h, hp=hp, cp=cp: s.activation(
                scrE[hp][:], scrD[hp][:], AF.Exp, bias=bT[cp][:, h:h + 1]),
                reads=(HP + "scrD", PRE + "bT"), writes=(HP + "scrE",))
            ch.add("vector", lambda v, hp=hp: v.tensor_tensor(Gbuf[hp][:], scrE[hp][:], P["SQ"](), ALU.mult),
                   reads=(HP + "scrE", HP + "psSQ"), writes=(HP + "G",))
            ch.add("tensor", lambda tt, kS=kS, h=h, hc=hc, ho=ho, hp=hp: tt.matmul(
                P["KS"](128, 64), kS(), Sbf[ho:ho + 64, hc * 64: hc * 64 + 64],
                start=True, stop=True, skip_group_check=True),
                reads=(SH,), writes=(HP + "psKS",))
            ch.add("vector", lambda v, h=h, hp=hp, cp=cp: v.tensor_scalar(
                vsc[hp][:], P["KS"](128, 64), negQ2T[cp][:, h:h + 1], None, ALU.mult),
                reads=(HP + "psKS", PRE + "q2T"), writes=(HP + "vsc",))
            ch.add("vector", lambda v, hp=hp: v.tensor_tensor(Xf[hp][:, 0:64], vsc[hp][:], vtok[hp][:], ALU.add),
                   reads=(HP + "vsc", HP + "vtok"), writes=(HP + "Xf",))
            ch.add("vector", lambda v, h=h, hp=hp, cp=cp: v.tensor_scalar(
                Xf[hp][:, 64:128], ktok[hp][:], negQ2T[cp][:, h:h + 1], None, ALU.mult),
                reads=(HP + "ktok", PRE + "q2T"), writes=(HP + "Xf",))
            ch.add("vector", lambda v, hp=hp: v.tensor_tensor(Ddg[hp][:], Dbuf[hp][:], mdg[:], ALU.mult),
                   reads=(HP + "D",), writes=(HP + "Ddg",))
            ch.add("vector", lambda v, hp=hp: v.tensor_tensor(Doff[hp][:], Dbuf[hp][:], Ddg[hp][:], ALU.subtract),
                   reads=(HP + "D", HP + "Ddg"), writes=(HP + "Doff",))
            ch.add("vector", lambda v, hp=hp: v.tensor_copy(pows[0][hp][:], Ddg[hp][:]),
                   reads=(HP + "Ddg",), writes=(HP + "pow0",))
            for kq in range(1, 5):
                ch.add("tensor", lambda tt, kq=kq, hp=hp: tt.transpose(
                    PS[hp]["TT"][:], pows[kq - 1][hp][:], identf[:]),
                    reads=(HP + "pow%d" % (kq - 1),), writes=(HP + "psTT2",))
                ch.add("vector", lambda v, hp=hp: v.tensor_copy(ptmp[hp][:], PS[hp]["TT"][:]),
                       reads=(HP + "psTT2",), writes=(HP + "ptmp",))
                ch.add("tensor", lambda tt, kq=kq, hp=hp: tt.matmul(
                    P["MM"](), ptmp[hp][:], pows[kq - 1][hp][:], start=True, stop=True,
                    skip_group_check=True),
                    reads=(HP + "ptmp", HP + "pow%d" % (kq - 1)), writes=(HP + "psMM",))
                ch.add("vector", lambda v, kq=kq, hp=hp: v.tensor_copy(pows[kq][hp][:], P["MM"]()),
                       reads=(HP + "psMM",), writes=(HP + "pow%d" % kq,))

            def td_apply(hp, P, HP):
                for kq in range(5):
                    ch.add("tensor", lambda tt, kq=kq, hp=hp: tt.matmul(
                        P["MM"](), pows[kq][hp][:], Yf[hp][:], start=True, stop=True,
                        skip_group_check=True),
                        reads=(HP + "pow%d" % kq, HP + "Yf"), writes=(HP + "psMM",))
                    ch.add("vector", lambda v, hp=hp: v.tensor_tensor(Yf[hp][:], Yf[hp][:], P["MM"](), ALU.add),
                           reads=(HP + "psMM", HP + "Yf"), writes=(HP + "Yf",))
            ch.add("vector", lambda v, hp=hp: v.tensor_copy(Yf[hp][:], Xf[hp][:]),
                   reads=(HP + "Xf",), writes=(HP + "Yf",))
            td_apply(hp, P, HP)
            ch.add("vector", lambda v, hp=hp: v.tensor_copy(Xacc[hp][:], Yf[hp][:]),
                   reads=(HP + "Yf",), writes=(HP + "Xacc",))
            for mh in range(3):
                ch.add("tensor", lambda tt, hp=hp: tt.matmul(
                    P["MM"](), Doff[hp][:], Yf[hp][:], start=True, stop=True, skip_group_check=True),
                    reads=(HP + "Doff", HP + "Yf"), writes=(HP + "psMM",))
                ch.add("vector", lambda v, hp=hp: v.tensor_copy(Yf[hp][:], P["MM"]()),
                       reads=(HP + "psMM",), writes=(HP + "Yf",))
                td_apply(hp, P, HP)
                ch.add("vector", lambda v, hp=hp: v.tensor_tensor(Xacc[hp][:], Xacc[hp][:], Yf[hp][:], ALU.add),
                       reads=(HP + "Yf", HP + "Xacc"), writes=(HP + "Xacc",))
            ch.add("vector", lambda v, hp=hp: v.tensor_copy(Xbuf[hp][:], Xacc[hp][:]),
                   reads=(HP + "Xacc",), writes=(HP + "X",))
            ch.add("tensor", lambda tt, hp=hp: tt.matmul(
                P["B"](64), Xbuf[hp][:, 64:128], Gbuf[hp][:], start=True, stop=True,
                skip_group_check=True),
                reads=(HP + "X", HP + "G"), writes=(HP + "psB",))
            ch.add("tensor", lambda tt, h=h, hp=hp, cp=cp: tt.matmul(
                P["QB"](64), hsel[0:16, h * 128: h * 128 + 64], expQ3[cp][:],
                start=True, stop=True, skip_group_check=True),
                reads=(PRE + "q3",), writes=(HP + "psQB",))
            ch.add("vector", lambda v, ho=ho, hp=hp: v.tensor_copy(qbb2[hp][ho:ho + 64, :], P["QB"](64)),
                   reads=(HP + "psQB",), writes=(HP + "qbb2",))
            ch.add("vector", lambda v, ho=ho, hp=hp, qS=qS: v.tensor_tensor(
                qgbuf[hp][ho:ho + 64, :], qbb2[hp][ho:ho + 64, :], qS(), ALU.mult),
                reads=(HP + "qbb2",), writes=(HP + "qg",))
            ch.add("tensor", lambda tt, hp=hp: tt.matmul(
                P["O"](64), Xbuf[hp][:, 0:64], Gbuf[hp][:], start=True, stop=False,
                skip_group_check=True),
                reads=(HP + "X", HP + "G"), writes=(HP + "psO",))
            ch.add("tensor", lambda tt, h=h, hc=hc, ho=ho, hp=hp: tt.matmul(
                P["O"](64), Sbf[ho:ho + 64, hc * 64: hc * 64 + 64], qgbuf[hp][ho:ho + 64, :],
                start=False, stop=True, skip_group_check=True),
                reads=(SH, HP + "qg"), writes=(HP + "psO",))
            ch.add("vector", lambda v, hp=hp: v.tensor_copy(obuf[hp][:], P["O"](64)),
                   reads=(HP + "psO",), writes=(HP + "obuf",))
            ch.dma(lambda sy, ci=ci, h=h, hp=hp: sy.dma_start(
                ot_h[h * 64:(h + 1) * 64, ci * 128:(ci + 1) * 128], obuf[hp][:]),
                reads=(HP + "obuf",), writes=())
            ch.add("vector", lambda v, ho=ho, hp=hp: v.tensor_copy(zbuf[hp][ho:ho + 64, :], P["B"](64)),
                   reads=(HP + "psB",), writes=(HP + "zbuf",))
            ch.add("vector", lambda v, ho=ho, hp=hp: v.tensor_tensor(
                zout[hp][:], zbuf[hp][ho:ho + 64, :], qgbuf[hp][ho:ho + 64, :], ALU.add),
                reads=(HP + "zbuf", HP + "qg"), writes=(HP + "zout",))
            ch.dma(lambda sy, ci=ci, h=h, hp=hp: sy.dma_start(z_h[ci, h, :, :], zout[hp][:]),
                   reads=(HP + "zout",), writes=())
            ch.add("tensor", lambda tt, h=h, hp=hp, cp=cp: tt.matmul(
                P["ST"](64, 1), hsel[0:16, h * 128: h * 128 + 64], crel[cp][0:16, 127:128],
                start=True, stop=True, skip_group_check=True),
                reads=(PRE + "crel",), writes=(HP + "psST",))
            ch.add("scalar", lambda s, hp=hp: s.activation(gcol[hp][:], P["ST"](64, 1), AF.Exp),
                   reads=(HP + "psST",), writes=(HP + "gcol",))
            ch.add("vector", lambda v, h=h, hp=hp, cp=cp: v.tensor_scalar(
                ktb[hp][:], ktok[hp][:], q4T[cp][:, h:h + 1], None, ALU.mult),
                reads=(HP + "ktok", PRE + "q4T"), writes=(HP + "ktb",))
            ch.add("tensor", lambda tt, hp=hp: tt.matmul(
                P["ST"](64, 64), ktb[hp][:], Xbuf[hp][:, 0:64], start=True, stop=True,
                skip_group_check=True),
                reads=(HP + "ktb", HP + "X"), writes=(HP + "psST",))
            ch.add("vector", lambda v, h=h, hp=hp: v.scalar_tensor_tensor(
                Sf[0:64, h * 64: h * 64 + 64], Sf[0:64, h * 64: h * 64 + 64], gcol[hp][0:64, 0:1],
                P["ST"](64, 64), ALU.mult, ALU.add),
                reads=(SH, HP + "gcol", HP + "psST"), writes=(SH,))
            ch.add("vector", lambda v, h=h, hc=hc, ho=ho, hp=hp: v.tensor_copy(
                Sbf[ho:ho + 64, hc * 64: hc * 64 + 64], Sf[0:64, h * 64: h * 64 + 64]),
                reads=(SH,), writes=(SH,))
            ch.add("tensor", lambda tt, hp=hp: tt.matmul(
                P["KS"](64, 64), Xbuf[hp][:, 64:128], ktb[hp][:], start=True, stop=True,
                skip_group_check=True),
                reads=(HP + "X", HP + "ktb"), writes=(HP + "psKS",))
            ch.add("vector", lambda v, hp=hp: v.scalar_tensor_tensor(
                pctbuf[hp][:], identf[0:64, 0:64], gcol[hp][0:64, 0:1],
                P["KS"](64, 64), ALU.mult, ALU.add),
                reads=(HP + "gcol", HP + "psKS"), writes=(HP + "pct",))
            ch.dma(lambda sy, ci=ci, h=h, hp=hp: sy.dma_start(pct_h[ci, h, :, :], pctbuf[hp][:]),
                   reads=(HP + "pct",), writes=())
            ch.add("tensor", lambda tt, h=h, hp=hp: tt.matmul(
                P["QB"](64, 64), pctbuf[hp][:], Ncum[0:64, h * 64: h * 64 + 64],
                start=True, stop=True, skip_group_check=True),
                reads=(HP + "pct", NH), writes=(HP + "psQB",))
            ch.add("tensor", lambda tt, h=h, hp=hp: tt.matmul(
                P["SQ"](64, 64), Ncum[0:64, h * 64: h * 64 + 64], pctbuf[hp][:],
                start=True, stop=True, skip_group_check=True),
                reads=(HP + "pct", NH), writes=(HP + "psSQ",))
            ch.add("vector", lambda v, h=h, hp=hp: v.tensor_copy(
                Ncum[0:64, h * 64: h * 64 + 64], P["QB"](64, 64)),
                reads=(HP + "psQB",), writes=(NH,))
            ch.add("vector", lambda v, h=h, hp=hp: v.tensor_copy(
                NcumT[0:64, h * 64: h * 64 + 64], P["SQ"](64, 64)),
                reads=(HP + "psSQ",), writes=(NH,))

    if os.environ.get("CPDBG"):
        for (nm, buf, rr, w, r0, key) in (
                ("Dbuf", Dbuf[0], 128, 128, 128, "hp0D"), ("Gbuf", Gbuf[0], 128, 128, 256, "hp0G"),
                ("Xbuf", Xbuf[0], 128, 128, 384, "hp0X"), ("vtok", vtok[0], 128, 64, 512, "hp0vtok"),
                ("ktok", ktok[0], 128, 64, 640, "hp0ktok"), ("scrE", scrE[0], 128, 128, 768, "hp0scrE"),
                ("qgbuf", qgbuf[0], 128, 128, 896, "hp0qg")):
            ch.gdma(lambda g, buf=buf, rr=rr, w=w, r0=r0: g.dma_start(
                ot_h[r0:r0 + rr, 0:w], buf[0:rr, 0:w]),
                reads=(key,), writes=())
    cpcut = int(os.environ.get("CPCUT", "-1"))
    if cpcut >= 0:
        ch.steps = ch.steps[:cpcut]
    ch.emit(nc)
    p.barrier(nc)


def emit_l1loc(p, nc, h, nchunks=16, nheads=16):
    x_h = h["x1_h"]
    wq_h, wk_h, wv_h = h["wq1_h"], h["wk1_h"], h["wv1_h"]
    qcw_h, qcb_h = h["qcw_h"], h["qcb_h"]
    kcw_h, kcb_h = h["kcw_h"], h["kcb_h"]
    vcw_h, vcb_h = h["vcw_h"], h["vcb_h"]
    wa_h, ba_h = h["wa_h"], h["ba_h"]
    wb_h, bb_h = h["wb_h"], h["bb_h"]
    id_h, idf_h = h["id_h"], h["idf_h"]
    mD_h, mG_h, mdg_h = h["mD_h"], h["mG_h"], h["mdg_h"]
    e8_h = h["e8_h"]
    ot_h, z_h, pct_h = h["ot_h"], h["z_h"], h["pct_h"]
    hsel_h = h["hsel_h"]
    stex = h["stex"]
    with ExitStack() as esPh:
        p.default_es = esPh

        ident = p.sb("sident", [128, 128], BF16)
        identf = p.sb("sidentf", [128, 128], F32)
        maskD = p.sb("smaskD", [128, 128], F32)
        maskG = p.sb("smaskG", [128, 128], F32)
        mdg = p.sb("smdg", [128, 128], F32)
        e8 = p.sb("se8", [128, 8 * 16], BF16)
        hselt = p.sb("hselt", [16, 16 * 128], F32)
        epst = p.sb("eps", [128, 1])
        colst = p.sb("cols", [128, 8])
        lnb = p.sb("lnb", [16, 2048])
        lnrq = p.sb("lnrq", [16, 2048])
        lnrk = p.sb("lnrk", [16, 2048])
        cpad = p.sb("cpad", [16, 2049])
        zc16 = p.sb("zc16", [16, 1])
        q1c = p.sb("q1c", [128, 8 * 2048], BF16)
        k1c = p.sb("k1c", [128, 8 * 2048], BF16)
        v1c = p.sb("v1c", [128, 8 * 2048], BF16)
        Sf = p.sb("Sf", [64, 16 * 64])
        Sbf = p.sb("Sbf", [128, 8 * 64], BF16)
        Ncum = p.sb("Ncum", [64, 16 * 64], BF16)
        NcumT = p.sb("NcumT", [64, 16 * 64], BF16)
        psp = [p.ps("psp%d" % z, [128, 512]) for z in range(4)]
        psx = p.ps("psx", [128, 128])
        psm = p.ps("psm", [128, 128])
        pst = [p.ps("pst%d" % z, [128, 128], BF16) for z in range(2)]

        with nc.Block() as blk:
            @blk.gpsimd
            def _(g):
                for (dst, srch) in ((ident, id_h), (identf, idf_h), (maskD, mD_h), (maskG, mG_h), (mdg, mdg_h)):
                    p.inc(g.dma_start(dst[:], srch[:, :]), "gd", 16)
                p.inc(g.dma_start(e8[:, :].rearrange("p (e h) -> p e h", e=8),
                                  e8_h[:, :, :].rearrange("e p h -> p e h")), "gd", 16)
                p.inc(g.dma_start(hselt[:, :], hsel_h[:, :]), "gd", 16)
                g.memset(zc16[:], 0.0)
                g.memset(cpad[:, 0:1], 0.0)
                g.memset(epst[:], 1e-5)
                g.memset(Sf[:], 0.0)
                g.memset(Sbf[:], 0.0)
                g.wait_ge(p.sem("gd"), p.cnt["gd"])
            @blk.vector
            def _(v):
                pass

        with nc.Block() as blk:
            @blk.vector
            def _(v):
                for h in range(16):
                    v.tensor_copy(Ncum[0:64, h * 64:(h + 1) * 64], ident[0:64, 0:64])
                    v.tensor_copy(NcumT[0:64, h * 64:(h + 1) * 64], ident[0:64, 0:64])

        esP = ExitStack()
        xnT = p.sb("xnT", [128, 8 * TCH], BF16, es=esP)
        with ExitStack() as esLN:
            scr = {"xt": p.sb("xt", [128, 1024], es=esLN), "sq": p.sb("sq", [128, 1024], es=esLN),
                   "cols": colst, "rot": p.sb("rot", [128, 2048], BF16, es=esLN),
                   "eps": epst}
            with nc.Block() as blk:
                emit_ln_xnT_fixed(p, blk, x_h, ROWS17, xnT, TCH, ident, pst, scr, tag="lnt")

        with ExitStack() as esA:
            wab = p.sb("wab", [128, 8 * 16 * 2], BF16, es=esA)
            wstg = p.sb("wabstg", [128, 8 * 16], F32, es=esA)
            bac = p.sb("bac", [16, 1], F32, es=esA)
            bbc = p.sb("bbc", [16, 1], F32, es=esA)
            asig = p.sb("asig", [16, 2048], F32, es=esA)
            with nc.Block() as blk:
                @blk.sync
                def _(sync):
                    p.inc(sync.dma_start(wstg[:, :].rearrange("p (c n) -> p c n", c=8),
                                         wa_h[:, :].rearrange("(c p) n -> p c n", p=128)), "ld", 16)
                    sync.wait_ge(p.sem("ld"), p.cnt["ld"])
                    p.inc(sync.dma_start(bac[:], ba_h[:, :]), "ld", 16)
                    p.inc(sync.dma_start(bbc[:], bb_h[:, :]), "ld", 16)
                @blk.vector
                def _(v):
                    v.wait_ge(p.sem("ld"), p.cnt["ld"] - 32)
                    v.tensor_copy(wab[:, 0:128], wstg[:])
            with nc.Block() as blk:
                @blk.sync
                def _(sync):
                    p.inc(sync.dma_start(wstg[:, :].rearrange("p (c n) -> p c n", c=8),
                                         wb_h[:, :].rearrange("(c p) n -> p c n", p=128)), "ld", 16)
                @blk.vector
                def _(v):
                    p.wait(v, "ld")
                    v.tensor_copy(wab[:, 128:256], wstg[:])
            for which in range(2):
                with nc.Block() as blk:
                    pe0 = p.cnt.get("ab_pe", 0)
                    sg0 = p.cnt.get("ab_sg", 0)

                    @blk.tensor
                    def _(t, which=which, pe0=pe0, sg0=sg0):
                        for g4 in range(4):
                            goff, gw = GRP4[g4]
                            if g4 >= 2:
                                t.wait_ge(p.sem("ab_sg"), sg0 + g4 - 1)
                            for dc in range(8):
                                ins = t.matmul(psp[g4 % 2][0:16, :],
                                               wab[:, which * 128 + dc * 16: which * 128 + dc * 16 + 16],
                                               xnT[:, dc * TCH + 3 + goff: dc * TCH + 3 + goff + gw],
                                               start=(dc == 0), stop=(dc == 7))
                            p.inc(ins, "ab_pe", 1)

                    @blk.scalar
                    def _(s, which=which, pe0=pe0):
                        for g4 in range(4):
                            goff, gw = GRP4[g4]
                            s.wait_ge(p.sem("ab_pe"), pe0 + g4 + 1)
                            p.inc(s.activation((asig if which == 0 else lnb)[0:16, goff:goff + gw],
                                               psp[g4 % 2][0:16, :], AF.Sigmoid,
                                               bias=(bac if which == 0 else bbc)[0:16, 0:1]),
                                  "ab_sg", 1)
            with nc.Block() as blk:
                @blk.vector
                def _(v):
                    p.inc(v.tensor_scalar(asig[:], asig[:], 0.1, None, ALU.max), "c_c", 1)
                @blk.scalar
                def _(s):
                    s.wait_ge(p.sem("c_c"), p.cnt["c_c"])
                    p.inc(s.activation(cpad[0:16, 1:2049], asig[:], AF.Ln), "c_c", 1)
                    p.inc(s.activation(lnb[:], lnb[:], AF.Ln), "c_c", 1)
            with nc.Block() as blk:
                @blk.vector
                def _(v):
                    p.inc(v.tensor_tensor_scan(cpad[0:16, 1:2049],
                                               bass.AP(zc16, 0, [[1, 16], [0, 2048]]),
                                               cpad[0:16, 1:2049], 0.0, ALU.add, ALU.add), "c_c", 1)

        with ExitStack() as esB:
            wbf = p.sb("w1bf", [128, 8 * 1024], BF16, es=esB)
            wstg2 = p.sb("wstg2", [128, 256], F32, es=esB)
            raw = [p.sb("raw%d" % z, [128, TCH], BF16, es=esB) for z in range(2)]
            cwc = p.sb("cwc", [128, 8 * 4], F32, es=esB)
            cbc = p.sb("cbc", [128, 8], F32, es=esB)
            acc = [p.sb("acc%d" % z, [128, 512], BF16, es=esB) for z in range(2)]
            sqrot = raw[0]

            for pi, (w_h, cw_h, cb_h, outslab) in enumerate(
                    ((wq_h, qcw_h, qcb_h, q1c), (wk_h, kcw_h, kcb_h, k1c), (wv_h, vcw_h, vcb_h, v1c))):
                for dc in range(8):
                    for hf in range(4):
                        with nc.Block() as blk:
                            @blk.sync
                            def _(sync, dc=dc, hf=hf, w_h=w_h):
                                p.inc(sync.dma_start(wstg2[:, :],
                                                     w_h[dc * 128:(dc + 1) * 128, hf * 256:(hf + 1) * 256]),
                                      "ld", 16)
                            @blk.vector
                            def _(v, dc=dc, hf=hf):
                                p.wait(v, "ld")
                                v.tensor_copy(wbf[:, dc * 1024 + hf * 256: dc * 1024 + hf * 256 + 256],
                                              wstg2[:])
                with nc.Block() as blk:
                    @blk.sync
                    def _(sync, cw_h=cw_h, cb_h=cb_h):
                        p.inc(sync.dma_start(cwc[:, :].rearrange("p (c n) -> p c n", c=8),
                                             cw_h[:, :].rearrange("(c p) n -> p c n", p=128)), "ld", 16)
                        p.inc(sync.dma_start(cbc[:, :], cb_h[:, :]), "ld", 16)
                with nc.Block() as blk:
                    pe0 = p.cnt.get("pj_pe", 0)
                    ev0 = p.cnt.get("pj_ev", 0)
                    cv0 = p.cnt.get("cv_c", 0)
                    cvs0 = p.cnt.get("cv_s", 0)

                    @blk.tensor
                    def _(t, pe0=pe0, ev0=ev0, cvs0=cvs0):
                        n = 0
                        for ec in range(8):
                            if ec >= 2:
                                t.wait_ge(p.sem("cv_s"), cvs0 + 4 * (ec - 1))
                            for (goff, gw) in GRP5:
                                if n >= 2:
                                    t.wait_ge(p.sem("pj_ev"), ev0 + n - 1)
                                for dc in range(8):
                                    ins = t.matmul(psp[n % 2][:, 0:gw],
                                                   wbf[:, dc * 1024 + ec * 128: dc * 1024 + ec * 128 + 128],
                                                   xnT[:, dc * TCH + goff: dc * TCH + goff + gw],
                                                   start=(dc == 0), stop=(dc == 7))
                                p.inc(ins, "pj_pe", 1)
                                n += 1

                    @blk.vector
                    def _(v, pe0=pe0, cvs0=cvs0):
                        n = 0
                        for ec in range(8):
                            for gi, (goff, gw) in enumerate(GRP5):
                                v.wait_ge(p.sem("pj_pe"), pe0 + n + 1)
                                p.inc(v.tensor_copy(raw[ec % 2][:, goff: goff + gw],
                                                    psp[n % 2][:, 0:gw]), "pj_ev", 1)
                                n += 1
                            for g4 in range(4):
                                goff, gw = GRP4[g4]
                                m = ec * 4 + g4
                                if m >= 2:
                                    v.wait_ge(p.sem("cv_s"), cvs0 + m - 1)
                                v.tensor_scalar(acc[m % 2][:, :], raw[ec % 2][:, goff: goff + 512],
                                                cwc[:, ec * 4: ec * 4 + 1], None, ALU.mult)
                                for j in range(1, 4):
                                    ins = v.scalar_tensor_tensor(acc[m % 2][:, :],
                                                                 raw[ec % 2][:, goff + j: goff + j + 512],
                                                                 cwc[:, ec * 4 + j: ec * 4 + j + 1],
                                                                 acc[m % 2][:, :], ALU.mult, ALU.add)
                                p.inc(ins, "cv_c", 1)

                    @blk.scalar
                    def _(s, cv0=cv0, outslab=outslab):
                        for ec in range(8):
                            for g4 in range(4):
                                goff, gw = GRP4[g4]
                                m = ec * 4 + g4
                                s.wait_ge(p.sem("cv_c"), cv0 + m + 1)
                                p.inc(s.activation(outslab[:, ec * 2048 + goff: ec * 2048 + goff + 512],
                                                   acc[m % 2][:, :], AF.Silu,
                                                   bias=cbc[:, ec: ec + 1]), "cv_s", 1)
            for pi, (slab, lnout) in enumerate(((q1c, lnrq), (k1c, lnrk))):
                with nc.Block() as blk:
                    sq0 = p.cnt.get("rn_sq", 0)
                    pe0 = p.cnt.get("rn_pe", 0)
                    @blk.scalar
                    def _(s, slab=slab, sq0=sq0, pe0=pe0):
                        for dc in range(8):
                            if dc >= 1:
                                s.wait_ge(p.sem("rn_pe"), pe0 + 4 * dc)
                            p.inc(s.activation(sqrot[:, 0:2048], slab[:, dc * 2048:(dc + 1) * 2048],
                                               AF.Square), "rn_sq", 1)
                    @blk.tensor
                    def _(t, sq0=sq0):
                        for dc in range(8):
                            t.wait_ge(p.sem("rn_sq"), sq0 + dc + 1)
                            for g4 in range(4):
                                goff, gw = GRP4[g4]
                                ins = t.matmul(psp[g4][0:16, :], e8[:, dc * 16:(dc + 1) * 16],
                                               sqrot[:, goff:goff + 512],
                                               start=(dc == 0), stop=(dc == 7), skip_group_check=True)
                                p.inc(ins, "rn_pe", 1)
                with nc.Block() as blk:
                    @blk.scalar
                    def _(s, lnout=lnout):
                        for g4 in range(4):
                            goff, gw = GRP4[g4]
                            p.inc(s.activation(lnout[0:16, goff:goff + 512], psp[g4][0:16, :],
                                               AF.Sqrt), "rn_c", 1)
                with nc.Block() as blk:
                    @blk.vector
                    def _(v, lnout=lnout):
                        p.inc(v.tensor_scalar(lnout[:], lnout[:], 1e-6, None, ALU.max), "rn_c", 1)
                        v.wait_ge(p.sem("rn_c"), p.cnt["rn_c"])
                        p.inc(v.reciprocal(lnout[:], lnout[:]), "rn_c", 1)
                with nc.Block() as blk:
                    @blk.scalar
                    def _(s, lnout=lnout):
                        p.inc(s.activation(lnout[:], lnout[:], AF.Ln), "rn_c", 1)

        esP.close()
        emit_chunk_pass(p, nc, dict(
            q1c=q1c, k1c=k1c, v1c=v1c, lnb=lnb, lnrq=lnrq, lnrk=lnrk,
            cpad=cpad, Sf=Sf, Sbf=Sbf, Ncum=Ncum, NcumT=NcumT,
            ident=ident, identf=identf, maskD=maskD, maskG=maskG, mdg=mdg,
            hsel=hselt,
            psp=psp, psx=psx, psm=psm, pst=pst,
            ot_h=ot_h, z_h=z_h, pct_h=pct_h),
            nchunks=nchunks, nheads=nheads)

        if stex[0] == "out":
            sloc_h, pcore_h = stex[1], stex[2]
            with nc.Block() as blk:
                @blk.sync
                def _(sync):
                    p.inc(sync.dma_start(sloc_h[:, :], Sf[:, :]), "st", 16)
                    p.inc(sync.dma_start(pcore_h[:, :], NcumT[:, :]), "st", 16)
                    sync.wait_ge(p.sem("st"), p.cnt["st"])
            p.default_es = None
        else:
            ag_in, ag_out = stex[1], stex[2]
            sfb = p.sb("sfbf", [64, 16 * 64], BF16, es=esPh)
            with nc.Block() as blk:
                @blk.vector
                def _(v):
                    p.inc(v.tensor_copy(sfb[:, :], Sf[:, :]), "agc", 1)
                @blk.sync
                def _(sync):
                    sync.wait_ge(p.sem("agc"), p.cnt["agc"])
                    p.inc(sync.dma_start(ag_in[0:64, :], NcumT[:, :]), "st", 16)
                    p.inc(sync.dma_start(ag_in[64:128, :], sfb[:, :]), "st", 16)
                    sync.wait_ge(p.sem("st"), p.cnt["st"])
            with nc.Block() as blk:
                @blk.gpsimd
                def _(g):
                    p.inc(g.collective_compute(
                        "AllGather", ALU.bypass,
                        replica_groups=[[0, 1, 2, 3, 4, 5, 6, 7]],
                        ins=[ag_in.ap().opt()], outs=[ag_out.ap().opt()]), "cc", 1)
                    g.wait_ge(p.sem("cc"), p.cnt["cc"])
        p.default_es = None




def build_l1loc(nchunks=16, nheads=16):
    p = Prog("l1loc")
    nc = p.nc
    h = {"x1_h": p.din("x1", [TCH, D]),
         "wq1_h": p.din("wq1", [D, D]), "wk1_h": p.din("wk1", [D, D]), "wv1_h": p.din("wv1", [D, D]),
         "qcw_h": p.din("qcw", [D, 4]), "qcb_h": p.din("qcb", [128, 8]),
         "kcw_h": p.din("kcw", [D, 4]), "kcb_h": p.din("kcb", [128, 8]),
         "vcw_h": p.din("vcw", [D, 4]), "vcb_h": p.din("vcb", [128, 8]),
         "wa_h": p.din("wa", [D, 16]), "ba_h": p.din("ba", [16, 1]),
         "wb_h": p.din("wb", [D, 16]), "bb_h": p.din("bb", [16, 1]),
         "id_h": p.din("identbf", [128, 128], BF16), "idf_h": p.din("identf32", [128, 128], F32),
         "mD_h": p.din("maskD", [128, 128], F32), "mG_h": p.din("maskG", [128, 128], F32),
         "mdg_h": p.din("mdg", [128, 128], F32), "e8_h": p.din("e8", [8, 128, 16], BF16),
         "ot_h": p.dout("OT", [D, TC]), "z_h": p.dout("Z", [16, 16, 64, 128], BF16),
         "pct_h": p.dout("PcT", [16, 16, 64, 64], BF16),
         "hsel_h": p.din("hsel", [16, 16 * 128], F32),
         "stex": ("out", None, None)}
    h["stex"] = ("out", p.dout("Sloc", [64, 16 * 64]), p.dout("PcoreT", [64, 16 * 64], BF16))
    with p.es:
        emit_l1loc(p, nc, h, nchunks, nheads)
    return nc



def emit_l1fix(p, nc, h):
    x_h, ot_h, z_h, pct_h = h["x1_h"], h["ot_h"], h["z_h"], h["pct_h"]
    rmsw_h = h["rmsw_h"]
    wgd_h, wgu_h, wo_h = h["wgd_h"], h["wgu_h"], h["wo_h"]
    wg8_h, wu8_h, wd8_h = h["wg8_h"], h["wu8_h"], h["wd8_h"]
    id_h, idf_h = h["id_h"], h["idf_h"]
    e8_h, e2t_h, e816_h = h["e8_h"], h["e2t_h"], h["e816_h"]
    x2_h = h["x2_h"]
    xn1_d, wqg_d, wqu_d, wqd_d, rr_d = h["xn1_d"], h["wqg_d"], h["wqu_d"], h["wqd_d"], h["rr_d"]
    pred = h["pred"]
    with ExitStack() as esFix:
        p.default_es = esFix

        ident = p.sb("sident", [128, 128], BF16)
        identf = p.sb("sidentf", [128, 128], F32)
        e8 = p.sb("se8", [128, 8 * 16], BF16)
        e2t = p.sb("se2t", [2, 128], F32)
        e816 = p.sb("se816", [8, 128], F32)
        rmsw = p.sb("srmsw", [128, 8], F32)
        epst = p.sb("eps", [128, 1])
        colst = p.sb("cols", [128, 8])
        y1 = p.sb("y1", [128, 16 * 1024], BF16)
        psp = [p.ps("psp%d" % z, [128, 512]) for z in range(4)]
        psx = p.ps("psx", [128, 128])
        psm = p.ps("psm", [128, 128])
        pst = [p.ps("pst%d" % z, [128, 128], BF16) for z in range(2)]
        zb = p.sb("zbl", [64, 128], BF16)
        pcb = p.sb("pcbl", [64, 64], BF16)
        rr2 = p.sb("rr2", [2, 128], F32)

        with nc.Block() as blk:
            @blk.gpsimd
            def _(g):
                for (dst, srch) in ((ident, id_h), (identf, idf_h), (e2t, e2t_h),
                                    (e816, e816_h), (rmsw, rmsw_h)):
                    p.inc(g.dma_start(dst[:], srch[:, :]), "gd", 16)
                p.inc(g.dma_start(e8[:, :].rearrange("p (e h) -> p e h", e=8),
                                  e8_h[:, :, :].rearrange("e p h -> p e h")), "gd", 16)
                g.memset(epst[:], 1e-5)
                g.wait_ge(p.sem("gd"), p.cnt["gd"])

        esP = ExitStack()
        xnT = p.sb("xnT", [128, 8 * TCH], BF16, es=esP)
        oslab = p.sb("oslab", [128, 8 * 2048], BF16, es=esP)
        grw = p.sb("grw", [128, 8 * 2048], BF16, es=esP)
        DS = p.sb("DSl", [64, 16 * 128], BF16, es=esP)
        Sin = p.sb("Sinl", [64, 16 * 64], BF16, es=esP)
        npred = 3 if pred[0] == "input" else 6
        predp = p.sb("predp", [64, npred * 1024], BF16, es=esP)
        preds = p.sb("preds", [64, npred * 1024], BF16 if pred[0] == "ag" else F32, es=esP)
        hmask = p.sb("hmask", [64, 8], es=esP)
        himask = p.sb("himask", [64, 6 * 64], BF16, es=esP)
        with nc.Block() as blk:
            @blk.gpsimd
            def _(g):
                if pred[0] == "input":
                    predp_h, preds_h = pred[1], pred[2]
                    p.inc(g.dma_start(predp[:, 0:3 * 1024].rearrange("p (l n) -> p l n", l=3),
                                      predp_h[:, :, :].rearrange("l p n -> p l n")), "gd", 16)
                    p.inc(g.dma_start(preds[:, 0:3 * 1024].rearrange("p (l n) -> p l n", l=3),
                                      preds_h[:, :, :].rearrange("l p n -> p l n")), "gd", 16)
                else:
                    ag_out, hmask_h, himask_h = pred[1], pred[2], pred[3]
                    for l, s in enumerate((0, 1, 2, 4, 5, 6)):
                        p.inc(g.dma_start(predp[:, l * 1024:(l + 1) * 1024],
                                          ag_out[s * 128: s * 128 + 64, :]), "gd", 16)
                        p.inc(g.dma_start(preds[:, l * 1024:(l + 1) * 1024],
                                          ag_out[s * 128 + 64: s * 128 + 128, :]), "gd", 16)
                    p.inc(g.dma_start(hmask[:, 0:6], hmask_h[:, :]), "gd", 16)
                    p.inc(g.dma_start(himask[:, :], himask_h[:, :]), "gd", 16)
                g.memset(DS[:], 0.0)
                g.memset(Sin[:], 0.0)
                g.wait_ge(p.sem("gd"), p.cnt["gd"])
        with ExitStack() as esLN:
            scr = {"xt": p.sb("xt", [128, 1024], es=esLN), "sq": p.sb("sq", [128, 1024], es=esLN),
                   "cols": colst, "rot": p.sb("rot", [128, 2048], BF16, es=esLN),
                   "eps": epst}
            xnf = p.sb("xnf", [128, 2 * 1024], F32, es=esLN)
            gd0 = p.cnt.get("gd", 0)
            with nc.Block() as blk:
                emit_ln_xnT_fixed(p, blk, x_h, ROWS17, xnT, TCH, ident, pst, scr, tag="lnt2",
                                  xn_tok_slab=xnf, xn_rot=2, xn_guard=("gd", gd0))

                @blk.gpsimd
                def _(g, gd0=gd0):
                    r0 = 0
                    for i in range(17):
                        r = ROWS17[i]
                        g.wait_ge(p.sem("lnt2_ln"), i + 1)
                        p.inc(g.dma_start(xn1_d[r0:r0 + r, :], xnf[0:r, (i % 2) * 1024:(i % 2) * 1024 + 1024]), "gd", 16)
                        r0 += r
                    g.wait_ge(p.sem("gd"), p.cnt["gd"])

        with ExitStack() as esO:
            ostg = p.sb("ostg", [128, 2048], F32, es=esO)
            for hc in range(8):
                with nc.Block() as blk:
                    @blk.sync
                    def _(sync, hc=hc):
                        p.inc(sync.dma_start(ostg[:, :], ot_h[hc * 128:(hc + 1) * 128, :]), "ld", 16)
                    @blk.vector
                    def _(v, hc=hc):
                        p.wait(v, "ld")
                        v.tensor_copy(oslab[:, hc * 2048:(hc + 1) * 2048], ostg[:])

        if not hasattr(p, "c2_alias"):
            p.c2_alias, p.c2_mutex = {}, set()
        p.c2_alias["fxp0"] = "mxPX"
        p.c2_alias["fxp1"] = "mxPM"
        p.c2_mutex |= {"mxP0", "mxP1", "mxP2", "mxP3", "mxPX", "mxPM", "mxPT0", "mxPT1"}
        p.barrier(nc)
        ch = Chain2(p, tag="fx")
        pb1 = [p.sb("pb1%d" % z, [64, 64], BF16, es=esP) for z in range(2)]
        sb1 = [p.sb("sb1%d" % z, [64, 64], es=esP) for z in range(2)]
        def fxps(par, rows=64):
            t = psx if par == 0 else psm
            return t[0:rows, 0:128]
        for hh in range(16):
            par = hh % 2
            SK = "Sin%d" % hh
            for l in range(npred):
                if pred[0] == "ag":
                    ch.add("vector", lambda v, hh=hh, l=l, par=par: v.tensor_scalar(
                        pb1[par][:, :], predp[0:64, l * 1024 + hh * 64: l * 1024 + hh * 64 + 64],
                        hmask[0:64, l:l + 1], None, ALU.mult),
                        reads=(), writes=("pb%d" % par,))
                    ch.add("vector", lambda v, l=l, par=par: v.tensor_tensor(
                        pb1[par][:, :], pb1[par][:, :], himask[0:64, l * 64:(l + 1) * 64], ALU.add),
                        reads=("pb%d" % par,), writes=("pb%d" % par,))
                    ch.add("vector", lambda v, hh=hh, l=l, par=par: v.tensor_scalar(
                        sb1[par][:, :], preds[0:64, l * 1024 + hh * 64: l * 1024 + hh * 64 + 64],
                        hmask[0:64, l:l + 1], None, ALU.mult),
                        reads=(), writes=("sb%d" % par,))
                    ch.add("tensor", lambda tt, hh=hh, par=par: tt.matmul(
                        fxps(par)[0:64, 0:64], pb1[par][:, :], Sin[0:64, hh * 64: hh * 64 + 64],
                        start=True, stop=True, skip_group_check=True),
                        reads=("pb%d" % par, SK), writes=("fxp%d" % par,))
                    ch.add("vector", lambda v, hh=hh, par=par: v.tensor_tensor(
                        Sin[0:64, hh * 64: hh * 64 + 64], sb1[par][:, :], fxps(par)[0:64, 0:64],
                        ALU.add),
                        reads=("sb%d" % par, "fxp%d" % par), writes=(SK,))
                else:
                    ch.add("tensor", lambda tt, hh=hh, l=l, par=par: tt.matmul(
                        fxps(par)[0:64, 0:64], predp[0:64, l * 1024 + hh * 64: l * 1024 + hh * 64 + 64],
                        Sin[0:64, hh * 64: hh * 64 + 64], start=True, stop=True, skip_group_check=True),
                        reads=(SK,), writes=("fxp%d" % par,))
                    ch.add("vector", lambda v, hh=hh, l=l, par=par: v.tensor_tensor(
                        Sin[0:64, hh * 64: hh * 64 + 64],
                        preds[0:64, l * 1024 + hh * 64: l * 1024 + hh * 64 + 64],
                        fxps(par)[0:64, 0:64], ALU.add),
                        reads=("fxp%d" % par,), writes=(SK,))
            ho = (hh % 2) * 64
            ch.add("vector", lambda v, hh=hh, ho=ho: v.tensor_copy(
                DS[0:64, hh * 128 + ho: hh * 128 + ho + 64], Sin[0:64, hh * 64: hh * 64 + 64]),
                reads=(SK,), writes=("DS%d" % hh,))
        zb = [p.sb("zbl%d" % z, [64, 128], BF16, es=esP) for z in range(2)]
        pcb = [p.sb("pcbl%d" % z, [64, 64], BF16, es=esP) for z in range(2)]
        gidx = 0
        for ci in range(16):
            for hh in range(16):
                hc = hh // 2
                par = gidx % 2
                gidx += 1
                DK_ = "DS%d" % hh
                ch.dma(lambda sy, ci=ci, hh=hh, par=par: sy.dma_start(zb[par][:, :], z_h[ci, hh, :, :]),
                       reads=(), writes=("zb%d" % par,))
                ch.dma(lambda sy, ci=ci, hh=hh, par=par: sy.dma_start(pcb[par][:, :], pct_h[ci, hh, :, :]),
                       reads=(), writes=("pcb%d" % par,))
                ch.add("tensor", lambda tt, hh=hh, par=par: tt.matmul(
                    fxps(par, 128), DS[0:64, hh * 128: hh * 128 + 128], zb[par][:, :],
                    start=True, stop=True, skip_group_check=True),
                    reads=("zb%d" % par, DK_), writes=("fxp%d" % par,))
                ch.add("vector", lambda v, ci=ci, hc=hc, par=par: v.tensor_tensor(
                    oslab[:, hc * 2048 + ci * 128: hc * 2048 + ci * 128 + 128],
                    oslab[:, hc * 2048 + ci * 128: hc * 2048 + ci * 128 + 128],
                    fxps(par, 128), ALU.add),
                    reads=("fxp%d" % par,), writes=("os",))
                ch.add("tensor", lambda tt, hh=hh, par=par: tt.matmul(
                    fxps(par)[0:64, 0:128], pcb[par][:, :], DS[0:64, hh * 128: hh * 128 + 128],
                    start=True, stop=True, skip_group_check=True),
                    reads=("pcb%d" % par, DK_), writes=("fxp%d" % par,))
                ch.add("vector", lambda v, hh=hh, par=par: v.tensor_copy(
                    DS[0:64, hh * 128: hh * 128 + 128], fxps(par)[0:64, 0:128]),
                    reads=("fxp%d" % par,), writes=(DK_,))
        ch.emit(nc)
        p.barrier(nc)

        with ExitStack() as esR:
            sqs = p.sb("sqs", [128, 2048], BF16, es=esR)
            rrow = p.sb("rrow", [16, 2048], F32, es=esR)
            with nc.Block() as blk:
                sq0 = p.cnt.get("rm_sq", 0)
                pe0 = p.cnt.get("rm_pe", 0)
                @blk.scalar
                def _(s, sq0=sq0, pe0=pe0):
                    for hc in range(8):
                        if hc >= 1:
                            s.wait_ge(p.sem("rm_pe"), pe0 + 4 * hc)
                        p.inc(s.activation(sqs[:, :], oslab[:, hc * 2048:(hc + 1) * 2048],
                                           AF.Square), "rm_sq", 1)
                @blk.tensor
                def _(t, sq0=sq0):
                    for hc in range(8):
                        t.wait_ge(p.sem("rm_sq"), sq0 + hc + 1)
                        for g4 in range(4):
                            goff = g4 * 512
                            ins = t.matmul(psp[g4][0:16, :], e8[:, hc * 16:(hc + 1) * 16],
                                           sqs[:, goff:goff + 512],
                                           start=(hc == 0), stop=(hc == 7), skip_group_check=True)
                            p.inc(ins, "rm_pe", 1)
            with nc.Block() as blk:
                @blk.vector
                def _(v):
                    for g4 in range(4):
                        goff = g4 * 512
                        p.inc(v.tensor_scalar(rrow[0:16, goff:goff + 512], psp[g4][0:16, :],
                                              1.0 / 64.0, 1e-6, ALU.mult, ALU.add), "rm_c", 1)
            with nc.Block() as blk:
                @blk.scalar
                def _(s):
                    p.inc(s.activation(rrow[:], rrow[:], AF.Sqrt), "rm_c", 1)
            with nc.Block() as blk:
                @blk.vector
                def _(v):
                    p.inc(v.reciprocal(rrow[:], rrow[:]), "rm_c", 1)
            with nc.Block() as blk:
                @blk.sync
                def _(sync):
                    p.inc(sync.dma_start(rr_d[:, :], rrow[:]), "st", 16)
                    sync.wait_ge(p.sem("st"), p.cnt["st"])

        with ExitStack() as esG:
            uT = p.sb("uT", [128, 4 * 2048], BF16, es=esG)
            wstg = p.sb("wstg", [128, 1024], F32, es=esG)
            wbf = p.sb("wbf", [128, 8 * 1024], BF16, es=esG)
            for dc in range(8):
                with nc.Block() as blk:
                    @blk.sync
                    def _(sync, dc=dc):
                        p.inc(sync.dma_start(wstg[:, 0:512], wgd_h[dc * 128:(dc + 1) * 128, :]), "ld", 16)
                    @blk.vector
                    def _(v, dc=dc):
                        p.wait(v, "ld")
                        v.tensor_copy(wbf[:, dc * 512: dc * 512 + 512], wstg[:, 0:512])
            with nc.Block() as blk:
                pe0 = p.cnt.get("ug_pe", 0)
                @blk.tensor
                def _(t, pe0=pe0):
                    n = 0
                    for uc in range(4):
                        for g4 in range(4):
                            goff = g4 * 512
                            if n >= 2:
                                t.wait_ge(p.sem("ug_ev"), p.cnt.get("ug_ev", 0) - 16 + n - 1) if False else \
                                    t.wait_ge(p.sem("ug_ev"), n - 1)
                            for dc in range(8):
                                ins = t.matmul(psp[n % 2][:, 0:512],
                                               wbf[:, dc * 512 + uc * 128: dc * 512 + uc * 128 + 128],
                                               xnT[:, dc * TCH + 3 + goff: dc * TCH + 3 + goff + 512],
                                               start=(dc == 0), stop=(dc == 7))
                            p.inc(ins, "ug_pe", 1)
                            n += 1
                @blk.scalar
                def _(s, pe0=pe0):
                    n = 0
                    for uc in range(4):
                        for g4 in range(4):
                            goff = g4 * 512
                            s.wait_ge(p.sem("ug_pe"), pe0 + n + 1)
                            p.inc(s.activation(uT[:, uc * 2048 + goff: uc * 2048 + goff + 512],
                                               psp[n % 2][:, 0:512], AF.Silu), "ug_ev", 1)
                            n += 1
            for uc2 in range(4):
                with nc.Block() as blk:
                    @blk.sync
                    def _(sync, uc2=uc2):
                        p.inc(sync.dma_start(wstg[:, :], wgu_h[uc2 * 128:(uc2 + 1) * 128, :]), "ld", 16)
                    @blk.vector
                    def _(v, uc2=uc2):
                        p.wait(v, "ld")
                        v.tensor_copy(wbf[:, uc2 * 1024:(uc2 + 1) * 1024], wstg[:])
            with nc.Block() as blk:
                pe0 = p.cnt.get("gg_pe", 0)
                sg0 = p.cnt.get("gg_sg", 0)
                @blk.tensor
                def _(t, pe0=pe0, sg0=sg0):
                    n = 0
                    for gc in range(8):
                        for g4 in range(4):
                            goff = g4 * 512
                            if n >= 2:
                                t.wait_ge(p.sem("gg_ev"), n - 1)
                            for uc in range(4):
                                ins = t.matmul(psp[n % 2][:, 0:512],
                                               wbf[:, uc * 1024 + gc * 128: uc * 1024 + gc * 128 + 128],
                                               uT[:, uc * 2048 + goff: uc * 2048 + goff + 512],
                                               start=(uc == 0), stop=(uc == 3))
                            p.inc(ins, "gg_pe", 1)
                            n += 1
                @blk.scalar
                def _(s, pe0=pe0):
                    n = 0
                    for gc in range(8):
                        for g4 in range(4):
                            goff = g4 * 512
                            s.wait_ge(p.sem("gg_pe"), pe0 + n + 1)
                            p.inc(s.activation(grw[:, gc * 2048 + goff: gc * 2048 + goff + 512],
                                               psp[n % 2][:, 0:512], AF.Sigmoid), "gg_sg", 1)
                            n += 1
                @blk.vector
                def _(v, sg0=sg0):
                    n = 0
                    for gc in range(8):
                        for g4 in range(4):
                            goff = g4 * 512
                            v.wait_ge(p.sem("gg_sg"), sg0 + n + 1)
                            p.inc(v.tensor_scalar(grw[:, gc * 2048 + goff: gc * 2048 + goff + 512],
                                                  grw[:, gc * 2048 + goff: gc * 2048 + goff + 512],
                                                  rmsw[:, gc: gc + 1], None, ALU.mult), "gg_ev", 1)
                            n += 1

        p.c2_alias["rap0"] = "mxPX"
        p.c2_alias["rap1"] = "mxPM"
        p.barrier(nc)
        ch2 = Chain2(p, tag="ra")
        rr2r = [p.sb("rr2%d" % z, [2, 128], F32, es=esP) for z in range(2)]
        gidx2 = 0
        for ci in range(16):
            for hc in range(8):
                par = gidx2 % 2
                gidx2 += 1
                ch2.dma(lambda sy, ci=ci, hc=hc, par=par: sy.dma_start(
                    rr2r[par][:, :], rr_d[2 * hc: 2 * hc + 2, ci * 128:(ci + 1) * 128]),
                    reads=(), writes=("rr%d" % par,))
                ch2.add("tensor", lambda tt, par=par: tt.matmul(
                    (psm if par else psx)[:, :], e2t[:, :], rr2r[par][:, :],
                    start=True, stop=True, skip_group_check=True),
                    reads=("rr%d" % par,), writes=("rap%d" % par,))
                ch2.add("vector", lambda v, ci=ci, hc=hc, par=par: v.tensor_tensor(
                    oslab[:, hc * 2048 + ci * 128: hc * 2048 + ci * 128 + 128],
                    oslab[:, hc * 2048 + ci * 128: hc * 2048 + ci * 128 + 128],
                    (psm if par else psx)[:, :], ALU.mult),
                    reads=("rap%d" % par,), writes=("os",))
                ch2.add("vector", lambda v, ci=ci, hc=hc, par=par: v.tensor_tensor(
                    oslab[:, hc * 2048 + ci * 128: hc * 2048 + ci * 128 + 128],
                    oslab[:, hc * 2048 + ci * 128: hc * 2048 + ci * 128 + 128],
                    grw[:, hc * 2048 + ci * 128: hc * 2048 + ci * 128 + 128], ALU.mult),
                    reads=("grw",), writes=("os",))
        ch2.emit(nc)
        p.barrier(nc)

        with ExitStack() as esW:
            wstg = p.sb("wstg2", [128, 1024], F32, es=esW)
            wbf = p.sb("wobf", [128, 8 * 1024], BF16, es=esW)
            for hc in range(8):
                with nc.Block() as blk:
                    @blk.sync
                    def _(sync, hc=hc):
                        p.inc(sync.dma_start(wstg[:, :], wo_h[hc * 128:(hc + 1) * 128, :]), "ld", 16)
                    @blk.vector
                    def _(v, hc=hc):
                        p.wait(v, "ld")
                        v.tensor_copy(wbf[:, hc * 1024:(hc + 1) * 1024], wstg[:])
            with nc.Block() as blk:
                pe0 = p.cnt.get("wo_pe", 0)
                @blk.tensor
                def _(t, pe0=pe0):
                    n = 0
                    for i in range(16):
                        for half in range(2):
                            if n >= 2:
                                t.wait_ge(p.sem("wo_ev"), n - 1)
                            for hc in range(8):
                                ins = t.matmul(psp[n % 2][:, 0:512],
                                               oslab[:, hc * 2048 + i * 128: hc * 2048 + i * 128 + 128],
                                               wbf[:, hc * 1024 + half * 512: hc * 1024 + half * 512 + 512],
                                               start=(hc == 0), stop=(hc == 7))
                            p.inc(ins, "wo_pe", 1)
                            n += 1
                @blk.vector
                def _(v, pe0=pe0):
                    n = 0
                    for i in range(16):
                        for half in range(2):
                            v.wait_ge(p.sem("wo_pe"), pe0 + n + 1)
                            p.inc(v.tensor_copy(y1[:, i * 1024 + half * 512: i * 1024 + half * 512 + 512],
                                                psp[n % 2][:, 0:512]), "wo_ev", 1)
                            n += 1
        esP.close()

        p.barrier(nc)
        ffn_bufs = emit_ffn(p, nc, dict(ident=ident, identf=identf, e816=e816,
                                        psp=psp, psx=psx, psm=psm, pst=pst,
                                        y1=y1, xn1_d=xn1_d, wg8_h=wg8_h, wu8_h=wu8_h, wd8_h=wd8_h,
                                        wqg_d=wqg_d, wqu_d=wqu_d, wqd_d=wqd_d, epst=epst))

        with nc.Block() as blk:
            xo = ffn_bufs["xstg"]
            yo = ffn_bufs["wtile"][:, 0:1024] if False else ffn_bufs["wtile"]
            st0 = p.cnt.get("st", 0)
            ld0 = p.cnt.get("ld", 0)

            @blk.sync
            def _(sync, st0=st0, ld0=ld0):
                for i in range(16):
                    if i >= 1:
                        sync.wait_ge(p.sem("st"), st0 + 16 * i)
                    p.inc(sync.dma_start(xo[:, 0:1024], x_h[HALO + i * 128: HALO + i * 128 + 128, :]), "ld", 16)
                    sync.wait_ge(p.sem("x2f"), i + 1)
                    p.inc(sync.dma_start(x2_h[i * 128:(i + 1) * 128, :], xo[:, 0:1024]), "st", 16)
                sync.wait_ge(p.sem("st"), p.cnt["st"])

            @blk.vector
            def _(v, ld0=ld0):
                for i in range(16):
                    v.wait_ge(p.sem("ld"), ld0 + 16 * (i + 1))
                    p.inc(v.tensor_copy(yo[:, 0:1024], y1[:, i * 1024:(i + 1) * 1024]), "x2c", 1)
                    v.wait_ge(p.sem("x2c"), p.cnt["x2c"])
                    p.inc(v.tensor_tensor(xo[:, 0:1024], xo[:, 0:1024], yo[:, 0:1024], ALU.add), "x2f", 1)
    return nc




def build_l1fix():
    p = Prog("l1fix")
    nc = p.nc
    h = {"x1_h": p.din("x1", [TCH, D]), "ot_h": p.din("OT", [D, TC]),
         "z_h": p.din("Z", [16, 16, 64, 128], BF16), "pct_h": p.din("PcT", [16, 16, 64, 64], BF16),
         "pred": ("input", p.din("predP", [3, 64, 16 * 64], BF16), p.din("predS", [3, 64, 16 * 64])),
         "rmsw_h": p.din("rmsw", [128, 8]),
         "wgd_h": p.din("wgd", [D, 512]), "wgu_h": p.din("wgu", [512, D]), "wo_h": p.din("wo", [D, D]),
         "wg8_h": p.din("wgate", [2816, D]), "wu8_h": p.din("wup", [2816, D]),
         "wd8_h": p.din("wdown", [D, 2816]),
         "id_h": p.din("identbf", [128, 128], BF16), "idf_h": p.din("identf32", [128, 128], F32),
         "e8_h": p.din("e8", [8, 128, 16], BF16), "e2t_h": p.din("e2t", [2, 128], F32),
         "e816_h": p.din("e816", [8, 128], F32),
         "x2_h": p.dout("x2", [TC, D]),
         "xn1_d": p.nc.dram_tensor("xn1d", [TCH, D], F32),
         "wqg_d": p.nc.dram_tensor("wqgd", [D, 2816], BF16),
         "wqu_d": p.nc.dram_tensor("wqud", [D, 2816], BF16),
         "wqd_d": p.nc.dram_tensor("wqdd", [2816, D], BF16),
         "rr_d": p.nc.dram_tensor("rrd", [16, 2048], F32)}
    with p.es:
        emit_l1fix(p, nc, h)
    return nc


def emit_quant_fm(p, ch, xin, W, qs, dst_slab, dst_off, identf, psp, psx, psm, e816,
                  par, inkey, outkey):
    nb = W // 16
    nch = W // 128
    bm, inv = qs["bm"][par], qs["inv"][par]
    sT = qs["sT"][par]
    xT = qs["xT"][par]
    ex = qs["ex"][par]
    ax, qb, qt = qs["ax"][par], qs["qb"][par], qs["qt"][par]
    Q = "q%d" % par
    def xtr(c):
        return psp[2][:, ((par * 2) + (c % 2)) * 128: ((par * 2) + (c % 2)) * 128 + 128]
    def strp(c):
        t = psx if par == 0 else psm
        return t[0:8, 0:128]
    def expp(c):
        return psp[3][:, ((par * 2) + (c % 2)) * 128: ((par * 2) + (c % 2)) * 128 + 128]
    ch.add("vector", lambda v: v.tensor_reduce(
        bm[:, 0:nb], xin.rearrange("p (b k) -> p b k", k=16), AX.X, ALU.max,
        apply_absolute_value=True), reads=(inkey,), writes=(Q + "bm",))
    ch.add("vector", lambda v: v.tensor_scalar(bm[:, 0:nb], bm[:, 0:nb], 1.0 / 6.0, 1e-12,
                                               ALU.mult, ALU.max),
           reads=(Q + "bm",), writes=(Q + "bm",))
    ch.add("vector", lambda v: v.reciprocal(inv[:, 0:nb], bm[:, 0:nb]),
           reads=(Q + "bm",), writes=(Q + "inv",))
    ch.add("vector", lambda v: v.tensor_scalar(bm[:, 0:nb], bm[:, 0:nb], 0.5, None, ALU.mult),
           reads=(Q + "bm",), writes=(Q + "bm",))
    for c in range(nch):
        ch.add("tensor", lambda tt, c=c: tt.transpose(
            xtr(c), xin[:, c * 128:(c + 1) * 128], identf[:]),
            reads=(inkey,), writes=(Q + "xtr%d" % (c % 2),))
        ch.add("vector", lambda v, c=c: v.tensor_copy(xT[:, c * 128:(c + 1) * 128], xtr(c)),
               reads=(Q + "xtr%d" % (c % 2),), writes=(Q + "xT",))
    for c in range(nch):
        ch.add("tensor", lambda tt, c=c: tt.transpose(
            strp(c), inv[:, c * 8:(c + 1) * 8], identf[:]),
            reads=(Q + "inv",), writes=(Q + "str",))
        ch.add("vector", lambda v, c=c: v.tensor_copy(sT[0:8, c * 128:(c + 1) * 128],
                                                      strp(c)),
               reads=(Q + "str",), writes=(Q + "sT",))
    for c in range(nch):
        ch.add("tensor", lambda tt, c=c: tt.matmul(
            expp(c), e816[0:8, :], sT[0:8, c * 128:(c + 1) * 128],
            start=True, stop=True, skip_group_check=True),
            reads=(Q + "sT",), writes=(Q + "exp%d" % (c % 2),))
        ch.add("vector", lambda v, c=c: v.tensor_copy(ex[:, c * 128:(c + 1) * 128], expp(c)),
               reads=(Q + "exp%d" % (c % 2),), writes=(Q + "ex",))
    ch.add("scalar", lambda s: s.activation(ax[:, 0:W], xT[:, 0:W], AF.Abs),
           reads=(Q + "xT",), writes=(Q + "ax",))
    ch.add("vector", lambda v: v.tensor_tensor(ax[:, 0:W], ax[:, 0:W], ex[:, 0:W], ALU.mult),
           reads=(Q + "ax", Q + "ex"), writes=(Q + "ax",))
    ch.add("vector", lambda v: v.tensor_scalar(qb[:, 0:W], ax[:, 0:W], 5.0, None, ALU.is_gt),
           reads=(Q + "ax",), writes=(Q + "qb",))
    for m in (3.5, 2.5):
        ch.add("vector", lambda v, m=m: v.scalar_tensor_tensor(
            qb[:, 0:W], ax[:, 0:W], m, qb[:, 0:W], ALU.is_gt, ALU.add),
            reads=(Q + "ax", Q + "qb"), writes=(Q + "qb",))
    ch.add("vector", lambda v: v.tensor_scalar(qb[:, 0:W], qb[:, 0:W], 2.0, None, ALU.mult),
           reads=(Q + "qb",), writes=(Q + "qb",))
    for m in (1.75, 1.25, 0.75, 0.25):
        ch.add("vector", lambda v, m=m: v.scalar_tensor_tensor(
            qb[:, 0:W], ax[:, 0:W], m, qb[:, 0:W], ALU.is_gt, ALU.add),
            reads=(Q + "ax", Q + "qb"), writes=(Q + "qb",))
    for c in range(nch):
        ch.add("tensor", lambda tt, c=c: tt.transpose(
            strp(c), bm[:, c * 8:(c + 1) * 8], identf[:]),
            reads=(Q + "bm",), writes=(Q + "str",))
        ch.add("vector", lambda v, c=c: v.tensor_copy(sT[0:8, c * 128:(c + 1) * 128],
                                                      strp(c)),
               reads=(Q + "str",), writes=(Q + "sT",))
    for c in range(nch):
        ch.add("tensor", lambda tt, c=c: tt.matmul(
            expp(c), e816[0:8, :], sT[0:8, c * 128:(c + 1) * 128],
            start=True, stop=True, skip_group_check=True),
            reads=(Q + "sT",), writes=(Q + "exp%d" % (c % 2),))
        ch.add("vector", lambda v, c=c: v.tensor_copy(ex[:, c * 128:(c + 1) * 128], expp(c)),
               reads=(Q + "exp%d" % (c % 2),), writes=(Q + "ex",))
    ch.add("scalar", lambda s: s.activation(qt[:, 0:W], xT[:, 0:W], AF.Sign),
           reads=(Q + "xT",), writes=(Q + "qt",))
    ch.add("vector", lambda v: v.tensor_tensor(qt[:, 0:W], qt[:, 0:W], ex[:, 0:W], ALU.mult),
           reads=(Q + "qt", Q + "ex"), writes=(Q + "qt",))
    ch.add("vector", lambda v, dst_off=dst_off, W=W: v.tensor_tensor(
        dst_slab[:, dst_off:dst_off + W], qb[:, 0:W], qt[:, 0:W], ALU.mult),
        reads=(Q + "qb", Q + "qt"), writes=(outkey,))


def emit_ffn(p, nc, t):
    ident, psp, psx, psm, pst = t["ident"], t["psp"], t["psx"], t["psm"], t["pst"]
    identf = t["identf"]
    e816 = t["e816"]
    y1, xn1_d = t["y1"], t["xn1_d"]
    wg8_h, wu8_h, wd8_h = t["wg8_h"], t["wu8_h"], t["wd8_h"]
    wqg_d, wqu_d, wqd_d = t["wqg_d"], t["wqu_d"], t["wqd_d"]

    if not hasattr(p, "c2_alias"):
        p.c2_alias, p.c2_mutex = {}, set()
    p.c2_alias.update({"ffps0": "mxP0", "ffps1": "mxP1"})
    for z in range(2):
        p.c2_alias["q%dxtr0" % z] = "mxP2"
        p.c2_alias["q%dxtr1" % z] = "mxP2"
        p.c2_alias["q%dexp0" % z] = "mxP3"
        p.c2_alias["q%dexp1" % z] = "mxP3"
        p.c2_alias["q%dstr" % z] = "mxPX" if z == 0 else "mxPM"
    p.c2_mutex |= {"mxP0", "mxP1", "mxP2", "mxP3", "mxPX", "mxPM"}

    def rot2(name, shape, dt=F32):
        t = p.sb(name, shape, dt)
        return [t, t]
    qs = {"bm": rot2("q_bm", [128, 192]), "inv": rot2("q_inv", [128, 192]),
          "sT": rot2("q_sT", [8, 2816]),
          "xT": rot2("q_xT", [128, 2816]), "ex": rot2("q_ex", [128, 2816]),
          "ax": rot2("q_ax", [128, 2816]),
          "qb": rot2("q_qb", [128, 2816]), "qt": rot2("q_qt", [128, 2816])}
    wqt = rot2("f_wqt", [128, 2816], BF16)
    xqt = p.sb("f_xqt", [128, 16 * 1024], BF16)

    ch = Chain2(p, tag="fw")
    ti_ = 0
    for (w_h, wq_d, wk) in ((wg8_h, wqg_d, "g"), (wu8_h, wqu_d, "u")):
        for nt in range(22):
            par = 0
            ti_ += 1
            ch.dma(lambda sy, w_h=w_h, nt=nt, par=par: sy.dma_start(
                qs["ax"][par][:, 0:1024], w_h[nt * 128:(nt + 1) * 128, :]),
                reads=(), writes=("q%dax" % par,))
            emit_quant_fm(p, ch, qs["ax"][par][:, 0:1024], 1024, qs, wqt[par], 0,
                          identf, psp, psx, psm, e816, par, "q%dax" % par, "q%dwqt" % par)
            ch.dma(lambda sy, wq_d=wq_d, nt=nt, par=par: sy.dma_start(
                wq_d[:, nt * 128:(nt + 1) * 128].rearrange("(c p) n -> p c n", p=128),
                wqt[par][:, 0:1024].rearrange("p (c n) -> p c n", c=8)),
                reads=("q%dwqt" % par,), writes=())
    for dt_ in range(8):
        par = 0
        ti_ += 1
        ch.dma(lambda sy, dt_=dt_, par=par: sy.dma_start(
            qs["ax"][par][:, 0:2816], wd8_h[dt_ * 128:(dt_ + 1) * 128, :]),
            reads=(), writes=("q%dax" % par,))
        emit_quant_fm(p, ch, qs["ax"][par][:, 0:2816], 2816, qs, wqt[par], 0,
                      identf, psp, psx, psm, e816, par, "q%dax" % par, "q%dwqt" % par)
        ch.dma(lambda sy, dt_=dt_, par=par: sy.dma_start(
            wqd_d[:, dt_ * 128:(dt_ + 1) * 128].rearrange("(c p) n -> p c n", p=128),
            wqt[par][:, 0:2816].rearrange("p (c n) -> p c n", c=22)),
            reads=("q%dwqt" % par,), writes=())
    for i in range(16):
        par = 0
        ti_ += 1
        ch.dma(lambda sy, i=i, par=par: sy.dma_start(
            qs["ax"][par][:, 0:1024], xn1_d[HALO + i * 128: HALO + i * 128 + 128, :]),
            reads=(), writes=("q%dax" % par,))
        emit_quant_fm(p, ch, qs["ax"][par][:, 0:1024], 1024, qs, xqt, i * 1024,
                      identf, psp, psx, psm, e816, par, "q%dax" % par, "xqt")
    ch.emit(nc)

    hgrp = p.sb("f_h", [128, 4 * 2816], BF16)
    hqT = p.sb("f_hqT", [128, 4 * 2816], BF16)
    wrhs = [p.sb("f_wrhs%d" % z, [128, 512], BF16) for z in range(4)]
    gtile = rot2("f_gt", [128, 512], BF16)
    for grp in range(4):
        ch = Chain2(p, tag="ff%d" % grp)
        gi = 0
        wi = 0
        for which, wq_d in ((0, wqg_d), (1, wqu_d)):
            for ti in range(4):
                i = grp * 4 + ti
                for ng in range(6):
                    noff = ng * 512
                    nw = 512 if ng < 5 else 256
                    gp = gi % 2
                    gi += 1
                    for kc in range(8):
                        wp = wi % 4
                        wi += 1
                        ch.dma(lambda sy, wq_d=wq_d, kc=kc, noff=noff, nw=nw, wp=wp: sy.dma_start(
                            wrhs[wp][:, 0:nw], wq_d[kc * 128:(kc + 1) * 128, noff:noff + nw]),
                            reads=(), writes=("wr%d" % wp,))
                        ch.add("tensor", lambda tt, kc=kc, i=i, nw=nw, wp=wp, gp=gp: tt.matmul(
                            psp[gp][:, 0:nw], xqt[:, i * 1024 + kc * 128: i * 1024 + kc * 128 + 128],
                            wrhs[wp][:, 0:nw],
                            start=(kc == 0), stop=(kc == 7), skip_group_check=True),
                            reads=("wr%d" % wp, "xqt"), writes=("ffps%d" % gp,))
                    if which == 0:
                        ch.add("scalar", lambda s, ti=ti, noff=noff, nw=nw, gp=gp: s.activation(
                            hgrp[:, ti * 2816 + noff: ti * 2816 + noff + nw],
                            psp[gp][:, 0:nw], AF.Silu),
                            reads=("ffps%d" % gp,), writes=("hg%d" % ti,))
                    else:
                        ch.add("vector", lambda v, nw=nw, gp=gp: v.tensor_copy(
                            gtile[gp][:, 0:nw], psp[gp][:, 0:nw]),
                            reads=("ffps%d" % gp,), writes=("gt%d" % gp,))
                        ch.add("vector", lambda v, ti=ti, noff=noff, nw=nw, gp=gp: v.tensor_tensor(
                            hgrp[:, ti * 2816 + noff: ti * 2816 + noff + nw],
                            hgrp[:, ti * 2816 + noff: ti * 2816 + noff + nw],
                            gtile[gp][:, 0:nw], ALU.mult),
                            reads=("gt%d" % gp, "hg%d" % ti), writes=("hg%d" % ti,))
        for ti in range(4):
            par = 0
            ch.add("vector", lambda v, ti=ti, par=par: v.tensor_copy(
                qs["ax"][par][:, 0:2816], hgrp[:, ti * 2816:(ti + 1) * 2816]),
                reads=("hg%d" % ti,), writes=("q%dax" % par,))
            emit_quant_fm(p, ch, qs["ax"][par][:, 0:2816], 2816, qs, hqT, ti * 2816,
                          identf, psp, psx, psm, e816, par, "q%dax" % par, "hq%d" % ti)
        for ti in range(4):
            i = grp * 4 + ti
            for half in range(2):
                gp = gi % 2
                gi += 1
                for c in range(22):
                    wp = wi % 4
                    wi += 1
                    ch.dma(lambda sy, c=c, half=half, wp=wp: sy.dma_start(
                        wrhs[wp][:, 0:512], wqd_d[c * 128:(c + 1) * 128, half * 512: half * 512 + 512]),
                        reads=(), writes=("wr%d" % wp,))
                    ch.add("tensor", lambda tt, c=c, ti=ti, wp=wp, gp=gp: tt.matmul(
                        psp[gp][:, 0:512], hqT[:, ti * 2816 + c * 128: ti * 2816 + c * 128 + 128],
                        wrhs[wp][:, 0:512],
                        start=(c == 0), stop=(c == 21), skip_group_check=True),
                        reads=("wr%d" % wp, "hq%d" % ti), writes=("ffps%d" % gp,))
                ch.add("vector", lambda v, gp=gp: v.tensor_copy(gtile[gp][:, 0:512], psp[gp][:, 0:512]),
                       reads=("ffps%d" % gp,), writes=("gt%d" % gp,))
                ch.add("vector", lambda v, i=i, half=half, gp=gp: v.tensor_tensor(
                    y1[:, i * 1024 + half * 512: i * 1024 + half * 512 + 512],
                    y1[:, i * 1024 + half * 512: i * 1024 + half * 512 + 512],
                    gtile[gp][:, 0:512], ALU.add),
                    reads=("gt%d" % gp,), writes=("y1",))
        ch.emit(nc)
    p.barrier(nc)
    return {"xstg": qs["qb"][0], "wtile": qs["qt"][0]}


def build_fused():
    p = Prog("fused")
    nc = p.nc
    x_h = p.din("x", [TCH, D])
    id_h = p.din("identbf", [128, 128], BF16)
    idf_h = p.din("identf32", [128, 128], F32)
    kv_in = nc.dram_tensor("kv_in", [D, D], F32)
    kv_out = nc.dram_tensor("kv_out", [D, D], F32)
    x1_d = nc.dram_tensor("x1d", [TCH, D], F32)
    ot_d = nc.dram_tensor("otd", [D, TC], F32)
    z_d = nc.dram_tensor("zd", [16, 16, 64, 128], BF16)
    pct_d = nc.dram_tensor("pctd", [16, 16, 64, 64], BF16)
    ag_in = nc.dram_tensor("ag_in", [128, 1024], BF16)
    from concourse.replica_groups import maybe_share_collective_output_space
    sp_ag = maybe_share_collective_output_space("AllGather", [[0, 1, 2, 3, 4, 5, 6, 7]])
    ag_out = nc.dram_tensor("ag_out", [8 * 128, 1024], BF16, addr_space=sp_ag)

    h0 = {"x_h": x_h, "wq_h": p.din("wq", [D, D]), "wg_h": p.din("wg", [D, 8]),
          "wl1_h": p.din("wl1", [8, D, 4096]), "wl2_h": p.din("wl2", [8, 2048, D]),
          "id_h": id_h, "x1_h": x1_d,
          "kv_ar": (p.din("wk", [D, D]), p.din("wv", [D, D]), kv_in, kv_out)}
    h1 = {"x1_h": x1_d,
          "wq1_h": p.din("wq1", [D, D]), "wk1_h": p.din("wk1", [D, D]), "wv1_h": p.din("wv1", [D, D]),
          "qcw_h": p.din("qcw", [D, 4]), "qcb_h": p.din("qcb", [128, 8]),
          "kcw_h": p.din("kcw", [D, 4]), "kcb_h": p.din("kcb", [128, 8]),
          "vcw_h": p.din("vcw", [D, 4]), "vcb_h": p.din("vcb", [128, 8]),
          "wa_h": p.din("wa", [D, 16]), "ba_h": p.din("ba", [16, 1]),
          "wb_h": p.din("wb", [D, 16]), "bb_h": p.din("bb", [16, 1]),
          "id_h": id_h, "idf_h": idf_h,
          "mD_h": p.din("maskD", [128, 128], F32), "mG_h": p.din("maskG", [128, 128], F32),
          "mdg_h": p.din("mdg", [128, 128], F32), "e8_h": p.din("e8", [8, 128, 16], BF16),
          "ot_h": ot_d, "z_h": z_d, "pct_h": pct_d,
          "hsel_h": p.din("hsel", [16, 16 * 128], F32),
          "stex": ("ag", ag_in, ag_out)}
    h2 = {"x1_h": x1_d, "ot_h": ot_d, "z_h": z_d, "pct_h": pct_d,
          "pred": ("ag", ag_out, p.din("hmask", [64, 6], F32), p.din("himask", [64, 6 * 64], BF16)),
          "rmsw_h": p.din("rmsw", [128, 8]),
          "wgd_h": p.din("wgd", [D, 512]), "wgu_h": p.din("wgu", [512, D]),
          "wo_h": p.din("wo", [D, D]),
          "wg8_h": p.din("wgate", [2816, D]), "wu8_h": p.din("wup", [2816, D]),
          "wd8_h": p.din("wdown", [D, 2816]),
          "id_h": id_h, "idf_h": idf_h,
          "e8_h": h1["e8_h"], "e2t_h": p.din("e2t", [2, 128], F32),
          "e816_h": p.din("e816", [8, 128], F32),
          "x2_h": p.dout("x2", [TC, D]),
          "xn1_d": nc.dram_tensor("xn1d", [TCH, D], F32),
          "wqg_d": nc.dram_tensor("wqgd", [D, 2816], BF16),
          "wqu_d": nc.dram_tensor("wqud", [D, 2816], BF16),
          "wqd_d": nc.dram_tensor("wqdd", [2816, D], BF16),
          "rr_d": nc.dram_tensor("rrd", [16, 2048], F32)}
    with p.es:
        with nc.named_scope("L0"):
            emit_l0main(p, nc, h0)
        with nc.named_scope("L1loc"):
            emit_l1loc(p, nc, h1)
        with nc.named_scope("L1fix"):
            emit_l1fix(p, nc, h2)
    return nc


def kernel_fused(inp):
    f32 = np.float32
    xf = np.ascontiguousarray(inp["L0_attn_x"].reshape(B * T, D).astype(f32))
    cores = list(range(NCORES))
    xsh = shard_x(xf)
    nc = build_fused()
    rmsw = np.ascontiguousarray(np.asarray(inp["L1_attn_rms_w"], f32).reshape(1024).reshape(8, 128).T)
    maps = []
    for c in cores:
        b0 = (c // 4) * 4
        preds = list(range(b0, c))
        slots = [0, 1, 2, 4, 5, 6]
        hm = np.zeros((64, 6), f32)
        him = np.zeros((64, 6 * 64), f32)
        for l, s in enumerate(slots):
            if s in preds:
                hm[:, l] = 1.0
            else:
                him[:, l * 64:(l + 1) * 64] = np.eye(64, dtype=f32)
        maps.append({
            "x": xsh[c],
            "wq": inp["L0_attn_Wq"], "wk": inp["L0_attn_Wk"], "wv": inp["L0_attn_Wv"],
            "wg": inp["L0_ffn_Wg_DN"], "wl1": inp["L0_ffn_Wl1_ND2H"], "wl2": inp["L0_ffn_Wl2_NHD"],
            "wq1": inp["L1_attn_Wq"], "wk1": inp["L1_attn_Wk"], "wv1": inp["L1_attn_Wv"],
            "qcw": inp["L1_attn_qcw"], "qcb": np.ascontiguousarray(np.asarray(inp["L1_attn_qcb"], f32).reshape(8, 128).T),
            "kcw": inp["L1_attn_kcw"], "kcb": np.ascontiguousarray(np.asarray(inp["L1_attn_kcb"], f32).reshape(8, 128).T),
            "vcw": inp["L1_attn_vcw"], "vcb": np.ascontiguousarray(np.asarray(inp["L1_attn_vcb"], f32).reshape(8, 128).T),
            "wa": inp["L1_attn_Wa"], "ba": np.asarray(inp["L1_attn_ba"], f32).reshape(16, 1),
            "wb": inp["L1_attn_Wb"], "bb": np.asarray(inp["L1_attn_bb"], f32).reshape(16, 1),
            "rmsw": rmsw, "wgd": inp["L1_attn_Wgd"], "wgu": inp["L1_attn_Wgu"],
            "wo": inp["L1_attn_Wo"],
            "wgate": inp["L1_ffn_w_gate"], "wup": inp["L1_ffn_w_up"], "wdown": inp["L1_ffn_w_down"],
            "identbf": IDENT_BF, "identf32": IDENT_F32, "maskD": MASKD, "maskG": MASKG,
            "mdg": MDG, "e8": E8C, "e2t": E2T, "e816": E816, "hsel": HSEL,
            "hmask": hm, "himask": him.astype(BF),
        })
    r = run_bass_kernel_spmd(nc, maps, core_ids=cores)
    x2 = np.concatenate([np.asarray(r.results[c]["x2"]) for c in cores], 0)
    return np.ascontiguousarray(x2.reshape(B, T, D))



IDENT_BF = np.eye(128, dtype=np.float32).astype(BF)
IDENT_F32 = np.eye(128, dtype=np.float32)
MASKD = np.where(np.arange(128)[:, None] < np.arange(128)[None, :], 0.0, -1e30).astype(np.float32)
MASKG = np.where(np.arange(128)[:, None] <= np.arange(128)[None, :], 0.0, -1e30).astype(np.float32)
MDG = (np.arange(128)[:, None] // 32 == np.arange(128)[None, :] // 32).astype(np.float32)
E8C = np.zeros((8, 128, 16), np.float32)
for _dc in range(8):
    for _pp in range(128):
        E8C[_dc, _pp, 2 * _dc + _pp // 64] = 1.0
E8C = E8C.astype(BF)
E2T = np.zeros((2, 128), np.float32)
for _pp in range(128):
    E2T[_pp // 64, _pp] = 1.0
E816 = np.zeros((8, 128), np.float32)
for _kk in range(128):
    E816[_kk // 16, _kk] = 1.0
EYEH = np.tile(np.eye(64, dtype=np.float32), (1, 16)).astype(BF)
HSEL = np.zeros((16, 16 * 128), np.float32)
for _h in range(16):
    HSEL[_h, _h * 128:(_h + 1) * 128] = 1.0


def shard_x(xf):
    out = []
    for c in range(NCORES):
        own = xf[c * TC:(c + 1) * TC]
        if c % 4 == 0:
            h = np.zeros((HALO, D), np.float32)
        else:
            h = xf[c * TC - HALO: c * TC]
        out.append(np.ascontiguousarray(np.concatenate([h, own], 0)))
    return out


def kernel_multi(**inp):
    f32 = np.float32
    xf = np.ascontiguousarray(inp["L0_attn_x"].reshape(B * T, D).astype(f32))
    cores = list(range(NCORES))

    nc1 = build_l0kv()
    maps1 = [{"x": np.ascontiguousarray(xf[c * TC:(c + 1) * TC]),
              "wk": inp["L0_attn_Wk"], "wv": inp["L0_attn_Wv"], "identbf": IDENT_BF}
             for c in cores]
    r1 = run_bass_kernel_spmd(nc1, maps1, core_ids=cores)
    kvp = [np.asarray(r1.results[c]["kvpart"]) for c in cores]

    nc2 = build_l0main()
    xsh = shard_x(xf)
    maps2 = [{"x": xsh[c], "kvp": np.ascontiguousarray(np.stack(kvp[(c // 4) * 4:(c // 4) * 4 + 4])),
              "wq": inp["L0_attn_Wq"], "wg": inp["L0_ffn_Wg_DN"],
              "wl1": inp["L0_ffn_Wl1_ND2H"], "wl2": inp["L0_ffn_Wl2_NHD"],
              "identbf": IDENT_BF} for c in cores]
    r2 = run_bass_kernel_spmd(nc2, maps2, core_ids=cores)
    x1sh = [np.array(r2.results[c]["x1"]) for c in cores]
    for c in cores:
        if c % 4 != 0:
            x1sh[c][0:HALO] = x1sh[c - 1][HALO + TC - HALO: HALO + TC]

    nc3 = build_l1loc()
    maps3 = [{
        "x1": x1sh[c], "wq1": inp["L1_attn_Wq"], "wk1": inp["L1_attn_Wk"], "wv1": inp["L1_attn_Wv"],
        "qcw": inp["L1_attn_qcw"], "qcb": np.ascontiguousarray(np.asarray(inp["L1_attn_qcb"], f32).reshape(8, 128).T),
        "kcw": inp["L1_attn_kcw"], "kcb": np.ascontiguousarray(np.asarray(inp["L1_attn_kcb"], f32).reshape(8, 128).T),
        "vcw": inp["L1_attn_vcw"], "vcb": np.ascontiguousarray(np.asarray(inp["L1_attn_vcb"], f32).reshape(8, 128).T),
        "wa": inp["L1_attn_Wa"], "ba": np.asarray(inp["L1_attn_ba"], f32).reshape(16, 1),
        "wb": inp["L1_attn_Wb"], "bb": np.asarray(inp["L1_attn_bb"], f32).reshape(16, 1),
        "identbf": IDENT_BF, "identf32": IDENT_F32, "maskD": MASKD, "maskG": MASKG,
        "mdg": MDG, "e8": E8C, "hsel": HSEL,
    } for c in cores]
    r3 = run_bass_kernel_spmd(nc3, maps3, core_ids=cores)

    nc4 = build_l1fix()
    rmsw = np.ascontiguousarray(np.asarray(inp["L1_attn_rms_w"], f32).reshape(1024).reshape(8, 128).T)
    maps4 = []
    for c in cores:
        b0 = (c // 4) * 4
        preds = list(range(b0, c))
        predP = [EYEH] * (3 - len(preds)) + [np.asarray(r3.results[j]["PcoreT"]) for j in preds]
        predS = [np.zeros((64, 1024), f32)] * (3 - len(preds)) + \
                [np.asarray(r3.results[j]["Sloc"]) for j in preds]
        maps4.append({
            "x1": x1sh[c], "OT": np.asarray(r3.results[c]["OT"]),
            "Z": np.asarray(r3.results[c]["Z"]), "PcT": np.asarray(r3.results[c]["PcT"]),
            "predP": np.ascontiguousarray(np.stack(predP)), "predS": np.ascontiguousarray(np.stack(predS)),
            "rmsw": rmsw, "wgd": inp["L1_attn_Wgd"], "wgu": inp["L1_attn_Wgu"],
            "wo": inp["L1_attn_Wo"],
            "wgate": inp["L1_ffn_w_gate"], "wup": inp["L1_ffn_w_up"], "wdown": inp["L1_ffn_w_down"],
            "identbf": IDENT_BF, "identf32": IDENT_F32, "e8": E8C, "e2t": E2T, "e816": E816,
        })
    r4 = run_bass_kernel_spmd(nc4, maps4, core_ids=cores)
    x2 = np.concatenate([np.asarray(r4.results[c]["x2"]) for c in cores], 0)
    return np.ascontiguousarray(x2.reshape(B, T, D))


def kernel(**inp):
    import os
    if os.environ.get("MK_MULTI"):
        return kernel_multi(**inp)
    return kernel_fused(inp)
